# version 14
# baseline (speedup 1.0000x reference)
import numpy as np
import ml_dtypes
from contextlib import ExitStack

import concourse.bass as bass
import concourse.mybir as mybir
from concourse.bass_utils import run_bass_kernel_spmd

import concourse.bass_utils as _BU
_orig_run_command = _BU.run_command


def _run_command_ldwopt(cmd, *a, **kw):
    if isinstance(cmd, list):
        cmd = ["--enable-ldw-opt=true" if c == "--enable-ldw-opt=false" else c for c in cmd]
    return _orig_run_command(cmd, *a, **kw)


_BU.run_command = _run_command_ldwopt

F32 = mybir.dt.float32
BF16 = mybir.dt.bfloat16
AF = mybir.ActivationFunctionType
ALU = mybir.AluOpType
AX = mybir.AxisListType.X
BF = ml_dtypes.bfloat16

CONST_VALS = {
    "eps_ln": 1e-5, "eps_l1a": 1e-6, "one": 1.0, "zero": 0.0,
    "m03": -0.3, "p03": 0.3,
    "rm1": -1.0, "rm2": -2.0, "rm3": -3.0, "rm4": -4.0, "rm5": -5.0, "rm6": -6.0,
    "bigneg": -3e8, "negc": -12.0,
}
CONST_NAMES = list(CONST_VALS)
CONST_COL = {n: i for i, n in enumerate(CONST_NAMES)}

B = 8
T = 2048
D = 1024
H = 16
DH = 64
NT = T // 128
NC = D // 128
DHID = 4096
EPS_LN0 = 1e-5
EPS_FFN = 1e-5
EPS_L1A = 1e-6



def _rope_tables():
    d = DH
    inv_freq = 1.0 / (10000.0 ** (np.arange(0, d, 2, dtype=np.float64) / d))
    tpos = np.arange(T, dtype=np.float64)
    freqs = np.outer(tpos, inv_freq)
    full = np.concatenate([freqs, freqs], axis=1)
    cos = np.cos(full).astype(np.float32).T
    sin = np.sin(full).astype(np.float32).T
    cos2 = np.concatenate([cos, cos], axis=0)
    sin2 = np.concatenate([sin, sin], axis=0)
    return np.ascontiguousarray(cos2), np.ascontiguousarray(sin2)


def _rot_matrix():
    P = np.zeros((64, 64), dtype=np.float32)
    for dp in range(32):
        P[dp + 32, dp] = -1.0
    for dp in range(32, 64):
        P[dp - 32, dp] = 1.0
    R128 = np.zeros((128, 128), dtype=np.float32)
    R128[0:64, 0:64] = P
    R128[64:128, 64:128] = P
    return R128


def host_prep(inputs):
    f32 = lambda a: np.ascontiguousarray(np.asarray(a), dtype=np.float32)
    sh = {}
    sh["wq"] = f32(inputs["L0_attn_Wq"])
    sh["wkv"] = f32(inputs["L0_attn_Wkv"])
    sh["wout"] = f32(inputs["L0_attn_Wout"])
    sh["bout_row"] = f32(np.asarray(inputs["L0_attn_bout"]).reshape(1, D))
    cos2, sin2 = _rope_tables()
    sh["cos2"] = cos2
    sh["sin2"] = sin2
    sh["rot128"] = _rot_matrix()
    sh["ident"] = np.eye(128, dtype=np.float32)
    sh["ones_row"] = np.ones((1, 128), dtype=np.float32)
    sh["sigraw"] = f32(inputs["L0_ffn_signatures_raw"])
    sh["knots"] = f32(np.asarray(inputs["L0_ffn_knot_values"]).reshape(1, 8))
    sh["temp"] = f32(np.asarray(inputs["L0_ffn_temperature"]).reshape(1, 1))
    sh["cw1"] = f32(inputs["L0_ffn_compress_w1"])
    sh["cb1_cols"] = f32(np.asarray(inputs["L0_ffn_compress_b1"]).reshape(2, 128).T)
    sh["cw2"] = f32(inputs["L0_ffn_compress_w2"])
    sh["cb2_col"] = f32(np.asarray(inputs["L0_ffn_compress_b2"]).reshape(2, 1))
    w1 = np.asarray(inputs["L0_ffn_spline_w1"], dtype=np.float32)
    W1g = np.zeros((16, 8, 128), dtype=np.float32)
    for grp in range(8):
        for r in range(8):
            W1g[2 * r:2 * r + 2, grp, 16 * r:16 * r + 16] = w1[grp * 8 + r]
    sh["w1g"] = np.ascontiguousarray(W1g.reshape(16, 8 * 128))
    sh["b1g"] = f32(np.asarray(inputs["L0_ffn_spline_b1"]).reshape(8, 128).T)
    w2 = np.asarray(inputs["L0_ffn_spline_w2"], dtype=np.float32)[:, :, 0]
    W2g = np.zeros((128, 8, 64), dtype=np.float32)
    for grp in range(8):
        for r in range(8):
            W2g[16 * r:16 * r + 16, grp, grp * 8 + r] = w2[grp * 8 + r]
    sh["w2g"] = np.ascontiguousarray(W2g.reshape(128, 8 * 64))
    sh["b2_col"] = f32(np.asarray(inputs["L0_ffn_spline_b2"]).reshape(64, 1))
    sh["dirs"] = f32(inputs["L0_ffn_directions"])
    sh["gam_row"] = f32(np.asarray(inputs["L0_ffn_norm_gamma"]).reshape(1, D))
    sh["bet_row"] = f32(np.asarray(inputs["L0_ffn_norm_beta"]).reshape(1, D))
    sh["oscale"] = f32(np.asarray(inputs["L0_ffn_output_scale"]).reshape(1, 1))
    rep = np.zeros((2, 16), dtype=np.float32)
    for r in range(8):
        rep[0, 2 * r] = 1.0
        rep[1, 2 * r + 1] = 1.0
    sh["comp_rep"] = rep
    sh["iota64"] = np.tile(np.arange(64, dtype=np.float32), (128, 1))
    sh["iota64m"] = sh["iota64"] - 64.0
    E = np.zeros((8, 64), dtype=np.float32)
    for r in range(64):
        E[r // 8, r] = 1.0
    sh["expand8"] = E
    G = np.zeros((64, 8), dtype=np.float32)
    for t in range(64):
        G[t, t // 8] = 0.125
    sh["gmean"] = G
    sh["wqkvT"] = np.ascontiguousarray(np.asarray(inputs["L1_attn_w_qkv"], dtype=np.float32).T)
    sh["woutT1"] = np.ascontiguousarray(np.asarray(inputs["L1_attn_w_out"], dtype=np.float32).T)
    sh["mask_incl"] = np.triu(np.ones((128, 128), dtype=np.float32))
    sh["mask4"] = np.ascontiguousarray(np.tile(sh["mask_incl"], (1, 4)))
    for nm, w, s in (("w1", "L1_ffn_w1_w", "L1_ffn_w1_s"), ("w2", "L1_ffn_w2_w", "L1_ffn_w2_s"),
                     ("h1", "L1_ffn_h1_w", "L1_ffn_h1_s"), ("h2", "L1_ffn_h2_w", "L1_ffn_h2_s"),
                     ("w3", "L1_ffn_w3_w", "L1_ffn_w3_s")):
        wa = np.asarray(inputs[w], dtype=np.float32)
        sh[nm + "T"] = np.ascontiguousarray(wa.T)
        sv = np.asarray(inputs[s], dtype=np.float32)
        sh[nm + "s"] = np.ascontiguousarray(sv.reshape(-1, 128).T)
        sh[nm + "s_row"] = np.ascontiguousarray(sv.reshape(1, -1))
    cvals = [CONST_VALS[n] for n in CONST_NAMES]
    sh["sconsts"] = np.tile(np.asarray(cvals, dtype=np.float32), (128, 1))
    x = np.asarray(inputs["L0_attn_x"], dtype=np.float32)
    xs = [np.ascontiguousarray(x[c]) for c in range(B)]
    return sh, xs



ENGINES = ("sync", "gpsimd", "vector", "scalar", "tensor")


class Prog:
    def __init__(self, nc, ctx):
        self.nc = nc
        self.ctx = ctx
        self.block = ctx.enter_context(nc.Block())
        self.cnt = {}
        self.hsems = {}
        self.nbar = 0

    _ALIAS = {"vone": "mz", "stT": "st", "cast": "ca",
              "stxa0": "stf0", "stxa1": "stf1",
              "sus": "ca", "suv": "dv"}

    @classmethod
    def _canon(cls, name):
        if name.startswith("sw"):
            return name
        if "_" in name:
            name = name.split("_", 1)[1]
        return cls._ALIAS.get(name, name)

    def sem(self, name):
        name = self._canon(name)
        if name not in self.hsems:
            self.hsems[name] = self.ctx.enter_context(self.nc.semaphore(name))
            self.cnt[name] = 0
        return self.hsems[name]

    def add(self, name, d=1):
        name = self._canon(name)
        self.sem(name)
        self.cnt[name] += d
        return self.cnt[name]

    def val(self, name):
        name = self._canon(name)
        self.sem(name)
        return self.cnt[name]

    def swdge(self, q, out, in_):
        n = f"sw{len([k for k in self.hsems if k.startswith('sw')])}"
        s = self.sem(n)
        q.dma_start(out=out, in_=in_).then_inc(s, 16)
        self.add(n, 16)
        return s, 16

    def barrier_sync(self):
        ph = self.sem("ph")
        self.nbar += 1
        target = self.nbar
        waits = [(self.hsems[n], self.cnt[n]) for n in list(self.hsems)
                 if n != "ph" and self.cnt[n] > 0]

        def syncb(q):
            for s, v in waits:
                q.wait_ge(s, v)
            q.nop().then_inc(ph, 1)
        self.block.sync(syncb)

        def body(q):
            q.wait_ge(ph, target)
        for eng in ENGINES:
            getattr(self.block, eng)(body)
        self.cnt["ph"] = target

    def barrier(self):
        ph = self.sem("ph")
        self.nbar += 1
        target = 5 * self.nbar

        def body(q):
            q.nop().then_inc(ph, 1)
            q.wait_ge(ph, target)
        for eng in ENGINES:
            getattr(self.block, eng)(body)
        self.cnt["ph"] = target


def dram_io(nc, name, shape, dtype, kind):
    return nc.dram_tensor(name, list(shape), dtype, kind=kind).ap()



def emit_ln(p, x_hbm, xn_tok_hbm, xnT_hbm, ident_bf, cpool, tag, write_xn_tok=True,
            xnT_res=None, store_res=False, xt_res=None, xgate=None, defer_store=False):
    nc, blk = p.nc, p.block
    ctx = ExitStack()
    sb = lambda n, shp, dt: ctx.enter_context(nc.sbuf_tensor(tag + n, list(shp), dt))
    ps = lambda n, shp, dt: ctx.enter_context(nc.psum_tensor(tag + n, list(shp), dt))
    xt = xt_res if xt_res is not None else [sb(f"_xt{i}", (128, D), F32) for i in range(2)]
    NS = len(xt)
    nsum = sb("_nsum", (128, NT), F32)
    sqc = sb("_sq", (128, NT), F32)
    sqscr = sb("_sqscr", (128, D), F32)
    nmu = sb("_nmu", (128, NT), F32)
    musq = sb("_musq", (128, NT), F32)
    varc = sb("_varc", (128, NT), F32)
    sdev = sb("_sdev", (128, NT), F32)
    rstd = sb("_rstd", (128, NT), F32)
    bcol = sb("_bcol", (128, NT), F32)
    xnf = [sb(f"_xnf{i}", (128, D), F32) for i in range(2)]
    xnb = [sb(f"_xnb{i}", (128, D), BF16) for i in range(2)]
    xnT_full = xnT_res if xnT_res is not None else sb("_xnTf", (128, NC * T), BF16)
    ptp = [ps(f"_ptp{u}", (128, 128), BF16) for u in range(8)]

    ldx = [p.sem(tag + f"_ldx{s}") for s in range(NS)]
    stf = [p.sem(tag + f"_lnf{s}") for s in range(2)]
    lnv = p.sem(tag + "_lnv")
    lns = p.sem(tag + "_lns")
    af = p.sem(tag + "_af")
    tp = p.sem(tag + "_tp")
    ev = p.sem(tag + "_ev")
    stT = p.sem(tag + "_stT")

    b_ldx = [p.val(tag + f"_ldx{s}") for s in range(NS)]
    b_stf = [p.val(tag + f"_lnf{s}") for s in range(2)]
    b_lnv = p.val(tag + "_lnv")
    b_lns = p.val(tag + "_lns")
    b_af = p.val(tag + "_af")
    b_tp = p.val(tag + "_tp")
    b_ev = p.val(tag + "_ev")
    b_stT = p.val(tag + "_stT")
    for i in range(NT):
        p.add(tag + f"_ldx{i % NS}", 16)
        p.add(tag + "_lnv", 6)
        p.add(tag + "_lns", 2)
        p.add(tag + "_af", 2)
        p.add(tag + "_tp", 8)
        p.add(tag + "_ev", 8)
        if write_xn_tok:
            p.add(tag + f"_lnf{i % 2}", 16)
    if (xnT_res is None or store_res) and not defer_store:
        p.add(tag + "_stT", 16 * NC)
    stT_target = p.val(tag + "_stT")

    def sync1(q):
        for i in range(NT):
            if xgate is not None:
                q.wait_ge(xgate[0][i % 2], xgate[1][i % 2] + 16 * (i // 2 + 1))
            if i >= NS:
                q.wait_ge(af, b_af + 2 * (i - NS) + 2)
                q.wait_ge(lns, b_lns + 2 * (i - NS) + 1)
            q.dma_start(out=xt[i % NS][:], in_=x_hbm[128 * i:128 * (i + 1), :]).then_inc(ldx[i % NS], 16)
    blk.sync(sync1)

    def vec1(q):
        for i in range(NT):
            n0 = b_lnv + 6 * i
            q.wait_ge(ldx[i % NS], b_ldx[i % NS] + 16 * (i // NS + 1))
            q.reduce_sum(nsum[:, i:i + 1], xt[i % NS][:], axis=AX, negate=True).then_inc(lnv, 1)
            q.wait_ge(lnv, n0 + 1)
            q.tensor_scalar(nmu[:, i:i + 1], nsum[:, i:i + 1], 1.0 / D, None, ALU.mult).then_inc(lnv, 1)
            q.wait_ge(lnv, n0 + 2)
            q.tensor_tensor(musq[:, i:i + 1], nmu[:, i:i + 1], nmu[:, i:i + 1], ALU.mult).then_inc(lnv, 1)
            q.wait_ge(lnv, n0 + 3)
            q.wait_ge(lns, b_lns + 2 * i + 1)
            q.scalar_tensor_tensor(varc[:, i:i + 1], sqc[:, i:i + 1], 1.0 / D,
                                   musq[:, i:i + 1], ALU.mult, ALU.subtract).then_inc(lnv, 1)
            q.wait_ge(lns, b_lns + 2 * i + 2)
            q.reciprocal(rstd[:, i:i + 1], sdev[:, i:i + 1]).then_inc(lnv, 1)
            q.wait_ge(lnv, n0 + 5)
            q.tensor_tensor(bcol[:, i:i + 1], nmu[:, i:i + 1], rstd[:, i:i + 1], ALU.mult).then_inc(lnv, 1)
            q.wait_ge(lnv, n0 + 6)
            if i >= 2:
                q.wait_ge(tp, b_tp + 8 * (i - 1))
                if write_xn_tok:
                    q.wait_ge(stf[i % 2], b_stf[i % 2] + 16 * ((i - 2) // 2 + 1))
            if write_xn_tok:
                q.tensor_scalar(xnf[i % 2][:], xt[i % NS][:], rstd[:, i:i + 1],
                                bcol[:, i:i + 1], ALU.mult, ALU.add).then_inc(af, 1)
                q.wait_ge(af, b_af + 2 * i + 1)
                q.tensor_copy(xnb[i % 2][:], xnf[i % 2][:]).then_inc(af, 1)
            else:
                q.tensor_scalar(xnb[i % 2][:], xt[i % NS][:], rstd[:, i:i + 1],
                                bcol[:, i:i + 1], ALU.mult, ALU.add).then_inc(af, 2)
    blk.vector(vec1)

    def scal1(q):
        for i in range(NT):
            q.wait_ge(ldx[i % NS], b_ldx[i % NS] + 16 * (i // NS + 1))
            if i > 0:
                q.wait_ge(lns, b_lns + 2 * i - 1)
            q.activation(sqscr[:], xt[i % NS][:], AF.Square,
                         accum_out=sqc[:, i:i + 1]).then_inc(lns, 1)
            q.wait_ge(lnv, b_lnv + 6 * i + 4)
            q.activation(sdev[:, i:i + 1], varc[:, i:i + 1], AF.Sqrt,
                         bias=cpool[:, CONST_COL["eps_ln"]:CONST_COL["eps_ln"] + 1]).then_inc(lns, 1)
            if i >= 1:
                for c in range(NC):
                    q.wait_ge(tp, b_tp + 8 * (i - 1) + c + 1)
                    q.copy(xnT_full[:, c * T + 128 * (i - 1): c * T + 128 * i],
                           ptp[c][:]).then_inc(ev, 1)
        for c in range(NC):
            q.wait_ge(tp, b_tp + 8 * (NT - 1) + c + 1)
            q.copy(xnT_full[:, c * T + 128 * (NT - 1): c * T + 128 * NT],
                   ptp[c][:]).then_inc(ev, 1)
    blk.scalar(scal1)

    def gp1(q):
        if write_xn_tok:
            for i in range(NT):
                q.wait_ge(af, b_af + 2 * i + 2)
                q.dma_start(out=xn_tok_hbm[128 * i:128 * (i + 1), :],
                            in_=xnf[i % 2][:]).then_inc(stf[i % 2], 16)
    blk.gpsimd(gp1)

    def tens1(q):
        for i in range(NT):
            q.wait_ge(af, b_af + 2 * i + 2)
            for c in range(NC):
                g = 8 * i + c
                if g >= 8:
                    q.wait_ge(ev, b_ev + g - 7)
                q.transpose(ptp[c][:], xnb[i % 2][:, 128 * c:128 * (c + 1)],
                            ident_bf[:]).then_inc(tp, 1)
    blk.tensor(tens1)

    def sync3(q):
        if xnT_res is None or store_res:
            q.wait_ge(ev, b_ev + 8 * NT)
            for c in range(NC):
                q.dma_start(out=xnT_hbm[128 * c:128 * (c + 1), :],
                            in_=xnT_full[:, c * T:(c + 1) * T]).then_inc(stT, 16)
            if not defer_store:
                q.wait_ge(stT, stT_target)
        if write_xn_tok:
            for s in range(2):
                q.wait_ge(stf[s], b_stf[s] + 16 * (NT // 2))
    blk.sync(sync3)
    ctx.close()

def _const_loader(p, ctx, consts):
    nc = p.nc
    out = {}
    ldc = p.sem("cst_ld")
    ccc = p.sem("cst_cast")
    loads = []
    casts = []
    for name, (hbm, shape, want) in consts.items():
        if want == "bf16":
            f32t = ctx.enter_context(nc.sbuf_tensor("cst_f_" + name, list(shape), F32))
            bft = ctx.enter_context(nc.sbuf_tensor("cst_b_" + name, list(shape), BF16))
            loads.append((f32t, hbm))
            casts.append((bft, f32t))
            out[name] = bft
            out[name + "_f32"] = f32t
        else:
            f32t = ctx.enter_context(nc.sbuf_tensor("cst_f_" + name, list(shape), F32))
            loads.append((f32t, hbm))
            out[name] = f32t
    nld = len(loads)

    def s0(q):
        for dst, src in loads:
            q.dma_start(out=dst[:], in_=src[:]).then_inc(ldc, 16)
    p.block.sync(s0)
    p.add("cst_ld", 16 * nld)

    def v0(q):
        q.wait_ge(ldc, p.val("cst_ld"))
        for dst, src in casts:
            q.tensor_copy(dst[:], src[:])
        q.engine_nop().then_inc(ccc, 1)
    p.block.vector(v0)
    p.add("cst_cast", 1)

    def w0(q):
        q.wait_ge(ccc, p.val("cst_cast"))
    for e in ENGINES:
        getattr(p.block, e)(w0)
    return out


def build_ln_program(write_xn_tok=True):
    nc = bass.Bass("TRN2", target_bir_lowering=False)
    x = dram_io(nc, "x", (T, D), F32, "ExternalInput")
    identf = dram_io(nc, "ident", (128, 128), F32, "ExternalInput")
    scon = dram_io(nc, "sconsts", (128, len(CONST_NAMES)), F32, "ExternalInput")
    xn_tok = dram_io(nc, "xn_tok", (T, D), F32, "ExternalOutput") if write_xn_tok else None
    xnT = dram_io(nc, "xnT", (D, T), BF16, "ExternalOutput")
    with ExitStack() as ctx:
        p = Prog(nc, ctx)
        cs = _const_loader(p, ctx, {"ident": (identf, (128, 128), "bf16"),
                                    "sconsts": (scon, (128, len(CONST_NAMES)), "f32")})
        emit_ln(p, x, xn_tok, xnT, cs["ident"], cs["sconsts"], "ln0", write_xn_tok=write_xn_tok)
    return nc




def emit_l0_qkv(p, xnT_hbm, wq_hbm, wkv_hbm, cos_hbm, sin_hbm, rot_hbm, ident_bf,
                qT_hbm, kvT_hbm, kvtok_hbm, tag="b1", xnT_res=None, res_out=None,
                pre=None):
    nc, blk = p.nc, p.block
    ctx = ExitStack()
    sb = lambda n, shp, dt: ctx.enter_context(nc.sbuf_tensor(tag + n, list(shp), dt))
    pctx = ExitStack()
    psb = lambda n, shp, dt: pctx.enter_context(nc.psum_tensor(tag + n, list(shp), dt))
    HT = T // 2

    if pre is not None:
        pre_sems, pb = pre
        wqb, wkvb, cosb, sinf, rotb = (pb["wqb"], pb["wkvb"], pb["cosb"],
                                       pb["sinf"], pb["rotb"])
    else:
        wqb = sb("_wq", (128, NC * D), BF16)
        wkvb = sb("_wkv", (128, NC * D), BF16)
        cosb = sb("_cos", (128, T), BF16)
        sinf = sb("_sin", (128, T), F32)
        rotb = sb("_rot", (128, 128), BF16)
    xnT = xnT_res if xnT_res is not None else sb("_xnT", (128, NC * T), BF16)
    tbf = [sb(f"_tbf{i}", (128, HT), BF16) for i in range(2)]
    tcb = [sb(f"_tcb{i}", (128, HT), BF16) for i in range(2)]
    tsb = [sb(f"_tsb{i}", (128, HT), BF16) for i in range(2)]
    if res_out is not None:
        qT, kvT = res_out[0], res_out[1]
    else:
        qT = sb("_qTr", (128, NC * T), BF16)
        kvT = sb("_kvTr", (128, NC * T), BF16)
    pp = [psb(f"_pp{i}", (128, HT), F32) for i in range(2)]
    pr = [psb(f"_pr{i}", (128, HT), F32) for i in range(2)]

    ld = p.sem(tag + "_ld")
    mm = p.sem(tag + "_mm")
    ca = p.sem(tag + "_ca")
    dv = p.sem(tag + "_dv")
    st = p.sem(tag + "_st")
    b_ld = p.val(tag + "_ld")
    b_mm = p.val(tag + "_mm")
    b_ca = p.val(tag + "_ca")
    b_dv = p.val(tag + "_dv")
    b_st = p.val(tag + "_st")
    if pre is None:
        p.add(tag + "_ld", 16)
    NIDX = 2 * NC * 2
    for idx in range(NIDX):
        p.add(tag + "_mm", 2)
        p.add(tag + "_ca", 1)
        p.add(tag + "_dv", 3)
    if res_out is None:
        for i in range(2 * NC):
            p.add(tag + "_st", 16)

    def decode(idx):
        c = idx // 4
        v = (idx // 2) % 2
        hf = idx % 2
        return c, v, hf

    gsems = [] if pre is None else list(pre_sems)

    def gp0(q):
        if xnT_res is None:
            gsems.append(p.swdge(q, xnT.rearrange("p (c t) -> p c t", c=NC), xnT_hbm.rearrange("(c p) t -> p c t", p=128)))
        if pre is None:
            gsems.append(p.swdge(q, wqb.rearrange("p (c o) -> p c o", c=NC), wq_hbm.rearrange("(c p) o -> p c o", p=128)))
            gsems.append(p.swdge(q, wkvb.rearrange("p (c o) -> p c o", c=NC), wkv_hbm.rearrange("(c p) o -> p c o", p=128)))
            gsems.append(p.swdge(q, cosb[:], cos_hbm[:]))
            gsems.append(p.swdge(q, rotb[:], rot_hbm[:]))
    if xnT_res is None or pre is None:
        blk.gpsimd(gp0)

    def sy0(q):
        q.dma_start(out=sinf[:], in_=sin_hbm[:]).then_inc(ld, 16)
    if pre is None:
        blk.sync(sy0)

    def tens(q):
        if pre is None:
            q.wait_ge(ld, b_ld + 16)
        for s, tgt in gsems:
            q.wait_ge(s, tgt)
        for idx in range(NIDX):
            c, v, hf = decode(idx)
            par = idx % 2
            w = wqb if v == 0 else wkvb
            if idx >= 2:
                q.wait_ge(ca, b_ca + (idx - 2) + 1)
            for k in range(NC):
                for ns in range(2):
                    t0 = hf * HT + 512 * ns
                    ins = q.matmul(pp[par][:, 512 * ns:512 * (ns + 1)],
                                   w[:, k * D + 128 * c:k * D + 128 * (c + 1)],
                                   xnT[:, k * T + t0:k * T + t0 + 512],
                                   start=(k == 0), stop=(k == NC - 1))
            ins.then_inc(mm, 1)
            q.wait_ge(ca, b_ca + idx + 1)
            if idx >= 2:
                q.wait_ge(dv, b_dv + 3 * (idx - 2) + 2)
            for ns in range(2):
                ins = q.matmul(pr[par][:, 512 * ns:512 * (ns + 1)], rotb[:],
                               tbf[par][:, 512 * ns:512 * (ns + 1)], start=True, stop=True)
            ins.then_inc(mm, 1)
    blk.tensor(tens)

    def scal(q):
        for idx in range(NIDX):
            par = idx % 2
            q.wait_ge(mm, b_mm + 2 * idx + 1)
            if idx >= 2:
                q.wait_ge(dv, b_dv + 3 * (idx - 2) + 1)
                q.wait_ge(mm, b_mm + 2 * (idx - 2) + 2)
            q.copy(tbf[par][:], pp[par][:]).then_inc(ca, 1)
    blk.scalar(scal)

    def vec(q):
        for idx in range(NIDX):
            c, v, hf = decode(idx)
            par = idx % 2
            dst = qT if v == 0 else kvT
            hslice = lambda t: t[:, hf * HT:(hf + 1) * HT]
            q.wait_ge(ca, b_ca + idx + 1)
            if idx >= 2:
                q.wait_ge(dv, b_dv + 3 * (idx - 2) + 3)
            q.tensor_tensor(tcb[par][:], tbf[par][:], hslice(cosb), ALU.mult).then_inc(dv, 1)
            q.wait_ge(mm, b_mm + 2 * idx + 2)
            q.tensor_tensor(tsb[par][:], pr[par][:], hslice(sinf), ALU.mult).then_inc(dv, 1)
            q.wait_ge(dv, b_dv + 3 * idx + 2)
            q.tensor_tensor(dst[:, c * T + hf * HT:c * T + (hf + 1) * HT],
                            tcb[par][:], tsb[par][:], ALU.add).then_inc(dv, 1)
    blk.vector(vec)

    def sy1(q):
        for i in range(2 * NC):
            c, v = i // 2, i % 2
            q.wait_ge(dv, b_dv + 3 * (4 * c + 2 * v + 1) + 3)
            src = qT if v == 0 else kvT
            dst = qT_hbm if v == 0 else kvT_hbm
            q.dma_start(out=dst[128 * c:128 * (c + 1), :],
                        in_=src[:, c * T:(c + 1) * T]).then_inc(st, 16)
    if res_out is None:
        blk.sync(sy1)

    pctx.close()
    ktok = res_out[2] if res_out is not None else sb("_ktok", (128, H * 16 * 65), BF16)

    tp = p.sem(tag + "_tp")
    ev = p.sem(tag + "_ev")
    vone = p.sem(tag + "_vone")
    stk = p.sem(tag + "_stk") if res_out is None else None
    b_tp = p.val(tag + "_tp")
    b_ev = p.val(tag + "_ev")
    b_stk = p.val(tag + "_stk") if res_out is None else 0
    for c in range(NC):
        p.add(tag + "_tp", 16)
        p.add(tag + "_ev", 16)
    p.add(tag + "_vone", 1)
    if res_out is None:
        p.add(tag + "_stk", 16 * H)
    pctx2 = ExitStack()
    ptt = [pctx2.enter_context(nc.psum_tensor(tag + f"_pt{i}", [128, 128], BF16)) for i in range(4)]

    def tens2(q):
        for c in range(NC):
            q.wait_ge(dv, b_dv + 3 * (4 * c + 3) + 3)
            for j in range(16):
                g = 16 * c + j
                if g >= 4:
                    q.wait_ge(ev, b_ev + g - 3)
                q.transpose(ptt[g % 4][:], kvT[:, c * T + 128 * j:c * T + 128 * (j + 1)],
                            ident_bf[:]).then_inc(tp, 1)
    blk.tensor(tens2)

    def vec2(q):
        for h in range(H):
            q.memset(ktok.rearrange("p (h j e) -> p h j e", h=H, j=16)[:, h, :, 64:65], 1.0)
        q.engine_nop().then_inc(vone, 1)
    blk.vector(vec2)

    def scal2(q):
        for c in range(NC):
            hA, hB = 2 * c, 2 * c + 1
            for j in range(16):
                g = 16 * c + j
                q.wait_ge(tp, b_tp + g + 1)
                q.copy(ktok[:, hA * 16 * 65 + j * 65:hA * 16 * 65 + j * 65 + 64],
                       ptt[g % 4][:, 0:64])
                q.copy(ktok[:, hB * 16 * 65 + j * 65:hB * 16 * 65 + j * 65 + 64],
                       ptt[g % 4][:, 64:128]).then_inc(ev, 1)
    blk.scalar(scal2)

    def sy2(q):
        q.wait_ge(ev, b_ev + 16 * NC)
        q.wait_ge(vone, 1)
        for h in range(H):
            q.dma_start(out=kvtok_hbm.rearrange("(h j p) e -> h p j e", p=128, j=16)[h],
                        in_=ktok.rearrange("p (h j e) -> p h j e", h=H, j=16)[:, h]).then_inc(stk, 16)
        q.wait_ge(stk, b_stk + 16 * H)
        q.wait_ge(st, b_st + 16 * 2 * NC)
    if res_out is None:
        blk.sync(sy2)
    pctx2.close()
    ctx.close()



def emit_l0_attn(p, qT_hbm, kvT_hbm, kvtok_hbm, cpool, ones_row_sb, xoT_hbm, tag="b2",
                 res_in=None, xoT_res=None):
    nc, blk = p.nc, p.block
    ctx = ExitStack()
    sb = lambda n, shp, dt: ctx.enter_context(nc.sbuf_tensor(tag + n, list(shp), dt))
    pctx = ExitStack()
    psb = lambda n, shp, dt: pctx.enter_context(nc.psum_tensor(tag + n, list(shp), dt))

    if res_in is not None:
        qT, kvT, ktok = res_in
    else:
        qT = sb("_qT", (128, NC * T), BF16)
        kvT = sb("_kvT", (128, NC * T), BF16)
        ktok = sb("_ktok", (128, H * 16 * 65), BF16)
    PT = [sb(f"_PT{i}", (128, T), BF16) for i in range(4)]
    zsb = sb("_zsb", (1, T), F32)
    zln = sb("_zln", (1, T), F32)
    zr = sb("_zr", (1, T), F32)
    zbc = sb("_zbc", (128, 512), F32)
    rot2 = res_in is not None and xoT_res is None
    if xoT_res is not None:
        xoT = xoT_res
    elif rot2:
        xoT = sb("_xoT", (128, 2 * T), BF16)
    else:
        xoT = sb("_xoT", (128, NC * T), BF16)
    pOut = psb("_pOut", (128, T), F32)
    pS = [psb(f"_pS{i}", (128, T // 2), F32) for i in range(2)]

    negc = cpool[:, CONST_COL["negc"]:CONST_COL["negc"] + 1]

    ld = p.sem(tag + "_ld")
    sS = p.sem(tag + "_sS")
    sE = p.sem(tag + "_sE")
    sPV = p.sem(tag + "_sPV")
    sZ = p.sem(tag + "_sZ")
    sB = p.sem(tag + "_sB")
    sEv = p.sem(tag + "_sEv")
    sN = p.sem(tag + "_sN")
    st = p.sem(tag + "_st")
    b_ld = p.val(tag + "_ld")
    b = {}
    for n in ("sS", "sE", "sPV", "sZ", "sB", "sEv", "sN"):
        b[n] = p.val(tag + "_" + n)
    b_st = p.val(tag + "_st")
    if res_in is None:
        p.add(tag + "_ld", 16 * 3)
    for h in range(H):
        p.add(tag + "_sS", 32)
        p.add(tag + "_sE", 32)
        p.add(tag + "_sPV", 16)
        p.add(tag + "_sZ", 3)
        p.add(tag + "_sB", 4)
        p.add(tag + "_sEv", 4)
        p.add(tag + "_sN", 4)
    stf = [p.sem(tag + f"_stf{s}") for s in range(2)] if rot2 else None
    b_stf = [p.val(tag + f"_stf{s}") for s in range(2)] if rot2 else None
    if xoT_res is None:
        if rot2:
            for cc in range(NC):
                p.add(tag + f"_stf{cc % 2}", 16)
        else:
            p.add(tag + "_st", 16 * NC)
    per = {"sS": 32, "sE": 32, "sPV": 16, "sZ": 3, "sB": 4, "sEv": 4, "sN": 4}

    def hb(name, h):
        return b[name] + per[name] * h

    def sy0(q):
        q.dma_start(out=qT.rearrange("p (c t) -> p c t", c=NC),
                    in_=qT_hbm.rearrange("(c p) t -> p c t", p=128)).then_inc(ld, 16)
        q.dma_start(out=kvT.rearrange("p (c t) -> p c t", c=NC),
                    in_=kvT_hbm.rearrange("(c p) t -> p c t", p=128)).then_inc(ld, 16)
        q.dma_start(out=ktok.rearrange("p (h j e) -> p h j e", h=H, j=16),
                    in_=kvtok_hbm.rearrange("(h j p) e -> p h j e", p=128, j=16)).then_inc(ld, 16)
    if res_in is None:
        blk.sync(sy0)

    def tens(q):
        if res_in is None:
            q.wait_ge(ld, b_ld + 48)
        for h in range(H):
            c, rb = h // 2, 64 * (h % 2)
            for j in range(16):
                for piece in range(2):
                    if j > 0:
                        q.wait_ge(sE, hb("sE", h) + 2 * (j - 1) + piece + 1)
                    elif h > 0:
                        q.wait_ge(sE, hb("sE", h - 1) + 32)
                    for ns in range(2):
                        qs = 2 * piece + ns
                        ins = q.matmul(pS[piece][:, 512 * ns:512 * (ns + 1)],
                                       kvT[rb:rb + 64, c * T + 128 * j:c * T + 128 * (j + 1)],
                                       qT[rb:rb + 64, c * T + 512 * qs:c * T + 512 * (qs + 1)],
                                       start=True, stop=True)
                    ins.then_inc(sS, 1)
                if j == 2 and h > 0:
                    q.wait_ge(sZ, hb("sZ", h - 1) + 3)
                    for qs in range(4):
                        ins = q.matmul(pOut[64:128, 512 * qs:512 * (qs + 1)], ones_row_sb[0:1, 0:64],
                                       zr[0:1, 512 * qs:512 * (qs + 1)], start=True, stop=True)
                        ins.then_inc(sB, 1)
                jj = j - 3
                if jj >= 0:
                    q.wait_ge(sE, hb("sE", h) + 2 * jj + 2)
                    if jj == 0 and h > 0:
                        q.wait_ge(sN, hb("sN", h - 1) + 4)
                    for qs in range(4):
                        ins = q.matmul(pOut[0:65, 512 * qs:512 * (qs + 1)],
                                       ktok[:, h * 16 * 65 + 65 * jj:h * 16 * 65 + 65 * jj + 65],
                                       PT[jj % 4][:, 512 * qs:512 * (qs + 1)],
                                       start=(jj == 0), stop=False)
                    ins.then_inc(sPV, 1)
            for jj in (13, 14, 15):
                q.wait_ge(sE, hb("sE", h) + 2 * jj + 2)
                for qs in range(4):
                    ins = q.matmul(pOut[0:65, 512 * qs:512 * (qs + 1)],
                                   ktok[:, h * 16 * 65 + 65 * jj:h * 16 * 65 + 65 * jj + 65],
                                   PT[jj % 4][:, 512 * qs:512 * (qs + 1)],
                                   start=False, stop=(jj == 15))
                ins.then_inc(sPV, 1)
        q.wait_ge(sZ, hb("sZ", H - 1) + 3)
        for qs in range(4):
            ins = q.matmul(pOut[64:128, 512 * qs:512 * (qs + 1)], ones_row_sb[0:1, 0:64],
                           zr[0:1, 512 * qs:512 * (qs + 1)], start=True, stop=True)
            ins.then_inc(sB, 1)
    blk.tensor(tens)

    def zpart1(q, hh):
        q.wait_ge(sPV, hb("sPV", hh) + 16)
        q.copy(zsb[:], pOut[64:65, :]).then_inc(sZ, 1)
        q.wait_ge(sZ, hb("sZ", hh) + 1)
        q.activation(zln[:], zsb[:], AF.Ln).then_inc(sZ, 1)

    def zpart2(q, hh):
        q.wait_ge(sZ, hb("sZ", hh) + 2)
        q.activation(zr[:], zln[:], AF.Exp, scale=-1.0).then_inc(sZ, 1)

    def zpart3(q, hh):
        for qs in range(4):
            q.wait_ge(sB, hb("sB", hh) + qs + 1)
            if qs > 0:
                q.wait_ge(sN, hb("sN", hh) + qs)
            elif hh > 0:
                q.wait_ge(sN, hb("sN", hh - 1) + 4)
            q.copy(zbc[0:64, :], pOut[64:128, 512 * qs:512 * (qs + 1)]).then_inc(sEv, 1)

    def scal(q):
        for h in range(H):
            for j in range(16):
                for piece in range(2):
                    q.wait_ge(sS, hb("sS", h) + 2 * j + piece + 1)
                    if j >= 4:
                        q.wait_ge(sPV, hb("sPV", h) + j - 3)
                    elif h > 0:
                        q.wait_ge(sPV, hb("sPV", h - 1) + 16)
                    q.activation(PT[j % 4][:, 1024 * piece:1024 * (piece + 1)],
                                 pS[piece][:], AF.Exp, bias=negc, scale=0.125).then_inc(sE, 1)
                if h > 0:
                    if j == 0:
                        zpart1(q, h - 1)
                    elif j == 1:
                        zpart2(q, h - 1)
                    elif j == 2:
                        zpart3(q, h - 1)
        zpart1(q, H - 1)
        zpart2(q, H - 1)
        zpart3(q, H - 1)
    blk.scalar(scal)

    def vec(q):
        for h in range(H):
            c, rb = h // 2, 64 * (h % 2)
            dc = (c % 2) if rot2 else c
            for qs in range(4):
                q.wait_ge(sEv, hb("sEv", h) + qs + 1)
                if rot2 and h % 2 == 0 and qs == 0 and c >= 2:
                    q.wait_ge(stf[c % 2], b_stf[c % 2] + 16 * ((c - 2) // 2 + 1))
                q.tensor_tensor(xoT[rb:rb + 64, dc * T + 512 * qs:dc * T + 512 * (qs + 1)],
                                pOut[0:64, 512 * qs:512 * (qs + 1)], zbc[0:64, :],
                                ALU.mult).then_inc(sN, 1)
    blk.vector(vec)

    def sy1(q):
        for cc in range(NC):
            q.wait_ge(sN, hb("sN", 2 * cc + 1) + 4)
            sc2 = (cc % 2) if rot2 else cc
            q.dma_start(out=xoT_hbm[128 * cc:128 * (cc + 1), :],
                        in_=xoT[:, sc2 * T:(sc2 + 1) * T]).then_inc(stf[cc % 2] if rot2 else st, 16)
        if rot2:
            for s in range(2):
                q.wait_ge(stf[s], b_stf[s] + 16 * (NC // 2))
        else:
            q.wait_ge(st, b_st + 16 * NC)
    if xoT_res is None:
        blk.sync(sy1)
    pctx.close()
    ctx.close()



def emit_out_proj(p, xoT_hbm, w_hbm, brow_hbm, ones_row_sb, s_hbm, tag="b3",
                  with_bias=True, xoT_res=None):
    nc, blk = p.nc, p.block
    ctx = ExitStack()
    sb = lambda n, shp, dt: ctx.enter_context(nc.sbuf_tensor(tag + n, list(shp), dt))
    pctx = ExitStack()
    psb = lambda n, shp, dt: pctx.enter_context(nc.psum_tensor(tag + n, list(shp), dt))
    xoT = xoT_res if xoT_res is not None else sb("_xoT", (128, NC * T), BF16)
    wb = sb("_w", (128, NC * D), BF16)
    brow = sb("_brow", (1, D), F32) if with_bias else None
    bbc = sb("_bbc", (128, D), F32) if with_bias else None
    ssb = [sb(f"_s{i}", (128, D), F32) for i in range(2)]
    pS = [psb(f"_pS{i}", (128, D), F32) for i in range(2)]
    pB = psb("_pB", (128, D), F32)

    ld = p.sem(tag + "_ld")
    mm = p.sem(tag + "_mm")
    dv = p.sem(tag + "_dv")
    ev = p.sem(tag + "_ev")
    stf = [p.sem(tag + f"_stf{s}") for s in range(2)]
    ldq = [p.sem(tag + f"_ldq{s}") for s in range(2)] if xoT_res is None else None
    b_ld = p.val(tag + "_ld")
    b_mm = p.val(tag + "_mm")
    b_dv = p.val(tag + "_dv")
    b_ev = p.val(tag + "_ev")
    b_stf = [p.val(tag + f"_stf{s}") for s in range(2)]
    nld = 1 if with_bias else 0
    if nld:
        p.add(tag + "_ld", 16 * nld)
    if xoT_res is None:
        b_ldq = [p.val(tag + f"_ldq{s}") for s in range(2)]
        for s in range(2):
            p.add(tag + f"_ldq{s}", 16)
    nbias = 1 if with_bias else 0
    if with_bias:
        p.add(tag + "_mm", 1)
        p.add(tag + "_ev", 1)
    for tt in range(NT):
        p.add(tag + "_mm", 1)
        p.add(tag + "_dv", 1)
        p.add(tag + f"_stf{tt % 2}", 16)

    gsems = []

    def gp0(q):
        gsems.append(p.swdge(q, wb.rearrange("p (c o) -> p c o", c=NC),
                             w_hbm.rearrange("(c p) o -> p c o", p=128)))
    blk.gpsimd(gp0)

    def sy0(q):
        if with_bias:
            q.dma_start(out=brow[:], in_=brow_hbm[:]).then_inc(ld, 16)
        if xoT_res is None:
            for s in range(2):
                q.dma_start(out=xoT.rearrange("p (c t) -> p c t", c=NC)[:, :, 1024 * s:1024 * (s + 1)],
                            in_=xoT_hbm.rearrange("(c p) t -> p c t", p=128)[:, :, 1024 * s:1024 * (s + 1)]).then_inc(ldq[s], 16)
    if nld or xoT_res is None:
        blk.sync(sy0)

    def tens(q):
        if nld:
            q.wait_ge(ld, b_ld + 16 * nld)
        for s, tgt in gsems:
            q.wait_ge(s, tgt)
        if with_bias:
            for ns in range(2):
                ins = q.matmul(pB[:, 512 * ns:512 * (ns + 1)], ones_row_sb[:],
                               brow[0:1, 512 * ns:512 * (ns + 1)], start=True, stop=True)
            ins.then_inc(mm, 1)
        for tt in range(NT):
            par = tt % 2
            if xoT_res is None and tt % 8 == 0:
                q.wait_ge(ldq[tt // 8], b_ldq[tt // 8] + 16)
            if tt >= 2:
                q.wait_ge(dv, b_dv + tt - 1)
            for c in range(NC):
                for ns in range(2):
                    ins = q.matmul(pS[par][:, 512 * ns:512 * (ns + 1)],
                                   xoT[:, c * T + 128 * tt:c * T + 128 * (tt + 1)],
                                   wb[:, c * D + 512 * ns:c * D + 512 * (ns + 1)],
                                   start=(c == 0), stop=(c == NC - 1))
            ins.then_inc(mm, 1)
    blk.tensor(tens)

    def scal(q):
        if with_bias:
            q.wait_ge(mm, b_mm + 1)
            q.copy(bbc[:], pB[:]).then_inc(ev, 1)
    blk.scalar(scal)

    def vec(q):
        if with_bias:
            q.wait_ge(ev, b_ev + 1)
        for tt in range(NT):
            par = tt % 2
            q.wait_ge(mm, b_mm + nbias + tt + 1)
            if tt >= 2:
                q.wait_ge(stf[par], b_stf[par] + 16 * ((tt - 2) // 2 + 1))
            if with_bias:
                q.tensor_tensor(ssb[par][:], pS[par][:], bbc[:], ALU.add).then_inc(dv, 1)
            else:
                q.tensor_copy(ssb[par][:], pS[par][:]).then_inc(dv, 1)
    blk.vector(vec)

    def sy1(q):
        for tt in range(NT):
            par = tt % 2
            q.wait_ge(dv, b_dv + tt + 1)
            q.dma_start(out=s_hbm[128 * tt:128 * (tt + 1), :], in_=ssb[par][:]).then_inc(stf[par], 16)
        for s in range(2):
            q.wait_ge(stf[s], b_stf[s] + 16 * (NT // 2))
    blk.sync(sy1)
    pctx.close()
    ctx.close()


def build_l0_attn_program():
    nc = bass.Bass("TRN2", target_bir_lowering=False)
    qT = dram_io(nc, "qT", (D, T), BF16, "ExternalInput")
    kvT = dram_io(nc, "kvT", (D, T), BF16, "ExternalInput")
    kvtok = dram_io(nc, "kvtok", (H * 16 * 128, 65), BF16, "ExternalInput")
    scon = dram_io(nc, "sconsts", (128, len(CONST_NAMES)), F32, "ExternalInput")
    onesr = dram_io(nc, "ones_row", (1, 128), F32, "ExternalInput")
    xoT = dram_io(nc, "xoT", (D, T), BF16, "ExternalOutput")
    with ExitStack() as ctx:
        p = Prog(nc, ctx)
        cs = _const_loader(p, ctx, {"sconsts": (scon, (128, len(CONST_NAMES)), "f32"),
                                    "ones_row": (onesr, (1, 128), "f32")})
        emit_l0_attn(p, qT, kvT, kvtok, cs["sconsts"], cs["ones_row"], xoT)
    return nc


def build_out_proj_program(with_bias=True):
    nc = bass.Bass("TRN2", target_bir_lowering=False)
    xoT = dram_io(nc, "xoT", (D, T), BF16, "ExternalInput")
    w = dram_io(nc, "w", (D, D), F32, "ExternalInput")
    brow = dram_io(nc, "brow", (1, D), F32, "ExternalInput")
    onesr = dram_io(nc, "ones_row", (1, 128), F32, "ExternalInput")
    s_out = dram_io(nc, "s_out", (T, D), F32, "ExternalOutput")
    with ExitStack() as ctx:
        p = Prog(nc, ctx)
        cs = _const_loader(p, ctx, {"ones_row": (onesr, (1, 128), "f32")})
        emit_out_proj(p, xoT, w, brow, cs["ones_row"], s_out, with_bias=with_bias)
    return nc


def build_l0_qkv_program():
    nc = bass.Bass("TRN2", target_bir_lowering=False)
    xnT = dram_io(nc, "xnT", (D, T), BF16, "ExternalInput")
    wq = dram_io(nc, "wq", (D, D), F32, "ExternalInput")
    wkv = dram_io(nc, "wkv", (D, D), F32, "ExternalInput")
    cos2 = dram_io(nc, "cos2", (128, T), F32, "ExternalInput")
    sin2 = dram_io(nc, "sin2", (128, T), F32, "ExternalInput")
    rot128 = dram_io(nc, "rot128", (128, 128), F32, "ExternalInput")
    identf = dram_io(nc, "ident", (128, 128), F32, "ExternalInput")
    qT = dram_io(nc, "qT", (D, T), BF16, "ExternalOutput")
    kvT = dram_io(nc, "kvT", (D, T), BF16, "ExternalOutput")
    kvtok = dram_io(nc, "kvtok", (H * 16 * 128, 65), BF16, "ExternalOutput")
    with ExitStack() as ctx:
        p = Prog(nc, ctx)
        cs = _const_loader(p, ctx, {"ident": (identf, (128, 128), "bf16")})
        emit_l0_qkv(p, xnT, wq, wkv, cos2, sin2, rot128, cs["ident"], qT, kvT, kvtok)
    return nc



def emit_l0_ffn(p, xnT_hbm, xn_tok_hbm, x_hbm, s0_hbm, prm, cpool, ones_row_sb,
                ident_f32, ident_bf, y_hbm, x1_hbm, tag="c", xnT_res=None):
    nc, blk = p.nc, p.block
    ctx = ExitStack()
    sb = lambda n, shp, dt: ctx.enter_context(nc.sbuf_tensor(tag + n, list(shp), dt))

    xnT = xnT_res if xnT_res is not None else sb("_xnT", (128, NC * T), BF16)
    sigraw = sb("_sigraw", (64, D), F32)
    sigs_bf = sb("_sigs", (64, D), BF16)
    gmean_bf = sb("_gmean", (64, 8), BF16)
    csig_bf = sb("_csig", (8, D), BF16)
    sigsT = sb("_sigsT", (128, NC * 64), BF16)
    csigT = sb("_csigT", (128, NC * 8), BF16)
    knots = sb("_knots", (1, 8), F32)
    temp = sb("_temp", (1, 1), F32)
    oscale = sb("_oscale", (1, 1), F32)
    cols = sb("_cols", (72, 10), F32)
    cw1b = sb("_cw1", (128, NC * 256), BF16)
    cb1c = sb("_cb1", (128, 2), F32)
    cw2b = sb("_cw2", (128, 4), BF16)
    cb2c = sb("_cb2", (2, 1), F32)
    comp_bf = sb("_comp", (2, T), BF16)
    reppat = sb("_reppat", (2, 16), BF16)
    comprep = sb("_comprep", (16, T), BF16)
    gacc = [sb(f"_gacc{i}", (72, T), F32) for i in range(2)]
    iota64c = sb("_iota64", (128, 64), F32)
    iota64m = sb("_iota64m", (128, 64), F32)
    coh8T = sb("_coh8T", (8, T), BF16)
    expand_bf = sb("_expand", (8, 64), BF16)
    tohfm = sb("_tohfm", (64, T), F32)
    w1gb = sb("_w1g", (16, 8 * 128), BF16)
    b1gc = sb("_b1g", (128, 8), F32)
    w2gb = sb("_w2g", (128, 8 * 64), BF16)
    b2c = sb("_b2c", (64, 1), F32)
    dirs_bf = sb("_dirs", (64, D), BF16)
    selmag = sb("_selmag", (64, T), BF16)
    gamr = sb("_gamr", (1, D), F32)
    betr = sb("_betr", (1, D), F32)
    ctxA = ExitStack()
    sbA = lambda n, shp, dt: ctxA.enter_context(nc.sbuf_tensor(tag + n, list(shp), dt))
    sg1 = sbA("_sg1", (64, D), F32)
    sg2 = sbA("_sg2", (64, D), F32)
    sgt = sbA("_sgt", (64, D), F32)
    tmp11 = sbA("_tmp11", (1, 1), F32)
    dsb = sbA("_dsb", (1, 7), F32)
    csb = sbA("_csb", (1, 6), F32)
    prow = sbA("_prow", (1, 10), F32)
    hT = sbA("_hT", (128, 2 * T), BF16)
    nrm = sbA("_nrm", (72, T), F32)
    relu_sb = [sbA(f"_relu{i}", (72, T), F32) for i in range(2)]
    ctxB = ExitStack()
    sbB = lambda n, shp, dt: ctxB.enter_context(nc.sbuf_tensor(tag + n, list(shp), dt))
    coh_tok = sbB("_cohtok", (128, 8), F32)
    mxc = sbB("_mxc", (128, 1), F32)
    eqc = sbB("_eqc", (128, 8), F32)
    t1c = sbB("_t1c", (128, 8), F32)
    t2c = sbB("_t2c", (128, 8), F32)
    cidx = sbB("_cidx", (128, 1), F32)
    gm_t1 = sbB("_gmt1", (64, T), F32)
    gm_v = sbB("_gmv", (64, T), F32)
    gm = sbB("_gm", (64, T), F32)
    mxt = sbB("_mxt", (128, 1), F32)
    eqt = sbB("_eqt", (128, 64), F32)
    t1t = sbB("_t1t", (128, 64), F32)
    t2t = sbB("_t2t", (128, 64), F32)
    tidx = sbB("_tidx", (128, 1), F32)
    toh_tok = sbB("_tohtok", (128, 64), F32)
    S = lambda n: p.sem(tag + n)
    V = lambda n: p.val(tag + n)
    A = lambda n, d=1: p.add(tag + n, d)
    ld = S("_ld")
    c3 = S("_c3")
    c1s = S("_c1s")
    c1v = S("_c1v")
    c2t = S("_c2t")
    c2v = S("_c2v")
    c2e = S("_c2e")
    c4v = S("_c4v")
    c5v = S("_c5v")
    c5w = S("_c5w")
    c5g = S("_c5g")
    c5s = S("_c5s")
    c5e = S("_c5e")
    stxa = [S(f"_stxa{i}") for i in range(2)]

    gsems = []

    def gp0(q):
        if xnT_res is None:
            gsems.append(p.swdge(q, xnT.rearrange("p (c t) -> p c t", c=NC),
                                 xnT_hbm.rearrange("(c p) t -> p c t", p=128)))
        gsems.append(p.swdge(q, cw1b.rearrange("p (c o) -> p c o", c=NC),
                             prm["cw1"].rearrange("(c p) o -> p c o", p=128)))
        gsems.append(p.swdge(q, cw2b.rearrange("p (c o) -> p c o", c=2),
                             prm["cw2"].rearrange("(c p) o -> p c o", p=128)))
        gsems.append(p.swdge(q, w1gb[:], prm["w1g"][:]))
        gsems.append(p.swdge(q, w2gb[:], prm["w2g"][:]))
        gsems.append(p.swdge(q, dirs_bf[:], prm["dirs"][:]))
        gsems.append(p.swdge(q, gmean_bf[:], prm["gmean"][:]))
        gsems.append(p.swdge(q, expand_bf[:], prm["expand8"][:]))
        gsems.append(p.swdge(q, reppat[:], prm["comp_rep"][:]))
    blk.gpsimd(gp0)

    b_ld = V("_ld")
    A("_ld", 16 * 12)
    ld_tgt = V("_ld")

    def sy0(q):
        for dst, key in ((sigraw, "sigraw"), (knots, "knots"), (temp, "temp"),
                         (oscale, "oscale"), (cb1c, "cb1_cols"), (cb2c, "cb2_col"),
                         (b1gc, "b1g"), (b2c, "b2_col"), (iota64c, "iota64"),
                         (iota64m, "iota64m"), (gamr, "gam_row"), (betr, "bet_row")):
            q.dma_start(out=dst[:], in_=prm[key][:]).then_inc(ld, 16)
    blk.sync(sy0)

    c0s = S("_c0s")
    c0v2 = S("_c0v2")
    c0m = S("_c0m")
    c0e = S("_c0e")
    b0s = V("_c0s"); A("_c0s", 2)
    b0v = V("_c0v2"); A("_c0v2", 10)
    b0m = V("_c0m"); A("_c0m", 2)
    b0e = V("_c0e"); A("_c0e", 2)

    def c0_scal(q):
        q.wait_ge(ld, ld_tgt)
        q.sign(sg1[:], sigraw[:], bias=cpool[0:64, CONST_COL["m03"]:CONST_COL["m03"] + 1]).then_inc(c0s, 1)
        q.wait_ge(c0s, b0s + 1)
        q.sign(sg2[:], sigraw[:], bias=cpool[0:64, CONST_COL["p03"]:CONST_COL["p03"] + 1]).then_inc(c0s, 1)
    blk.scalar(c0_scal)

    def c0_vec(q):
        q.wait_ge(ld, ld_tgt)
        q.wait_ge(c0s, b0s + 2)
        q.tensor_tensor(sgt[:], sg1[:], sg2[:], ALU.add).then_inc(c0v2, 1)
        q.wait_ge(c0v2, b0v + 1)
        q.tensor_scalar(sigs_bf[:], sgt[:], 0.5, None, ALU.mult).then_inc(c0v2, 1)
        q.tensor_tensor(dsb[:], knots[0:1, 1:8], knots[0:1, 0:7], ALU.subtract).then_inc(c0v2, 1)
        q.wait_ge(c0v2, b0v + 3)
        q.tensor_tensor(csb[:], dsb[0:1, 1:7], dsb[0:1, 0:6], ALU.subtract).then_inc(c0v2, 1)
        q.wait_ge(c0v2, b0v + 4)
        q.tensor_scalar(tmp11[:], temp[:], 1e-6, None, ALU.add).then_inc(c0v2, 1)
        q.wait_ge(c0v2, b0v + 5)
        q.reciprocal(prow[0:1, 0:1], tmp11[:]).then_inc(c0v2, 1)
        q.wait_ge(c0v2, b0v + 6)
        q.tensor_copy(prow[0:1, 1:2], knots[0:1, 0:1]).then_inc(c0v2, 1)
        q.wait_ge(c0v2, b0v + 7)
        q.tensor_scalar(prow[0:1, 2:3], dsb[0:1, 0:1], 7.0, None, ALU.mult).then_inc(c0v2, 1)
        q.wait_ge(c0v2, b0v + 8)
        q.tensor_copy(prow[0:1, 3:9], csb[:]).then_inc(c0v2, 1)
        q.wait_ge(c0v2, b0v + 9)
        q.tensor_copy(prow[0:1, 9:10], oscale[:]).then_inc(c0v2, 1)
    blk.vector(c0_vec)

    pctx0 = ExitStack()
    pcs = pctx0.enter_context(nc.psum_tensor(tag + "_pcs", [8, D], F32))
    pcols = pctx0.enter_context(nc.psum_tensor(tag + "_pcols", [72, 10], F32))
    ptr = [pctx0.enter_context(nc.psum_tensor(tag + f"_ptr{i}", [128, 128], BF16)) for i in range(2)]

    def c0_tens(q):
        for s, tgt in gsems:
            q.wait_ge(s, tgt)
        q.wait_ge(c0v2, b0v + 2)
        for ns in range(2):
            ins = q.matmul(pcs[:, 512 * ns:512 * (ns + 1)], gmean_bf[:],
                           sigs_bf[:, 512 * ns:512 * (ns + 1)], start=True, stop=True)
        ins.then_inc(c0m, 1)
        q.wait_ge(c0v2, b0v + 10)
        ins = q.matmul(pcols[:], ones_row_sb[0:1, 0:72], prow[:], start=True, stop=True)
        ins.then_inc(c0m, 1)
    blk.tensor(c0_tens)

    def c0_scal2(q):
        q.wait_ge(c0m, b0m + 1)
        q.sign(csig_bf[:], pcs[:]).then_inc(c0e, 1)
        q.wait_ge(c0m, b0m + 2)
        q.copy(cols[:], pcols[:]).then_inc(c0e, 1)
    blk.scalar(c0_scal2)
    c0_done = None

    b_t0 = V("_c2t")
    b_e0 = V("_c2e")
    A("_c2t", NC)
    A("_c2e", NC)

    def c0_tens2(q):
        q.wait_ge(c0e, b0e + 1)
        for k in range(NC):
            if k >= 2:
                q.wait_ge(c2e, b_e0 + k - 1)
            q.transpose(ptr[k % 2][:, 0:64], sigs_bf[:, 128 * k:128 * (k + 1)], ident_bf[0:64, 0:64])
            ins = q.transpose(ptr[k % 2][:, 64:72], csig_bf[:, 128 * k:128 * (k + 1)], ident_bf[0:8, 0:8])
            ins.then_inc(c2t, 1)
    blk.tensor(c0_tens2)

    def c0_scal3(q):
        for k in range(NC):
            q.wait_ge(c2t, b_t0 + k + 1)
            q.copy(sigsT[:, 64 * k:64 * (k + 1)], ptr[k % 2][:, 0:64])
            q.copy(csigT[:, 8 * k:8 * (k + 1)], ptr[k % 2][:, 64:72]).then_inc(c2e, 1)
    blk.scalar(c0_scal3)

    pctx0.close()
    pctx1 = ExitStack()
    ph = pctx1.enter_context(nc.psum_tensor(tag + "_ph", [128, T], F32))
    pc2 = pctx1.enter_context(nc.psum_tensor(tag + "_pc2", [2, T], F32))
    b3 = V("_c3")
    A("_c3", 8)

    def c3_tens(q):
        q.wait_ge(c2e, b_e0 + NC)
        for m in range(2):
            if m == 1:
                q.wait_ge(c3, b3 + 2)
            for k in range(NC):
                for ns in range(4):
                    ins = q.matmul(ph[:, 512 * ns:512 * (ns + 1)],
                                   cw1b[:, k * 256 + 128 * m:k * 256 + 128 * (m + 1)],
                                   xnT[:, k * T + 512 * ns:k * T + 512 * (ns + 1)],
                                   start=(k == 0), stop=(k == NC - 1))
            ins.then_inc(c3, 1)
        q.wait_ge(c3, b3 + 4)
        for m in range(2):
            for ns in range(4):
                ins = q.matmul(pc2[:, 512 * ns:512 * (ns + 1)], cw2b[:, 2 * m:2 * (m + 1)],
                               hT[:, m * T + 512 * ns:m * T + 512 * (ns + 1)],
                               start=(m == 0), stop=(m == 1))
        ins.then_inc(c3, 1)
        q.wait_ge(c3, b3 + 6)
        for ns in range(4):
            ins = q.matmul(ph[0:16, 512 * ns:512 * (ns + 1)], reppat[:],
                           comp_bf[:, 512 * ns:512 * (ns + 1)], start=True, stop=True)
        ins.then_inc(c3, 1)
    blk.tensor(c3_tens)

    def c3_scal(q):
        for m in range(2):
            q.wait_ge(c3, b3 + 2 * m + 1)
            q.activation(hT[:, m * T:(m + 1) * T], ph[:], AF.Gelu,
                         bias=cb1c[:, m:m + 1]).then_inc(c3, 1)
        q.wait_ge(c3, b3 + 5)
        q.activation(comp_bf[:], pc2[:], AF.Tanh, bias=cb2c[:]).then_inc(c3, 1)
        q.wait_ge(c3, b3 + 7)
        q.copy(comprep[:], ph[0:16, :]).then_inc(c3, 1)
    blk.scalar(c3_scal)
    c3_done = b3 + 8

    pctx1.close()
    pctx2 = ExitStack()
    psc = pctx2.enter_context(nc.psum_tensor(tag + "_psc", [64, T], F32))
    pcl = pctx2.enter_context(nc.psum_tensor(tag + "_pcl", [8, T], F32))
    b1s = V("_c1s")
    b1v = V("_c1v")
    A("_c1s", 10)
    A("_c1v", 6)

    def c1_tens(q):
        q.wait_ge(c2e, b_e0 + NC)
        q.wait_ge(c3, c3_done)
        for k in range(NC):
            for ns in range(4):
                q.matmul(psc[:, 512 * ns:512 * (ns + 1)], sigsT[:, 64 * k:64 * (k + 1)],
                         xnT[:, k * T + 512 * ns:k * T + 512 * (ns + 1)],
                         start=(k == 0), stop=(k == NC - 1))
                ins = q.matmul(pcl[:, 512 * ns:512 * (ns + 1)], csigT[:, 8 * k:8 * (k + 1)],
                               xnT[:, k * T + 512 * ns:k * T + 512 * (ns + 1)],
                               start=(k == 0), stop=(k == NC - 1))
        ins.then_inc(c1s, 1)
    blk.tensor(c1_tens)

    def c1_scal(q):
        q.wait_ge(c1s, b1s + 1)
        q.wait_ge(c0e, b0e + 2)
        q.activation(nrm[0:64, :], psc[:], AF.Sigmoid, scale=cols[0:64, 0:1]).then_inc(c1s, 1)
        q.wait_ge(c1s, b1s + 2)
        q.activation(nrm[64:72, :], pcl[:], AF.Sigmoid, scale=cols[64:72, 0:1]).then_inc(c1s, 1)
        q.wait_ge(c1s, b1s + 3)
        q.activation(gacc[0][:], nrm[:], AF.Identity, scale=cols[:, 2:3],
                     bias=cols[:, 1:2]).then_inc(c1s, 1)
        for i in range(1, 7):
            q.wait_ge(c1s, b1s + 3 + i)
            if i >= 3:
                q.wait_ge(c1v, b1v + i - 2)
            q.activation(relu_sb[i % 2][:], nrm[:], AF.Relu, scale=7.0,
                         bias=cpool[0:72, CONST_COL[f"rm{i}"]:CONST_COL[f"rm{i}"] + 1]).then_inc(c1s, 1)
    blk.scalar(c1_scal)

    def c1_vec(q):
        for i in range(1, 7):
            q.wait_ge(c1s, b1s + 4 + i)
            if i >= 2:
                q.wait_ge(c1v, b1v + i - 1)
            q.scalar_tensor_tensor(gacc[i % 2][:], relu_sb[i % 2][:], cols[:, 2 + i:3 + i],
                                   gacc[(i - 1) % 2][:], ALU.mult, ALU.add).then_inc(c1v, 1)
    blk.vector(c1_vec)
    c1v_done = b1v + 6
    gfin = 0

    pctx2.close()
    pctx3 = ExitStack()
    pmsk = pctx3.enter_context(nc.psum_tensor(tag + "_pmsk", [64, T], F32))
    ptg = [pctx3.enter_context(nc.psum_tensor(tag + f"_ptg{i}", [128, 64], F32)) for i in range(2)]
    ptb = [pctx3.enter_context(nc.psum_tensor(tag + f"_ptb{i}", [64, 128], F32)) for i in range(2)]

    bt1 = V("_c2t")
    A("_c2t", NT)
    bv1 = V("_c2v")
    A("_c2v", 6 * NT)
    c2b = S("_c2b")
    bb1 = V("_c2b")
    A("_c2b", NT)
    be1 = V("_c2e")
    A("_c2e", NT)

    def c2_tensAB(q):
        q.wait_ge(c1v, c1v_done)
        for t in range(NT + 1):
            if t < NT:
                if t >= 2:
                    q.wait_ge(c2v, bv1 + 6 * (t - 1))
                ins = q.transpose(ptg[t % 2][:, 0:8], gacc[gfin][64:72, 128 * t:128 * (t + 1)],
                                  ident_f32[64:72, 64:72])
                ins.then_inc(c2t, 1)
            if t >= 1:
                tb = t - 1
                q.wait_ge(c2v, bv1 + 6 * tb + 6)
                if tb >= 2:
                    q.wait_ge(c2e, be1 + tb - 1)
                ins = q.transpose(ptb[tb % 2][0:8, :], coh_tok[:], ident_f32[:])
                ins.then_inc(c2b, 1)
    blk.tensor(c2_tensAB)

    def c2_vecA(q):
        for t in range(NT):
            q.wait_ge(c2t, bt1 + t + 1)
            n0 = bv1 + 6 * t
            q.reduce_max(mxc[:], ptg[t % 2][:, 0:8], axis=AX).then_inc(c2v, 1)
            q.wait_ge(c2v, n0 + 1)
            q.tensor_scalar(eqc[:], ptg[t % 2][:, 0:8], mxc[:], None, ALU.is_equal).then_inc(c2v, 1)
            q.wait_ge(c2v, n0 + 2)
            q.tensor_tensor(t1c[:], eqc[:], iota64m[:, 56:64], ALU.mult).then_inc(c2v, 1)
            q.wait_ge(c2v, n0 + 3)
            q.tensor_scalar(t2c[:], t1c[:], 8.0, None, ALU.add).then_inc(c2v, 1)
            q.wait_ge(c2v, n0 + 4)
            q.tensor_reduce(cidx[:], t2c[:], axis=AX, op=ALU.min).then_inc(c2v, 1)
            q.wait_ge(c2v, n0 + 5)
            if t >= 1:
                q.wait_ge(c2b, bb1 + t)
            q.tensor_scalar(coh_tok[:], iota64c[:, 0:8], cidx[:], None, ALU.is_equal).then_inc(c2v, 1)
    blk.vector(c2_vecA)

    def c2_scalA(q):
        for t in range(NT):
            q.wait_ge(c2b, bb1 + t + 1)
            q.copy(coh8T[:, 128 * t:128 * (t + 1)], ptb[t % 2][0:8, :]).then_inc(c2e, 1)
    blk.scalar(c2_scalA)

    bt3 = V("_c2t")
    A("_c2t", 1)
    bv2 = V("_c2v")
    A("_c2v", 3)
    bt4 = V("_c2t")
    A("_c2t", NT)
    bv3 = V("_c2v")
    A("_c2v", 6 * NT)
    bb2 = V("_c2b")
    A("_c2b", NT)
    be2 = V("_c2e")
    A("_c2e", NT)

    def c2_tensC(q):
        q.wait_ge(c2e, be1 + NT)
        for ns in range(4):
            ins = q.matmul(pmsk[:, 512 * ns:512 * (ns + 1)], expand_bf[:],
                           coh8T[:, 512 * ns:512 * (ns + 1)], start=True, stop=True)
        ins.then_inc(c2t, 1)
    blk.tensor(c2_tensC)

    def c2_vecB(q):
        q.wait_ge(c2t, bt3 + 1)
        q.tensor_tensor(gm_t1[:], gacc[gfin][0:64, :], pmsk[:], ALU.mult).then_inc(c2v, 1)
        q.wait_ge(c2v, bv2 + 1)
        q.tensor_scalar(gm_v[:], pmsk[:], 1.0, None, ALU.subtract).then_inc(c2v, 1)
        q.wait_ge(c2v, bv2 + 2)
        q.scalar_tensor_tensor(gm[:], gm_v[:], 3e8, gm_t1[:], ALU.mult, ALU.add).then_inc(c2v, 1)
    blk.vector(c2_vecB)

    def c2_tensDE(q):
        q.wait_ge(c2v, bv2 + 3)
        for t in range(NT + 1):
            if t < NT:
                if t >= 2:
                    q.wait_ge(c2v, bv3 + 6 * (t - 1))
                ins = q.transpose(ptg[t % 2][:], gm[:, 128 * t:128 * (t + 1)], ident_f32[0:64, 0:64])
                ins.then_inc(c2t, 1)
            if t >= 1:
                tb = t - 1
                q.wait_ge(c2v, bv3 + 6 * tb + 6)
                if tb >= 2:
                    q.wait_ge(c2e, be2 + tb - 1)
                ins = q.transpose(ptb[tb % 2][0:64, :], toh_tok[:], ident_f32[:])
                ins.then_inc(c2b, 1)
    blk.tensor(c2_tensDE)

    def c2_vecC(q):
        for t in range(NT):
            q.wait_ge(c2t, bt4 + t + 1)
            n0 = bv3 + 6 * t
            q.reduce_max(mxt[:], ptg[t % 2][:], axis=AX).then_inc(c2v, 1)
            q.wait_ge(c2v, n0 + 1)
            q.tensor_scalar(eqt[:], ptg[t % 2][:], mxt[:], None, ALU.is_equal).then_inc(c2v, 1)
            q.wait_ge(c2v, n0 + 2)
            q.tensor_tensor(t1t[:], eqt[:], iota64m[:], ALU.mult).then_inc(c2v, 1)
            q.wait_ge(c2v, n0 + 3)
            q.tensor_scalar(t2t[:], t1t[:], 64.0, None, ALU.add).then_inc(c2v, 1)
            q.wait_ge(c2v, n0 + 4)
            q.tensor_reduce(tidx[:], t2t[:], axis=AX, op=ALU.min).then_inc(c2v, 1)
            q.wait_ge(c2v, n0 + 5)
            if t >= 1:
                q.wait_ge(c2b, bb2 + t)
            q.tensor_scalar(toh_tok[:], iota64c[:], tidx[:], None, ALU.is_equal).then_inc(c2v, 1)
    blk.vector(c2_vecC)


    def c2_scalB(q):
        for t in range(NT):
            q.wait_ge(c2b, bb2 + t + 1)
            q.copy(tohfm[:, 128 * t:128 * (t + 1)], ptb[t % 2][0:64, :]).then_inc(c2e, 1)
    blk.scalar(c2_scalB)
    c2_done = be2 + NT

    pctx3.close()
    ctxB.close()
    ctxA.close()
    ctxC = ExitStack()
    sbC = lambda n, shp, dt: ctxC.enter_context(nc.sbuf_tensor(tag + n, list(shp), dt))
    hh_bf = sbC("_hh", (128, T), BF16)
    magp = sbC("_magp", (64, T), F32)
    xnt = [sbC(f"_xnt{i}", (128, D), F32) for i in range(2)]
    ysb = [sbC(f"_y{i}", (128, D), F32) for i in range(2)]
    ytl = [sbC(f"_ytl{i}", (128, D), F32) for i in range(2)]
    sqc = sbC("_sqc", (128, NT), F32)
    sqscr = sbC("_sqscr", (128, D), F32)
    nsum = sbC("_nsum", (128, NT), F32)
    nmu = sbC("_nmu", (128, NT), F32)
    musq = sbC("_musq", (128, NT), F32)
    varc = sbC("_varc", (128, NT), F32)
    sdev = sbC("_sdev", (128, NT), F32)
    rstd = sbC("_rstd", (128, NT), F32)
    bcol = sbC("_bcol", (128, NT), F32)
    gbc = sbC("_gbc", (128, D), F32)
    bbc2 = sbC("_bbc2", (128, D), F32)
    z1 = sbC("_z1", (128, D), F32)
    z2 = sbC("_z2", (128, D), F32)
    z3 = sbC("_z3", (128, D), F32)
    gsl = [sbC(f"_gsl{i}", (128, D), F32) for i in range(2)]
    xti = [sbC(f"_xti{i}", (128, D), F32) for i in range(2)]
    s0t = [sbC(f"_s0t{i}", (128, D), F32) for i in range(2)]
    tt1 = sbC("_tt1", (128, D), F32)
    x1sb = [sbC(f"_x1sb{i}", (128, D), F32) for i in range(2)]
    pctx4 = ExitStack()
    pHH = pctx4.enter_context(nc.psum_tensor(tag + "_pHH", [128, T], F32))
    pMag = pctx4.enter_context(nc.psum_tensor(tag + "_pMag", [64, T], F32))
    c4m = S("_c4m")
    c4s = S("_c4s")
    b4m = V("_c4m"); A("_c4m", 16)
    b4s = V("_c4s"); A("_c4s", 8)
    b4v = V("_c4v")
    A("_c4v", 2)

    def c4_tens(q):
        q.wait_ge(c3, c3_done)
        q.wait_ge(c2e, c2_done)
        for grp in range(8):
            if grp > 0:
                q.wait_ge(c4s, b4s + grp)
            for ns in range(4):
                ins = q.matmul(pHH[:, 512 * ns:512 * (ns + 1)],
                               w1gb[:, 128 * grp:128 * (grp + 1)],
                               comprep[:, 512 * ns:512 * (ns + 1)], start=True, stop=True)
            ins.then_inc(c4m, 1)
            q.wait_ge(c4s, b4s + grp + 1)
            for ns in range(4):
                ins = q.matmul(pMag[:, 512 * ns:512 * (ns + 1)],
                               w2gb[:, 64 * grp:64 * (grp + 1)],
                               hh_bf[:, 512 * ns:512 * (ns + 1)],
                               start=(grp == 0), stop=(grp == 7))
            ins.then_inc(c4m, 1)
    blk.tensor(c4_tens)

    def c4_scal(q):
        for grp in range(8):
            q.wait_ge(c4m, b4m + 2 * grp + 1)
            q.activation(hh_bf[:], pHH[:], AF.Relu, bias=b1gc[:, grp:grp + 1]).then_inc(c4s, 1)
    blk.scalar(c4_scal)

    def c4_vec(q):
        q.wait_ge(c4m, b4m + 16)
        q.wait_ge(c2e, c2_done)
        q.tensor_scalar(magp[:], pMag[:], b2c[:], cols[0:64, 9:10], ALU.add, ALU.mult).then_inc(c4v, 1)
        q.wait_ge(c4v, b4v + 1)
        q.tensor_tensor(selmag[:], magp[:], tohfm[:], ALU.mult).then_inc(c4v, 1)
    blk.vector(c4_vec)
    c4v_done = b4v + 2

    pctx4.close()
    pctx5 = ExitStack()
    pF = [pctx5.enter_context(nc.psum_tensor(tag + f"_pF{i}", [128, D], F32)) for i in range(2)]
    pGB = pctx5.enter_context(nc.psum_tensor(tag + "_pGB", [128, D], F32))
    pBB = pctx5.enter_context(nc.psum_tensor(tag + "_pBB", [128, D], F32))
    b5m = V("_c4m")
    A("_c4m", NT + 2)
    c5la = [S(f"_c5la{i}") for i in range(2)]
    b5la = [V(f"_c5la{i}") for i in range(2)]
    for t in range(NT):
        A(f"_c5la{t % 2}", 16)
    b5v = V("_c5v")
    A("_c5v", 2 * NT)
    b5s = V("_c5s")
    A("_c5s", NT)
    sty = [S(f"_sty{i}") for i in range(2)]
    b_sty = [V(f"_sty{i}") for i in range(2)]
    for t in range(NT):
        A(f"_sty{t % 2}", 16)
    b5e = V("_c5e")
    A("_c5e", 5)
    c5q = S("_c5q")
    b5q = V("_c5q")
    A("_c5q", 3)

    def c5_tens(q):
        q.wait_ge(c4v, c4v_done)
        for t in range(NT):
            par = t % 2
            if t >= 2:
                q.wait_ge(c5v, b5v + 2 * (t - 1))
            for ns in range(2):
                ins = q.matmul(pF[par][:, 512 * ns:512 * (ns + 1)],
                               selmag[:, 128 * t:128 * (t + 1)],
                               dirs_bf[:, 512 * ns:512 * (ns + 1)], start=True, stop=True)
            ins.then_inc(c4m, 1)
        for ns in range(2):
            ins = q.matmul(pGB[:, 512 * ns:512 * (ns + 1)], ones_row_sb[:],
                           gamr[0:1, 512 * ns:512 * (ns + 1)], start=True, stop=True)
        ins.then_inc(c4m, 1)
        for ns in range(2):
            ins = q.matmul(pBB[:, 512 * ns:512 * (ns + 1)], ones_row_sb[:],
                           betr[0:1, 512 * ns:512 * (ns + 1)], start=True, stop=True)
        ins.then_inc(c4m, 1)
    blk.tensor(c5_tens)

    def c5_sy(q):
        q.wait_ge(c4v, c4v_done)
        for t in range(NT):
            if t >= 2:
                q.wait_ge(c5v, b5v + 2 * (t - 1))
            q.dma_start(out=xnt[t % 2][:],
                        in_=xn_tok_hbm[128 * t:128 * (t + 1), :]).then_inc(c5la[t % 2], 16)
    blk.sync(c5_sy)

    def c5_vec(q):
        for t in range(NT):
            par = t % 2
            q.wait_ge(c5la[t % 2], b5la[t % 2] + 16 * (t // 2 + 1))
            q.wait_ge(c4m, b5m + t + 1)
            if t >= 2:
                q.wait_ge(c5s, b5s + t - 1)
                q.wait_ge(sty[par], b_sty[par] + 16 * ((t - 2) // 2 + 1))
            q.tensor_tensor(ysb[par][:], xnt[par][:], pF[par][:], ALU.add).then_inc(c5v, 1)
            q.wait_ge(c5v, b5v + 2 * t + 1)
            q.reduce_sum(nsum[:, t:t + 1], ysb[par][:],
                         axis=AX, negate=True).then_inc(c5v, 1)
    blk.vector(c5_vec)

    def c5_gp(q):
        for t in range(NT):
            par = t % 2
            q.wait_ge(c5v, b5v + 2 * t + 1)
            q.dma_start(out=y_hbm[128 * t:128 * (t + 1), :], in_=ysb[par][:]).then_inc(sty[par], 16)
    blk.gpsimd(c5_gp)

    def c5_scal(q):
        for t in range(NT):
            par = t % 2
            q.wait_ge(c5v, b5v + 2 * t + 1)
            if t > 0:
                q.wait_ge(c5s, b5s + t)
            q.activation(sqscr[:], ysb[par][:], AF.Square,
                         accum_out=sqc[:, t:t + 1]).then_inc(c5s, 1)
    blk.scalar(c5_scal)

    def c5_vec2(q):
        q.wait_ge(c5v, b5v + 2 * NT)
        q.wait_ge(c5s, b5s + NT)
        q.tensor_scalar(nmu[:], nsum[:], 1.0 / D, None, ALU.mult).then_inc(c5e, 1)
        q.wait_ge(c5e, b5e + 1)
        q.tensor_tensor(musq[:], nmu[:], nmu[:], ALU.mult).then_inc(c5e, 1)
        q.wait_ge(c5e, b5e + 2)
        q.scalar_tensor_tensor(varc[:], sqc[:], 1.0 / D, musq[:], ALU.mult, ALU.subtract).then_inc(c5e, 1)
    blk.vector(c5_vec2)

    def c5_scal2(q):
        q.wait_ge(c5e, b5e + 3)
        q.activation(sdev[:], varc[:], AF.Sqrt,
                     bias=cpool[:, CONST_COL["eps_ln"]:CONST_COL["eps_ln"] + 1]).then_inc(c5q, 1)
        q.wait_ge(c4m, b5m + NT + 2)
        q.copy(gbc[:], pGB[:]).then_inc(c5q, 1)
        q.wait_ge(c5q, b5q + 2)
        q.copy(bbc2[:], pBB[:]).then_inc(c5q, 1)
    blk.scalar(c5_scal2)

    def c5_vec3(q):
        q.wait_ge(c5q, b5q + 1)
        q.reciprocal(rstd[:], sdev[:]).then_inc(c5e, 1)
        q.wait_ge(c5e, b5e + 4)
        q.tensor_tensor(bcol[:], nmu[:], rstd[:], ALU.mult).then_inc(c5e, 1)
    blk.vector(c5_vec3)
    c5e_done = b5e + 5

    c5lb = [S(f"_c5lb{i}") for i in range(2)]
    b5lb = [V(f"_c5lb{i}") for i in range(2)]
    for t in range(NT):
        A(f"_c5lb{t % 2}", 48)
    b5w = V("_c5w")
    A("_c5w", 5 * NT)
    b5g = V("_c5g")
    A("_c5g", NT)
    b_stxa = [V(f"_stxa{i}") for i in range(2)]
    for t in range(NT):
        A(f"_stxa{t % 2}", 16)

    def c5_sy2(q):
        for s in range(2):
            q.wait_ge(sty[s], b_sty[s] + 16 * (NT // 2))
        for t in range(NT + 2):
            if t < NT:
                if t >= 2:
                    q.wait_ge(c5w, b5w + 5 * (t - 1))
                q.dma_start(out=xti[t % 2][:], in_=x_hbm[128 * t:128 * (t + 1), :]).then_inc(c5lb[t % 2], 16)
                q.dma_start(out=s0t[t % 2][:], in_=s0_hbm[128 * t:128 * (t + 1), :]).then_inc(c5lb[t % 2], 16)
                q.dma_start(out=ytl[t % 2][:], in_=y_hbm[128 * t:128 * (t + 1), :]).then_inc(c5lb[t % 2], 16)
            if t >= 2:
                ts = t - 2
                q.wait_ge(c5w, b5w + 5 * ts + 5)
                q.dma_start(out=x1_hbm[128 * ts:128 * (ts + 1), :],
                            in_=x1sb[ts % 2][:]).then_inc(stxa[ts % 2], 16)
        for s in range(2):
            q.wait_ge(stxa[s], b_stxa[s] + 16 * (NT // 2))
    blk.sync(c5_sy2)


    def c5_scal3(q):
        for t in range(NT):
            q.wait_ge(c5w, b5w + 5 * t + 3)
            if t >= 2:
                q.wait_ge(c5w, b5w + 5 * (t - 2) + 4)
            q.activation(gsl[t % 2][:], z3[:], AF.Silu).then_inc(c5g, 1)
    blk.scalar(c5_scal3)

    def c5_vec4(q):
        q.wait_ge(c5e, c5e_done)
        q.wait_ge(c5q, b5q + 3)
        for t in range(NT):
            par = t % 2
            q.wait_ge(c5lb[par], b5lb[par] + 48 * (t // 2 + 1))
            q.tensor_scalar(z1[:], ytl[t % 2][:], rstd[:, t:t + 1],
                            bcol[:, t:t + 1], ALU.mult, ALU.add).then_inc(c5w, 1)
            q.wait_ge(c5w, b5w + 5 * t + 1)
            q.tensor_tensor(z2[:], z1[:], gbc[:], ALU.mult).then_inc(c5w, 1)
            q.wait_ge(c5w, b5w + 5 * t + 2)
            if t >= 1:
                q.wait_ge(c5g, b5g + t)
            q.tensor_tensor(z3[:], z2[:], bbc2[:], ALU.add).then_inc(c5w, 1)
            q.wait_ge(c5w, b5w + 5 * t + 3)
            q.wait_ge(c5g, b5g + t + 1)
            q.tensor_tensor(tt1[:], gsl[par][:], s0t[par][:], ALU.mult).then_inc(c5w, 1)
            q.wait_ge(c5w, b5w + 5 * t + 4)
            if t >= 2:
                q.wait_ge(stxa[par], b_stxa[par] + 16 * ((t - 2) // 2 + 1))
            q.tensor_tensor(x1sb[par][:], tt1[:], xti[par][:], ALU.add).then_inc(c5w, 1)
    blk.vector(c5_vec4)

    pctx5.close()
    ctxC.close()
    ctx.close()


def build_l0_ffn_program():
    nc = bass.Bass("TRN2", target_bir_lowering=False)
    xnT = dram_io(nc, "xnT", (D, T), BF16, "ExternalInput")
    xn_tok = dram_io(nc, "xn_tok", (T, D), F32, "ExternalInput")
    x = dram_io(nc, "x", (T, D), F32, "ExternalInput")
    s0 = dram_io(nc, "s0", (T, D), F32, "ExternalInput")
    prm_names = {
        "sigraw": (64, D), "knots": (1, 8), "temp": (1, 1), "oscale": (1, 1),
        "cw1": (D, 256), "cb1_cols": (128, 2), "cw2": (256, 2), "cb2_col": (2, 1),
        "w1g": (16, 8 * 128), "b1g": (128, 8), "w2g": (128, 8 * 64), "b2_col": (64, 1),
        "dirs": (64, D), "gam_row": (1, D), "bet_row": (1, D),
        "comp_rep": (2, 16), "iota64": (128, 64), "iota64m": (128, 64),
        "expand8": (8, 64), "gmean": (64, 8),
    }
    prm = {k: dram_io(nc, "p_" + k, shp, F32, "ExternalInput") for k, shp in prm_names.items()}
    prm["b2_col"] = prm.pop("b2_col")
    scon = dram_io(nc, "sconsts", (128, len(CONST_NAMES)), F32, "ExternalInput")
    onesr = dram_io(nc, "ones_row", (1, 128), F32, "ExternalInput")
    identf = dram_io(nc, "ident", (128, 128), F32, "ExternalInput")
    yscr = dram_io(nc, "yscr", (T, D), F32, "ExternalOutput")
    x1 = dram_io(nc, "x1", (T, D), F32, "ExternalOutput")
    with ExitStack() as ctx:
        p = Prog(nc, ctx)
        cs = _const_loader(p, ctx, {"sconsts": (scon, (128, len(CONST_NAMES)), "f32"),
                                    "ones_row": (onesr, (1, 128), "f32"),
                                    "ident": (identf, (128, 128), "bf16")})
        pm = dict(prm)
        pm["b2_col"] = prm["b2_col"]
        emit_l0_ffn(p, xnT, xn_tok, x, s0, pm, cs["sconsts"], cs["ones_row"],
                    cs["ident_f32"], cs["ident"], yscr, x1)
    return nc



def emit_l1_qkv(p, xnT_hbm, wqkvT_hbm, ident_bf, qT_hbm, kT_hbm, ktok_hbm,
                vtok_hbm, tag="d1", xnT_res=None, wpre=None, res_out=None):
    nc, blk = p.nc, p.block
    ctx = ExitStack()
    sb = lambda n, shp, dt: ctx.enter_context(nc.sbuf_tensor(tag + n, list(shp), dt))
    pctx = ExitStack()
    psb = lambda n, shp, dt: pctx.enter_context(nc.psum_tensor(tag + n, list(shp), dt))
    HT = T // 2

    xnT = xnT_res if xnT_res is not None else sb("_xnT", (128, NC * T), BF16)
    if wpre is not None:
        wpre_sems, wb = wpre
    else:
        wb = sb("_w", (128, NC * 3 * D), BF16)
    rsc = [sb(f"_r{i}", (128, HT), F32) for i in range(2)]
    dsc = [sb(f"_d{i}", (128, HT), F32) for i in range(2)]
    esc = [sb(f"_e{i}", (128, HT), F32) for i in range(2)]
    phib = [sb(f"_phib{i}", (128, T), BF16) for i in range(2)]
    if res_out is not None:
        kTs, ktok, vtok = res_out
        VS = 65
    else:
        kTs = None
        ktok = sb("_ktok", (128, H * 16 * 64), BF16)
        vtok = sb("_vtok", (128, H * 16 * 64), BF16)
        VS = 64
    kres = kTs is not None
    pp = [psb(f"_pp{i}", (128, HT), F32) for i in range(2)]

    mm = p.sem(tag + "_mm")
    ract = p.sem(tag + "_ract")
    dv = p.sem(tag + "_dv")
    tp = p.sem(tag + "_tp")
    ev = p.sem(tag + "_ev")
    sta = [p.sem(tag + f"_sta{i}") for i in range(2)]
    stk = p.sem(tag + "_stk") if res_out is None else None

    b_mm = p.val(tag + "_mm")
    b_ract = p.val(tag + "_ract")
    b_dv = p.val(tag + "_dv")
    b_tp = p.val(tag + "_tp")
    b_ev = p.val(tag + "_ev")
    b_sta = [p.val(tag + f"_sta{i}") for i in range(2)]
    b_stk = p.val(tag + "_stk") if res_out is None else 0
    NM = 3 * NC

    def kind_of(m):
        return m // NC

    def nract(m):
        return sum(4 if kind_of(x) < 2 else 2 for x in range(m))

    def ndv(m):
        return sum(4 if kind_of(x) < 2 else 0 for x in range(m))

    def ntp(m):
        return sum(16 if kind_of(x) >= 1 else 0 for x in range(m))

    def nst(m):
        return sum(16 if kind_of(x) < 2 else 0 for x in range(m))

    for m in range(NM):
        p.add(tag + "_mm", 2)
        p.add(tag + "_ract", 4 if kind_of(m) < 2 else 2)
        if kind_of(m) < 2:
            p.add(tag + "_dv", 4)
            if kind_of(m) == 0 or not kres:
                p.add(tag + f"_sta{m % 2}", 16)
        if kind_of(m) >= 1:
            p.add(tag + "_tp", 16)
            p.add(tag + "_ev", 16)
    if res_out is None:
        p.add(tag + "_stk", 16 * 2 * H)
    ract_tot = p.val(tag + "_ract")
    dv_tot = p.val(tag + "_dv")
    ev_tot = p.val(tag + "_ev")
    sta_tot = [p.val(tag + f"_sta{i}") for i in range(2)]
    stk_tot = p.val(tag + "_stk") if res_out is None else 0

    gsems = [] if wpre is None else list(wpre_sems)

    def gp0(q):
        if xnT_res is None:
            gsems.append(p.swdge(q, xnT.rearrange("p (c t) -> p c t", c=NC),
                                 xnT_hbm.rearrange("(c p) t -> p c t", p=128)))
        if wpre is None:
            gsems.append(p.swdge(q, wb.rearrange("p (c o) -> p c o", c=NC),
                                 wqkvT_hbm.rearrange("(c p) o -> p c o", p=128)))
    if xnT_res is None or wpre is None:
        blk.gpsimd(gp0)

    ptt = [pctx.enter_context(nc.psum_tensor(tag + f"_ptt{i}", [128, 128], BF16)) for i in range(2)]

    def tens(q):
        for s, tgt in gsems:
            q.wait_ge(s, tgt)

        def transposes(mm2):
            kd2 = kind_of(mm2)
            par2 = mm2 % 2
            if kd2 == 1:
                q.wait_ge(dv, b_dv + ndv(mm2) + 4)
            else:
                q.wait_ge(ract, b_ract + nract(mm2) + 2)
            for j in range(16):
                g = ntp(mm2) + j
                if g >= 2:
                    q.wait_ge(ev, b_ev + g - 1)
                if kd2 == 1 and kres:
                    src = kTs[:, (mm2 % NC) * T + 128 * j:(mm2 % NC) * T + 128 * (j + 1)]
                else:
                    src = phib[par2][:, 128 * j:128 * (j + 1)]
                ins2 = q.transpose(ptt[g % 2][:], src, ident_bf[:])
                ins2.then_inc(tp, 1)

        for m in range(NM):
            kd = kind_of(m)
            par = m % 2
            for hf in range(2):
                if m >= 1:
                    if kind_of(m - 1) < 2:
                        q.wait_ge(dv, b_dv + ndv(m - 1) + 2 * hf + 1)
                        q.wait_ge(ract, b_ract + nract(m - 1) + 2 * hf + 1)
                    else:
                        q.wait_ge(ract, b_ract + nract(m - 1) + hf + 1)
                for k in range(NC):
                    for ns in range(2):
                        t0 = hf * HT + 512 * ns
                        ins = q.matmul(pp[hf][:, 512 * ns:512 * (ns + 1)],
                                       wb[:, k * 3 * D + 128 * m:k * 3 * D + 128 * (m + 1)],
                                       xnT[:, k * T + t0:k * T + t0 + 512],
                                       start=(k == 0), stop=(k == NC - 1))
                ins.then_inc(mm, 1)
            if m >= 1 and kind_of(m - 1) >= 1:
                transposes(m - 1)
        transposes(NM - 1)
    blk.tensor(tens)

    def scal2(q):
        for m in range(NM):
            kd = kind_of(m)
            par = m % 2
            if kd < 2:
                for hf in range(2):
                    q.wait_ge(mm, b_mm + 2 * m + hf + 1)
                    if m >= 1 and kind_of(m - 1) < 2:
                        q.wait_ge(dv, b_dv + ndv(m - 1) + 2 * hf + 2)
                    q.activation(rsc[hf][:], pp[hf][:], AF.Relu).then_inc(ract, 1)
                    q.wait_ge(dv, b_dv + ndv(m) + 2 * hf + 1)
                    q.activation(esc[hf][:], dsc[hf][:], AF.Exp).then_inc(ract, 1)
            else:
                for hf in range(2):
                    q.wait_ge(mm, b_mm + 2 * m + hf + 1)
                    if m >= 1 and kind_of(m - 1) >= 2:
                        q.wait_ge(tp, b_tp + ntp(m - 1) + 16)
                    if m >= 2 and kind_of(m - 2) == 1:
                        if not kres:
                            q.wait_ge(sta[par], b_sta[par] + 16 * ((m - 2) // 2 + 1))
                            q.wait_ge(tp, b_tp + ntp(m - 2) + 16)
                        else:
                            q.wait_ge(sta[par], b_sta[par] + 16 * 4)
                    q.activation(phib[par][:, HT * hf:HT * (hf + 1)], pp[hf][:],
                                 AF.Identity).then_inc(ract, 1)
            if kd >= 1:
                c = m % NC
                hA, hB = 2 * c, 2 * c + 1
                dstt, DS = (ktok, 64) if kd == 1 else (vtok, VS)
                for j in range(16):
                    g = ntp(m) + j
                    q.wait_ge(tp, b_tp + g + 1)
                    q.copy(dstt[:, hA * 16 * DS + DS * j:hA * 16 * DS + DS * j + 64],
                           ptt[g % 2][:, 0:64])
                    q.copy(dstt[:, hB * 16 * DS + DS * j:hB * 16 * DS + DS * j + 64],
                           ptt[g % 2][:, 64:128]).then_inc(ev, 1)
    blk.scalar(scal2)

    def vec(q):
        for m in range(NM):
            kd = kind_of(m)
            par = m % 2
            if kd < 2:
                for hf in range(2):
                    q.wait_ge(ract, b_ract + nract(m) + 2 * hf + 1)
                    if m >= 1 and kind_of(m - 1) < 2:
                        q.wait_ge(ract, b_ract + nract(m - 1) + 2 * hf + 2)
                    q.tensor_tensor(dsc[hf][:], pp[hf][:],
                                    rsc[hf][:], ALU.subtract).then_inc(dv, 1)
                    q.wait_ge(ract, b_ract + nract(m) + 2 * hf + 2)
                    if kd == 1 and kres:
                        q.tensor_tensor(kTs[:, (m % NC) * T + HT * hf:(m % NC) * T + HT * (hf + 1)],
                                        rsc[hf][:], esc[hf][:], ALU.add).then_inc(dv, 1)
                        continue
                    if m >= 2 and kind_of(m - 2) < 2:
                        q.wait_ge(sta[m % 2], b_sta[m % 2] + 16 * ((m - 2) // 2 + 1))
                        if kind_of(m - 2) == 1:
                            q.wait_ge(tp, b_tp + ntp(m - 2) + 16)
                    q.tensor_tensor(phib[par][:, HT * hf:HT * (hf + 1)], rsc[hf][:],
                                    esc[hf][:], ALU.add).then_inc(dv, 1)
    blk.vector(vec)

    def sy0(q):
        for m in range(NM):
            kd = kind_of(m)
            par = m % 2
            if kd == 0 or (kd == 1 and not kres):
                q.wait_ge(dv, b_dv + ndv(m) + 4)
                dst = qT_hbm if kd == 0 else kT_hbm
                q.dma_start(out=dst[128 * (m % NC):128 * (m % NC + 1), :],
                            in_=phib[par][:]).then_inc(sta[par], 16)
        if res_out is None:
            q.wait_ge(ev, b_ev + ev_tot - b_ev)
            for h in range(H):
                q.dma_start(out=ktok_hbm.rearrange("(h j p) e -> h p j e", p=128, j=16)[h],
                            in_=ktok.rearrange("p (h j e) -> p h j e", h=H, j=16)[:, h]).then_inc(stk, 16)
                q.dma_start(out=vtok_hbm.rearrange("(h j p) e -> h p j e", p=128, j=16)[h],
                            in_=vtok.rearrange("p (h j e) -> p h j e", h=H, j=16)[:, h]).then_inc(stk, 16)
            q.wait_ge(stk, stk_tot)
        for s2 in range(2):
            q.wait_ge(sta[s2], sta_tot[s2])
    blk.sync(sy0)
    pctx.close()
    ctx.close()



def emit_l1_scan(p, qT_hbm, kT_hbm, ktok_hbm, vtok_hbm, mask_hbm, ident_bf,
                 xoT_hbm, tag="d2", res_in=None, xoT_res=None):
    nc, blk = p.nc, p.block
    ctx = ExitStack()
    sb = lambda n, shp, dt: ctx.enter_context(nc.sbuf_tensor(tag + n, list(shp), dt))
    pctx = ExitStack()
    psb = lambda n, shp, dt: pctx.enter_context(nc.psum_tensor(tag + n, list(shp), dt))

    qT = sb("_qT", (128, NC * T), BF16)
    kload = res_in is not None and res_in[0] is None
    if res_in is not None:
        kT, ktok, vtok = res_in
        if kT is None:
            kT = sb("_kT", (128, NC * (T // 2)), BF16)
    else:
        kT = sb("_kT", (128, NC * T), BF16)
        ktok = sb("_ktok", (128, H * 16 * 64), BF16)
        vtok = sb("_vtok", (128, H * 16 * 65), BF16)
    maskf = sb("_mask", (128, 128), F32)
    Sf = [sb(f"_Sf{i}", (128, H * 65), F32) for i in range(2)]
    Sb = sb("_Sb", (128, H * 65), BF16)
    Am = [sb(f"_Am{i}", (128, 128), BF16) for i in range(2)]
    rcol = sb("_rcol", (128, 1), F32)
    xotok = [sb(f"_xo{i}", (128, 64), BF16) for i in range(2)]
    xoT = xoT_res if xoT_res is not None else sb("_xoT", (128, NC * T), BF16)
    pA = [psb(f"_pA{i}", (128, 128), F32) for i in range(2)]
    pO = [psb(f"_pO{i}", (128, 65), F32) for i in range(2)]
    pS = [psb(f"_pS{i}", (128, 65), F32) for i in range(2)]
    ptt = [psb(f"_ptt{i}", (64, 128), BF16) for i in range(2)]

    ld = p.sem(tag + "_ld")
    mz = p.sem(tag + "_mz")
    sA = p.sem(tag + "_sA")
    sM = p.sem(tag + "_sM")
    sO = p.sem(tag + "_sO")
    sSt = p.sem(tag + "_sSt")
    sSv = p.sem(tag + "_sSv")
    sSb = p.sem(tag + "_sSb")
    sDn = p.sem(tag + "_sDn")
    sTp = p.sem(tag + "_sTp")
    sEv = p.sem(tag + "_sEv")
    st = p.sem(tag + "_st")
    ldq = [p.sem(tag + f"_ldq{s}") for s in range(2)] if res_in is not None else None
    ldk = [p.sem(tag + f"_ldk{s}") for s in range(2)] if kload else None
    Bv = {}
    for n in ("_ld", "_mz", "_sA", "_sM", "_sO", "_sSt", "_sSv", "_sSb", "_sDn",
              "_sTp", "_sEv", "_st"):
        Bv[n] = p.val(tag + n)
    if res_in is not None:
        b_ldq = [p.val(tag + f"_ldq{s}") for s in range(2)]
        p.add(tag + "_ld", 16)
        for s in range(2):
            p.add(tag + f"_ldq{s}", 16)
        if kload:
            b_ldk = [p.val(tag + f"_ldk{s}") for s in range(2)]
            for s in range(2):
                p.add(tag + f"_ldk{s}", 16)
    else:
        p.add(tag + "_ld", 16 * 5)
    p.add(tag + "_mz", 3)
    NJH = 16 * H
    for n, d in (("_sA", NJH), ("_sM", NJH), ("_sO", NJH), ("_sSt", NJH),
                 ("_sSv", NJH), ("_sSb", NJH), ("_sDn", 2 * NJH), ("_sTp", NJH),
                 ("_sEv", NJH)):
        p.add(tag + n, d)
    if xoT_res is None:
        p.add(tag + "_st", 16 * NC)

    idx = lambda j, h: j * H + h

    def sy0(q):
        q.dma_start(out=qT.rearrange("p (c t) -> p c t", c=NC),
                    in_=qT_hbm.rearrange("(c p) t -> p c t", p=128)).then_inc(ld, 16)
        q.dma_start(out=kT.rearrange("p (c t) -> p c t", c=NC),
                    in_=kT_hbm.rearrange("(c p) t -> p c t", p=128)).then_inc(ld, 16)
        q.dma_start(out=ktok.rearrange("p (h j e) -> p h j e", h=H, j=16),
                    in_=ktok_hbm.rearrange("(h j p) e -> p h j e", p=128, j=16)).then_inc(ld, 16)
        q.dma_start(out=vtok.rearrange("p (h j e) -> p h j e", h=H, j=16)[:, :, :, 0:64],
                    in_=vtok_hbm.rearrange("(h j p) e -> p h j e", p=128, j=16)).then_inc(ld, 16)
        q.dma_start(out=maskf[:], in_=mask_hbm[:]).then_inc(ld, 16)

    def sy0r(q):
        q.dma_start(out=maskf[:], in_=mask_hbm[:]).then_inc(ld, 16)
        for s in range(4):
            if s < 2:
                q.dma_start(out=qT.rearrange("p (c t) -> p c t", c=NC)[:, :, 1024 * s:1024 * (s + 1)],
                            in_=qT_hbm.rearrange("(c p) t -> p c t", p=128)[:, :, 1024 * s:1024 * (s + 1)]).then_inc(ldq[s], 16)
            if kload and s < 2:
                if s == 1:
                    q.wait_ge(sA, Bv["_sA"] + idx(7, H - 1) + 1)
                q.dma_start(out=kT.rearrange("p (c t) -> p c t", c=NC),
                            in_=kT_hbm.rearrange("(c p) t -> p c t", p=128)[:, :, 1024 * s:1024 * (s + 1)]).then_inc(ldk[s], 16)
    blk.sync(sy0 if res_in is None else sy0r)

    def vz(q):
        q.wait_ge(ld, Bv["_ld"] + (16 if res_in is not None else 16 * 5))
        q.memset(vtok.rearrange("p (h j e) -> p h j e", h=H, j=16)[:, :, :, 64:65], 1.0).then_inc(mz, 1)
        q.memset(Sf[1][:], 0.0).then_inc(mz, 1)
        q.memset(Sb[:], 0.0).then_inc(mz, 1)
    blk.vector(vz)

    def tens(q):
        q.wait_ge(mz, Bv["_mz"] + 3)
        for j in range(16):
            if res_in is not None and j % 8 == 0:
                q.wait_ge(ldq[j // 8], b_ldq[j // 8] + 16)
                if kload and j % 8 == 0:
                    q.wait_ge(ldk[j // 8], b_ldk[j // 8] + 16)
            jk = (j % 8) if kload else j
            for h in range(H):
                i = idx(j, h)
                par = i % 2
                c, rb = h // 2, 64 * (h % 2)
                if i >= 2:
                    q.wait_ge(sM, Bv["_sM"] + i - 1)
                KW = (T // 2) if kload else T
                ins = q.matmul(pA[par][:], kT[rb:rb + 64, c * KW + 128 * jk:c * KW + 128 * (jk + 1)],
                               qT[rb:rb + 64, c * T + 128 * j:c * T + 128 * (j + 1)],
                               start=True, stop=True)
                ins.then_inc(sA, 1)
                q.wait_ge(sM, Bv["_sM"] + i + 1)
                if i >= 2:
                    q.wait_ge(sDn, Bv["_sDn"] + 2 * (i - 1))
                if j > 0:
                    q.wait_ge(sSb, Bv["_sSb"] + idx(j - 1, h) + 1)
                q.matmul(pO[par][:], Am[par][:],
                         vtok[:, h * 16 * 65 + 65 * j:h * 16 * 65 + 65 * (j + 1)],
                         start=True, stop=False)
                ins = q.matmul(pO[par][:], qT[rb:rb + 64, c * T + 128 * j:c * T + 128 * (j + 1)],
                               Sb[rb:rb + 64, h * 65:(h + 1) * 65], start=False, stop=True)
                ins.then_inc(sO, 1)
                if i >= 2:
                    q.wait_ge(sSv, Bv["_sSv"] + i - 1)
                ins = q.matmul(pS[par][rb:rb + 64, :],
                               ktok[:, h * 16 * 64 + 64 * j:h * 16 * 64 + 64 * (j + 1)],
                               vtok[:, h * 16 * 65 + 65 * j:h * 16 * 65 + 65 * (j + 1)],
                               start=True, stop=True)
                ins.then_inc(sSt, 1)
                q.wait_ge(sDn, Bv["_sDn"] + 2 * i + 2)
                if i >= 2:
                    q.wait_ge(sEv, Bv["_sEv"] + i - 1)
                ins = q.transpose(ptt[par][:], xotok[par][:], ident_bf[:])
                ins.then_inc(sTp, 1)
    blk.tensor(tens)

    def vec(q):
        for j in range(16):
            for h in range(H):
                i = idx(j, h)
                par = i % 2
                jpar = j % 2
                q.wait_ge(sA, Bv["_sA"] + i + 1)
                if i >= 2:
                    q.wait_ge(sO, Bv["_sO"] + i - 1)
                q.tensor_tensor(Am[par][:], pA[par][:], maskf[:], ALU.mult).then_inc(sM, 1)
                q.wait_ge(sSt, Bv["_sSt"] + i + 1)
                if j > 0:
                    q.wait_ge(sSb, Bv["_sSb"] + idx(j - 1, h) + 1)
                c2, rb2 = h // 2, 64 * (h % 2)
                q.tensor_tensor(Sf[jpar][rb2:rb2 + 64, h * 65:(h + 1) * 65],
                                pS[par][rb2:rb2 + 64, :],
                                Sf[1 - jpar][rb2:rb2 + 64, h * 65:(h + 1) * 65],
                                ALU.add).then_inc(sSv, 1)
                q.wait_ge(sO, Bv["_sO"] + i + 1)
                q.reciprocal(rcol[:], pO[par][:, 64:65]).then_inc(sDn, 1)
                q.wait_ge(sDn, Bv["_sDn"] + 2 * i + 1)
                if i >= 2:
                    q.wait_ge(sTp, Bv["_sTp"] + i - 1)
                q.tensor_scalar(xotok[par][:], pO[par][:, 0:64], rcol[:], None,
                                ALU.mult).then_inc(sDn, 1)
    blk.vector(vec)

    def scal(q):
        for j in range(16):
            for h in range(H):
                i = idx(j, h)
                par = i % 2
                jpar = j % 2
                q.wait_ge(sSv, Bv["_sSv"] + i + 1)
                c2, rb2 = h // 2, 64 * (h % 2)
                q.copy(Sb[rb2:rb2 + 64, h * 65:(h + 1) * 65],
                       Sf[jpar][rb2:rb2 + 64, h * 65:(h + 1) * 65]).then_inc(sSb, 1)
                q.wait_ge(sTp, Bv["_sTp"] + i + 1)
                c, rb = h // 2, 64 * (h % 2)
                q.copy(xoT[rb:rb + 64, c * T + 128 * j:c * T + 128 * (j + 1)],
                       ptt[par][:]).then_inc(sEv, 1)
    blk.scalar(scal)

    def sy1(q):
        for cc in range(NC):
            q.wait_ge(sEv, Bv["_sEv"] + idx(15, 2 * cc + 1) + 1)
            q.dma_start(out=xoT_hbm[128 * cc:128 * (cc + 1), :],
                        in_=xoT[:, cc * T:(cc + 1) * T]).then_inc(st, 16)
        q.wait_ge(st, Bv["_st"] + 16 * NC)
    if xoT_res is None:
        blk.sync(sy1)
    pctx.close()
    ctx.close()



def emit_l1_ffn_in(p, xnT_hbm, w1T_hbm, w2T_hbm, s1c_hbm, s2c_hbm, h_hbm, tag="e1",
                   xnT_res=None):
    nc, blk = p.nc, p.block
    ctx = ExitStack()
    sb = lambda n, shp, dt: ctx.enter_context(nc.sbuf_tensor(tag + n, list(shp), dt))
    pctx = ExitStack()
    psb = lambda n, shp, dt: pctx.enter_context(nc.psum_tensor(tag + n, list(shp), dt))
    HT = T // 2
    NH = DHID // 128

    xnT = xnT_res if xnT_res is not None else sb("_xnT", (128, NC * T), BF16)
    w1b = sb("_w1", (128, NC * DHID), BF16)
    w2b = sb("_w2", (128, NC * DHID), BF16)
    s1c = sb("_s1c", (128, NH), F32)
    s2c = sb("_s2c", (128, NH), F32)
    tsl = [sb(f"_tsl{i}", (128, HT), F32) for i in range(2)]
    tg2 = [sb(f"_tg2{i}", (128, HT), F32) for i in range(2)]
    hb = [sb(f"_hb{i}", (128, T), BF16) for i in range(2)]
    pb1 = [psb(f"_pb1{i}", (128, HT), F32) for i in range(2)]
    pb2 = [psb(f"_pb2{i}", (128, HT), F32) for i in range(2)]

    ld = p.sem(tag + "_ld")
    mm = p.sem(tag + "_mm")
    ac = p.sem(tag + "_ac")
    dv = p.sem(tag + "_dv")
    stf = [p.sem(tag + f"_stf{i}") for i in range(2)]
    b_ld = p.val(tag + "_ld")
    b_mm = p.val(tag + "_mm")
    b_ac = p.val(tag + "_ac")
    b_dv = p.val(tag + "_dv")
    b_stf = [p.val(tag + f"_stf{i}") for i in range(2)]
    p.add(tag + "_ld", 32)
    NI = NH * 2
    for i in range(NI):
        p.add(tag + "_mm", 2)
        p.add(tag + "_ac", 1)
        p.add(tag + "_dv", 2)
    for m in range(NH):
        p.add(tag + f"_stf{m % 2}", 16)

    gsems = []

    def gp0(q):
        if xnT_res is None:
            gsems.append(p.swdge(q, xnT.rearrange("p (c t) -> p c t", c=NC),
                                 xnT_hbm.rearrange("(c p) t -> p c t", p=128)))
        gsems.append(p.swdge(q, w1b.rearrange("p (c o) -> p c o", c=NC),
                             w1T_hbm.rearrange("(c p) o -> p c o", p=128)))
        gsems.append(p.swdge(q, w2b.rearrange("p (c o) -> p c o", c=NC),
                             w2T_hbm.rearrange("(c p) o -> p c o", p=128)))
    blk.gpsimd(gp0)

    def sy0(q):
        q.dma_start(out=s1c[:], in_=s1c_hbm[:]).then_inc(ld, 16)
        q.dma_start(out=s2c[:], in_=s2c_hbm[:]).then_inc(ld, 16)
    blk.sync(sy0)

    def tens(q):
        q.wait_ge(ld, b_ld + 32)
        for s, tgt in gsems:
            q.wait_ge(s, tgt)
        for i in range(NI):
            m, hf = i // 2, i % 2
            par = i % 2
            if i >= 2:
                q.wait_ge(ac, b_ac + i - 1)
                q.wait_ge(dv, b_dv + 2 * (i - 2) + 1)
            for k in range(NC):
                for ns in range(2):
                    t0 = hf * HT + 512 * ns
                    ins = q.matmul(pb1[par][:, 512 * ns:512 * (ns + 1)],
                                   w1b[:, k * DHID + 128 * m:k * DHID + 128 * (m + 1)],
                                   xnT[:, k * T + t0:k * T + t0 + 512],
                                   start=(k == 0), stop=(k == NC - 1))
            ins.then_inc(mm, 1)
            for k in range(NC):
                for ns in range(2):
                    t0 = hf * HT + 512 * ns
                    ins = q.matmul(pb2[par][:, 512 * ns:512 * (ns + 1)],
                                   w2b[:, k * DHID + 128 * m:k * DHID + 128 * (m + 1)],
                                   xnT[:, k * T + t0:k * T + t0 + 512],
                                   start=(k == 0), stop=(k == NC - 1))
            ins.then_inc(mm, 1)
    blk.tensor(tens)

    def scal(q):
        for i in range(NI):
            m, hf = i // 2, i % 2
            par = i % 2
            q.wait_ge(mm, b_mm + 2 * i + 1)
            if i >= 2:
                q.wait_ge(dv, b_dv + 2 * (i - 2) + 2)
            q.activation(tsl[par][:], pb1[par][:], AF.Silu,
                         scale=s1c[:, m:m + 1]).then_inc(ac, 1)
    blk.scalar(scal)

    def vec(q):
        for i in range(NI):
            m, hf = i // 2, i % 2
            par = i % 2
            q.wait_ge(mm, b_mm + 2 * i + 2)
            if i >= 2:
                q.wait_ge(dv, b_dv + 2 * (i - 2) + 2)
            q.tensor_scalar(tg2[par][:], pb2[par][:], s2c[:, m:m + 1], None,
                            ALU.mult).then_inc(dv, 1)
            q.wait_ge(dv, b_dv + 2 * i + 1)
            q.wait_ge(ac, b_ac + i + 1)
            if i >= 4:
                mprev = (i - 4) // 2
                q.wait_ge(stf[mprev % 2], b_stf[mprev % 2] + 16 * (mprev // 2 + 1))
            q.tensor_tensor(hb[m % 2][:, HT * hf:HT * (hf + 1)], tsl[par][:],
                            tg2[par][:], ALU.mult).then_inc(dv, 1)
    blk.vector(vec)

    def sy1(q):
        for m in range(NH):
            q.wait_ge(dv, b_dv + 2 * (2 * m + 1) + 2)
            q.dma_start(out=h_hbm[128 * m:128 * (m + 1), :], in_=hb[m % 2][:]).then_inc(stf[m % 2], 16)
        for s2 in range(2):
            q.wait_ge(stf[s2], b_stf[s2] + 16 * (NH // 2))
    blk.sync(sy1)
    pctx.close()
    ctx.close()


def emit_l1_ffn_in_v2(p, xnT_hbm, w1T_hbm, w2T_hbm, s1c_hbm, s2c_hbm, hres, tag="e1"):
    nc, blk = p.nc, p.block
    ctx = ExitStack()
    sb = lambda n, shp, dt: ctx.enter_context(nc.sbuf_tensor(tag + n, list(shp), dt))
    pctx = ExitStack()
    psb = lambda n, shp, dt: pctx.enter_context(nc.psum_tensor(tag + n, list(shp), dt))
    HT = T // 2
    NH = DHID // 128

    xnT = sb("_xnT", (128, NC * T), BF16)
    s1c = sb("_s1c", (128, NH), F32)
    s2c = sb("_s2c", (128, NH), F32)
    w1s = [sb(f"_w1s{i}", (128, NC * 128), F32) for i in range(2)]
    w2s = [sb(f"_w2s{i}", (128, NC * 128), F32) for i in range(2)]
    w1c = [sb(f"_w1c{i}", (128, NC * 128), BF16) for i in range(2)]
    w2c = [sb(f"_w2c{i}", (128, NC * 128), BF16) for i in range(2)]
    tsl = [sb(f"_tsl{i}", (128, HT), F32) for i in range(2)]
    tg2 = [sb(f"_tg2{i}", (128, HT), F32) for i in range(2)]
    pb1 = [psb(f"_pb1{i}", (128, HT), F32) for i in range(2)]
    pb2 = [psb(f"_pb2{i}", (128, HT), F32) for i in range(2)]

    ld = p.sem(tag + "_ld")
    wld = [p.sem(tag + f"_wld{i}") for i in range(2)]
    w2ld = [p.sem(tag + f"_w2ld{i}") for i in range(2)]
    pc = p.sem(tag + "_pc")
    mm = p.sem(tag + "_mm")
    ac = p.sem(tag + "_ac")
    dv = p.sem(tag + "_dv")
    b_ld = p.val(tag + "_ld")
    b_wld = [p.val(tag + f"_wld{i}") for i in range(2)]
    b_w2ld = [p.val(tag + f"_w2ld{i}") for i in range(2)]
    b_pc = p.val(tag + "_pc")
    b_mm = p.val(tag + "_mm")
    b_ac = p.val(tag + "_ac")
    b_dv = p.val(tag + "_dv")
    p.add(tag + "_ld", 32)
    NI = NH * 2
    for m in range(NH):
        p.add(tag + f"_wld{m % 2}", 16)
        p.add(tag + f"_w2ld{m % 2}", 16)
        p.add(tag + "_pc", 2)
    for i in range(NI):
        p.add(tag + "_mm", 2)
        p.add(tag + "_ac", 1)
        p.add(tag + "_dv", 2)

    gsems = []

    def gpx(q):
        gsems.append(p.swdge(q, xnT.rearrange("p (c t) -> p c t", c=NC),
                             xnT_hbm.rearrange("(c p) t -> p c t", p=128)))
    blk.gpsimd(gpx)

    def sy0(q):
        q.dma_start(out=s1c[:], in_=s1c_hbm[:]).then_inc(ld, 16)
        q.dma_start(out=s2c[:], in_=s2c_hbm[:]).then_inc(ld, 16)
        for m in range(NH):
            par = m % 2
            if m >= 2:
                q.wait_ge(pc, b_pc + 2 * (m - 1))
            q.dma_start(out=w1s[par].rearrange("p (k o) -> p k o", k=NC),
                        in_=w1T_hbm.rearrange("(k p) o -> p k o", p=128)[:, :, 128 * m:128 * (m + 1)]).then_inc(wld[par], 16)
            q.dma_start(out=w2s[par].rearrange("p (k o) -> p k o", k=NC),
                        in_=w2T_hbm.rearrange("(k p) o -> p k o", p=128)[:, :, 128 * m:128 * (m + 1)]).then_inc(w2ld[par], 16)
    blk.sync(sy0)

    def gp0(q):
        for m in range(NH):
            par = m % 2
            q.wait_ge(wld[par], b_wld[par] + 16 * (m // 2 + 1))
            q.wait_ge(w2ld[par], b_w2ld[par] + 16 * (m // 2 + 1))
            if m >= 2:
                q.wait_ge(mm, b_mm + 4 * (m - 1))
            q.tensor_copy(w1c[par][:], w1s[par][:]).then_inc(pc, 1)
            q.tensor_copy(w2c[par][:], w2s[par][:]).then_inc(pc, 1)
    blk.gpsimd(gp0)

    def tens(q):
        q.wait_ge(ld, b_ld + 32)
        for s, tgt in gsems:
            q.wait_ge(s, tgt)
        for i in range(NI):
            m, hf = i // 2, i % 2
            par = i % 2
            wpar = m % 2
            if hf == 0:
                q.wait_ge(pc, b_pc + 2 * m + 2)
            if i >= 2:
                q.wait_ge(ac, b_ac + i - 1)
                q.wait_ge(dv, b_dv + 2 * (i - 2) + 1)
            for k in range(NC):
                for ns in range(2):
                    t0 = hf * HT + 512 * ns
                    ins = q.matmul(pb1[par][:, 512 * ns:512 * (ns + 1)],
                                   w1c[wpar][:, 128 * k:128 * (k + 1)],
                                   xnT[:, k * T + t0:k * T + t0 + 512],
                                   start=(k == 0), stop=(k == NC - 1))
            ins.then_inc(mm, 1)
            for k in range(NC):
                for ns in range(2):
                    t0 = hf * HT + 512 * ns
                    ins = q.matmul(pb2[par][:, 512 * ns:512 * (ns + 1)],
                                   w2c[wpar][:, 128 * k:128 * (k + 1)],
                                   xnT[:, k * T + t0:k * T + t0 + 512],
                                   start=(k == 0), stop=(k == NC - 1))
            ins.then_inc(mm, 1)
    blk.tensor(tens)

    def scal(q):
        for i in range(NI):
            m, hf = i // 2, i % 2
            par = i % 2
            q.wait_ge(mm, b_mm + 2 * i + 1)
            if i >= 2:
                q.wait_ge(dv, b_dv + 2 * (i - 2) + 2)
            q.activation(tsl[par][:], pb1[par][:], AF.Silu,
                         scale=s1c[:, m:m + 1]).then_inc(ac, 1)
    blk.scalar(scal)

    def vec(q):
        for i in range(NI):
            m, hf = i // 2, i % 2
            par = i % 2
            q.wait_ge(mm, b_mm + 2 * i + 2)
            if i >= 2:
                q.wait_ge(dv, b_dv + 2 * (i - 2) + 2)
            q.tensor_scalar(tg2[par][:], pb2[par][:], s2c[:, m:m + 1], None,
                            ALU.mult).then_inc(dv, 1)
            q.wait_ge(dv, b_dv + 2 * i + 1)
            q.wait_ge(ac, b_ac + i + 1)
            q.tensor_tensor(hres[:, m * T + HT * hf:m * T + HT * (hf + 1)],
                            tsl[par][:], tg2[par][:], ALU.mult).then_inc(dv, 1)
    blk.vector(vec)
    pctx.close()
    ctx.close()



def emit_l1_ffn_mid(p, in_hbm, wT_hbm, sc_hbm, out_hbm, tag="e2", hin_res=None):
    nc, blk = p.nc, p.block
    ctx = ExitStack()
    sb = lambda n, shp, dt: ctx.enter_context(nc.sbuf_tensor(tag + n, list(shp), dt))
    pctx = ExitStack()
    psb = lambda n, shp, dt: pctx.enter_context(nc.psum_tensor(tag + n, list(shp), dt))
    NH = DHID // 128

    hin = hin_res if hin_res is not None else sb("_hin", (128, NH * T), BF16)
    scb = sb("_sc", (128, NH), F32)
    wstg = [sb(f"_ws{i}", (128, NH * 128), F32) for i in range(2)]
    wcol = [sb(f"_wc{i}", (128, NH * 128), BF16) for i in range(2)]
    ob = [sb(f"_ob{i}", (128, T), BF16) for i in range(2)]
    pp = [psb(f"_pp{i}", (128, T), F32) for i in range(2)]

    ld = p.sem(tag + "_ld")
    wld = [p.sem(tag + f"_wld{i}") for i in range(2)]
    pc = p.sem(tag + "_pc")
    mm = p.sem(tag + "_mm")
    ac = p.sem(tag + "_ac")
    stf = [p.sem(tag + f"_stf{i}") for i in range(2)]
    b_ld = p.val(tag + "_ld")
    b_wld = [p.val(tag + f"_wld{i}") for i in range(2)]
    b_pc = p.val(tag + "_pc")
    b_mm = p.val(tag + "_mm")
    b_ac = p.val(tag + "_ac")
    b_stf = [p.val(tag + f"_stf{i}") for i in range(2)]
    p.add(tag + "_ld", 16)
    for m in range(NH):
        p.add(tag + f"_wld{m % 2}", 16)
        p.add(tag + "_pc", 1)
        p.add(tag + "_mm", 1)
        p.add(tag + "_ac", 1)
        p.add(tag + f"_stf{m % 2}", 16)

    hsem = []

    def gp0(q):
        if hin_res is None:
            hsem.append(p.swdge(q, hin.rearrange("p (c t) -> p c t", c=NH),
                                in_hbm.rearrange("(c p) t -> p c t", p=128)))
        for m in range(NH):
            par = m % 2
            q.wait_ge(wld[par], b_wld[par] + 16 * (m // 2 + 1))
            if m >= 2:
                q.wait_ge(mm, b_mm + m - 1)
            q.tensor_copy(wcol[par][:], wstg[par][:]).then_inc(pc, 1)
    blk.gpsimd(gp0)

    def sy0(q):
        q.dma_start(out=scb[:], in_=sc_hbm[:]).then_inc(ld, 16)
        for m in range(NH):
            par = m % 2
            if m >= 2:
                q.wait_ge(pc, b_pc + m - 1)
            q.dma_start(out=wstg[par].rearrange("p (k o) -> p k o", k=NH),
                        in_=wT_hbm.rearrange("(k p) o -> p k o", p=128)[:, :, 128 * m:128 * (m + 1)]).then_inc(wld[par], 16)
    blk.sync(sy0)

    def tens(q):
        q.wait_ge(ld, b_ld + 16)
        if hin_res is None:
            q.wait_ge(hsem[0][0], hsem[0][1])
        for m in range(NH):
            par = m % 2
            q.wait_ge(pc, b_pc + m + 1)
            if m >= 2:
                q.wait_ge(ac, b_ac + m - 1)
            for k in range(NH):
                for ns in range(4):
                    ins = q.matmul(pp[par][:, 512 * ns:512 * (ns + 1)],
                                   wcol[par][:, 128 * k:128 * (k + 1)],
                                   hin[:, k * T + 512 * ns:k * T + 512 * (ns + 1)],
                                   start=(k == 0), stop=(k == NH - 1))
            ins.then_inc(mm, 1)
    blk.tensor(tens)

    def scal(q):
        for m in range(NH):
            par = m % 2
            q.wait_ge(mm, b_mm + m + 1)
            if m >= 2:
                q.wait_ge(stf[par], b_stf[par] + 16 * ((m - 2) // 2 + 1))
            q.activation(ob[par][:], pp[par][:], AF.Gelu,
                         scale=scb[:, m:m + 1]).then_inc(ac, 1)
            q.wait_ge(ac, b_ac + m + 1)
            q.dma_start(out=out_hbm[128 * m:128 * (m + 1), :], in_=ob[par][:]).then_inc(stf[par], 16)
        for s2 in range(2):
            q.wait_ge(stf[s2], b_stf[s2] + 16 * (NH // 2))
    blk.scalar(scal)
    pctx.close()
    ctx.close()


def emit_l1_ffn_out(p, h2_hbm, w3T_hbm, s3r_hbm, x1_hbm, s1_hbm, ones_row_sb,
                    out_hbm, tag="e4"):
    nc, blk = p.nc, p.block
    ctx = ExitStack()
    sb = lambda n, shp, dt: ctx.enter_context(nc.sbuf_tensor(tag + n, list(shp), dt))
    pctx = ExitStack()
    psb = lambda n, shp, dt: pctx.enter_context(nc.psum_tensor(tag + n, list(shp), dt))
    NH = DHID // 128

    HT2 = T // 2
    hin = sb("_hin", (128, NH * HT2), BF16)
    w3b = sb("_w3", (128, NH * D), BF16)
    s3r = sb("_s3r", (1, D), F32)
    s3bc = sb("_s3bc", (128, D), F32)
    zsc = [sb(f"_zsc{i}", (128, D), F32) for i in range(2)]
    gsl = [sb(f"_gsl{i}", (128, D), F32) for i in range(2)]
    x1t = [sb(f"_x1t{i}", (128, D), F32) for i in range(2)]
    s1t = [sb(f"_s1t{i}", (128, D), F32) for i in range(2)]
    tt1 = sb("_tt1", (128, D), F32)
    x2sb = [sb(f"_x2sb{i}", (128, D), F32) for i in range(2)]
    pp = [psb(f"_pp{i}", (128, D), F32) for i in range(3)]
    pB = psb("_pB", (128, D), F32)

    ld = p.sem(tag + "_ld")
    lda = [p.sem(tag + f"_lda{i}") for i in range(2)]
    mm = p.sem(tag + "_mm")
    ac = p.sem(tag + "_ac")
    dv = p.sem(tag + "_dv")
    ev = p.sem(tag + "_ev")
    stf = [p.sem(tag + f"_stf{i}") for i in range(2)]
    b_ld = p.val(tag + "_ld")
    b_lda = [p.val(tag + f"_lda{i}") for i in range(2)]
    b_mm = p.val(tag + "_mm")
    b_ac = p.val(tag + "_ac")
    b_dv = p.val(tag + "_dv")
    b_ev = p.val(tag + "_ev")
    b_stf = [p.val(tag + f"_stf{i}") for i in range(2)]
    p.add(tag + "_ld", 16)
    p.add(tag + "_mm", 1 + NT)
    p.add(tag + "_ev", 1)
    p.add(tag + "_ac", NT)
    p.add(tag + "_dv", 3 * NT)
    for t in range(NT):
        p.add(tag + f"_lda{t % 2}", 32)
        p.add(tag + f"_stf{t % 2}", 16)

    gsems = []
    hsems = []

    def gp0(q):
        wv = w3b.rearrange("p (c o) -> p c o", c=NH)
        wh = w3T_hbm.rearrange("(c p) o -> p c o", p=128)
        hv = hin.rearrange("p (c t) -> p c t", c=NH)
        hh = h2_hbm.rearrange("(c p) t -> p c t", p=128)
        for s in range(2):
            gsems.append(p.swdge(q, wv[:, 8 * s:8 * (s + 1)], wh[:, 8 * s:8 * (s + 1)]))
            hsems.append(p.swdge(q, hv[:, :, 512 * s:512 * (s + 1)],
                                 hh[:, :, 512 * s:512 * (s + 1)]))
        for s in range(2, 4):
            gsems.append(p.swdge(q, wv[:, 8 * s:8 * (s + 1)], wh[:, 8 * s:8 * (s + 1)]))
        q.wait_ge(mm, b_mm + 1 + NT // 2)
        for s in range(2):
            hsems.append(p.swdge(q, hv[:, :, 512 * s:512 * (s + 1)],
                                 hh[:, :, HT2 + 512 * s:HT2 + 512 * (s + 1)]))
    blk.gpsimd(gp0)

    def sy0(q):
        q.dma_start(out=s3r[:], in_=s3r_hbm[:]).then_inc(ld, 16)
        for t in range(NT + 2):
            if t < NT:
                if t >= 2:
                    q.wait_ge(dv, b_dv + 3 * (t - 2) + 3)
                q.dma_start(out=x1t[t % 2][:], in_=x1_hbm[128 * t:128 * (t + 1), :]).then_inc(lda[t % 2], 16)
                q.dma_start(out=s1t[t % 2][:], in_=s1_hbm[128 * t:128 * (t + 1), :]).then_inc(lda[t % 2], 16)
            if t >= 2:
                ts2 = t - 2
                q.wait_ge(dv, b_dv + 3 * ts2 + 3)
                q.dma_start(out=out_hbm[128 * ts2:128 * (ts2 + 1), :],
                            in_=x2sb[ts2 % 2][:]).then_inc(stf[ts2 % 2], 16)
        for s2 in range(2):
            q.wait_ge(stf[s2], b_stf[s2] + 16 * (NT // 2))
    blk.sync(sy0)

    def tens(q):
        q.wait_ge(ld, b_ld + 16)
        for ns in range(2):
            ins = q.matmul(pB[:, 512 * ns:512 * (ns + 1)], ones_row_sb[:],
                           s3r[0:1, 512 * ns:512 * (ns + 1)], start=True, stop=True)
        ins.then_inc(mm, 1)
        for t in range(NT):
            par = t % 3
            hf2, tl = t // (NT // 2), t % (NT // 2)
            if tl % 4 == 0:
                s2, tgt = hsems[2 * hf2 + tl // 4]
                q.wait_ge(s2, tgt)
            if t >= 3:
                q.wait_ge(dv, b_dv + 3 * (t - 3) + 1)
            for k in range(NH):
                if t == 0 and k % 8 == 0:
                    s2, tgt = gsems[k // 8]
                    q.wait_ge(s2, tgt)
                for ns in range(2):
                    ins = q.matmul(pp[par][:, 512 * ns:512 * (ns + 1)],
                                   hin[:, k * HT2 + 128 * tl:k * HT2 + 128 * (tl + 1)],
                                   w3b[:, k * D + 512 * ns:k * D + 512 * (ns + 1)],
                                   start=(k == 0), stop=(k == NH - 1))
            ins.then_inc(mm, 1)
    blk.tensor(tens)

    def scal(q):
        q.wait_ge(mm, b_mm + 1)
        q.copy(s3bc[:], pB[:]).then_inc(ev, 1)
        for t in range(NT):
            par = t % 2
            q.wait_ge(dv, b_dv + 3 * t + 1)
            if t >= 2:
                q.wait_ge(dv, b_dv + 3 * (t - 2) + 2)
            q.activation(gsl[par][:], zsc[par][:], AF.Silu).then_inc(ac, 1)
    blk.scalar(scal)

    def vec(q):
        q.wait_ge(ev, b_ev + 1)
        for t in range(NT):
            par = t % 2
            q.wait_ge(mm, b_mm + 1 + t + 1)
            if t >= 2:
                q.wait_ge(ac, b_ac + t - 1)
            q.tensor_tensor(zsc[par][:], pp[t % 3][:], s3bc[:], ALU.mult).then_inc(dv, 1)
            q.wait_ge(ac, b_ac + t + 1)
            q.wait_ge(lda[par], b_lda[par] + 32 * (t // 2 + 1))
            q.tensor_tensor(tt1[:], gsl[par][:], s1t[par][:], ALU.mult).then_inc(dv, 1)
            q.wait_ge(dv, b_dv + 3 * t + 2)
            if t >= 2:
                q.wait_ge(stf[par], b_stf[par] + 16 * ((t - 2) // 2 + 1))
            q.tensor_tensor(x2sb[par][:], tt1[:], x1t[par][:], ALU.add).then_inc(dv, 1)
    blk.vector(vec)

    pctx.close()
    ctx.close()


def build_l1_qkv_program():
    nc = bass.Bass("TRN2", target_bir_lowering=False)
    xnT = dram_io(nc, "xnT", (D, T), BF16, "ExternalInput")
    wqkvT = dram_io(nc, "wqkvT", (D, 3 * D), F32, "ExternalInput")
    identf = dram_io(nc, "ident", (128, 128), F32, "ExternalInput")
    qT = dram_io(nc, "qT", (D, T), BF16, "ExternalOutput")
    kT = dram_io(nc, "kT", (D, T), BF16, "ExternalOutput")
    ktok = dram_io(nc, "ktok", (H * 16 * 128, 64), BF16, "ExternalOutput")
    vtok = dram_io(nc, "vtok", (H * 16 * 128, 64), BF16, "ExternalOutput")
    with ExitStack() as ctx:
        p = Prog(nc, ctx)
        cs = _const_loader(p, ctx, {"ident": (identf, (128, 128), "bf16")})
        emit_l1_qkv(p, xnT, wqkvT, cs["ident"], qT, kT, ktok, vtok)
    return nc


def build_l1_scan_program():
    nc = bass.Bass("TRN2", target_bir_lowering=False)
    qT = dram_io(nc, "qT", (D, T), BF16, "ExternalInput")
    kT = dram_io(nc, "kT", (D, T), BF16, "ExternalInput")
    ktok = dram_io(nc, "ktok", (H * 16 * 128, 64), BF16, "ExternalInput")
    vtok = dram_io(nc, "vtok", (H * 16 * 128, 64), BF16, "ExternalInput")
    maskm = dram_io(nc, "maskm", (128, 128), F32, "ExternalInput")
    identf = dram_io(nc, "ident", (128, 128), F32, "ExternalInput")
    xoT = dram_io(nc, "xoT", (D, T), BF16, "ExternalOutput")
    with ExitStack() as ctx:
        p = Prog(nc, ctx)
        cs = _const_loader(p, ctx, {"ident": (identf, (128, 128), "bf16")})
        emit_l1_scan(p, qT, kT, ktok, vtok, maskm, cs["ident"], xoT)
    return nc


def build_l1_ffn_in_program():
    nc = bass.Bass("TRN2", target_bir_lowering=False)
    xnT = dram_io(nc, "xnT", (D, T), BF16, "ExternalInput")
    w1T = dram_io(nc, "w1T", (D, DHID), F32, "ExternalInput")
    w2T = dram_io(nc, "w2T", (D, DHID), F32, "ExternalInput")
    s1c = dram_io(nc, "s1c", (128, DHID // 128), F32, "ExternalInput")
    s2c = dram_io(nc, "s2c", (128, DHID // 128), F32, "ExternalInput")
    h = dram_io(nc, "h", (DHID, T), BF16, "ExternalOutput")
    with ExitStack() as ctx:
        p = Prog(nc, ctx)
        emit_l1_ffn_in(p, xnT, w1T, w2T, s1c, s2c, h)
    return nc


def build_l1_ffn_mid_program():
    nc = bass.Bass("TRN2", target_bir_lowering=False)
    hin = dram_io(nc, "hin", (DHID, T), BF16, "ExternalInput")
    wT = dram_io(nc, "wT", (DHID, DHID), F32, "ExternalInput")
    sc = dram_io(nc, "sc", (128, DHID // 128), F32, "ExternalInput")
    hout = dram_io(nc, "hout", (DHID, T), BF16, "ExternalOutput")
    with ExitStack() as ctx:
        p = Prog(nc, ctx)
        emit_l1_ffn_mid(p, hin, wT, sc, hout)
    return nc


def build_l1_ffn_out_program():
    nc = bass.Bass("TRN2", target_bir_lowering=False)
    h2 = dram_io(nc, "h2", (DHID, T), BF16, "ExternalInput")
    w3T = dram_io(nc, "w3T", (DHID, D), F32, "ExternalInput")
    s3r = dram_io(nc, "s3r", (1, D), F32, "ExternalInput")
    x1 = dram_io(nc, "x1", (T, D), F32, "ExternalInput")
    s1 = dram_io(nc, "s1", (T, D), F32, "ExternalInput")
    onesr = dram_io(nc, "ones_row", (1, 128), F32, "ExternalInput")
    out = dram_io(nc, "out", (T, D), F32, "ExternalOutput")
    with ExitStack() as ctx:
        p = Prog(nc, ctx)
        cs = _const_loader(p, ctx, {"ones_row": (onesr, (1, 128), "f32")})
        emit_l1_ffn_out(p, h2, w3T, s3r, x1, s1, cs["ones_row"], out)
    return nc




def build_fused_program():
    nc = bass.Bass("TRN2", target_bir_lowering=False)
    EI = lambda n, shp, dt=F32: dram_io(nc, n, shp, dt, "ExternalInput")
    IN = lambda n, shp, dt=F32: dram_io(nc, n, shp, dt, "Internal")

    x = EI("x", (T, D))
    identf = EI("ident", (128, 128))
    scon = EI("sconsts", (128, len(CONST_NAMES)))
    onesr = EI("ones_row", (1, 128))
    wq = EI("wq", (D, D))
    wkv = EI("wkv", (D, D))
    wout = EI("wout", (D, D))
    bout_row = EI("bout_row", (1, D))
    cos2 = EI("cos2", (128, T))
    sin2 = EI("sin2", (128, T))
    rot128 = EI("rot128", (128, 128))
    prm_shapes = {
        "sigraw": (64, D), "knots": (1, 8), "temp": (1, 1), "oscale": (1, 1),
        "cw1": (D, 256), "cb1_cols": (128, 2), "cw2": (256, 2), "cb2_col": (2, 1),
        "w1g": (16, 8 * 128), "b1g": (128, 8), "w2g": (128, 8 * 64), "b2_col": (64, 1),
        "dirs": (64, D), "gam_row": (1, D), "bet_row": (1, D),
        "comp_rep": (2, 16), "iota64": (128, 64), "iota64m": (128, 64),
        "expand8": (8, 64), "gmean": (64, 8),
    }
    prm = {k: EI("p_" + k, shp) for k, shp in prm_shapes.items()}
    wqkvT = EI("wqkvT", (D, 3 * D))
    woutT1 = EI("woutT1", (D, D))
    maskm = EI("maskm", (128, 128))
    w1T = EI("w1T", (D, DHID))
    w2T = EI("w2T", (D, DHID))
    s1c = EI("s1c", (128, DHID // 128))
    s2c = EI("s2c", (128, DHID // 128))
    h1T = EI("h1T", (DHID, DHID))
    h1sc = EI("h1sc", (128, DHID // 128))
    h2T = EI("h2T", (DHID, DHID))
    h2sc = EI("h2sc", (128, DHID // 128))
    w3T = EI("w3T", (DHID, D))
    s3r = EI("s3r", (1, D))
    out = dram_io(nc, "out", (T, D), F32, "ExternalOutput")

    xn_tok = IN("i_xn_tok", (T, D))
    xnT = IN("i_xnT", (D, T), BF16)
    qT0 = IN("i_qT0", (D, T), BF16)
    kvT0 = IN("i_kvT0", (D, T), BF16)
    kvtok = IN("i_kvtok", (H * 16 * 128, 65), BF16)
    xoT0 = IN("i_xoT0", (D, T), BF16)
    s0 = IN("i_s0", (T, D))
    yscr = IN("i_yscr", (T, D))
    x1 = IN("i_x1", (T, D))
    xn1T = IN("i_xn1T", (D, T), BF16)
    q1T = IN("i_q1T", (D, T), BF16)
    k1T = IN("i_k1T", (D, T), BF16)
    ktok1 = IN("i_ktok1", (H * 16 * 128, 64), BF16)
    vtok1 = IN("i_vtok1", (H * 16 * 128, 64), BF16)
    xo1T = IN("i_xo1T", (D, T), BF16)
    s1 = IN("i_s1", (T, D))
    hmat = IN("i_h", (DHID, T), BF16)
    h1m = IN("i_h1", (DHID, T), BF16)
    h2m = IN("i_h2", (DHID, T), BF16)

    with ExitStack() as ctx:
        p = Prog(nc, ctx)
        cs = _const_loader(p, ctx, {"ident": (identf, (128, 128), "bf16"),
                                    "sconsts": (scon, (128, len(CONST_NAMES)), "f32"),
                                    "ones_row": (onesr, (1, 128), "f32")})
        cpool, ones_sb, idb, idf = cs["sconsts"], cs["ones_row"], cs["ident"], cs["ident_f32"]
        lnx = ExitStack()
        ln1xt = [lnx.enter_context(nc.sbuf_tensor(f"sh_lnxt{i}", [128, D], F32))
                 for i in range(2)]
        lctx = ExitStack()
        xnTs0 = lctx.enter_context(nc.sbuf_tensor("sh_xnT0", [128, NC * T], BF16))
        s01 = ExitStack()
        qTs = s01.enter_context(nc.sbuf_tensor("sh_qT0", [128, NC * T], BF16))
        kvTs = s01.enter_context(nc.sbuf_tensor("sh_kvT0", [128, NC * T], BF16))
        ktokS = s01.enter_context(nc.sbuf_tensor("sh_ktok0", [128, H * 16 * 65], BF16))
        bctx = ExitStack()
        bsb = lambda n, shp, dt: bctx.enter_context(nc.sbuf_tensor("pf" + n, list(shp), dt))
        b1b = {"wqb": bsb("_wq", (128, NC * D), BF16),
               "wkvb": bsb("_wkv", (128, NC * D), BF16),
               "cosb": bsb("_cos", (128, T), BF16),
               "sinf": bsb("_sin", (128, T), F32),
               "rotb": bsb("_rot", (128, 128), BF16)}
        b1pre = []

        def preb1(q):
            b1pre.append(p.swdge(q, b1b["wqb"].rearrange("p (c o) -> p c o", c=NC),
                                 wq.rearrange("(c p) o -> p c o", p=128)))
            b1pre.append(p.swdge(q, b1b["wkvb"].rearrange("p (c o) -> p c o", c=NC),
                                 wkv.rearrange("(c p) o -> p c o", p=128)))
            b1pre.append(p.swdge(q, b1b["cosb"][:], cos2[:]))
            b1pre.append(p.swdge(q, b1b["sinf"][:], sin2[:]))
            b1pre.append(p.swdge(q, b1b["rotb"][:], rot128[:]))
        p.block.gpsimd(preb1)
        emit_ln(p, x, xn_tok, xnT, idb, cpool, "ln0", xnT_res=xnTs0)
        p.barrier_sync()
        emit_l0_qkv(p, xnT, wq, wkv, cos2, sin2, rot128, idb, qT0, kvT0, kvtok, tag="b1",
                    xnT_res=xnTs0, res_out=(qTs, kvTs, ktokS), pre=(b1pre, b1b))
        bctx.close()
        p.barrier_sync()
        emit_l0_attn(p, qT0, kvT0, kvtok, cpool, ones_sb, xoT0, tag="b2",
                     res_in=(qTs, kvTs, ktokS))
        s01.close()
        p.barrier_sync()
        emit_out_proj(p, xoT0, wout, bout_row, ones_sb, s0, tag="b3", with_bias=True)
        p.barrier_sync()
        emit_l0_ffn(p, xnT, xn_tok, x, s0, prm, cpool, ones_sb, idf, idb, yscr, x1, tag="c",
                    xnT_res=xnTs0)
        lctx.close()
        xg_sems = [p.sem("c_stxa0"), p.sem("c_stxa1")]
        xg_base = [p.val("c_stxa0") - 16 * (NT // 2), p.val("c_stxa1") - 16 * (NT // 2)]
        l2 = ExitStack()
        xn1Ts = l2.enter_context(nc.sbuf_tensor("sh_xnT1", [128, NC * T], BF16))
        dsh = ExitStack()
        ktokS1 = dsh.enter_context(nc.sbuf_tensor("sh_ktok1", [128, H * 16 * 64], BF16))
        vtokS1 = dsh.enter_context(nc.sbuf_tensor("sh_vtok1", [128, H * 16 * 65], BF16))
        dctx = ExitStack()
        wbs1 = dctx.enter_context(nc.sbuf_tensor("pf_wqkv", [128, NC * 3 * D], BF16))
        d1pre = []
        c_fin = [(p.sem("c_sty0"), p.val("c_sty0")), (p.sem("c_sty1"), p.val("c_sty1")),
                 (p.sem("c_stxa0"), p.val("c_stxa0")), (p.sem("c_stxa1"), p.val("c_stxa1"))]

        def pred1(q):
            for s2, v2 in c_fin:
                q.wait_ge(s2, v2)
            d1pre.append(p.swdge(q, wbs1.rearrange("p (c o) -> p c o", c=NC),
                                 wqkvT.rearrange("(c p) o -> p c o", p=128)))
        p.block.gpsimd(pred1)
        emit_ln(p, x1, None, xn1T, idb, cpool, "ln1", write_xn_tok=False, xnT_res=xn1Ts,
                store_res=True, xt_res=ln1xt, xgate=(xg_sems, xg_base), defer_store=True)
        p.barrier_sync()
        p.add("ln1_stT", 16 * NC)
        emit_l1_qkv(p, xn1T, wqkvT, idb, q1T, k1T, ktok1, vtok1, tag="d1",
                    xnT_res=xn1Ts, wpre=(d1pre, wbs1), res_out=(None, ktokS1, vtokS1))
        dctx.close()
        p.barrier_sync()
        emit_l1_scan(p, q1T, k1T, ktok1, vtok1, maskm, idb, xo1T, tag="d2",
                     res_in=(None, ktokS1, vtokS1))
        dsh.close()
        p.barrier_sync()
        emit_out_proj(p, xo1T, woutT1, bout_row, ones_sb, s1, tag="d3", with_bias=False)
        p.barrier_sync()
        l2.close()
        lnx.close()
        ectx = ExitStack()
        hres = ectx.enter_context(nc.sbuf_tensor("sh_h", [128, (DHID // 128) * T], BF16))
        emit_l1_ffn_in_v2(p, xn1T, w1T, w2T, s1c, s2c, hres, tag="e1")
        p.barrier_sync()
        emit_l1_ffn_mid(p, hmat, h1T, h1sc, h1m, tag="e2", hin_res=hres)
        ectx.close()
        p.barrier_sync()
        emit_l1_ffn_mid(p, h1m, h2T, h2sc, h2m, tag="e3")
        p.barrier_sync()
        emit_l1_ffn_out(p, h2m, w3T, s3r, x1, s1, ones_sb, out, tag="e4")
        nsems = len(p.hsems) + len([k for k in p.hsems])
    return nc


def kernel_fused(**inputs):
    sh, xs = host_prep(inputs)
    nc = _prog("fused", build_fused_program)
    ims = []
    for c in range(B):
        im = {"x": xs[c], "ident": sh["ident"], "sconsts": sh["sconsts"],
              "ones_row": sh["ones_row"], "wq": sh["wq"], "wkv": sh["wkv"],
              "wout": sh["wout"], "bout_row": sh["bout_row"], "cos2": sh["cos2"],
              "sin2": sh["sin2"], "rot128": sh["rot128"], "wqkvT": sh["wqkvT"],
              "woutT1": sh["woutT1"], "maskm": sh["mask_incl"], "w1T": sh["w1T"],
              "w2T": sh["w2T"], "s1c": sh["w1s"], "s2c": sh["w2s"],
              "h1T": sh["h1T"], "h1sc": sh["h1s"], "h2T": sh["h2T"],
              "h2sc": sh["h2s"], "w3T": sh["w3T"], "s3r": sh["w3s_row"]}
        for k2 in ("sigraw", "knots", "temp", "oscale", "cw1", "cb1_cols", "cw2",
                   "cb2_col", "w1g", "b1g", "w2g", "b2_col", "dirs", "gam_row",
                   "bet_row", "comp_rep", "iota64", "iota64m", "expand8", "gmean"):
            im["p_" + k2] = sh[k2]
        ims.append(im)
    res = run_bass_kernel_spmd(nc, ims, core_ids=list(range(B)))
    return np.stack([np.asarray(r["out"]) for r in res.results], axis=0)


_PROG_CACHE = {}


def _prog(name, builder, *args, **kw):
    key = (name,) + tuple(args) + tuple(sorted(kw.items()))
    if key not in _PROG_CACHE:
        _PROG_CACHE[key] = builder(*args, **kw)
    return _PROG_CACHE[key]


def _launch(nc, in_maps, outs):
    res = run_bass_kernel_spmd(nc, in_maps, core_ids=list(range(B)))
    return [{k: np.asarray(r[k]) for k in outs} for r in res.results]


def kernel(**inputs):
    return kernel_fused(**inputs)


def kernel_multilaunch(**inputs):
    sh, xs = host_prep(inputs)
    shared_con = {"ident": sh["ident"], "sconsts": sh["sconsts"], "ones_row": sh["ones_row"]}

    nc = _prog("ln0", build_ln_program, True)
    r = _launch(nc, [{"x": xs[c], "ident": sh["ident"], "sconsts": sh["sconsts"]} for c in range(B)],
                ["xn_tok", "xnT"])
    xn_tok = [r[c]["xn_tok"] for c in range(B)]
    xnT = [r[c]["xnT"] for c in range(B)]

    nc = _prog("b1", build_l0_qkv_program)
    r = _launch(nc, [{"xnT": xnT[c], "wq": sh["wq"], "wkv": sh["wkv"], "cos2": sh["cos2"],
                      "sin2": sh["sin2"], "rot128": sh["rot128"], "ident": sh["ident"]}
                     for c in range(B)], ["qT", "kvT", "kvtok"])
    qT = [r[c]["qT"] for c in range(B)]
    kvT = [r[c]["kvT"] for c in range(B)]
    kvtok = [r[c]["kvtok"] for c in range(B)]

    nc = _prog("b2", build_l0_attn_program)
    r = _launch(nc, [{"qT": qT[c], "kvT": kvT[c], "kvtok": kvtok[c],
                      "sconsts": sh["sconsts"], "ones_row": sh["ones_row"]}
                     for c in range(B)], ["xoT"])
    xoT = [r[c]["xoT"] for c in range(B)]

    nc = _prog("b3", build_out_proj_program, True)
    r = _launch(nc, [{"xoT": xoT[c], "w": sh["wout"], "brow": sh["bout_row"],
                      "ones_row": sh["ones_row"]} for c in range(B)], ["s_out"])
    s0 = [r[c]["s_out"] for c in range(B)]

    nc = _prog("c", build_l0_ffn_program)
    ims = []
    for c in range(B):
        im = {"xnT": xnT[c], "xn_tok": xn_tok[c], "x": xs[c], "s0": s0[c],
              "sconsts": sh["sconsts"], "ones_row": sh["ones_row"], "ident": sh["ident"]}
        for k2 in ("sigraw", "knots", "temp", "oscale", "cw1", "cb1_cols", "cw2",
                   "cb2_col", "w1g", "b1g", "w2g", "b2_col", "dirs", "gam_row",
                   "bet_row", "comp_rep", "iota64", "iota64m", "expand8", "gmean"):
            im["p_" + k2] = sh[k2]
        ims.append(im)
    r = _launch(nc, ims, ["x1"])
    x1 = [r[c]["x1"] for c in range(B)]

    nc = _prog("ln1", build_ln_program, False)
    r = _launch(nc, [{"x": x1[c], "ident": sh["ident"], "sconsts": sh["sconsts"]}
                     for c in range(B)], ["xnT"])
    xn1T = [r[c]["xnT"] for c in range(B)]

    nc = _prog("d1", build_l1_qkv_program)
    r = _launch(nc, [{"xnT": xn1T[c], "wqkvT": sh["wqkvT"], "ident": sh["ident"]}
                     for c in range(B)], ["qT", "kT", "ktok", "vtok"])
    q1T = [r[c]["qT"] for c in range(B)]
    k1T = [r[c]["kT"] for c in range(B)]
    ktok1 = [r[c]["ktok"] for c in range(B)]
    vtok1 = [r[c]["vtok"] for c in range(B)]

    nc = _prog("d2", build_l1_scan_program)
    r = _launch(nc, [{"qT": q1T[c], "kT": k1T[c], "ktok": ktok1[c], "vtok": vtok1[c],
                      "maskm": sh["mask_incl"], "ident": sh["ident"]} for c in range(B)],
                ["xoT"])
    xo1T = [r[c]["xoT"] for c in range(B)]

    nc = _prog("d3", build_out_proj_program, False)
    r = _launch(nc, [{"xoT": xo1T[c], "w": sh["woutT1"], "brow": sh["bout_row"],
                      "ones_row": sh["ones_row"]} for c in range(B)], ["s_out"])
    s1 = [r[c]["s_out"] for c in range(B)]

    nc = _prog("e1", build_l1_ffn_in_program)
    r = _launch(nc, [{"xnT": xn1T[c], "w1T": sh["w1T"], "w2T": sh["w2T"],
                      "s1c": sh["w1s"], "s2c": sh["w2s"]} for c in range(B)], ["h"])
    hmat = [r[c]["h"] for c in range(B)]

    nc = _prog("e2", build_l1_ffn_mid_program)
    r = _launch(nc, [{"hin": hmat[c], "wT": sh["h1T"], "sc": sh["h1s"]} for c in range(B)],
                ["hout"])
    h1m = [r[c]["hout"] for c in range(B)]
    r = _launch(nc, [{"hin": h1m[c], "wT": sh["h2T"], "sc": sh["h2s"]} for c in range(B)],
                ["hout"])
    h2m = [r[c]["hout"] for c in range(B)]

    nc = _prog("e4", build_l1_ffn_out_program)
    r = _launch(nc, [{"h2": h2m[c], "w3T": sh["w3T"], "s3r": sh["w3s_row"], "x1": x1[c],
                      "s1": s1[c], "ones_row": sh["ones_row"]} for c in range(B)], ["out"])
    out = np.stack([r[c]["out"] for c in range(B)], axis=0)
    return out


# version 15
# speedup vs baseline: 1.0563x; 1.0563x over previous
import numpy as np
import ml_dtypes
from contextlib import ExitStack

import concourse.bass as bass
import concourse.mybir as mybir
from concourse.bass_utils import run_bass_kernel_spmd

import concourse.bass_utils as _BU
_orig_run_command = _BU.run_command


def _run_command_ldwopt(cmd, *a, **kw):
    if isinstance(cmd, list):
        cmd = ["--enable-ldw-opt=true" if c == "--enable-ldw-opt=false" else c for c in cmd]
    return _orig_run_command(cmd, *a, **kw)


_BU.run_command = _run_command_ldwopt

F32 = mybir.dt.float32
BF16 = mybir.dt.bfloat16
AF = mybir.ActivationFunctionType
ALU = mybir.AluOpType
AX = mybir.AxisListType.X
BF = ml_dtypes.bfloat16

CONST_VALS = {
    "eps_ln": 1e-5, "eps_l1a": 1e-6, "one": 1.0, "zero": 0.0,
    "m03": -0.3, "p03": 0.3,
    "rm1": -1.0, "rm2": -2.0, "rm3": -3.0, "rm4": -4.0, "rm5": -5.0, "rm6": -6.0,
    "bigneg": -3e8, "negc": -12.0,
}
CONST_NAMES = list(CONST_VALS)
CONST_COL = {n: i for i, n in enumerate(CONST_NAMES)}

B = 8
T = 2048
D = 1024
H = 16
DH = 64
NT = T // 128
NC = D // 128
DHID = 4096
EPS_LN0 = 1e-5
EPS_FFN = 1e-5
EPS_L1A = 1e-6



def _rope_tables():
    d = DH
    inv_freq = 1.0 / (10000.0 ** (np.arange(0, d, 2, dtype=np.float64) / d))
    tpos = np.arange(T, dtype=np.float64)
    freqs = np.outer(tpos, inv_freq)
    full = np.concatenate([freqs, freqs], axis=1)
    cos = np.cos(full).astype(np.float32).T
    sin = np.sin(full).astype(np.float32).T
    cos2 = np.concatenate([cos, cos], axis=0)
    sin2 = np.concatenate([sin, sin], axis=0)
    return np.ascontiguousarray(cos2), np.ascontiguousarray(sin2)


def _rot_matrix():
    P = np.zeros((64, 64), dtype=np.float32)
    for dp in range(32):
        P[dp + 32, dp] = -1.0
    for dp in range(32, 64):
        P[dp - 32, dp] = 1.0
    R128 = np.zeros((128, 128), dtype=np.float32)
    R128[0:64, 0:64] = P
    R128[64:128, 64:128] = P
    return R128


def host_prep(inputs):
    f32 = lambda a: np.ascontiguousarray(np.asarray(a), dtype=np.float32)
    sh = {}
    sh["wq"] = f32(inputs["L0_attn_Wq"])
    sh["wkv"] = f32(inputs["L0_attn_Wkv"])
    sh["wout"] = f32(inputs["L0_attn_Wout"])
    sh["bout_row"] = f32(np.asarray(inputs["L0_attn_bout"]).reshape(1, D))
    cos2, sin2 = _rope_tables()
    sh["cos2"] = cos2
    sh["sin2"] = sin2
    sh["rot128"] = _rot_matrix()
    sh["ident"] = np.eye(128, dtype=np.float32)
    sh["ones_row"] = np.ones((1, 128), dtype=np.float32)
    sh["sigraw"] = f32(inputs["L0_ffn_signatures_raw"])
    sh["knots"] = f32(np.asarray(inputs["L0_ffn_knot_values"]).reshape(1, 8))
    sh["temp"] = f32(np.asarray(inputs["L0_ffn_temperature"]).reshape(1, 1))
    sh["cw1"] = f32(inputs["L0_ffn_compress_w1"])
    sh["cb1_cols"] = f32(np.asarray(inputs["L0_ffn_compress_b1"]).reshape(2, 128).T)
    sh["cw2"] = f32(inputs["L0_ffn_compress_w2"])
    sh["cb2_col"] = f32(np.asarray(inputs["L0_ffn_compress_b2"]).reshape(2, 1))
    w1 = np.asarray(inputs["L0_ffn_spline_w1"], dtype=np.float32)
    W1g = np.zeros((16, 8, 128), dtype=np.float32)
    for grp in range(8):
        for r in range(8):
            W1g[2 * r:2 * r + 2, grp, 16 * r:16 * r + 16] = w1[grp * 8 + r]
    sh["w1g"] = np.ascontiguousarray(W1g.reshape(16, 8 * 128))
    sh["b1g"] = f32(np.asarray(inputs["L0_ffn_spline_b1"]).reshape(8, 128).T)
    w2 = np.asarray(inputs["L0_ffn_spline_w2"], dtype=np.float32)[:, :, 0]
    W2g = np.zeros((128, 8, 64), dtype=np.float32)
    for grp in range(8):
        for r in range(8):
            W2g[16 * r:16 * r + 16, grp, grp * 8 + r] = w2[grp * 8 + r]
    sh["w2g"] = np.ascontiguousarray(W2g.reshape(128, 8 * 64))
    sh["b2_col"] = f32(np.asarray(inputs["L0_ffn_spline_b2"]).reshape(64, 1))
    sh["dirs"] = f32(inputs["L0_ffn_directions"])
    sh["gam_row"] = f32(np.asarray(inputs["L0_ffn_norm_gamma"]).reshape(1, D))
    sh["bet_row"] = f32(np.asarray(inputs["L0_ffn_norm_beta"]).reshape(1, D))
    sh["oscale"] = f32(np.asarray(inputs["L0_ffn_output_scale"]).reshape(1, 1))
    rep = np.zeros((2, 16), dtype=np.float32)
    for r in range(8):
        rep[0, 2 * r] = 1.0
        rep[1, 2 * r + 1] = 1.0
    sh["comp_rep"] = rep
    sh["iota64"] = np.tile(np.arange(64, dtype=np.float32), (128, 1))
    sh["iota64m"] = sh["iota64"] - 64.0
    E = np.zeros((8, 64), dtype=np.float32)
    for r in range(64):
        E[r // 8, r] = 1.0
    sh["expand8"] = E
    G = np.zeros((64, 8), dtype=np.float32)
    for t in range(64):
        G[t, t // 8] = 0.125
    sh["gmean"] = G
    sh["wqkvT"] = np.ascontiguousarray(np.asarray(inputs["L1_attn_w_qkv"], dtype=np.float32).T)
    sh["woutT1"] = np.ascontiguousarray(np.asarray(inputs["L1_attn_w_out"], dtype=np.float32).T)
    sh["mask_incl"] = np.triu(np.ones((128, 128), dtype=np.float32))
    sh["mask4"] = np.ascontiguousarray(np.tile(sh["mask_incl"], (1, 4)))
    for nm, w, s in (("w1", "L1_ffn_w1_w", "L1_ffn_w1_s"), ("w2", "L1_ffn_w2_w", "L1_ffn_w2_s"),
                     ("h1", "L1_ffn_h1_w", "L1_ffn_h1_s"), ("h2", "L1_ffn_h2_w", "L1_ffn_h2_s"),
                     ("w3", "L1_ffn_w3_w", "L1_ffn_w3_s")):
        wa = np.asarray(inputs[w], dtype=np.float32)
        sh[nm + "T"] = np.ascontiguousarray(wa.T)
        sv = np.asarray(inputs[s], dtype=np.float32)
        sh[nm + "s"] = np.ascontiguousarray(sv.reshape(-1, 128).T)
        sh[nm + "s_row"] = np.ascontiguousarray(sv.reshape(1, -1))
    cvals = [CONST_VALS[n] for n in CONST_NAMES]
    sh["sconsts"] = np.tile(np.asarray(cvals, dtype=np.float32), (128, 1))
    x = np.asarray(inputs["L0_attn_x"], dtype=np.float32)
    xs = [np.ascontiguousarray(x[c]) for c in range(B)]
    return sh, xs



ENGINES = ("sync", "gpsimd", "vector", "scalar", "tensor")


class Prog:
    def __init__(self, nc, ctx):
        self.nc = nc
        self.ctx = ctx
        self.block = ctx.enter_context(nc.Block())
        self.cnt = {}
        self.hsems = {}
        self.nbar = 0

    _ALIAS = {"vone": "mz", "stT": "st", "cast": "ca",
              "stxa0": "stf0", "stxa1": "stf1",
              "sus": "ca", "suv": "dv"}

    @classmethod
    def _canon(cls, name):
        if name.startswith("sw"):
            return name
        if "_" in name:
            name = name.split("_", 1)[1]
        return cls._ALIAS.get(name, name)

    def sem(self, name):
        name = self._canon(name)
        if name not in self.hsems:
            self.hsems[name] = self.ctx.enter_context(self.nc.semaphore(name))
            self.cnt[name] = 0
        return self.hsems[name]

    def add(self, name, d=1):
        name = self._canon(name)
        self.sem(name)
        self.cnt[name] += d
        return self.cnt[name]

    def val(self, name):
        name = self._canon(name)
        self.sem(name)
        return self.cnt[name]

    def swdge(self, q, out, in_):
        n = f"sw{len([k for k in self.hsems if k.startswith('sw')])}"
        s = self.sem(n)
        q.dma_start(out=out, in_=in_).then_inc(s, 16)
        self.add(n, 16)
        return s, 16

    def barrier_sync(self):
        ph = self.sem("ph")
        self.nbar += 1
        target = self.nbar
        waits = [(self.hsems[n], self.cnt[n]) for n in list(self.hsems)
                 if n != "ph" and self.cnt[n] > 0]

        def syncb(q):
            for s, v in waits:
                q.wait_ge(s, v)
            q.nop().then_inc(ph, 1)
        self.block.sync(syncb)

        def body(q):
            q.wait_ge(ph, target)
        for eng in ENGINES:
            getattr(self.block, eng)(body)
        self.cnt["ph"] = target

    def barrier(self):
        ph = self.sem("ph")
        self.nbar += 1
        target = 5 * self.nbar

        def body(q):
            q.nop().then_inc(ph, 1)
            q.wait_ge(ph, target)
        for eng in ENGINES:
            getattr(self.block, eng)(body)
        self.cnt["ph"] = target


def dram_io(nc, name, shape, dtype, kind):
    return nc.dram_tensor(name, list(shape), dtype, kind=kind).ap()



def emit_ln(p, x_hbm, xn_tok_hbm, xnT_hbm, ident_bf, cpool, tag, write_xn_tok=True,
            xnT_res=None, store_res=False, xt_res=None, xgate=None, defer_store=False):
    nc, blk = p.nc, p.block
    ctx = ExitStack()
    sb = lambda n, shp, dt: ctx.enter_context(nc.sbuf_tensor(tag + n, list(shp), dt))
    ps = lambda n, shp, dt: ctx.enter_context(nc.psum_tensor(tag + n, list(shp), dt))
    xt = xt_res if xt_res is not None else [sb(f"_xt{i}", (128, D), F32) for i in range(2)]
    NS = len(xt)
    nsum = sb("_nsum", (128, NT), F32)
    sqc = sb("_sq", (128, NT), F32)
    sqscr = sb("_sqscr", (128, D), F32)
    nmu = sb("_nmu", (128, NT), F32)
    musq = sb("_musq", (128, NT), F32)
    varc = sb("_varc", (128, NT), F32)
    sdev = sb("_sdev", (128, NT), F32)
    rstd = sb("_rstd", (128, NT), F32)
    bcol = sb("_bcol", (128, NT), F32)
    xnf = [sb(f"_xnf{i}", (128, D), F32) for i in range(2)]
    xnb = [sb(f"_xnb{i}", (128, D), BF16) for i in range(2)]
    xnT_full = xnT_res if xnT_res is not None else sb("_xnTf", (128, NC * T), BF16)
    ptp = [ps(f"_ptp{u}", (128, 128), BF16) for u in range(8)]

    ldx = [p.sem(tag + f"_ldx{s}") for s in range(NS)]
    stf = [p.sem(tag + f"_lnf{s}") for s in range(2)]
    lnv = p.sem(tag + "_lnv")
    lns = p.sem(tag + "_lns")
    af = p.sem(tag + "_af")
    tp = p.sem(tag + "_tp")
    ev = p.sem(tag + "_ev")
    stT = p.sem(tag + "_stT")

    b_ldx = [p.val(tag + f"_ldx{s}") for s in range(NS)]
    b_stf = [p.val(tag + f"_lnf{s}") for s in range(2)]
    b_lnv = p.val(tag + "_lnv")
    b_lns = p.val(tag + "_lns")
    b_af = p.val(tag + "_af")
    b_tp = p.val(tag + "_tp")
    b_ev = p.val(tag + "_ev")
    b_stT = p.val(tag + "_stT")
    for i in range(NT):
        p.add(tag + f"_ldx{i % NS}", 16)
        p.add(tag + "_lnv", 6)
        p.add(tag + "_lns", 2)
        p.add(tag + "_af", 2)
        p.add(tag + "_tp", 8)
        p.add(tag + "_ev", 8)
        if write_xn_tok:
            p.add(tag + f"_lnf{i % 2}", 16)
    if (xnT_res is None or store_res) and not defer_store:
        p.add(tag + "_stT", 16 * NC)
    stT_target = p.val(tag + "_stT")

    def sync1(q):
        for i in range(NT):
            if xgate is not None:
                q.wait_ge(xgate[0][i % 2], xgate[1][i % 2] + 16 * (i // 2 + 1))
            if i >= NS:
                q.wait_ge(af, b_af + 2 * (i - NS) + 2)
                q.wait_ge(lns, b_lns + 2 * (i - NS) + 1)
            q.dma_start(out=xt[i % NS][:], in_=x_hbm[128 * i:128 * (i + 1), :]).then_inc(ldx[i % NS], 16)
    blk.sync(sync1)

    def vec1(q):
        for i in range(NT):
            n0 = b_lnv + 6 * i
            q.wait_ge(ldx[i % NS], b_ldx[i % NS] + 16 * (i // NS + 1))
            q.reduce_sum(nsum[:, i:i + 1], xt[i % NS][:], axis=AX, negate=True).then_inc(lnv, 1)
            q.wait_ge(lnv, n0 + 1)
            q.tensor_scalar(nmu[:, i:i + 1], nsum[:, i:i + 1], 1.0 / D, None, ALU.mult).then_inc(lnv, 1)
            q.wait_ge(lnv, n0 + 2)
            q.tensor_tensor(musq[:, i:i + 1], nmu[:, i:i + 1], nmu[:, i:i + 1], ALU.mult).then_inc(lnv, 1)
            q.wait_ge(lnv, n0 + 3)
            q.wait_ge(lns, b_lns + 2 * i + 1)
            q.scalar_tensor_tensor(varc[:, i:i + 1], sqc[:, i:i + 1], 1.0 / D,
                                   musq[:, i:i + 1], ALU.mult, ALU.subtract).then_inc(lnv, 1)
            q.wait_ge(lns, b_lns + 2 * i + 2)
            q.reciprocal(rstd[:, i:i + 1], sdev[:, i:i + 1]).then_inc(lnv, 1)
            q.wait_ge(lnv, n0 + 5)
            q.tensor_tensor(bcol[:, i:i + 1], nmu[:, i:i + 1], rstd[:, i:i + 1], ALU.mult).then_inc(lnv, 1)
            if write_xn_tok:
                q.wait_ge(lnv, n0 + 6)
                if i >= 2:
                    q.wait_ge(tp, b_tp + 8 * (i - 1))
                    q.wait_ge(stf[i % 2], b_stf[i % 2] + 16 * ((i - 2) // 2 + 1))
                q.tensor_scalar(xnf[i % 2][:], xt[i % NS][:], rstd[:, i:i + 1],
                                bcol[:, i:i + 1], ALU.mult, ALU.add).then_inc(af, 1)
                q.wait_ge(af, b_af + 2 * i + 1)
                q.tensor_copy(xnb[i % 2][:], xnf[i % 2][:]).then_inc(af, 1)
    blk.vector(vec1)

    def gp2(q):
        for i in range(NT):
            q.wait_ge(lnv, b_lnv + 6 * i + 6)
            q.wait_ge(ldx[i % NS], b_ldx[i % NS] + 16 * (i // NS + 1))
            if i >= 2:
                q.wait_ge(tp, b_tp + 8 * (i - 1))
            q.tensor_scalar(xnb[i % 2][:], xt[i % NS][:], rstd[:, i:i + 1],
                            bcol[:, i:i + 1], ALU.mult, ALU.add).then_inc(af, 2)
    if not write_xn_tok:
        blk.gpsimd(gp2)

    def scal1(q):
        for i in range(NT):
            q.wait_ge(ldx[i % NS], b_ldx[i % NS] + 16 * (i // NS + 1))
            if i > 0:
                q.wait_ge(lns, b_lns + 2 * i - 1)
            q.activation(sqscr[:], xt[i % NS][:], AF.Square,
                         accum_out=sqc[:, i:i + 1]).then_inc(lns, 1)
            q.wait_ge(lnv, b_lnv + 6 * i + 4)
            q.activation(sdev[:, i:i + 1], varc[:, i:i + 1], AF.Sqrt,
                         bias=cpool[:, CONST_COL["eps_ln"]:CONST_COL["eps_ln"] + 1]).then_inc(lns, 1)
            if i >= 1:
                for c in range(NC):
                    q.wait_ge(tp, b_tp + 8 * (i - 1) + c + 1)
                    q.copy(xnT_full[:, c * T + 128 * (i - 1): c * T + 128 * i],
                           ptp[c][:]).then_inc(ev, 1)
        for c in range(NC):
            q.wait_ge(tp, b_tp + 8 * (NT - 1) + c + 1)
            q.copy(xnT_full[:, c * T + 128 * (NT - 1): c * T + 128 * NT],
                   ptp[c][:]).then_inc(ev, 1)
    blk.scalar(scal1)

    def gp1(q):
        if write_xn_tok:
            for i in range(NT):
                q.wait_ge(af, b_af + 2 * i + 2)
                q.dma_start(out=xn_tok_hbm[128 * i:128 * (i + 1), :],
                            in_=xnf[i % 2][:]).then_inc(stf[i % 2], 16)
    blk.gpsimd(gp1)

    def tens1(q):
        for i in range(NT):
            q.wait_ge(af, b_af + 2 * i + 2)
            for c in range(NC):
                g = 8 * i + c
                if g >= 8:
                    q.wait_ge(ev, b_ev + g - 7)
                q.transpose(ptp[c][:], xnb[i % 2][:, 128 * c:128 * (c + 1)],
                            ident_bf[:]).then_inc(tp, 1)
    blk.tensor(tens1)

    def sync3(q):
        if xnT_res is None or store_res:
            q.wait_ge(ev, b_ev + 8 * NT)
            for c in range(NC):
                q.dma_start(out=xnT_hbm[128 * c:128 * (c + 1), :],
                            in_=xnT_full[:, c * T:(c + 1) * T]).then_inc(stT, 16)
            if not defer_store:
                q.wait_ge(stT, stT_target)
        if write_xn_tok:
            for s in range(2):
                q.wait_ge(stf[s], b_stf[s] + 16 * (NT // 2))
    blk.sync(sync3)
    ctx.close()

def _const_loader(p, ctx, consts):
    nc = p.nc
    out = {}
    ldc = p.sem("cst_ld")
    ccc = p.sem("cst_cast")
    loads = []
    casts = []
    for name, (hbm, shape, want) in consts.items():
        if want == "bf16":
            f32t = ctx.enter_context(nc.sbuf_tensor("cst_f_" + name, list(shape), F32))
            bft = ctx.enter_context(nc.sbuf_tensor("cst_b_" + name, list(shape), BF16))
            loads.append((f32t, hbm))
            casts.append((bft, f32t))
            out[name] = bft
            out[name + "_f32"] = f32t
        else:
            f32t = ctx.enter_context(nc.sbuf_tensor("cst_f_" + name, list(shape), F32))
            loads.append((f32t, hbm))
            out[name] = f32t
    nld = len(loads)

    def s0(q):
        for dst, src in loads:
            q.dma_start(out=dst[:], in_=src[:]).then_inc(ldc, 16)
    p.block.sync(s0)
    p.add("cst_ld", 16 * nld)

    def v0(q):
        q.wait_ge(ldc, p.val("cst_ld"))
        for dst, src in casts:
            q.tensor_copy(dst[:], src[:])
        q.engine_nop().then_inc(ccc, 1)
    p.block.vector(v0)
    p.add("cst_cast", 1)

    def w0(q):
        q.wait_ge(ccc, p.val("cst_cast"))
    for e in ENGINES:
        getattr(p.block, e)(w0)
    return out


def build_ln_program(write_xn_tok=True):
    nc = bass.Bass("TRN2", target_bir_lowering=False)
    x = dram_io(nc, "x", (T, D), F32, "ExternalInput")
    identf = dram_io(nc, "ident", (128, 128), F32, "ExternalInput")
    scon = dram_io(nc, "sconsts", (128, len(CONST_NAMES)), F32, "ExternalInput")
    xn_tok = dram_io(nc, "xn_tok", (T, D), F32, "ExternalOutput") if write_xn_tok else None
    xnT = dram_io(nc, "xnT", (D, T), BF16, "ExternalOutput")
    with ExitStack() as ctx:
        p = Prog(nc, ctx)
        cs = _const_loader(p, ctx, {"ident": (identf, (128, 128), "bf16"),
                                    "sconsts": (scon, (128, len(CONST_NAMES)), "f32")})
        emit_ln(p, x, xn_tok, xnT, cs["ident"], cs["sconsts"], "ln0", write_xn_tok=write_xn_tok)
    return nc




def emit_l0_qkv(p, xnT_hbm, wq_hbm, wkv_hbm, cos_hbm, sin_hbm, rot_hbm, ident_bf,
                qT_hbm, kvT_hbm, kvtok_hbm, tag="b1", xnT_res=None, res_out=None,
                pre=None):
    nc, blk = p.nc, p.block
    ctx = ExitStack()
    sb = lambda n, shp, dt: ctx.enter_context(nc.sbuf_tensor(tag + n, list(shp), dt))
    pctx = ExitStack()
    psb = lambda n, shp, dt: pctx.enter_context(nc.psum_tensor(tag + n, list(shp), dt))
    HT = T // 2

    if pre is not None:
        pre_sems, pb = pre
        wqb, wkvb, cosb, sinf, rotb = (pb["wqb"], pb["wkvb"], pb["cosb"],
                                       pb["sinf"], pb["rotb"])
    else:
        wqb = sb("_wq", (128, NC * D), BF16)
        wkvb = sb("_wkv", (128, NC * D), BF16)
        cosb = sb("_cos", (128, T), BF16)
        sinf = sb("_sin", (128, T), F32)
        rotb = sb("_rot", (128, 128), BF16)
    xnT = xnT_res if xnT_res is not None else sb("_xnT", (128, NC * T), BF16)
    tbf = [sb(f"_tbf{i}", (128, HT), BF16) for i in range(2)]
    tcb = [sb(f"_tcb{i}", (128, HT), BF16) for i in range(2)]
    tsb = [sb(f"_tsb{i}", (128, HT), BF16) for i in range(2)]
    if res_out is not None:
        qT, kvT = res_out[0], res_out[1]
    else:
        qT = sb("_qTr", (128, NC * T), BF16)
        kvT = sb("_kvTr", (128, NC * T), BF16)
    pp = [psb(f"_pp{i}", (128, HT), F32) for i in range(2)]
    pr = [psb(f"_pr{i}", (128, HT), F32) for i in range(2)]

    ld = p.sem(tag + "_ld")
    mm = p.sem(tag + "_mm")
    ca = p.sem(tag + "_ca")
    dv = p.sem(tag + "_dv")
    st = p.sem(tag + "_st")
    b_ld = p.val(tag + "_ld")
    b_mm = p.val(tag + "_mm")
    b_ca = p.val(tag + "_ca")
    b_dv = p.val(tag + "_dv")
    b_st = p.val(tag + "_st")
    if pre is None:
        p.add(tag + "_ld", 16)
    NIDX = 2 * NC * 2
    for idx in range(NIDX):
        p.add(tag + "_mm", 2)
        p.add(tag + "_ca", 1)
        p.add(tag + "_dv", 3)
    if res_out is None:
        for i in range(2 * NC):
            p.add(tag + "_st", 16)

    def decode(idx):
        c = idx // 4
        v = (idx // 2) % 2
        hf = idx % 2
        return c, v, hf

    gsems = [] if pre is None else list(pre_sems)

    def gp0(q):
        if xnT_res is None:
            gsems.append(p.swdge(q, xnT.rearrange("p (c t) -> p c t", c=NC), xnT_hbm.rearrange("(c p) t -> p c t", p=128)))
        if pre is None:
            gsems.append(p.swdge(q, wqb.rearrange("p (c o) -> p c o", c=NC), wq_hbm.rearrange("(c p) o -> p c o", p=128)))
            gsems.append(p.swdge(q, wkvb.rearrange("p (c o) -> p c o", c=NC), wkv_hbm.rearrange("(c p) o -> p c o", p=128)))
            gsems.append(p.swdge(q, cosb[:], cos_hbm[:]))
            gsems.append(p.swdge(q, rotb[:], rot_hbm[:]))
    if xnT_res is None or pre is None:
        blk.gpsimd(gp0)

    def sy0(q):
        q.dma_start(out=sinf[:], in_=sin_hbm[:]).then_inc(ld, 16)
    if pre is None:
        blk.sync(sy0)

    def tens(q):
        if pre is None:
            q.wait_ge(ld, b_ld + 16)
        for s, tgt in gsems:
            q.wait_ge(s, tgt)
        for idx in range(NIDX):
            c, v, hf = decode(idx)
            par = idx % 2
            w = wqb if v == 0 else wkvb
            if idx >= 2:
                q.wait_ge(ca, b_ca + (idx - 2) + 1)
            for k in range(NC):
                for ns in range(2):
                    t0 = hf * HT + 512 * ns
                    ins = q.matmul(pp[par][:, 512 * ns:512 * (ns + 1)],
                                   w[:, k * D + 128 * c:k * D + 128 * (c + 1)],
                                   xnT[:, k * T + t0:k * T + t0 + 512],
                                   start=(k == 0), stop=(k == NC - 1))
            ins.then_inc(mm, 1)
            q.wait_ge(ca, b_ca + idx + 1)
            if idx >= 2:
                q.wait_ge(dv, b_dv + 3 * (idx - 2) + 2)
            for ns in range(2):
                ins = q.matmul(pr[par][:, 512 * ns:512 * (ns + 1)], rotb[:],
                               tbf[par][:, 512 * ns:512 * (ns + 1)], start=True, stop=True)
            ins.then_inc(mm, 1)
    blk.tensor(tens)

    def scal(q):
        for idx in range(NIDX):
            par = idx % 2
            q.wait_ge(mm, b_mm + 2 * idx + 1)
            if idx >= 2:
                q.wait_ge(dv, b_dv + 3 * (idx - 2) + 1)
                q.wait_ge(mm, b_mm + 2 * (idx - 2) + 2)
            q.copy(tbf[par][:], pp[par][:]).then_inc(ca, 1)
    blk.scalar(scal)

    def vec(q):
        for idx in range(NIDX):
            c, v, hf = decode(idx)
            par = idx % 2
            dst = qT if v == 0 else kvT
            hslice = lambda t: t[:, hf * HT:(hf + 1) * HT]
            q.wait_ge(ca, b_ca + idx + 1)
            if idx >= 2:
                q.wait_ge(dv, b_dv + 3 * (idx - 2) + 3)
            q.tensor_tensor(tcb[par][:], tbf[par][:], hslice(cosb), ALU.mult).then_inc(dv, 1)
            q.wait_ge(mm, b_mm + 2 * idx + 2)
            q.tensor_tensor(tsb[par][:], pr[par][:], hslice(sinf), ALU.mult).then_inc(dv, 1)
            q.wait_ge(dv, b_dv + 3 * idx + 2)
            q.tensor_tensor(dst[:, c * T + hf * HT:c * T + (hf + 1) * HT],
                            tcb[par][:], tsb[par][:], ALU.add).then_inc(dv, 1)
    blk.vector(vec)

    def sy1(q):
        for i in range(2 * NC):
            c, v = i // 2, i % 2
            q.wait_ge(dv, b_dv + 3 * (4 * c + 2 * v + 1) + 3)
            src = qT if v == 0 else kvT
            dst = qT_hbm if v == 0 else kvT_hbm
            q.dma_start(out=dst[128 * c:128 * (c + 1), :],
                        in_=src[:, c * T:(c + 1) * T]).then_inc(st, 16)
    if res_out is None:
        blk.sync(sy1)

    pctx.close()
    ktok = res_out[2] if res_out is not None else sb("_ktok", (128, H * 16 * 65), BF16)

    tp = p.sem(tag + "_tp")
    ev = p.sem(tag + "_ev")
    vone = p.sem(tag + "_vone")
    stk = p.sem(tag + "_stk") if res_out is None else None
    b_tp = p.val(tag + "_tp")
    b_ev = p.val(tag + "_ev")
    b_stk = p.val(tag + "_stk") if res_out is None else 0
    for c in range(NC):
        p.add(tag + "_tp", 16)
        p.add(tag + "_ev", 16)
    p.add(tag + "_vone", 1)
    if res_out is None:
        p.add(tag + "_stk", 16 * H)
    pctx2 = ExitStack()
    ptt = [pctx2.enter_context(nc.psum_tensor(tag + f"_pt{i}", [128, 128], BF16)) for i in range(4)]

    def tens2(q):
        for c in range(NC):
            q.wait_ge(dv, b_dv + 3 * (4 * c + 3) + 3)
            for j in range(16):
                g = 16 * c + j
                if g >= 4:
                    q.wait_ge(ev, b_ev + g - 3)
                q.transpose(ptt[g % 4][:], kvT[:, c * T + 128 * j:c * T + 128 * (j + 1)],
                            ident_bf[:]).then_inc(tp, 1)
    blk.tensor(tens2)

    def vec2(q):
        for h in range(H):
            q.memset(ktok.rearrange("p (h j e) -> p h j e", h=H, j=16)[:, h, :, 64:65], 1.0)
        q.engine_nop().then_inc(vone, 1)
    blk.vector(vec2)

    def scal2(q):
        for c in range(NC):
            hA, hB = 2 * c, 2 * c + 1
            for j in range(16):
                g = 16 * c + j
                q.wait_ge(tp, b_tp + g + 1)
                q.copy(ktok[:, hA * 16 * 65 + j * 65:hA * 16 * 65 + j * 65 + 64],
                       ptt[g % 4][:, 0:64])
                q.copy(ktok[:, hB * 16 * 65 + j * 65:hB * 16 * 65 + j * 65 + 64],
                       ptt[g % 4][:, 64:128]).then_inc(ev, 1)
    blk.scalar(scal2)

    def sy2(q):
        q.wait_ge(ev, b_ev + 16 * NC)
        q.wait_ge(vone, 1)
        for h in range(H):
            q.dma_start(out=kvtok_hbm.rearrange("(h j p) e -> h p j e", p=128, j=16)[h],
                        in_=ktok.rearrange("p (h j e) -> p h j e", h=H, j=16)[:, h]).then_inc(stk, 16)
        q.wait_ge(stk, b_stk + 16 * H)
        q.wait_ge(st, b_st + 16 * 2 * NC)
    if res_out is None:
        blk.sync(sy2)
    pctx2.close()
    ctx.close()



def emit_l0_attn(p, qT_hbm, kvT_hbm, kvtok_hbm, cpool, ones_row_sb, xoT_hbm, tag="b2",
                 res_in=None, xoT_res=None):
    nc, blk = p.nc, p.block
    ctx = ExitStack()
    sb = lambda n, shp, dt: ctx.enter_context(nc.sbuf_tensor(tag + n, list(shp), dt))
    pctx = ExitStack()
    psb = lambda n, shp, dt: pctx.enter_context(nc.psum_tensor(tag + n, list(shp), dt))

    if res_in is not None:
        qT, kvT, ktok = res_in
    else:
        qT = sb("_qT", (128, NC * T), BF16)
        kvT = sb("_kvT", (128, NC * T), BF16)
        ktok = sb("_ktok", (128, H * 16 * 65), BF16)
    PT = [sb(f"_PT{i}", (128, T), BF16) for i in range(4)]
    zsb = sb("_zsb", (1, T), F32)
    zln = sb("_zln", (1, T), F32)
    zr = sb("_zr", (1, T), F32)
    zbc = sb("_zbc", (128, 512), F32)
    rot2 = res_in is not None and xoT_res is None
    if xoT_res is not None:
        xoT = xoT_res
    elif rot2:
        xoT = sb("_xoT", (128, 2 * T), BF16)
    else:
        xoT = sb("_xoT", (128, NC * T), BF16)
    pOut = psb("_pOut", (128, T), F32)
    pS = [psb(f"_pS{i}", (128, T // 2), F32) for i in range(2)]

    negc = cpool[:, CONST_COL["negc"]:CONST_COL["negc"] + 1]

    ld = p.sem(tag + "_ld")
    sS = p.sem(tag + "_sS")
    sE = p.sem(tag + "_sE")
    sPV = p.sem(tag + "_sPV")
    sZ = p.sem(tag + "_sZ")
    sB = p.sem(tag + "_sB")
    sEv = p.sem(tag + "_sEv")
    sN = p.sem(tag + "_sN")
    st = p.sem(tag + "_st")
    b_ld = p.val(tag + "_ld")
    b = {}
    for n in ("sS", "sE", "sPV", "sZ", "sB", "sEv", "sN"):
        b[n] = p.val(tag + "_" + n)
    b_st = p.val(tag + "_st")
    if res_in is None:
        p.add(tag + "_ld", 16 * 3)
    for h in range(H):
        p.add(tag + "_sS", 32)
        p.add(tag + "_sE", 32)
        p.add(tag + "_sPV", 16)
        p.add(tag + "_sZ", 3)
        p.add(tag + "_sB", 4)
        p.add(tag + "_sEv", 4)
        p.add(tag + "_sN", 4)
    stf = [p.sem(tag + f"_stf{s}") for s in range(2)] if rot2 else None
    b_stf = [p.val(tag + f"_stf{s}") for s in range(2)] if rot2 else None
    if xoT_res is None:
        if rot2:
            for cc in range(NC):
                p.add(tag + f"_stf{cc % 2}", 16)
        else:
            p.add(tag + "_st", 16 * NC)
    per = {"sS": 32, "sE": 32, "sPV": 16, "sZ": 3, "sB": 4, "sEv": 4, "sN": 4}

    def hb(name, h):
        return b[name] + per[name] * h

    def sy0(q):
        q.dma_start(out=qT.rearrange("p (c t) -> p c t", c=NC),
                    in_=qT_hbm.rearrange("(c p) t -> p c t", p=128)).then_inc(ld, 16)
        q.dma_start(out=kvT.rearrange("p (c t) -> p c t", c=NC),
                    in_=kvT_hbm.rearrange("(c p) t -> p c t", p=128)).then_inc(ld, 16)
        q.dma_start(out=ktok.rearrange("p (h j e) -> p h j e", h=H, j=16),
                    in_=kvtok_hbm.rearrange("(h j p) e -> p h j e", p=128, j=16)).then_inc(ld, 16)
    if res_in is None:
        blk.sync(sy0)

    def tens(q):
        if res_in is None:
            q.wait_ge(ld, b_ld + 48)
        for h in range(H):
            c, rb = h // 2, 64 * (h % 2)
            for j in range(16):
                for piece in range(2):
                    if j > 0:
                        q.wait_ge(sE, hb("sE", h) + 2 * (j - 1) + piece + 1)
                    elif h > 0:
                        q.wait_ge(sE, hb("sE", h - 1) + 32)
                    for ns in range(2):
                        qs = 2 * piece + ns
                        ins = q.matmul(pS[piece][:, 512 * ns:512 * (ns + 1)],
                                       kvT[rb:rb + 64, c * T + 128 * j:c * T + 128 * (j + 1)],
                                       qT[rb:rb + 64, c * T + 512 * qs:c * T + 512 * (qs + 1)],
                                       start=True, stop=True)
                    ins.then_inc(sS, 1)
                if j == 2 and h > 0:
                    q.wait_ge(sZ, hb("sZ", h - 1) + 3)
                    for qs in range(4):
                        ins = q.matmul(pOut[64:128, 512 * qs:512 * (qs + 1)], ones_row_sb[0:1, 0:64],
                                       zr[0:1, 512 * qs:512 * (qs + 1)], start=True, stop=True)
                        ins.then_inc(sB, 1)
                jj = j - 3
                if jj >= 0:
                    q.wait_ge(sE, hb("sE", h) + 2 * jj + 2)
                    if jj == 0 and h > 0:
                        q.wait_ge(sN, hb("sN", h - 1) + 4)
                    for qs in range(4):
                        ins = q.matmul(pOut[0:65, 512 * qs:512 * (qs + 1)],
                                       ktok[:, h * 16 * 65 + 65 * jj:h * 16 * 65 + 65 * jj + 65],
                                       PT[jj % 4][:, 512 * qs:512 * (qs + 1)],
                                       start=(jj == 0), stop=False)
                    ins.then_inc(sPV, 1)
            for jj in (13, 14, 15):
                q.wait_ge(sE, hb("sE", h) + 2 * jj + 2)
                for qs in range(4):
                    ins = q.matmul(pOut[0:65, 512 * qs:512 * (qs + 1)],
                                   ktok[:, h * 16 * 65 + 65 * jj:h * 16 * 65 + 65 * jj + 65],
                                   PT[jj % 4][:, 512 * qs:512 * (qs + 1)],
                                   start=False, stop=(jj == 15))
                ins.then_inc(sPV, 1)
        q.wait_ge(sZ, hb("sZ", H - 1) + 3)
        for qs in range(4):
            ins = q.matmul(pOut[64:128, 512 * qs:512 * (qs + 1)], ones_row_sb[0:1, 0:64],
                           zr[0:1, 512 * qs:512 * (qs + 1)], start=True, stop=True)
            ins.then_inc(sB, 1)
    blk.tensor(tens)

    def zpart1(q, hh):
        q.wait_ge(sPV, hb("sPV", hh) + 16)
        q.copy(zsb[:], pOut[64:65, :]).then_inc(sZ, 1)
        q.wait_ge(sZ, hb("sZ", hh) + 1)
        q.activation(zln[:], zsb[:], AF.Ln).then_inc(sZ, 1)

    def zpart2(q, hh):
        q.wait_ge(sZ, hb("sZ", hh) + 2)
        q.activation(zr[:], zln[:], AF.Exp, scale=-1.0).then_inc(sZ, 1)

    def zpart3(q, hh):
        for qs in range(4):
            q.wait_ge(sB, hb("sB", hh) + qs + 1)
            if qs > 0:
                q.wait_ge(sN, hb("sN", hh) + qs)
            elif hh > 0:
                q.wait_ge(sN, hb("sN", hh - 1) + 4)
            q.copy(zbc[0:64, :], pOut[64:128, 512 * qs:512 * (qs + 1)]).then_inc(sEv, 1)

    def scal(q):
        for h in range(H):
            for j in range(16):
                for piece in range(2):
                    q.wait_ge(sS, hb("sS", h) + 2 * j + piece + 1)
                    if j >= 4:
                        q.wait_ge(sPV, hb("sPV", h) + j - 3)
                    elif h > 0:
                        q.wait_ge(sPV, hb("sPV", h - 1) + 16)
                    q.activation(PT[j % 4][:, 1024 * piece:1024 * (piece + 1)],
                                 pS[piece][:], AF.Exp, bias=negc, scale=0.125).then_inc(sE, 1)
                if h > 0:
                    if j == 0:
                        zpart1(q, h - 1)
                    elif j == 1:
                        zpart2(q, h - 1)
                    elif j == 2:
                        zpart3(q, h - 1)
        zpart1(q, H - 1)
        zpart2(q, H - 1)
        zpart3(q, H - 1)
    blk.scalar(scal)

    def vec(q):
        for h in range(H):
            c, rb = h // 2, 64 * (h % 2)
            dc = (c % 2) if rot2 else c
            for qs in range(4):
                q.wait_ge(sEv, hb("sEv", h) + qs + 1)
                if rot2 and h % 2 == 0 and qs == 0 and c >= 2:
                    q.wait_ge(stf[c % 2], b_stf[c % 2] + 16 * ((c - 2) // 2 + 1))
                q.tensor_tensor(xoT[rb:rb + 64, dc * T + 512 * qs:dc * T + 512 * (qs + 1)],
                                pOut[0:64, 512 * qs:512 * (qs + 1)], zbc[0:64, :],
                                ALU.mult).then_inc(sN, 1)
    blk.vector(vec)

    def sy1(q):
        for cc in range(NC):
            q.wait_ge(sN, hb("sN", 2 * cc + 1) + 4)
            sc2 = (cc % 2) if rot2 else cc
            q.dma_start(out=xoT_hbm[128 * cc:128 * (cc + 1), :],
                        in_=xoT[:, sc2 * T:(sc2 + 1) * T]).then_inc(stf[cc % 2] if rot2 else st, 16)
        if rot2:
            for s in range(2):
                q.wait_ge(stf[s], b_stf[s] + 16 * (NC // 2))
        else:
            q.wait_ge(st, b_st + 16 * NC)
    if xoT_res is None:
        blk.sync(sy1)
    pctx.close()
    ctx.close()



def emit_out_proj(p, xoT_hbm, w_hbm, brow_hbm, ones_row_sb, s_hbm, tag="b3",
                  with_bias=True, xoT_res=None):
    nc, blk = p.nc, p.block
    ctx = ExitStack()
    sb = lambda n, shp, dt: ctx.enter_context(nc.sbuf_tensor(tag + n, list(shp), dt))
    pctx = ExitStack()
    psb = lambda n, shp, dt: pctx.enter_context(nc.psum_tensor(tag + n, list(shp), dt))
    xoT = xoT_res if xoT_res is not None else sb("_xoT", (128, NC * T), BF16)
    wb = sb("_w", (128, NC * D), BF16)
    brow = sb("_brow", (1, D), F32) if with_bias else None
    bbc = sb("_bbc", (128, D), F32) if with_bias else None
    ssb = [sb(f"_s{i}", (128, D), F32) for i in range(2)]
    pS = [psb(f"_pS{i}", (128, D), F32) for i in range(2)]
    pB = psb("_pB", (128, D), F32)

    ld = p.sem(tag + "_ld")
    mm = p.sem(tag + "_mm")
    dv = p.sem(tag + "_dv")
    ev = p.sem(tag + "_ev")
    stf = [p.sem(tag + f"_stf{s}") for s in range(2)]
    ldq = [p.sem(tag + f"_ldq{s}") for s in range(2)] if xoT_res is None else None
    b_ld = p.val(tag + "_ld")
    b_mm = p.val(tag + "_mm")
    b_dv = p.val(tag + "_dv")
    b_ev = p.val(tag + "_ev")
    b_stf = [p.val(tag + f"_stf{s}") for s in range(2)]
    nld = 1 if with_bias else 0
    if nld:
        p.add(tag + "_ld", 16 * nld)
    if xoT_res is None:
        b_ldq = [p.val(tag + f"_ldq{s}") for s in range(2)]
        for s in range(2):
            p.add(tag + f"_ldq{s}", 16)
    nbias = 1 if with_bias else 0
    if with_bias:
        p.add(tag + "_mm", 1)
        p.add(tag + "_ev", 1)
    for tt in range(NT):
        p.add(tag + "_mm", 1)
        p.add(tag + "_dv", 1)
        p.add(tag + f"_stf{tt % 2}", 16)

    gsems = []

    def gp0(q):
        gsems.append(p.swdge(q, wb.rearrange("p (c o) -> p c o", c=NC),
                             w_hbm.rearrange("(c p) o -> p c o", p=128)))
    blk.gpsimd(gp0)

    def sy0(q):
        if with_bias:
            q.dma_start(out=brow[:], in_=brow_hbm[:]).then_inc(ld, 16)
        if xoT_res is None:
            for s in range(2):
                q.dma_start(out=xoT.rearrange("p (c t) -> p c t", c=NC)[:, :, 1024 * s:1024 * (s + 1)],
                            in_=xoT_hbm.rearrange("(c p) t -> p c t", p=128)[:, :, 1024 * s:1024 * (s + 1)]).then_inc(ldq[s], 16)
    if nld or xoT_res is None:
        blk.sync(sy0)

    def tens(q):
        if nld:
            q.wait_ge(ld, b_ld + 16 * nld)
        for s, tgt in gsems:
            q.wait_ge(s, tgt)
        if with_bias:
            for ns in range(2):
                ins = q.matmul(pB[:, 512 * ns:512 * (ns + 1)], ones_row_sb[:],
                               brow[0:1, 512 * ns:512 * (ns + 1)], start=True, stop=True)
            ins.then_inc(mm, 1)
        for tt in range(NT):
            par = tt % 2
            if xoT_res is None and tt % 8 == 0:
                q.wait_ge(ldq[tt // 8], b_ldq[tt // 8] + 16)
            if tt >= 2:
                q.wait_ge(dv, b_dv + tt - 1)
            for c in range(NC):
                for ns in range(2):
                    ins = q.matmul(pS[par][:, 512 * ns:512 * (ns + 1)],
                                   xoT[:, c * T + 128 * tt:c * T + 128 * (tt + 1)],
                                   wb[:, c * D + 512 * ns:c * D + 512 * (ns + 1)],
                                   start=(c == 0), stop=(c == NC - 1))
            ins.then_inc(mm, 1)
    blk.tensor(tens)

    def scal(q):
        if with_bias:
            q.wait_ge(mm, b_mm + 1)
            q.copy(bbc[:], pB[:]).then_inc(ev, 1)
    blk.scalar(scal)

    def vec(q):
        if with_bias:
            q.wait_ge(ev, b_ev + 1)
        for tt in range(NT):
            par = tt % 2
            q.wait_ge(mm, b_mm + nbias + tt + 1)
            if tt >= 2:
                q.wait_ge(stf[par], b_stf[par] + 16 * ((tt - 2) // 2 + 1))
            if with_bias:
                q.tensor_tensor(ssb[par][:], pS[par][:], bbc[:], ALU.add).then_inc(dv, 1)
            else:
                q.tensor_copy(ssb[par][:], pS[par][:]).then_inc(dv, 1)
    blk.vector(vec)

    def sy1(q):
        for tt in range(NT):
            par = tt % 2
            q.wait_ge(dv, b_dv + tt + 1)
            q.dma_start(out=s_hbm[128 * tt:128 * (tt + 1), :], in_=ssb[par][:]).then_inc(stf[par], 16)
        for s in range(2):
            q.wait_ge(stf[s], b_stf[s] + 16 * (NT // 2))
    blk.sync(sy1)
    pctx.close()
    ctx.close()


def build_l0_attn_program():
    nc = bass.Bass("TRN2", target_bir_lowering=False)
    qT = dram_io(nc, "qT", (D, T), BF16, "ExternalInput")
    kvT = dram_io(nc, "kvT", (D, T), BF16, "ExternalInput")
    kvtok = dram_io(nc, "kvtok", (H * 16 * 128, 65), BF16, "ExternalInput")
    scon = dram_io(nc, "sconsts", (128, len(CONST_NAMES)), F32, "ExternalInput")
    onesr = dram_io(nc, "ones_row", (1, 128), F32, "ExternalInput")
    xoT = dram_io(nc, "xoT", (D, T), BF16, "ExternalOutput")
    with ExitStack() as ctx:
        p = Prog(nc, ctx)
        cs = _const_loader(p, ctx, {"sconsts": (scon, (128, len(CONST_NAMES)), "f32"),
                                    "ones_row": (onesr, (1, 128), "f32")})
        emit_l0_attn(p, qT, kvT, kvtok, cs["sconsts"], cs["ones_row"], xoT)
    return nc


def build_out_proj_program(with_bias=True):
    nc = bass.Bass("TRN2", target_bir_lowering=False)
    xoT = dram_io(nc, "xoT", (D, T), BF16, "ExternalInput")
    w = dram_io(nc, "w", (D, D), F32, "ExternalInput")
    brow = dram_io(nc, "brow", (1, D), F32, "ExternalInput")
    onesr = dram_io(nc, "ones_row", (1, 128), F32, "ExternalInput")
    s_out = dram_io(nc, "s_out", (T, D), F32, "ExternalOutput")
    with ExitStack() as ctx:
        p = Prog(nc, ctx)
        cs = _const_loader(p, ctx, {"ones_row": (onesr, (1, 128), "f32")})
        emit_out_proj(p, xoT, w, brow, cs["ones_row"], s_out, with_bias=with_bias)
    return nc


def build_l0_qkv_program():
    nc = bass.Bass("TRN2", target_bir_lowering=False)
    xnT = dram_io(nc, "xnT", (D, T), BF16, "ExternalInput")
    wq = dram_io(nc, "wq", (D, D), F32, "ExternalInput")
    wkv = dram_io(nc, "wkv", (D, D), F32, "ExternalInput")
    cos2 = dram_io(nc, "cos2", (128, T), F32, "ExternalInput")
    sin2 = dram_io(nc, "sin2", (128, T), F32, "ExternalInput")
    rot128 = dram_io(nc, "rot128", (128, 128), F32, "ExternalInput")
    identf = dram_io(nc, "ident", (128, 128), F32, "ExternalInput")
    qT = dram_io(nc, "qT", (D, T), BF16, "ExternalOutput")
    kvT = dram_io(nc, "kvT", (D, T), BF16, "ExternalOutput")
    kvtok = dram_io(nc, "kvtok", (H * 16 * 128, 65), BF16, "ExternalOutput")
    with ExitStack() as ctx:
        p = Prog(nc, ctx)
        cs = _const_loader(p, ctx, {"ident": (identf, (128, 128), "bf16")})
        emit_l0_qkv(p, xnT, wq, wkv, cos2, sin2, rot128, cs["ident"], qT, kvT, kvtok)
    return nc



def emit_l0_ffn(p, xnT_hbm, xn_tok_hbm, x_hbm, s0_hbm, prm, cpool, ones_row_sb,
                ident_f32, ident_bf, y_hbm, x1_hbm, tag="c", xnT_res=None):
    nc, blk = p.nc, p.block
    ctx = ExitStack()
    sb = lambda n, shp, dt: ctx.enter_context(nc.sbuf_tensor(tag + n, list(shp), dt))

    xnT = xnT_res if xnT_res is not None else sb("_xnT", (128, NC * T), BF16)
    sigraw = sb("_sigraw", (64, D), F32)
    sigs_bf = sb("_sigs", (64, D), BF16)
    gmean_bf = sb("_gmean", (64, 8), BF16)
    csig_bf = sb("_csig", (8, D), BF16)
    sigsT = sb("_sigsT", (128, NC * 64), BF16)
    csigT = sb("_csigT", (128, NC * 8), BF16)
    knots = sb("_knots", (1, 8), F32)
    temp = sb("_temp", (1, 1), F32)
    oscale = sb("_oscale", (1, 1), F32)
    cols = sb("_cols", (72, 10), F32)
    cw1b = sb("_cw1", (128, NC * 256), BF16)
    cb1c = sb("_cb1", (128, 2), F32)
    cw2b = sb("_cw2", (128, 4), BF16)
    cb2c = sb("_cb2", (2, 1), F32)
    comp_bf = sb("_comp", (2, T), BF16)
    reppat = sb("_reppat", (2, 16), BF16)
    comprep = sb("_comprep", (16, T), BF16)
    gacc = [sb(f"_gacc{i}", (72, T), F32) for i in range(2)]
    iota64c = sb("_iota64", (128, 64), F32)
    iota64m = sb("_iota64m", (128, 64), F32)
    coh8T = sb("_coh8T", (8, T), BF16)
    expand_bf = sb("_expand", (8, 64), BF16)
    tohfm = sb("_tohfm", (64, T), F32)
    w1gb = sb("_w1g", (16, 8 * 128), BF16)
    b1gc = sb("_b1g", (128, 8), F32)
    w2gb = sb("_w2g", (128, 8 * 64), BF16)
    b2c = sb("_b2c", (64, 1), F32)
    dirs_bf = sb("_dirs", (64, D), BF16)
    selmag = sb("_selmag", (64, T), BF16)
    gamr = sb("_gamr", (1, D), F32)
    betr = sb("_betr", (1, D), F32)
    ctxA = ExitStack()
    sbA = lambda n, shp, dt: ctxA.enter_context(nc.sbuf_tensor(tag + n, list(shp), dt))
    sg1 = sbA("_sg1", (64, D), F32)
    sg2 = sbA("_sg2", (64, D), F32)
    sgt = sbA("_sgt", (64, D), F32)
    tmp11 = sbA("_tmp11", (1, 1), F32)
    dsb = sbA("_dsb", (1, 7), F32)
    csb = sbA("_csb", (1, 6), F32)
    prow = sbA("_prow", (1, 10), F32)
    hT = sbA("_hT", (128, 2 * T), BF16)
    nrm = sbA("_nrm", (72, T), F32)
    relu_sb = [sbA(f"_relu{i}", (72, T), F32) for i in range(2)]
    ctxB = ExitStack()
    sbB = lambda n, shp, dt: ctxB.enter_context(nc.sbuf_tensor(tag + n, list(shp), dt))
    coh_tok = sbB("_cohtok", (128, 8), F32)
    mxc = sbB("_mxc", (128, 1), F32)
    eqc = sbB("_eqc", (128, 8), F32)
    t1c = sbB("_t1c", (128, 8), F32)
    t2c = sbB("_t2c", (128, 8), F32)
    cidx = sbB("_cidx", (128, 1), F32)
    gm_t1 = sbB("_gmt1", (64, T), F32)
    gm_v = sbB("_gmv", (64, T), F32)
    gm = sbB("_gm", (64, T), F32)
    mxt = sbB("_mxt", (128, 1), F32)
    eqt = sbB("_eqt", (128, 64), F32)
    t1t = sbB("_t1t", (128, 64), F32)
    t2t = sbB("_t2t", (128, 64), F32)
    tidx = sbB("_tidx", (128, 1), F32)
    toh_tok = sbB("_tohtok", (128, 64), F32)
    S = lambda n: p.sem(tag + n)
    V = lambda n: p.val(tag + n)
    A = lambda n, d=1: p.add(tag + n, d)
    ld = S("_ld")
    c3 = S("_c3")
    c1s = S("_c1s")
    c1v = S("_c1v")
    c2t = S("_c2t")
    c2v = S("_c2v")
    c2e = S("_c2e")
    c4v = S("_c4v")
    c5v = S("_c5v")
    c5w = S("_c5w")
    c5g = S("_c5g")
    c5s = S("_c5s")
    c5e = S("_c5e")
    stxa = [S(f"_stxa{i}") for i in range(2)]

    gsems = []

    def gp0(q):
        if xnT_res is None:
            gsems.append(p.swdge(q, xnT.rearrange("p (c t) -> p c t", c=NC),
                                 xnT_hbm.rearrange("(c p) t -> p c t", p=128)))
        gsems.append(p.swdge(q, cw1b.rearrange("p (c o) -> p c o", c=NC),
                             prm["cw1"].rearrange("(c p) o -> p c o", p=128)))
        gsems.append(p.swdge(q, cw2b.rearrange("p (c o) -> p c o", c=2),
                             prm["cw2"].rearrange("(c p) o -> p c o", p=128)))
        gsems.append(p.swdge(q, w1gb[:], prm["w1g"][:]))
        gsems.append(p.swdge(q, w2gb[:], prm["w2g"][:]))
        gsems.append(p.swdge(q, dirs_bf[:], prm["dirs"][:]))
        gsems.append(p.swdge(q, gmean_bf[:], prm["gmean"][:]))
        gsems.append(p.swdge(q, expand_bf[:], prm["expand8"][:]))
        gsems.append(p.swdge(q, reppat[:], prm["comp_rep"][:]))
    blk.gpsimd(gp0)

    b_ld = V("_ld")
    A("_ld", 16 * 12)
    ld_tgt = V("_ld")

    def sy0(q):
        for dst, key in ((sigraw, "sigraw"), (knots, "knots"), (temp, "temp"),
                         (oscale, "oscale"), (cb1c, "cb1_cols"), (cb2c, "cb2_col"),
                         (b1gc, "b1g"), (b2c, "b2_col"), (iota64c, "iota64"),
                         (iota64m, "iota64m"), (gamr, "gam_row"), (betr, "bet_row")):
            q.dma_start(out=dst[:], in_=prm[key][:]).then_inc(ld, 16)
    blk.sync(sy0)

    c0s = S("_c0s")
    c0v2 = S("_c0v2")
    c0m = S("_c0m")
    c0e = S("_c0e")
    b0s = V("_c0s"); A("_c0s", 2)
    b0v = V("_c0v2"); A("_c0v2", 10)
    b0m = V("_c0m"); A("_c0m", 2)
    b0e = V("_c0e"); A("_c0e", 2)

    def c0_scal(q):
        q.wait_ge(ld, ld_tgt)
        q.sign(sg1[:], sigraw[:], bias=cpool[0:64, CONST_COL["m03"]:CONST_COL["m03"] + 1]).then_inc(c0s, 1)
        q.wait_ge(c0s, b0s + 1)
        q.sign(sg2[:], sigraw[:], bias=cpool[0:64, CONST_COL["p03"]:CONST_COL["p03"] + 1]).then_inc(c0s, 1)
    blk.scalar(c0_scal)

    def c0_vec(q):
        q.wait_ge(ld, ld_tgt)
        q.wait_ge(c0s, b0s + 2)
        q.tensor_tensor(sgt[:], sg1[:], sg2[:], ALU.add).then_inc(c0v2, 1)
        q.wait_ge(c0v2, b0v + 1)
        q.tensor_scalar(sigs_bf[:], sgt[:], 0.5, None, ALU.mult).then_inc(c0v2, 1)
        q.tensor_tensor(dsb[:], knots[0:1, 1:8], knots[0:1, 0:7], ALU.subtract).then_inc(c0v2, 1)
        q.wait_ge(c0v2, b0v + 3)
        q.tensor_tensor(csb[:], dsb[0:1, 1:7], dsb[0:1, 0:6], ALU.subtract).then_inc(c0v2, 1)
        q.wait_ge(c0v2, b0v + 4)
        q.tensor_scalar(tmp11[:], temp[:], 1e-6, None, ALU.add).then_inc(c0v2, 1)
        q.wait_ge(c0v2, b0v + 5)
        q.reciprocal(prow[0:1, 0:1], tmp11[:]).then_inc(c0v2, 1)
        q.wait_ge(c0v2, b0v + 6)
        q.tensor_copy(prow[0:1, 1:2], knots[0:1, 0:1]).then_inc(c0v2, 1)
        q.wait_ge(c0v2, b0v + 7)
        q.tensor_scalar(prow[0:1, 2:3], dsb[0:1, 0:1], 7.0, None, ALU.mult).then_inc(c0v2, 1)
        q.wait_ge(c0v2, b0v + 8)
        q.tensor_copy(prow[0:1, 3:9], csb[:]).then_inc(c0v2, 1)
        q.wait_ge(c0v2, b0v + 9)
        q.tensor_copy(prow[0:1, 9:10], oscale[:]).then_inc(c0v2, 1)
    blk.vector(c0_vec)

    pctx0 = ExitStack()
    pcs = pctx0.enter_context(nc.psum_tensor(tag + "_pcs", [8, D], F32))
    pcols = pctx0.enter_context(nc.psum_tensor(tag + "_pcols", [72, 10], F32))
    ptr = [pctx0.enter_context(nc.psum_tensor(tag + f"_ptr{i}", [128, 128], BF16)) for i in range(2)]

    def c0_tens(q):
        for s, tgt in gsems:
            q.wait_ge(s, tgt)
        q.wait_ge(c0v2, b0v + 2)
        for ns in range(2):
            ins = q.matmul(pcs[:, 512 * ns:512 * (ns + 1)], gmean_bf[:],
                           sigs_bf[:, 512 * ns:512 * (ns + 1)], start=True, stop=True)
        ins.then_inc(c0m, 1)
        q.wait_ge(c0v2, b0v + 10)
        ins = q.matmul(pcols[:], ones_row_sb[0:1, 0:72], prow[:], start=True, stop=True)
        ins.then_inc(c0m, 1)
    blk.tensor(c0_tens)

    def c0_scal2(q):
        q.wait_ge(c0m, b0m + 1)
        q.sign(csig_bf[:], pcs[:]).then_inc(c0e, 1)
        q.wait_ge(c0m, b0m + 2)
        q.copy(cols[:], pcols[:]).then_inc(c0e, 1)
    blk.scalar(c0_scal2)
    c0_done = None

    b_t0 = V("_c2t")
    b_e0 = V("_c2e")
    A("_c2t", NC)
    A("_c2e", NC)

    def c0_tens2(q):
        q.wait_ge(c0e, b0e + 1)
        for k in range(NC):
            if k >= 2:
                q.wait_ge(c2e, b_e0 + k - 1)
            q.transpose(ptr[k % 2][:, 0:64], sigs_bf[:, 128 * k:128 * (k + 1)], ident_bf[0:64, 0:64])
            ins = q.transpose(ptr[k % 2][:, 64:72], csig_bf[:, 128 * k:128 * (k + 1)], ident_bf[0:8, 0:8])
            ins.then_inc(c2t, 1)
    blk.tensor(c0_tens2)

    def c0_scal3(q):
        for k in range(NC):
            q.wait_ge(c2t, b_t0 + k + 1)
            q.copy(sigsT[:, 64 * k:64 * (k + 1)], ptr[k % 2][:, 0:64])
            q.copy(csigT[:, 8 * k:8 * (k + 1)], ptr[k % 2][:, 64:72]).then_inc(c2e, 1)
    blk.scalar(c0_scal3)

    pctx0.close()
    pctx1 = ExitStack()
    ph = pctx1.enter_context(nc.psum_tensor(tag + "_ph", [128, T], F32))
    pc2 = pctx1.enter_context(nc.psum_tensor(tag + "_pc2", [2, T], F32))
    b3 = V("_c3")
    A("_c3", 8)

    def c3_tens(q):
        q.wait_ge(c2e, b_e0 + NC)
        for m in range(2):
            if m == 1:
                q.wait_ge(c3, b3 + 2)
            for k in range(NC):
                for ns in range(4):
                    ins = q.matmul(ph[:, 512 * ns:512 * (ns + 1)],
                                   cw1b[:, k * 256 + 128 * m:k * 256 + 128 * (m + 1)],
                                   xnT[:, k * T + 512 * ns:k * T + 512 * (ns + 1)],
                                   start=(k == 0), stop=(k == NC - 1))
            ins.then_inc(c3, 1)
        q.wait_ge(c3, b3 + 4)
        for m in range(2):
            for ns in range(4):
                ins = q.matmul(pc2[:, 512 * ns:512 * (ns + 1)], cw2b[:, 2 * m:2 * (m + 1)],
                               hT[:, m * T + 512 * ns:m * T + 512 * (ns + 1)],
                               start=(m == 0), stop=(m == 1))
        ins.then_inc(c3, 1)
        q.wait_ge(c3, b3 + 6)
        for ns in range(4):
            ins = q.matmul(ph[0:16, 512 * ns:512 * (ns + 1)], reppat[:],
                           comp_bf[:, 512 * ns:512 * (ns + 1)], start=True, stop=True)
        ins.then_inc(c3, 1)
    blk.tensor(c3_tens)

    def c3_scal(q):
        for m in range(2):
            q.wait_ge(c3, b3 + 2 * m + 1)
            q.activation(hT[:, m * T:(m + 1) * T], ph[:], AF.Gelu,
                         bias=cb1c[:, m:m + 1]).then_inc(c3, 1)
        q.wait_ge(c3, b3 + 5)
        q.activation(comp_bf[:], pc2[:], AF.Tanh, bias=cb2c[:]).then_inc(c3, 1)
        q.wait_ge(c3, b3 + 7)
        q.copy(comprep[:], ph[0:16, :]).then_inc(c3, 1)
    blk.scalar(c3_scal)
    c3_done = b3 + 8

    pctx1.close()
    pctx2 = ExitStack()
    psc = pctx2.enter_context(nc.psum_tensor(tag + "_psc", [64, T], F32))
    pcl = pctx2.enter_context(nc.psum_tensor(tag + "_pcl", [8, T], F32))
    b1s = V("_c1s")
    b1v = V("_c1v")
    A("_c1s", 10)
    A("_c1v", 6)

    def c1_tens(q):
        q.wait_ge(c2e, b_e0 + NC)
        q.wait_ge(c3, c3_done)
        for k in range(NC):
            for ns in range(4):
                q.matmul(psc[:, 512 * ns:512 * (ns + 1)], sigsT[:, 64 * k:64 * (k + 1)],
                         xnT[:, k * T + 512 * ns:k * T + 512 * (ns + 1)],
                         start=(k == 0), stop=(k == NC - 1))
                ins = q.matmul(pcl[:, 512 * ns:512 * (ns + 1)], csigT[:, 8 * k:8 * (k + 1)],
                               xnT[:, k * T + 512 * ns:k * T + 512 * (ns + 1)],
                               start=(k == 0), stop=(k == NC - 1))
        ins.then_inc(c1s, 1)
    blk.tensor(c1_tens)

    def c1_scal(q):
        q.wait_ge(c1s, b1s + 1)
        q.wait_ge(c0e, b0e + 2)
        q.activation(nrm[0:64, :], psc[:], AF.Sigmoid, scale=cols[0:64, 0:1]).then_inc(c1s, 1)
        q.wait_ge(c1s, b1s + 2)
        q.activation(nrm[64:72, :], pcl[:], AF.Sigmoid, scale=cols[64:72, 0:1]).then_inc(c1s, 1)
        q.wait_ge(c1s, b1s + 3)
        q.activation(gacc[0][:], nrm[:], AF.Identity, scale=cols[:, 2:3],
                     bias=cols[:, 1:2]).then_inc(c1s, 1)
        for i in range(1, 7):
            q.wait_ge(c1s, b1s + 3 + i)
            if i >= 3:
                q.wait_ge(c1v, b1v + i - 2)
            q.activation(relu_sb[i % 2][:], nrm[:], AF.Relu, scale=7.0,
                         bias=cpool[0:72, CONST_COL[f"rm{i}"]:CONST_COL[f"rm{i}"] + 1]).then_inc(c1s, 1)
    blk.scalar(c1_scal)

    def c1_vec(q):
        for i in range(1, 7):
            q.wait_ge(c1s, b1s + 4 + i)
            if i >= 2:
                q.wait_ge(c1v, b1v + i - 1)
            q.scalar_tensor_tensor(gacc[i % 2][:], relu_sb[i % 2][:], cols[:, 2 + i:3 + i],
                                   gacc[(i - 1) % 2][:], ALU.mult, ALU.add).then_inc(c1v, 1)
    blk.vector(c1_vec)
    c1v_done = b1v + 6
    gfin = 0

    pctx2.close()
    pctx3 = ExitStack()
    pmsk = pctx3.enter_context(nc.psum_tensor(tag + "_pmsk", [64, T], F32))
    ptg = [pctx3.enter_context(nc.psum_tensor(tag + f"_ptg{i}", [128, 64], F32)) for i in range(2)]
    ptb = [pctx3.enter_context(nc.psum_tensor(tag + f"_ptb{i}", [64, 128], F32)) for i in range(2)]

    bt1 = V("_c2t")
    A("_c2t", NT)
    bv1 = V("_c2v")
    A("_c2v", 6 * NT)
    c2b = S("_c2b")
    bb1 = V("_c2b")
    A("_c2b", NT)
    be1 = V("_c2e")
    A("_c2e", NT)

    def c2_tensAB(q):
        q.wait_ge(c1v, c1v_done)
        for t in range(NT + 1):
            if t < NT:
                if t >= 2:
                    q.wait_ge(c2v, bv1 + 6 * (t - 1))
                ins = q.transpose(ptg[t % 2][:, 0:8], gacc[gfin][64:72, 128 * t:128 * (t + 1)],
                                  ident_f32[64:72, 64:72])
                ins.then_inc(c2t, 1)
            if t >= 1:
                tb = t - 1
                q.wait_ge(c2v, bv1 + 6 * tb + 6)
                if tb >= 2:
                    q.wait_ge(c2e, be1 + tb - 1)
                ins = q.transpose(ptb[tb % 2][0:8, :], coh_tok[:], ident_f32[:])
                ins.then_inc(c2b, 1)
    blk.tensor(c2_tensAB)

    def c2_vecA(q):
        for t in range(NT):
            q.wait_ge(c2t, bt1 + t + 1)
            n0 = bv1 + 6 * t
            q.reduce_max(mxc[:], ptg[t % 2][:, 0:8], axis=AX).then_inc(c2v, 1)
            q.wait_ge(c2v, n0 + 1)
            q.tensor_scalar(eqc[:], ptg[t % 2][:, 0:8], mxc[:], None, ALU.is_equal).then_inc(c2v, 1)
            q.wait_ge(c2v, n0 + 2)
            q.tensor_tensor(t1c[:], eqc[:], iota64m[:, 56:64], ALU.mult).then_inc(c2v, 1)
            q.wait_ge(c2v, n0 + 3)
            q.tensor_scalar(t2c[:], t1c[:], 8.0, None, ALU.add).then_inc(c2v, 1)
            q.wait_ge(c2v, n0 + 4)
            q.tensor_reduce(cidx[:], t2c[:], axis=AX, op=ALU.min).then_inc(c2v, 1)
            q.wait_ge(c2v, n0 + 5)
            if t >= 1:
                q.wait_ge(c2b, bb1 + t)
            q.tensor_scalar(coh_tok[:], iota64c[:, 0:8], cidx[:], None, ALU.is_equal).then_inc(c2v, 1)
    blk.vector(c2_vecA)

    def c2_scalA(q):
        for t in range(NT):
            q.wait_ge(c2b, bb1 + t + 1)
            q.copy(coh8T[:, 128 * t:128 * (t + 1)], ptb[t % 2][0:8, :]).then_inc(c2e, 1)
    blk.scalar(c2_scalA)

    bt3 = V("_c2t")
    A("_c2t", 1)
    bv2 = V("_c2v")
    A("_c2v", 3)
    bt4 = V("_c2t")
    A("_c2t", NT)
    bv3 = V("_c2v")
    A("_c2v", 6 * NT)
    bb2 = V("_c2b")
    A("_c2b", NT)
    be2 = V("_c2e")
    A("_c2e", NT)

    def c2_tensC(q):
        q.wait_ge(c2e, be1 + NT)
        for ns in range(4):
            ins = q.matmul(pmsk[:, 512 * ns:512 * (ns + 1)], expand_bf[:],
                           coh8T[:, 512 * ns:512 * (ns + 1)], start=True, stop=True)
        ins.then_inc(c2t, 1)
    blk.tensor(c2_tensC)

    def c2_vecB(q):
        q.wait_ge(c2t, bt3 + 1)
        q.tensor_tensor(gm_t1[:], gacc[gfin][0:64, :], pmsk[:], ALU.mult).then_inc(c2v, 1)
        q.wait_ge(c2v, bv2 + 1)
        q.tensor_scalar(gm_v[:], pmsk[:], 1.0, None, ALU.subtract).then_inc(c2v, 1)
        q.wait_ge(c2v, bv2 + 2)
        q.scalar_tensor_tensor(gm[:], gm_v[:], 3e8, gm_t1[:], ALU.mult, ALU.add).then_inc(c2v, 1)
    blk.vector(c2_vecB)

    def c2_tensDE(q):
        q.wait_ge(c2v, bv2 + 3)
        for t in range(NT + 1):
            if t < NT:
                if t >= 2:
                    q.wait_ge(c2v, bv3 + 6 * (t - 1))
                ins = q.transpose(ptg[t % 2][:], gm[:, 128 * t:128 * (t + 1)], ident_f32[0:64, 0:64])
                ins.then_inc(c2t, 1)
            if t >= 1:
                tb = t - 1
                q.wait_ge(c2v, bv3 + 6 * tb + 6)
                if tb >= 2:
                    q.wait_ge(c2e, be2 + tb - 1)
                ins = q.transpose(ptb[tb % 2][0:64, :], toh_tok[:], ident_f32[:])
                ins.then_inc(c2b, 1)
    blk.tensor(c2_tensDE)

    def c2_vecC(q):
        for t in range(NT):
            q.wait_ge(c2t, bt4 + t + 1)
            n0 = bv3 + 6 * t
            q.reduce_max(mxt[:], ptg[t % 2][:], axis=AX).then_inc(c2v, 1)
            q.wait_ge(c2v, n0 + 1)
            q.tensor_scalar(eqt[:], ptg[t % 2][:], mxt[:], None, ALU.is_equal).then_inc(c2v, 1)
            q.wait_ge(c2v, n0 + 2)
            q.tensor_tensor(t1t[:], eqt[:], iota64m[:], ALU.mult).then_inc(c2v, 1)
            q.wait_ge(c2v, n0 + 3)
            q.tensor_scalar(t2t[:], t1t[:], 64.0, None, ALU.add).then_inc(c2v, 1)
            q.wait_ge(c2v, n0 + 4)
            q.tensor_reduce(tidx[:], t2t[:], axis=AX, op=ALU.min).then_inc(c2v, 1)
            q.wait_ge(c2v, n0 + 5)
            if t >= 1:
                q.wait_ge(c2b, bb2 + t)
            q.tensor_scalar(toh_tok[:], iota64c[:], tidx[:], None, ALU.is_equal).then_inc(c2v, 1)
    blk.vector(c2_vecC)


    def c2_scalB(q):
        for t in range(NT):
            q.wait_ge(c2b, bb2 + t + 1)
            q.copy(tohfm[:, 128 * t:128 * (t + 1)], ptb[t % 2][0:64, :]).then_inc(c2e, 1)
    blk.scalar(c2_scalB)
    c2_done = be2 + NT

    pctx3.close()
    ctxB.close()
    ctxA.close()
    ctxC = ExitStack()
    sbC = lambda n, shp, dt: ctxC.enter_context(nc.sbuf_tensor(tag + n, list(shp), dt))
    hh_bf = sbC("_hh", (128, T), BF16)
    magp = sbC("_magp", (64, T), F32)
    xnt = [sbC(f"_xnt{i}", (128, D), F32) for i in range(2)]
    ysb = [sbC(f"_y{i}", (128, D), F32) for i in range(2)]
    ytl = [sbC(f"_ytl{i}", (128, D), F32) for i in range(2)]
    sqc = sbC("_sqc", (128, NT), F32)
    sqscr = sbC("_sqscr", (128, D), F32)
    nsum = sbC("_nsum", (128, NT), F32)
    nmu = sbC("_nmu", (128, NT), F32)
    musq = sbC("_musq", (128, NT), F32)
    varc = sbC("_varc", (128, NT), F32)
    sdev = sbC("_sdev", (128, NT), F32)
    rstd = sbC("_rstd", (128, NT), F32)
    bcol = sbC("_bcol", (128, NT), F32)
    gbc = sbC("_gbc", (128, D), F32)
    bbc2 = sbC("_bbc2", (128, D), F32)
    z1 = sbC("_z1", (128, D), F32)
    z2 = sbC("_z2", (128, D), F32)
    z3 = sbC("_z3", (128, D), F32)
    gsl = [sbC(f"_gsl{i}", (128, D), F32) for i in range(2)]
    xti = [sbC(f"_xti{i}", (128, D), F32) for i in range(2)]
    s0t = [sbC(f"_s0t{i}", (128, D), F32) for i in range(2)]
    tt1 = sbC("_tt1", (128, D), F32)
    x1sb = [sbC(f"_x1sb{i}", (128, D), F32) for i in range(2)]
    pctx4 = ExitStack()
    pHH = pctx4.enter_context(nc.psum_tensor(tag + "_pHH", [128, T], F32))
    pMag = pctx4.enter_context(nc.psum_tensor(tag + "_pMag", [64, T], F32))
    c4m = S("_c4m")
    c4s = S("_c4s")
    b4m = V("_c4m"); A("_c4m", 16)
    b4s = V("_c4s"); A("_c4s", 8)
    b4v = V("_c4v")
    A("_c4v", 2)

    def c4_tens(q):
        q.wait_ge(c3, c3_done)
        q.wait_ge(c2e, c2_done)
        for grp in range(8):
            if grp > 0:
                q.wait_ge(c4s, b4s + grp)
            for ns in range(4):
                ins = q.matmul(pHH[:, 512 * ns:512 * (ns + 1)],
                               w1gb[:, 128 * grp:128 * (grp + 1)],
                               comprep[:, 512 * ns:512 * (ns + 1)], start=True, stop=True)
            ins.then_inc(c4m, 1)
            q.wait_ge(c4s, b4s + grp + 1)
            for ns in range(4):
                ins = q.matmul(pMag[:, 512 * ns:512 * (ns + 1)],
                               w2gb[:, 64 * grp:64 * (grp + 1)],
                               hh_bf[:, 512 * ns:512 * (ns + 1)],
                               start=(grp == 0), stop=(grp == 7))
            ins.then_inc(c4m, 1)
    blk.tensor(c4_tens)

    def c4_scal(q):
        for grp in range(8):
            q.wait_ge(c4m, b4m + 2 * grp + 1)
            q.activation(hh_bf[:], pHH[:], AF.Relu, bias=b1gc[:, grp:grp + 1]).then_inc(c4s, 1)
    blk.scalar(c4_scal)

    def c4_vec(q):
        q.wait_ge(c4m, b4m + 16)
        q.wait_ge(c2e, c2_done)
        q.tensor_scalar(magp[:], pMag[:], b2c[:], cols[0:64, 9:10], ALU.add, ALU.mult).then_inc(c4v, 1)
        q.wait_ge(c4v, b4v + 1)
        q.tensor_tensor(selmag[:], magp[:], tohfm[:], ALU.mult).then_inc(c4v, 1)
    blk.vector(c4_vec)
    c4v_done = b4v + 2

    pctx4.close()
    pctx5 = ExitStack()
    pF = [pctx5.enter_context(nc.psum_tensor(tag + f"_pF{i}", [128, D], F32)) for i in range(2)]
    pGB = pctx5.enter_context(nc.psum_tensor(tag + "_pGB", [128, D], F32))
    pBB = pctx5.enter_context(nc.psum_tensor(tag + "_pBB", [128, D], F32))
    b5m = V("_c4m")
    A("_c4m", NT + 2)
    c5la = [S(f"_c5la{i}") for i in range(2)]
    b5la = [V(f"_c5la{i}") for i in range(2)]
    for t in range(NT):
        A(f"_c5la{t % 2}", 16)
    b5v = V("_c5v")
    A("_c5v", 2 * NT)
    b5s = V("_c5s")
    A("_c5s", NT)
    sty = [S(f"_sty{i}") for i in range(2)]
    b_sty = [V(f"_sty{i}") for i in range(2)]
    for t in range(NT):
        A(f"_sty{t % 2}", 16)
    b5e = V("_c5e")
    A("_c5e", 5)
    c5q = S("_c5q")
    b5q = V("_c5q")
    A("_c5q", 3)

    def c5_tens(q):
        q.wait_ge(c4v, c4v_done)
        for t in range(NT):
            par = t % 2
            if t >= 2:
                q.wait_ge(c5v, b5v + 2 * (t - 1))
            for ns in range(2):
                ins = q.matmul(pF[par][:, 512 * ns:512 * (ns + 1)],
                               selmag[:, 128 * t:128 * (t + 1)],
                               dirs_bf[:, 512 * ns:512 * (ns + 1)], start=True, stop=True)
            ins.then_inc(c4m, 1)
        for ns in range(2):
            ins = q.matmul(pGB[:, 512 * ns:512 * (ns + 1)], ones_row_sb[:],
                           gamr[0:1, 512 * ns:512 * (ns + 1)], start=True, stop=True)
        ins.then_inc(c4m, 1)
        for ns in range(2):
            ins = q.matmul(pBB[:, 512 * ns:512 * (ns + 1)], ones_row_sb[:],
                           betr[0:1, 512 * ns:512 * (ns + 1)], start=True, stop=True)
        ins.then_inc(c4m, 1)
    blk.tensor(c5_tens)

    def c5_sy(q):
        q.wait_ge(c4v, c4v_done)
        for t in range(NT):
            if t >= 2:
                q.wait_ge(c5v, b5v + 2 * (t - 1))
            q.dma_start(out=xnt[t % 2][:],
                        in_=xn_tok_hbm[128 * t:128 * (t + 1), :]).then_inc(c5la[t % 2], 16)
    blk.sync(c5_sy)

    def c5_vec(q):
        for t in range(NT):
            par = t % 2
            q.wait_ge(c5la[t % 2], b5la[t % 2] + 16 * (t // 2 + 1))
            q.wait_ge(c4m, b5m + t + 1)
            if t >= 2:
                q.wait_ge(c5s, b5s + t - 1)
                q.wait_ge(sty[par], b_sty[par] + 16 * ((t - 2) // 2 + 1))
            q.tensor_tensor(ysb[par][:], xnt[par][:], pF[par][:], ALU.add).then_inc(c5v, 1)
            q.wait_ge(c5v, b5v + 2 * t + 1)
            q.reduce_sum(nsum[:, t:t + 1], ysb[par][:],
                         axis=AX, negate=True).then_inc(c5v, 1)
    blk.vector(c5_vec)

    def c5_gp(q):
        for t in range(NT):
            par = t % 2
            q.wait_ge(c5v, b5v + 2 * t + 1)
            q.dma_start(out=y_hbm[128 * t:128 * (t + 1), :], in_=ysb[par][:]).then_inc(sty[par], 16)
    blk.gpsimd(c5_gp)

    def c5_scal(q):
        for t in range(NT):
            par = t % 2
            q.wait_ge(c5v, b5v + 2 * t + 1)
            if t > 0:
                q.wait_ge(c5s, b5s + t)
            q.activation(sqscr[:], ysb[par][:], AF.Square,
                         accum_out=sqc[:, t:t + 1]).then_inc(c5s, 1)
    blk.scalar(c5_scal)

    def c5_vec2(q):
        q.wait_ge(c5v, b5v + 2 * NT)
        q.wait_ge(c5s, b5s + NT)
        q.tensor_scalar(nmu[:], nsum[:], 1.0 / D, None, ALU.mult).then_inc(c5e, 1)
        q.wait_ge(c5e, b5e + 1)
        q.tensor_tensor(musq[:], nmu[:], nmu[:], ALU.mult).then_inc(c5e, 1)
        q.wait_ge(c5e, b5e + 2)
        q.scalar_tensor_tensor(varc[:], sqc[:], 1.0 / D, musq[:], ALU.mult, ALU.subtract).then_inc(c5e, 1)
    blk.vector(c5_vec2)

    def c5_scal2(q):
        q.wait_ge(c5e, b5e + 3)
        q.activation(sdev[:], varc[:], AF.Sqrt,
                     bias=cpool[:, CONST_COL["eps_ln"]:CONST_COL["eps_ln"] + 1]).then_inc(c5q, 1)
        q.wait_ge(c4m, b5m + NT + 2)
        q.copy(gbc[:], pGB[:]).then_inc(c5q, 1)
        q.wait_ge(c5q, b5q + 2)
        q.copy(bbc2[:], pBB[:]).then_inc(c5q, 1)
    blk.scalar(c5_scal2)

    def c5_vec3(q):
        q.wait_ge(c5q, b5q + 1)
        q.reciprocal(rstd[:], sdev[:]).then_inc(c5e, 1)
        q.wait_ge(c5e, b5e + 4)
        q.tensor_tensor(bcol[:], nmu[:], rstd[:], ALU.mult).then_inc(c5e, 1)
    blk.vector(c5_vec3)
    c5e_done = b5e + 5

    c5lb = [S(f"_c5lb{i}") for i in range(2)]
    b5lb = [V(f"_c5lb{i}") for i in range(2)]
    for t in range(NT):
        A(f"_c5lb{t % 2}", 48)
    b5w = V("_c5w")
    A("_c5w", 5 * NT)
    b5g = V("_c5g")
    A("_c5g", NT)
    b_stxa = [V(f"_stxa{i}") for i in range(2)]
    for t in range(NT):
        A(f"_stxa{t % 2}", 16)

    def c5_sy2(q):
        for s in range(2):
            q.wait_ge(sty[s], b_sty[s] + 16 * (NT // 2))
        for t in range(NT + 2):
            if t < NT:
                if t >= 2:
                    q.wait_ge(c5w, b5w + 5 * (t - 1))
                q.dma_start(out=xti[t % 2][:], in_=x_hbm[128 * t:128 * (t + 1), :]).then_inc(c5lb[t % 2], 16)
                q.dma_start(out=s0t[t % 2][:], in_=s0_hbm[128 * t:128 * (t + 1), :]).then_inc(c5lb[t % 2], 16)
                q.dma_start(out=ytl[t % 2][:], in_=y_hbm[128 * t:128 * (t + 1), :]).then_inc(c5lb[t % 2], 16)
            if t >= 2:
                ts = t - 2
                q.wait_ge(c5w, b5w + 5 * ts + 5)
                q.dma_start(out=x1_hbm[128 * ts:128 * (ts + 1), :],
                            in_=x1sb[ts % 2][:]).then_inc(stxa[ts % 2], 16)
        for s in range(2):
            q.wait_ge(stxa[s], b_stxa[s] + 16 * (NT // 2))
    blk.sync(c5_sy2)


    def c5_scal3(q):
        for t in range(NT):
            q.wait_ge(c5w, b5w + 5 * t + 3)
            if t >= 2:
                q.wait_ge(c5w, b5w + 5 * (t - 2) + 4)
            q.activation(gsl[t % 2][:], z3[:], AF.Silu).then_inc(c5g, 1)
    blk.scalar(c5_scal3)

    def c5_vec4(q):
        q.wait_ge(c5e, c5e_done)
        q.wait_ge(c5q, b5q + 3)
        for t in range(NT):
            par = t % 2
            q.wait_ge(c5lb[par], b5lb[par] + 48 * (t // 2 + 1))
            q.tensor_scalar(z1[:], ytl[t % 2][:], rstd[:, t:t + 1],
                            bcol[:, t:t + 1], ALU.mult, ALU.add).then_inc(c5w, 1)
            q.wait_ge(c5w, b5w + 5 * t + 1)
            q.tensor_tensor(z2[:], z1[:], gbc[:], ALU.mult).then_inc(c5w, 1)
            q.wait_ge(c5w, b5w + 5 * t + 2)
            if t >= 1:
                q.wait_ge(c5g, b5g + t)
            q.tensor_tensor(z3[:], z2[:], bbc2[:], ALU.add).then_inc(c5w, 1)
            q.wait_ge(c5w, b5w + 5 * t + 3)
            q.wait_ge(c5g, b5g + t + 1)
            q.tensor_tensor(tt1[:], gsl[par][:], s0t[par][:], ALU.mult).then_inc(c5w, 1)
            q.wait_ge(c5w, b5w + 5 * t + 4)
            if t >= 2:
                q.wait_ge(stxa[par], b_stxa[par] + 16 * ((t - 2) // 2 + 1))
            q.tensor_tensor(x1sb[par][:], tt1[:], xti[par][:], ALU.add).then_inc(c5w, 1)
    blk.vector(c5_vec4)

    pctx5.close()
    ctxC.close()
    ctx.close()


def build_l0_ffn_program():
    nc = bass.Bass("TRN2", target_bir_lowering=False)
    xnT = dram_io(nc, "xnT", (D, T), BF16, "ExternalInput")
    xn_tok = dram_io(nc, "xn_tok", (T, D), F32, "ExternalInput")
    x = dram_io(nc, "x", (T, D), F32, "ExternalInput")
    s0 = dram_io(nc, "s0", (T, D), F32, "ExternalInput")
    prm_names = {
        "sigraw": (64, D), "knots": (1, 8), "temp": (1, 1), "oscale": (1, 1),
        "cw1": (D, 256), "cb1_cols": (128, 2), "cw2": (256, 2), "cb2_col": (2, 1),
        "w1g": (16, 8 * 128), "b1g": (128, 8), "w2g": (128, 8 * 64), "b2_col": (64, 1),
        "dirs": (64, D), "gam_row": (1, D), "bet_row": (1, D),
        "comp_rep": (2, 16), "iota64": (128, 64), "iota64m": (128, 64),
        "expand8": (8, 64), "gmean": (64, 8),
    }
    prm = {k: dram_io(nc, "p_" + k, shp, F32, "ExternalInput") for k, shp in prm_names.items()}
    prm["b2_col"] = prm.pop("b2_col")
    scon = dram_io(nc, "sconsts", (128, len(CONST_NAMES)), F32, "ExternalInput")
    onesr = dram_io(nc, "ones_row", (1, 128), F32, "ExternalInput")
    identf = dram_io(nc, "ident", (128, 128), F32, "ExternalInput")
    yscr = dram_io(nc, "yscr", (T, D), F32, "ExternalOutput")
    x1 = dram_io(nc, "x1", (T, D), F32, "ExternalOutput")
    with ExitStack() as ctx:
        p = Prog(nc, ctx)
        cs = _const_loader(p, ctx, {"sconsts": (scon, (128, len(CONST_NAMES)), "f32"),
                                    "ones_row": (onesr, (1, 128), "f32"),
                                    "ident": (identf, (128, 128), "bf16")})
        pm = dict(prm)
        pm["b2_col"] = prm["b2_col"]
        emit_l0_ffn(p, xnT, xn_tok, x, s0, pm, cs["sconsts"], cs["ones_row"],
                    cs["ident_f32"], cs["ident"], yscr, x1)
    return nc



def emit_l1_qkv(p, xnT_hbm, wqkvT_hbm, ident_bf, qT_hbm, kT_hbm, ktok_hbm,
                vtok_hbm, tag="d1", xnT_res=None, wpre=None, res_out=None):
    nc, blk = p.nc, p.block
    ctx = ExitStack()
    sb = lambda n, shp, dt: ctx.enter_context(nc.sbuf_tensor(tag + n, list(shp), dt))
    pctx = ExitStack()
    psb = lambda n, shp, dt: pctx.enter_context(nc.psum_tensor(tag + n, list(shp), dt))
    HT = T // 2

    xnT = xnT_res if xnT_res is not None else sb("_xnT", (128, NC * T), BF16)
    if wpre is not None:
        wpre_sems, wb = wpre
    else:
        wb = sb("_w", (128, NC * 3 * D), BF16)
    rsc = [sb(f"_r{i}", (128, HT), F32) for i in range(2)]
    dsc = [sb(f"_d{i}", (128, HT), F32) for i in range(2)]
    esc = [sb(f"_e{i}", (128, HT), F32) for i in range(2)]
    phib = [sb(f"_phib{i}", (128, T), BF16) for i in range(2)]
    if res_out is not None:
        kTs, ktok, vtok = res_out
        VS = 65
    else:
        kTs = None
        ktok = sb("_ktok", (128, H * 16 * 64), BF16)
        vtok = sb("_vtok", (128, H * 16 * 64), BF16)
        VS = 64
    kres = kTs is not None
    pp = [psb(f"_pp{i}", (128, HT), F32) for i in range(2)]

    mm = p.sem(tag + "_mm")
    ract = p.sem(tag + "_ract")
    dv = p.sem(tag + "_dv")
    tp = p.sem(tag + "_tp")
    ev = p.sem(tag + "_ev")
    sta = [p.sem(tag + f"_sta{i}") for i in range(2)]
    stk = p.sem(tag + "_stk") if res_out is None else None

    b_mm = p.val(tag + "_mm")
    b_ract = p.val(tag + "_ract")
    b_dv = p.val(tag + "_dv")
    b_tp = p.val(tag + "_tp")
    b_ev = p.val(tag + "_ev")
    b_sta = [p.val(tag + f"_sta{i}") for i in range(2)]
    b_stk = p.val(tag + "_stk") if res_out is None else 0
    NM = 3 * NC

    def kind_of(m):
        return m // NC

    def nract(m):
        return sum(4 if kind_of(x) < 2 else 2 for x in range(m))

    def ndv(m):
        return sum(4 if kind_of(x) < 2 else 0 for x in range(m))

    def ntp(m):
        return sum(16 if kind_of(x) >= 1 else 0 for x in range(m))

    def nst(m):
        return sum(16 if kind_of(x) < 2 else 0 for x in range(m))

    for m in range(NM):
        p.add(tag + "_mm", 2)
        p.add(tag + "_ract", 4 if kind_of(m) < 2 else 2)
        if kind_of(m) < 2:
            p.add(tag + "_dv", 4)
            if kind_of(m) == 0 or not kres:
                p.add(tag + f"_sta{m % 2}", 16)
        if kind_of(m) >= 1:
            p.add(tag + "_tp", 16)
            p.add(tag + "_ev", 16)
    if res_out is None:
        p.add(tag + "_stk", 16 * 2 * H)
    ract_tot = p.val(tag + "_ract")
    dv_tot = p.val(tag + "_dv")
    ev_tot = p.val(tag + "_ev")
    sta_tot = [p.val(tag + f"_sta{i}") for i in range(2)]
    stk_tot = p.val(tag + "_stk") if res_out is None else 0

    gsems = [] if wpre is None else list(wpre_sems)

    def gp0(q):
        if xnT_res is None:
            gsems.append(p.swdge(q, xnT.rearrange("p (c t) -> p c t", c=NC),
                                 xnT_hbm.rearrange("(c p) t -> p c t", p=128)))
        if wpre is None:
            gsems.append(p.swdge(q, wb.rearrange("p (c o) -> p c o", c=NC),
                                 wqkvT_hbm.rearrange("(c p) o -> p c o", p=128)))
    if xnT_res is None or wpre is None:
        blk.gpsimd(gp0)

    ptt = [pctx.enter_context(nc.psum_tensor(tag + f"_ptt{i}", [128, 128], BF16)) for i in range(2)]

    def tens(q):
        for s, tgt in gsems:
            q.wait_ge(s, tgt)

        def transposes(mm2):
            kd2 = kind_of(mm2)
            par2 = mm2 % 2
            if kd2 == 1:
                q.wait_ge(dv, b_dv + ndv(mm2) + 4)
            else:
                q.wait_ge(ract, b_ract + nract(mm2) + 2)
            for j in range(16):
                g = ntp(mm2) + j
                if g >= 2:
                    q.wait_ge(ev, b_ev + g - 1)
                if kd2 == 1 and kres:
                    src = kTs[:, (mm2 % NC) * T + 128 * j:(mm2 % NC) * T + 128 * (j + 1)]
                else:
                    src = phib[par2][:, 128 * j:128 * (j + 1)]
                ins2 = q.transpose(ptt[g % 2][:], src, ident_bf[:])
                ins2.then_inc(tp, 1)

        for m in range(NM):
            kd = kind_of(m)
            par = m % 2
            for hf in range(2):
                if m >= 1:
                    if kind_of(m - 1) < 2:
                        q.wait_ge(dv, b_dv + ndv(m - 1) + 2 * hf + 1)
                        q.wait_ge(ract, b_ract + nract(m - 1) + 2 * hf + 1)
                    else:
                        q.wait_ge(ract, b_ract + nract(m - 1) + hf + 1)
                for k in range(NC):
                    for ns in range(2):
                        t0 = hf * HT + 512 * ns
                        ins = q.matmul(pp[hf][:, 512 * ns:512 * (ns + 1)],
                                       wb[:, k * 3 * D + 128 * m:k * 3 * D + 128 * (m + 1)],
                                       xnT[:, k * T + t0:k * T + t0 + 512],
                                       start=(k == 0), stop=(k == NC - 1))
                ins.then_inc(mm, 1)
            if m >= 1 and kind_of(m - 1) >= 1:
                transposes(m - 1)
        transposes(NM - 1)
    blk.tensor(tens)

    def scal2(q):
        for m in range(NM):
            kd = kind_of(m)
            par = m % 2
            if kd < 2:
                for hf in range(2):
                    q.wait_ge(mm, b_mm + 2 * m + hf + 1)
                    if m >= 1 and kind_of(m - 1) < 2:
                        q.wait_ge(dv, b_dv + ndv(m - 1) + 2 * hf + 2)
                    q.activation(rsc[hf][:], pp[hf][:], AF.Relu).then_inc(ract, 1)
                    q.wait_ge(dv, b_dv + ndv(m) + 2 * hf + 1)
                    q.activation(esc[hf][:], dsc[hf][:], AF.Exp).then_inc(ract, 1)
            else:
                for hf in range(2):
                    q.wait_ge(mm, b_mm + 2 * m + hf + 1)
                    if m >= 1 and kind_of(m - 1) >= 2:
                        q.wait_ge(tp, b_tp + ntp(m - 1) + 16)
                    if m >= 2 and kind_of(m - 2) == 1:
                        if not kres:
                            q.wait_ge(sta[par], b_sta[par] + 16 * ((m - 2) // 2 + 1))
                            q.wait_ge(tp, b_tp + ntp(m - 2) + 16)
                        else:
                            q.wait_ge(sta[par], b_sta[par] + 16 * 4)
                    q.activation(phib[par][:, HT * hf:HT * (hf + 1)], pp[hf][:],
                                 AF.Identity).then_inc(ract, 1)
            if kd >= 1:
                c = m % NC
                hA, hB = 2 * c, 2 * c + 1
                dstt, DS = (ktok, 64) if kd == 1 else (vtok, VS)
                for j in range(16):
                    g = ntp(m) + j
                    q.wait_ge(tp, b_tp + g + 1)
                    q.copy(dstt[:, hA * 16 * DS + DS * j:hA * 16 * DS + DS * j + 64],
                           ptt[g % 2][:, 0:64])
                    q.copy(dstt[:, hB * 16 * DS + DS * j:hB * 16 * DS + DS * j + 64],
                           ptt[g % 2][:, 64:128]).then_inc(ev, 1)
    blk.scalar(scal2)

    def vec(q):
        for m in range(NM):
            kd = kind_of(m)
            par = m % 2
            if kd < 2:
                for hf in range(2):
                    q.wait_ge(ract, b_ract + nract(m) + 2 * hf + 1)
                    if m >= 1 and kind_of(m - 1) < 2:
                        q.wait_ge(ract, b_ract + nract(m - 1) + 2 * hf + 2)
                    q.tensor_tensor(dsc[hf][:], pp[hf][:],
                                    rsc[hf][:], ALU.subtract).then_inc(dv, 1)
                    q.wait_ge(ract, b_ract + nract(m) + 2 * hf + 2)
                    if kd == 1 and kres:
                        q.tensor_tensor(kTs[:, (m % NC) * T + HT * hf:(m % NC) * T + HT * (hf + 1)],
                                        rsc[hf][:], esc[hf][:], ALU.add).then_inc(dv, 1)
                        continue
                    if m >= 2 and kind_of(m - 2) < 2:
                        q.wait_ge(sta[m % 2], b_sta[m % 2] + 16 * ((m - 2) // 2 + 1))
                        if kind_of(m - 2) == 1:
                            q.wait_ge(tp, b_tp + ntp(m - 2) + 16)
                    q.tensor_tensor(phib[par][:, HT * hf:HT * (hf + 1)], rsc[hf][:],
                                    esc[hf][:], ALU.add).then_inc(dv, 1)
    blk.vector(vec)

    def sy0(q):
        for m in range(NM):
            kd = kind_of(m)
            par = m % 2
            if kd == 0 or (kd == 1 and not kres):
                q.wait_ge(dv, b_dv + ndv(m) + 4)
                dst = qT_hbm if kd == 0 else kT_hbm
                q.dma_start(out=dst[128 * (m % NC):128 * (m % NC + 1), :],
                            in_=phib[par][:]).then_inc(sta[par], 16)
        if res_out is None:
            q.wait_ge(ev, b_ev + ev_tot - b_ev)
            for h in range(H):
                q.dma_start(out=ktok_hbm.rearrange("(h j p) e -> h p j e", p=128, j=16)[h],
                            in_=ktok.rearrange("p (h j e) -> p h j e", h=H, j=16)[:, h]).then_inc(stk, 16)
                q.dma_start(out=vtok_hbm.rearrange("(h j p) e -> h p j e", p=128, j=16)[h],
                            in_=vtok.rearrange("p (h j e) -> p h j e", h=H, j=16)[:, h]).then_inc(stk, 16)
            q.wait_ge(stk, stk_tot)
        for s2 in range(2):
            q.wait_ge(sta[s2], sta_tot[s2])
    blk.sync(sy0)
    pctx.close()
    ctx.close()



def emit_l1_scan(p, qT_hbm, kT_hbm, ktok_hbm, vtok_hbm, mask_hbm, ident_bf,
                 xoT_hbm, tag="d2", res_in=None, xoT_res=None):
    nc, blk = p.nc, p.block
    ctx = ExitStack()
    sb = lambda n, shp, dt: ctx.enter_context(nc.sbuf_tensor(tag + n, list(shp), dt))
    pctx = ExitStack()
    psb = lambda n, shp, dt: pctx.enter_context(nc.psum_tensor(tag + n, list(shp), dt))

    qT = sb("_qT", (128, NC * T), BF16)
    kload = res_in is not None and res_in[0] is None
    if res_in is not None:
        kT, ktok, vtok = res_in
        if kT is None:
            kT = sb("_kT", (128, NC * (T // 2)), BF16)
    else:
        kT = sb("_kT", (128, NC * T), BF16)
        ktok = sb("_ktok", (128, H * 16 * 64), BF16)
        vtok = sb("_vtok", (128, H * 16 * 65), BF16)
    maskf = sb("_mask", (128, 128), F32)
    Sf = [sb(f"_Sf{i}", (128, H * 65), F32) for i in range(2)]
    Sb = sb("_Sb", (128, H * 65), BF16)
    Am = [sb(f"_Am{i}", (128, 128), BF16) for i in range(2)]
    rcol = sb("_rcol", (128, 1), F32)
    xotok = [sb(f"_xo{i}", (128, 64), BF16) for i in range(2)]
    xoT = xoT_res if xoT_res is not None else sb("_xoT", (128, NC * T), BF16)
    pA = [psb(f"_pA{i}", (128, 128), F32) for i in range(2)]
    pO = [psb(f"_pO{i}", (128, 65), F32) for i in range(2)]
    pS = [psb(f"_pS{i}", (128, 65), F32) for i in range(2)]
    ptt = [psb(f"_ptt{i}", (64, 128), BF16) for i in range(2)]

    ld = p.sem(tag + "_ld")
    mz = p.sem(tag + "_mz")
    sA = p.sem(tag + "_sA")
    sM = p.sem(tag + "_sM")
    sO = p.sem(tag + "_sO")
    sSt = p.sem(tag + "_sSt")
    sSv = p.sem(tag + "_sSv")
    sSb = p.sem(tag + "_sSb")
    sDn = p.sem(tag + "_sDn")
    sTp = p.sem(tag + "_sTp")
    sEv = p.sem(tag + "_sEv")
    st = p.sem(tag + "_st")
    ldq = [p.sem(tag + f"_ldq{s}") for s in range(2)] if res_in is not None else None
    ldk = [p.sem(tag + f"_ldk{s}") for s in range(2)] if kload else None
    Bv = {}
    for n in ("_ld", "_mz", "_sA", "_sM", "_sO", "_sSt", "_sSv", "_sSb", "_sDn",
              "_sTp", "_sEv", "_st"):
        Bv[n] = p.val(tag + n)
    if res_in is not None:
        b_ldq = [p.val(tag + f"_ldq{s}") for s in range(2)]
        p.add(tag + "_ld", 16)
        for s in range(2):
            p.add(tag + f"_ldq{s}", 16)
        if kload:
            b_ldk = [p.val(tag + f"_ldk{s}") for s in range(2)]
            for s in range(2):
                p.add(tag + f"_ldk{s}", 16)
    else:
        p.add(tag + "_ld", 16 * 5)
    p.add(tag + "_mz", 3)
    NJH = 16 * H
    for n, d in (("_sA", NJH), ("_sM", NJH), ("_sO", NJH), ("_sSt", NJH),
                 ("_sSv", NJH), ("_sSb", NJH), ("_sDn", 2 * NJH), ("_sTp", NJH),
                 ("_sEv", NJH)):
        p.add(tag + n, d)
    if xoT_res is None:
        p.add(tag + "_st", 16 * NC)

    idx = lambda j, h: j * H + h

    def sy0(q):
        q.dma_start(out=qT.rearrange("p (c t) -> p c t", c=NC),
                    in_=qT_hbm.rearrange("(c p) t -> p c t", p=128)).then_inc(ld, 16)
        q.dma_start(out=kT.rearrange("p (c t) -> p c t", c=NC),
                    in_=kT_hbm.rearrange("(c p) t -> p c t", p=128)).then_inc(ld, 16)
        q.dma_start(out=ktok.rearrange("p (h j e) -> p h j e", h=H, j=16),
                    in_=ktok_hbm.rearrange("(h j p) e -> p h j e", p=128, j=16)).then_inc(ld, 16)
        q.dma_start(out=vtok.rearrange("p (h j e) -> p h j e", h=H, j=16)[:, :, :, 0:64],
                    in_=vtok_hbm.rearrange("(h j p) e -> p h j e", p=128, j=16)).then_inc(ld, 16)
        q.dma_start(out=maskf[:], in_=mask_hbm[:]).then_inc(ld, 16)

    def sy0r(q):
        q.dma_start(out=maskf[:], in_=mask_hbm[:]).then_inc(ld, 16)
        for s in range(4):
            if s < 2:
                q.dma_start(out=qT.rearrange("p (c t) -> p c t", c=NC)[:, :, 1024 * s:1024 * (s + 1)],
                            in_=qT_hbm.rearrange("(c p) t -> p c t", p=128)[:, :, 1024 * s:1024 * (s + 1)]).then_inc(ldq[s], 16)
            if kload and s < 2:
                if s == 1:
                    q.wait_ge(sA, Bv["_sA"] + idx(7, H - 1) + 1)
                q.dma_start(out=kT.rearrange("p (c t) -> p c t", c=NC),
                            in_=kT_hbm.rearrange("(c p) t -> p c t", p=128)[:, :, 1024 * s:1024 * (s + 1)]).then_inc(ldk[s], 16)
    blk.sync(sy0 if res_in is None else sy0r)

    def vz(q):
        q.wait_ge(ld, Bv["_ld"] + (16 if res_in is not None else 16 * 5))
        q.memset(vtok.rearrange("p (h j e) -> p h j e", h=H, j=16)[:, :, :, 64:65], 1.0).then_inc(mz, 1)
        q.memset(Sf[1][:], 0.0).then_inc(mz, 1)
        q.memset(Sb[:], 0.0).then_inc(mz, 1)
    blk.vector(vz)

    def tens(q):
        q.wait_ge(mz, Bv["_mz"] + 3)
        for j in range(16):
            if res_in is not None and j % 8 == 0:
                q.wait_ge(ldq[j // 8], b_ldq[j // 8] + 16)
                if kload and j % 8 == 0:
                    q.wait_ge(ldk[j // 8], b_ldk[j // 8] + 16)
            jk = (j % 8) if kload else j
            for h in range(H):
                i = idx(j, h)
                par = i % 2
                c, rb = h // 2, 64 * (h % 2)
                if i >= 2:
                    q.wait_ge(sM, Bv["_sM"] + i - 1)
                KW = (T // 2) if kload else T
                ins = q.matmul(pA[par][:], kT[rb:rb + 64, c * KW + 128 * jk:c * KW + 128 * (jk + 1)],
                               qT[rb:rb + 64, c * T + 128 * j:c * T + 128 * (j + 1)],
                               start=True, stop=True)
                ins.then_inc(sA, 1)
                q.wait_ge(sM, Bv["_sM"] + i + 1)
                if i >= 2:
                    q.wait_ge(sDn, Bv["_sDn"] + 2 * (i - 1))
                if j > 0:
                    q.wait_ge(sSb, Bv["_sSb"] + idx(j - 1, h) + 1)
                q.matmul(pO[par][:], Am[par][:],
                         vtok[:, h * 16 * 65 + 65 * j:h * 16 * 65 + 65 * (j + 1)],
                         start=True, stop=False)
                ins = q.matmul(pO[par][:], qT[rb:rb + 64, c * T + 128 * j:c * T + 128 * (j + 1)],
                               Sb[rb:rb + 64, h * 65:(h + 1) * 65], start=False, stop=True)
                ins.then_inc(sO, 1)
                if i >= 2:
                    q.wait_ge(sSv, Bv["_sSv"] + i - 1)
                ins = q.matmul(pS[par][rb:rb + 64, :],
                               ktok[:, h * 16 * 64 + 64 * j:h * 16 * 64 + 64 * (j + 1)],
                               vtok[:, h * 16 * 65 + 65 * j:h * 16 * 65 + 65 * (j + 1)],
                               start=True, stop=True)
                ins.then_inc(sSt, 1)
                q.wait_ge(sDn, Bv["_sDn"] + 2 * i + 2)
                if i >= 2:
                    q.wait_ge(sEv, Bv["_sEv"] + i - 1)
                ins = q.transpose(ptt[par][:], xotok[par][:], ident_bf[:])
                ins.then_inc(sTp, 1)
    blk.tensor(tens)

    def vec(q):
        for j in range(16):
            for h in range(H):
                i = idx(j, h)
                par = i % 2
                jpar = j % 2
                q.wait_ge(sA, Bv["_sA"] + i + 1)
                if i >= 2:
                    q.wait_ge(sO, Bv["_sO"] + i - 1)
                q.tensor_tensor(Am[par][:], pA[par][:], maskf[:], ALU.mult).then_inc(sM, 1)
                q.wait_ge(sSt, Bv["_sSt"] + i + 1)
                if j > 0:
                    q.wait_ge(sSb, Bv["_sSb"] + idx(j - 1, h) + 1)
                c2, rb2 = h // 2, 64 * (h % 2)
                q.tensor_tensor(Sf[jpar][rb2:rb2 + 64, h * 65:(h + 1) * 65],
                                pS[par][rb2:rb2 + 64, :],
                                Sf[1 - jpar][rb2:rb2 + 64, h * 65:(h + 1) * 65],
                                ALU.add).then_inc(sSv, 1)
                q.wait_ge(sO, Bv["_sO"] + i + 1)
                q.reciprocal(rcol[:], pO[par][:, 64:65]).then_inc(sDn, 1)
                q.wait_ge(sDn, Bv["_sDn"] + 2 * i + 1)
                if i >= 2:
                    q.wait_ge(sTp, Bv["_sTp"] + i - 1)
                q.tensor_scalar(xotok[par][:], pO[par][:, 0:64], rcol[:], None,
                                ALU.mult).then_inc(sDn, 1)
    blk.vector(vec)

    def scal(q):
        for j in range(16):
            for h in range(H):
                i = idx(j, h)
                par = i % 2
                jpar = j % 2
                q.wait_ge(sSv, Bv["_sSv"] + i + 1)
                c2, rb2 = h // 2, 64 * (h % 2)
                q.copy(Sb[rb2:rb2 + 64, h * 65:(h + 1) * 65],
                       Sf[jpar][rb2:rb2 + 64, h * 65:(h + 1) * 65]).then_inc(sSb, 1)
                q.wait_ge(sTp, Bv["_sTp"] + i + 1)
                c, rb = h // 2, 64 * (h % 2)
                q.copy(xoT[rb:rb + 64, c * T + 128 * j:c * T + 128 * (j + 1)],
                       ptt[par][:]).then_inc(sEv, 1)
    blk.scalar(scal)

    def sy1(q):
        for cc in range(NC):
            q.wait_ge(sEv, Bv["_sEv"] + idx(15, 2 * cc + 1) + 1)
            q.dma_start(out=xoT_hbm[128 * cc:128 * (cc + 1), :],
                        in_=xoT[:, cc * T:(cc + 1) * T]).then_inc(st, 16)
        q.wait_ge(st, Bv["_st"] + 16 * NC)
    if xoT_res is None:
        blk.sync(sy1)
    pctx.close()
    ctx.close()



def emit_l1_ffn_in(p, xnT_hbm, w1T_hbm, w2T_hbm, s1c_hbm, s2c_hbm, h_hbm, tag="e1",
                   xnT_res=None):
    nc, blk = p.nc, p.block
    ctx = ExitStack()
    sb = lambda n, shp, dt: ctx.enter_context(nc.sbuf_tensor(tag + n, list(shp), dt))
    pctx = ExitStack()
    psb = lambda n, shp, dt: pctx.enter_context(nc.psum_tensor(tag + n, list(shp), dt))
    HT = T // 2
    NH = DHID // 128

    xnT = xnT_res if xnT_res is not None else sb("_xnT", (128, NC * T), BF16)
    w1b = sb("_w1", (128, NC * DHID), BF16)
    w2b = sb("_w2", (128, NC * DHID), BF16)
    s1c = sb("_s1c", (128, NH), F32)
    s2c = sb("_s2c", (128, NH), F32)
    tsl = [sb(f"_tsl{i}", (128, HT), F32) for i in range(2)]
    tg2 = [sb(f"_tg2{i}", (128, HT), F32) for i in range(2)]
    hb = [sb(f"_hb{i}", (128, T), BF16) for i in range(2)]
    pb1 = [psb(f"_pb1{i}", (128, HT), F32) for i in range(2)]
    pb2 = [psb(f"_pb2{i}", (128, HT), F32) for i in range(2)]

    ld = p.sem(tag + "_ld")
    mm = p.sem(tag + "_mm")
    ac = p.sem(tag + "_ac")
    dv = p.sem(tag + "_dv")
    stf = [p.sem(tag + f"_stf{i}") for i in range(2)]
    b_ld = p.val(tag + "_ld")
    b_mm = p.val(tag + "_mm")
    b_ac = p.val(tag + "_ac")
    b_dv = p.val(tag + "_dv")
    b_stf = [p.val(tag + f"_stf{i}") for i in range(2)]
    p.add(tag + "_ld", 32)
    NI = NH * 2
    for i in range(NI):
        p.add(tag + "_mm", 2)
        p.add(tag + "_ac", 1)
        p.add(tag + "_dv", 2)
    for m in range(NH):
        p.add(tag + f"_stf{m % 2}", 16)

    gsems = []

    def gp0(q):
        if xnT_res is None:
            gsems.append(p.swdge(q, xnT.rearrange("p (c t) -> p c t", c=NC),
                                 xnT_hbm.rearrange("(c p) t -> p c t", p=128)))
        gsems.append(p.swdge(q, w1b.rearrange("p (c o) -> p c o", c=NC),
                             w1T_hbm.rearrange("(c p) o -> p c o", p=128)))
        gsems.append(p.swdge(q, w2b.rearrange("p (c o) -> p c o", c=NC),
                             w2T_hbm.rearrange("(c p) o -> p c o", p=128)))
    blk.gpsimd(gp0)

    def sy0(q):
        q.dma_start(out=s1c[:], in_=s1c_hbm[:]).then_inc(ld, 16)
        q.dma_start(out=s2c[:], in_=s2c_hbm[:]).then_inc(ld, 16)
    blk.sync(sy0)

    def tens(q):
        q.wait_ge(ld, b_ld + 32)
        for s, tgt in gsems:
            q.wait_ge(s, tgt)
        for i in range(NI):
            m, hf = i // 2, i % 2
            par = i % 2
            if i >= 2:
                q.wait_ge(ac, b_ac + i - 1)
                q.wait_ge(dv, b_dv + 2 * (i - 2) + 1)
            for k in range(NC):
                for ns in range(2):
                    t0 = hf * HT + 512 * ns
                    ins = q.matmul(pb1[par][:, 512 * ns:512 * (ns + 1)],
                                   w1b[:, k * DHID + 128 * m:k * DHID + 128 * (m + 1)],
                                   xnT[:, k * T + t0:k * T + t0 + 512],
                                   start=(k == 0), stop=(k == NC - 1))
            ins.then_inc(mm, 1)
            for k in range(NC):
                for ns in range(2):
                    t0 = hf * HT + 512 * ns
                    ins = q.matmul(pb2[par][:, 512 * ns:512 * (ns + 1)],
                                   w2b[:, k * DHID + 128 * m:k * DHID + 128 * (m + 1)],
                                   xnT[:, k * T + t0:k * T + t0 + 512],
                                   start=(k == 0), stop=(k == NC - 1))
            ins.then_inc(mm, 1)
    blk.tensor(tens)

    def scal(q):
        for i in range(NI):
            m, hf = i // 2, i % 2
            par = i % 2
            q.wait_ge(mm, b_mm + 2 * i + 1)
            if i >= 2:
                q.wait_ge(dv, b_dv + 2 * (i - 2) + 2)
            q.activation(tsl[par][:], pb1[par][:], AF.Silu,
                         scale=s1c[:, m:m + 1]).then_inc(ac, 1)
    blk.scalar(scal)

    def vec(q):
        for i in range(NI):
            m, hf = i // 2, i % 2
            par = i % 2
            q.wait_ge(mm, b_mm + 2 * i + 2)
            if i >= 2:
                q.wait_ge(dv, b_dv + 2 * (i - 2) + 2)
            q.tensor_scalar(tg2[par][:], pb2[par][:], s2c[:, m:m + 1], None,
                            ALU.mult).then_inc(dv, 1)
            q.wait_ge(dv, b_dv + 2 * i + 1)
            q.wait_ge(ac, b_ac + i + 1)
            if i >= 4:
                mprev = (i - 4) // 2
                q.wait_ge(stf[mprev % 2], b_stf[mprev % 2] + 16 * (mprev // 2 + 1))
            q.tensor_tensor(hb[m % 2][:, HT * hf:HT * (hf + 1)], tsl[par][:],
                            tg2[par][:], ALU.mult).then_inc(dv, 1)
    blk.vector(vec)

    def sy1(q):
        for m in range(NH):
            q.wait_ge(dv, b_dv + 2 * (2 * m + 1) + 2)
            q.dma_start(out=h_hbm[128 * m:128 * (m + 1), :], in_=hb[m % 2][:]).then_inc(stf[m % 2], 16)
        for s2 in range(2):
            q.wait_ge(stf[s2], b_stf[s2] + 16 * (NH // 2))
    blk.sync(sy1)
    pctx.close()
    ctx.close()


def emit_l1_ffn_in_v2(p, xnT_hbm, w1T_hbm, w2T_hbm, s1c_hbm, s2c_hbm, hres, tag="e1"):
    nc, blk = p.nc, p.block
    ctx = ExitStack()
    sb = lambda n, shp, dt: ctx.enter_context(nc.sbuf_tensor(tag + n, list(shp), dt))
    pctx = ExitStack()
    psb = lambda n, shp, dt: pctx.enter_context(nc.psum_tensor(tag + n, list(shp), dt))
    HT = T // 2
    NH = DHID // 128

    xnT = sb("_xnT", (128, NC * T), BF16)
    s1c = sb("_s1c", (128, NH), F32)
    s2c = sb("_s2c", (128, NH), F32)
    w1s = [sb(f"_w1s{i}", (128, NC * 128), F32) for i in range(2)]
    w2s = [sb(f"_w2s{i}", (128, NC * 128), F32) for i in range(2)]
    w1c = [sb(f"_w1c{i}", (128, NC * 128), BF16) for i in range(2)]
    w2c = [sb(f"_w2c{i}", (128, NC * 128), BF16) for i in range(2)]
    tsl = [sb(f"_tsl{i}", (128, HT), F32) for i in range(2)]
    tg2 = [sb(f"_tg2{i}", (128, HT), F32) for i in range(2)]
    pb1 = [psb(f"_pb1{i}", (128, HT), F32) for i in range(2)]
    pb2 = [psb(f"_pb2{i}", (128, HT), F32) for i in range(2)]

    ld = p.sem(tag + "_ld")
    wld = [p.sem(tag + f"_wld{i}") for i in range(2)]
    w2ld = [p.sem(tag + f"_w2ld{i}") for i in range(2)]
    pc = p.sem(tag + "_pc")
    mm = p.sem(tag + "_mm")
    ac = p.sem(tag + "_ac")
    dv = p.sem(tag + "_dv")
    b_ld = p.val(tag + "_ld")
    b_wld = [p.val(tag + f"_wld{i}") for i in range(2)]
    b_w2ld = [p.val(tag + f"_w2ld{i}") for i in range(2)]
    b_pc = p.val(tag + "_pc")
    b_mm = p.val(tag + "_mm")
    b_ac = p.val(tag + "_ac")
    b_dv = p.val(tag + "_dv")
    p.add(tag + "_ld", 32)
    NI = NH * 2
    for m in range(NH):
        p.add(tag + f"_wld{m % 2}", 16)
        p.add(tag + f"_w2ld{m % 2}", 16)
        p.add(tag + "_pc", 2)
    for i in range(NI):
        p.add(tag + "_mm", 2)
        p.add(tag + "_ac", 1)
        p.add(tag + "_dv", 2)

    gsems = []

    def gpx(q):
        gsems.append(p.swdge(q, xnT.rearrange("p (c t) -> p c t", c=NC),
                             xnT_hbm.rearrange("(c p) t -> p c t", p=128)))
    blk.gpsimd(gpx)

    def sy0(q):
        q.dma_start(out=s1c[:], in_=s1c_hbm[:]).then_inc(ld, 16)
        q.dma_start(out=s2c[:], in_=s2c_hbm[:]).then_inc(ld, 16)
        for m in range(NH):
            par = m % 2
            if m >= 2:
                q.wait_ge(pc, b_pc + 2 * (m - 1))
            q.dma_start(out=w1s[par].rearrange("p (k o) -> p k o", k=NC),
                        in_=w1T_hbm.rearrange("(k p) o -> p k o", p=128)[:, :, 128 * m:128 * (m + 1)]).then_inc(wld[par], 16)
            q.dma_start(out=w2s[par].rearrange("p (k o) -> p k o", k=NC),
                        in_=w2T_hbm.rearrange("(k p) o -> p k o", p=128)[:, :, 128 * m:128 * (m + 1)]).then_inc(w2ld[par], 16)
    blk.sync(sy0)

    def gp0(q):
        for m in range(NH):
            par = m % 2
            q.wait_ge(wld[par], b_wld[par] + 16 * (m // 2 + 1))
            q.wait_ge(w2ld[par], b_w2ld[par] + 16 * (m // 2 + 1))
            if m >= 2:
                q.wait_ge(mm, b_mm + 4 * (m - 1))
            q.tensor_copy(w1c[par][:], w1s[par][:]).then_inc(pc, 1)
            q.tensor_copy(w2c[par][:], w2s[par][:]).then_inc(pc, 1)
    blk.gpsimd(gp0)

    def tens(q):
        q.wait_ge(ld, b_ld + 32)
        for s, tgt in gsems:
            q.wait_ge(s, tgt)
        for i in range(NI):
            m, hf = i // 2, i % 2
            par = i % 2
            wpar = m % 2
            if hf == 0:
                q.wait_ge(pc, b_pc + 2 * m + 2)
            if i >= 2:
                q.wait_ge(ac, b_ac + i - 1)
                q.wait_ge(dv, b_dv + 2 * (i - 2) + 1)
            for k in range(NC):
                for ns in range(2):
                    t0 = hf * HT + 512 * ns
                    ins = q.matmul(pb1[par][:, 512 * ns:512 * (ns + 1)],
                                   w1c[wpar][:, 128 * k:128 * (k + 1)],
                                   xnT[:, k * T + t0:k * T + t0 + 512],
                                   start=(k == 0), stop=(k == NC - 1))
            ins.then_inc(mm, 1)
            for k in range(NC):
                for ns in range(2):
                    t0 = hf * HT + 512 * ns
                    ins = q.matmul(pb2[par][:, 512 * ns:512 * (ns + 1)],
                                   w2c[wpar][:, 128 * k:128 * (k + 1)],
                                   xnT[:, k * T + t0:k * T + t0 + 512],
                                   start=(k == 0), stop=(k == NC - 1))
            ins.then_inc(mm, 1)
    blk.tensor(tens)

    def scal(q):
        for i in range(NI):
            m, hf = i // 2, i % 2
            par = i % 2
            q.wait_ge(mm, b_mm + 2 * i + 1)
            if i >= 2:
                q.wait_ge(dv, b_dv + 2 * (i - 2) + 2)
            q.activation(tsl[par][:], pb1[par][:], AF.Silu,
                         scale=s1c[:, m:m + 1]).then_inc(ac, 1)
    blk.scalar(scal)

    def vec(q):
        for i in range(NI):
            m, hf = i // 2, i % 2
            par = i % 2
            q.wait_ge(mm, b_mm + 2 * i + 2)
            if i >= 2:
                q.wait_ge(dv, b_dv + 2 * (i - 2) + 2)
            q.tensor_scalar(tg2[par][:], pb2[par][:], s2c[:, m:m + 1], None,
                            ALU.mult).then_inc(dv, 1)
            q.wait_ge(dv, b_dv + 2 * i + 1)
            q.wait_ge(ac, b_ac + i + 1)
            q.tensor_tensor(hres[:, m * T + HT * hf:m * T + HT * (hf + 1)],
                            tsl[par][:], tg2[par][:], ALU.mult).then_inc(dv, 1)
    blk.vector(vec)
    pctx.close()
    ctx.close()



def emit_l1_ffn_mid(p, in_hbm, wT_hbm, sc_hbm, out_hbm, tag="e2", hin_res=None):
    nc, blk = p.nc, p.block
    ctx = ExitStack()
    sb = lambda n, shp, dt: ctx.enter_context(nc.sbuf_tensor(tag + n, list(shp), dt))
    pctx = ExitStack()
    psb = lambda n, shp, dt: pctx.enter_context(nc.psum_tensor(tag + n, list(shp), dt))
    NH = DHID // 128

    hin = hin_res if hin_res is not None else sb("_hin", (128, NH * T), BF16)
    scb = sb("_sc", (128, NH), F32)
    wstg = [sb(f"_ws{i}", (128, NH * 128), F32) for i in range(2)]
    wcol = [sb(f"_wc{i}", (128, NH * 128), BF16) for i in range(2)]
    ob = [sb(f"_ob{i}", (128, T), BF16) for i in range(2)]
    pp = [psb(f"_pp{i}", (128, T), F32) for i in range(2)]

    ld = p.sem(tag + "_ld")
    wld = [p.sem(tag + f"_wld{i}") for i in range(2)]
    pc = p.sem(tag + "_pc")
    mm = p.sem(tag + "_mm")
    ac = p.sem(tag + "_ac")
    stf = [p.sem(tag + f"_stf{i}") for i in range(2)]
    b_ld = p.val(tag + "_ld")
    b_wld = [p.val(tag + f"_wld{i}") for i in range(2)]
    b_pc = p.val(tag + "_pc")
    b_mm = p.val(tag + "_mm")
    b_ac = p.val(tag + "_ac")
    b_stf = [p.val(tag + f"_stf{i}") for i in range(2)]
    p.add(tag + "_ld", 16)
    for m in range(NH):
        p.add(tag + f"_wld{m % 2}", 16)
        p.add(tag + "_pc", 1)
        p.add(tag + "_mm", 1)
        p.add(tag + "_ac", 1)
        p.add(tag + f"_stf{m % 2}", 16)

    hsem = []

    def gp0(q):
        if hin_res is None:
            hsem.append(p.swdge(q, hin.rearrange("p (c t) -> p c t", c=NH),
                                in_hbm.rearrange("(c p) t -> p c t", p=128)))
        for m in range(NH):
            par = m % 2
            q.wait_ge(wld[par], b_wld[par] + 16 * (m // 2 + 1))
            if m >= 2:
                q.wait_ge(mm, b_mm + m - 1)
            q.tensor_copy(wcol[par][:], wstg[par][:]).then_inc(pc, 1)
    blk.gpsimd(gp0)

    def sy0(q):
        q.dma_start(out=scb[:], in_=sc_hbm[:]).then_inc(ld, 16)
        for m in range(NH):
            par = m % 2
            if m >= 2:
                q.wait_ge(pc, b_pc + m - 1)
            q.dma_start(out=wstg[par].rearrange("p (k o) -> p k o", k=NH),
                        in_=wT_hbm.rearrange("(k p) o -> p k o", p=128)[:, :, 128 * m:128 * (m + 1)]).then_inc(wld[par], 16)
    blk.sync(sy0)

    def tens(q):
        q.wait_ge(ld, b_ld + 16)
        if hin_res is None:
            q.wait_ge(hsem[0][0], hsem[0][1])
        for m in range(NH):
            par = m % 2
            q.wait_ge(pc, b_pc + m + 1)
            if m >= 2:
                q.wait_ge(ac, b_ac + m - 1)
            for k in range(NH):
                for ns in range(4):
                    ins = q.matmul(pp[par][:, 512 * ns:512 * (ns + 1)],
                                   wcol[par][:, 128 * k:128 * (k + 1)],
                                   hin[:, k * T + 512 * ns:k * T + 512 * (ns + 1)],
                                   start=(k == 0), stop=(k == NH - 1))
            ins.then_inc(mm, 1)
    blk.tensor(tens)

    def scal(q):
        for m in range(NH):
            par = m % 2
            q.wait_ge(mm, b_mm + m + 1)
            if m >= 2:
                q.wait_ge(stf[par], b_stf[par] + 16 * ((m - 2) // 2 + 1))
            q.activation(ob[par][:], pp[par][:], AF.Gelu,
                         scale=scb[:, m:m + 1]).then_inc(ac, 1)
            q.wait_ge(ac, b_ac + m + 1)
            q.dma_start(out=out_hbm[128 * m:128 * (m + 1), :], in_=ob[par][:]).then_inc(stf[par], 16)
        for s2 in range(2):
            q.wait_ge(stf[s2], b_stf[s2] + 16 * (NH // 2))
    blk.scalar(scal)
    pctx.close()
    ctx.close()


def emit_l1_ffn_out(p, h2_hbm, w3T_hbm, s3r_hbm, x1_hbm, s1_hbm, ones_row_sb,
                    out_hbm, tag="e4"):
    nc, blk = p.nc, p.block
    ctx = ExitStack()
    sb = lambda n, shp, dt: ctx.enter_context(nc.sbuf_tensor(tag + n, list(shp), dt))
    pctx = ExitStack()
    psb = lambda n, shp, dt: pctx.enter_context(nc.psum_tensor(tag + n, list(shp), dt))
    NH = DHID // 128

    HT2 = T // 2
    hin = sb("_hin", (128, NH * HT2), BF16)
    w3b = sb("_w3", (128, NH * D), BF16)
    s3r = sb("_s3r", (1, D), F32)
    s3bc = sb("_s3bc", (128, D), F32)
    zsc = [sb(f"_zsc{i}", (128, D), F32) for i in range(2)]
    gsl = [sb(f"_gsl{i}", (128, D), F32) for i in range(2)]
    x1t = [sb(f"_x1t{i}", (128, D), F32) for i in range(2)]
    s1t = [sb(f"_s1t{i}", (128, D), F32) for i in range(2)]
    tt1 = sb("_tt1", (128, D), F32)
    x2sb = [sb(f"_x2sb{i}", (128, D), F32) for i in range(2)]
    pp = [psb(f"_pp{i}", (128, D), F32) for i in range(2)]
    pB = psb("_pB", (128, D), F32)

    ld = p.sem(tag + "_ld")
    lda = [p.sem(tag + f"_lda{i}") for i in range(2)]
    mm = p.sem(tag + "_mm")
    ac = p.sem(tag + "_ac")
    dv = p.sem(tag + "_dv")
    ev = p.sem(tag + "_ev")
    stf = [p.sem(tag + f"_stf{i}") for i in range(2)]
    b_ld = p.val(tag + "_ld")
    b_lda = [p.val(tag + f"_lda{i}") for i in range(2)]
    b_mm = p.val(tag + "_mm")
    b_ac = p.val(tag + "_ac")
    b_dv = p.val(tag + "_dv")
    b_ev = p.val(tag + "_ev")
    b_stf = [p.val(tag + f"_stf{i}") for i in range(2)]
    p.add(tag + "_ld", 16)
    p.add(tag + "_mm", 1 + NT)
    p.add(tag + "_ev", 1)
    p.add(tag + "_ac", NT)
    p.add(tag + "_dv", 3 * NT)
    for t in range(NT):
        p.add(tag + f"_lda{t % 2}", 32)
        p.add(tag + f"_stf{t % 2}", 16)

    gsems = []
    hsems = []

    def gp0(q):
        wv = w3b.rearrange("p (c o) -> p c o", c=NH)
        wh = w3T_hbm.rearrange("(c p) o -> p c o", p=128)
        hv = hin.rearrange("p (c t) -> p c t", c=NH)
        hh = h2_hbm.rearrange("(c p) t -> p c t", p=128)
        for s in range(2):
            gsems.append(p.swdge(q, wv[:, 8 * s:8 * (s + 1)], wh[:, 8 * s:8 * (s + 1)]))
            hsems.append(p.swdge(q, hv[:, :, 512 * s:512 * (s + 1)],
                                 hh[:, :, 512 * s:512 * (s + 1)]))
        for s in range(2, 4):
            gsems.append(p.swdge(q, wv[:, 8 * s:8 * (s + 1)], wh[:, 8 * s:8 * (s + 1)]))
        q.wait_ge(mm, b_mm + 1 + NT // 2)
        for s in range(2):
            hsems.append(p.swdge(q, hv[:, :, 512 * s:512 * (s + 1)],
                                 hh[:, :, HT2 + 512 * s:HT2 + 512 * (s + 1)]))
    blk.gpsimd(gp0)

    def sy0(q):
        q.dma_start(out=s3r[:], in_=s3r_hbm[:]).then_inc(ld, 16)
        for t in range(NT + 2):
            if t < NT:
                if t >= 2:
                    q.wait_ge(dv, b_dv + 3 * (t - 2) + 3)
                q.dma_start(out=x1t[t % 2][:], in_=x1_hbm[128 * t:128 * (t + 1), :]).then_inc(lda[t % 2], 16)
                q.dma_start(out=s1t[t % 2][:], in_=s1_hbm[128 * t:128 * (t + 1), :]).then_inc(lda[t % 2], 16)
            if t >= 2:
                ts2 = t - 2
                q.wait_ge(dv, b_dv + 3 * ts2 + 3)
                q.dma_start(out=out_hbm[128 * ts2:128 * (ts2 + 1), :],
                            in_=x2sb[ts2 % 2][:]).then_inc(stf[ts2 % 2], 16)
        for s2 in range(2):
            q.wait_ge(stf[s2], b_stf[s2] + 16 * (NT // 2))
    blk.sync(sy0)

    def tens(q):
        q.wait_ge(ld, b_ld + 16)
        for ns in range(2):
            ins = q.matmul(pB[:, 512 * ns:512 * (ns + 1)], ones_row_sb[:],
                           s3r[0:1, 512 * ns:512 * (ns + 1)], start=True, stop=True)
        ins.then_inc(mm, 1)
        for t in range(NT):
            par = t % 2
            hf2, tl = t // (NT // 2), t % (NT // 2)
            if tl % 4 == 0:
                s2, tgt = hsems[2 * hf2 + tl // 4]
                q.wait_ge(s2, tgt)
            if t >= 2:
                q.wait_ge(dv, b_dv + 3 * (t - 2) + 1)
            for k in range(NH):
                if t == 0 and k % 8 == 0:
                    s2, tgt = gsems[k // 8]
                    q.wait_ge(s2, tgt)
                for ns in range(2):
                    ins = q.matmul(pp[par][:, 512 * ns:512 * (ns + 1)],
                                   hin[:, k * HT2 + 128 * tl:k * HT2 + 128 * (tl + 1)],
                                   w3b[:, k * D + 512 * ns:k * D + 512 * (ns + 1)],
                                   start=(k == 0), stop=(k == NH - 1))
            ins.then_inc(mm, 1)
    blk.tensor(tens)

    def scal(q):
        q.wait_ge(mm, b_mm + 1)
        q.copy(s3bc[:], pB[:]).then_inc(ev, 1)
        for t in range(NT):
            par = t % 2
            q.wait_ge(dv, b_dv + 3 * t + 1)
            if t >= 2:
                q.wait_ge(dv, b_dv + 3 * (t - 2) + 2)
            q.activation(gsl[par][:], zsc[par][:], AF.Silu).then_inc(ac, 1)
    blk.scalar(scal)

    def vec(q):
        q.wait_ge(ev, b_ev + 1)
        for t in range(NT):
            par = t % 2
            q.wait_ge(mm, b_mm + 1 + t + 1)
            if t >= 2:
                q.wait_ge(ac, b_ac + t - 1)
            q.tensor_tensor(zsc[par][:], pp[par][:], s3bc[:], ALU.mult).then_inc(dv, 1)
            q.wait_ge(ac, b_ac + t + 1)
            q.wait_ge(lda[par], b_lda[par] + 32 * (t // 2 + 1))
            q.tensor_tensor(tt1[:], gsl[par][:], s1t[par][:], ALU.mult).then_inc(dv, 1)
            q.wait_ge(dv, b_dv + 3 * t + 2)
            if t >= 2:
                q.wait_ge(stf[par], b_stf[par] + 16 * ((t - 2) // 2 + 1))
            q.tensor_tensor(x2sb[par][:], tt1[:], x1t[par][:], ALU.add).then_inc(dv, 1)
    blk.vector(vec)

    pctx.close()
    ctx.close()


def build_l1_qkv_program():
    nc = bass.Bass("TRN2", target_bir_lowering=False)
    xnT = dram_io(nc, "xnT", (D, T), BF16, "ExternalInput")
    wqkvT = dram_io(nc, "wqkvT", (D, 3 * D), F32, "ExternalInput")
    identf = dram_io(nc, "ident", (128, 128), F32, "ExternalInput")
    qT = dram_io(nc, "qT", (D, T), BF16, "ExternalOutput")
    kT = dram_io(nc, "kT", (D, T), BF16, "ExternalOutput")
    ktok = dram_io(nc, "ktok", (H * 16 * 128, 64), BF16, "ExternalOutput")
    vtok = dram_io(nc, "vtok", (H * 16 * 128, 64), BF16, "ExternalOutput")
    with ExitStack() as ctx:
        p = Prog(nc, ctx)
        cs = _const_loader(p, ctx, {"ident": (identf, (128, 128), "bf16")})
        emit_l1_qkv(p, xnT, wqkvT, cs["ident"], qT, kT, ktok, vtok)
    return nc


def build_l1_scan_program():
    nc = bass.Bass("TRN2", target_bir_lowering=False)
    qT = dram_io(nc, "qT", (D, T), BF16, "ExternalInput")
    kT = dram_io(nc, "kT", (D, T), BF16, "ExternalInput")
    ktok = dram_io(nc, "ktok", (H * 16 * 128, 64), BF16, "ExternalInput")
    vtok = dram_io(nc, "vtok", (H * 16 * 128, 64), BF16, "ExternalInput")
    maskm = dram_io(nc, "maskm", (128, 128), F32, "ExternalInput")
    identf = dram_io(nc, "ident", (128, 128), F32, "ExternalInput")
    xoT = dram_io(nc, "xoT", (D, T), BF16, "ExternalOutput")
    with ExitStack() as ctx:
        p = Prog(nc, ctx)
        cs = _const_loader(p, ctx, {"ident": (identf, (128, 128), "bf16")})
        emit_l1_scan(p, qT, kT, ktok, vtok, maskm, cs["ident"], xoT)
    return nc


def build_l1_ffn_in_program():
    nc = bass.Bass("TRN2", target_bir_lowering=False)
    xnT = dram_io(nc, "xnT", (D, T), BF16, "ExternalInput")
    w1T = dram_io(nc, "w1T", (D, DHID), F32, "ExternalInput")
    w2T = dram_io(nc, "w2T", (D, DHID), F32, "ExternalInput")
    s1c = dram_io(nc, "s1c", (128, DHID // 128), F32, "ExternalInput")
    s2c = dram_io(nc, "s2c", (128, DHID // 128), F32, "ExternalInput")
    h = dram_io(nc, "h", (DHID, T), BF16, "ExternalOutput")
    with ExitStack() as ctx:
        p = Prog(nc, ctx)
        emit_l1_ffn_in(p, xnT, w1T, w2T, s1c, s2c, h)
    return nc


def build_l1_ffn_mid_program():
    nc = bass.Bass("TRN2", target_bir_lowering=False)
    hin = dram_io(nc, "hin", (DHID, T), BF16, "ExternalInput")
    wT = dram_io(nc, "wT", (DHID, DHID), F32, "ExternalInput")
    sc = dram_io(nc, "sc", (128, DHID // 128), F32, "ExternalInput")
    hout = dram_io(nc, "hout", (DHID, T), BF16, "ExternalOutput")
    with ExitStack() as ctx:
        p = Prog(nc, ctx)
        emit_l1_ffn_mid(p, hin, wT, sc, hout)
    return nc


def build_l1_ffn_out_program():
    nc = bass.Bass("TRN2", target_bir_lowering=False)
    h2 = dram_io(nc, "h2", (DHID, T), BF16, "ExternalInput")
    w3T = dram_io(nc, "w3T", (DHID, D), F32, "ExternalInput")
    s3r = dram_io(nc, "s3r", (1, D), F32, "ExternalInput")
    x1 = dram_io(nc, "x1", (T, D), F32, "ExternalInput")
    s1 = dram_io(nc, "s1", (T, D), F32, "ExternalInput")
    onesr = dram_io(nc, "ones_row", (1, 128), F32, "ExternalInput")
    out = dram_io(nc, "out", (T, D), F32, "ExternalOutput")
    with ExitStack() as ctx:
        p = Prog(nc, ctx)
        cs = _const_loader(p, ctx, {"ones_row": (onesr, (1, 128), "f32")})
        emit_l1_ffn_out(p, h2, w3T, s3r, x1, s1, cs["ones_row"], out)
    return nc




def build_fused_program():
    nc = bass.Bass("TRN2", target_bir_lowering=False)
    EI = lambda n, shp, dt=F32: dram_io(nc, n, shp, dt, "ExternalInput")
    IN = lambda n, shp, dt=F32: dram_io(nc, n, shp, dt, "Internal")

    x = EI("x", (T, D))
    identf = EI("ident", (128, 128))
    scon = EI("sconsts", (128, len(CONST_NAMES)))
    onesr = EI("ones_row", (1, 128))
    wq = EI("wq", (D, D))
    wkv = EI("wkv", (D, D))
    wout = EI("wout", (D, D))
    bout_row = EI("bout_row", (1, D))
    cos2 = EI("cos2", (128, T))
    sin2 = EI("sin2", (128, T))
    rot128 = EI("rot128", (128, 128))
    prm_shapes = {
        "sigraw": (64, D), "knots": (1, 8), "temp": (1, 1), "oscale": (1, 1),
        "cw1": (D, 256), "cb1_cols": (128, 2), "cw2": (256, 2), "cb2_col": (2, 1),
        "w1g": (16, 8 * 128), "b1g": (128, 8), "w2g": (128, 8 * 64), "b2_col": (64, 1),
        "dirs": (64, D), "gam_row": (1, D), "bet_row": (1, D),
        "comp_rep": (2, 16), "iota64": (128, 64), "iota64m": (128, 64),
        "expand8": (8, 64), "gmean": (64, 8),
    }
    prm = {k: EI("p_" + k, shp) for k, shp in prm_shapes.items()}
    wqkvT = EI("wqkvT", (D, 3 * D))
    woutT1 = EI("woutT1", (D, D))
    maskm = EI("maskm", (128, 128))
    w1T = EI("w1T", (D, DHID))
    w2T = EI("w2T", (D, DHID))
    s1c = EI("s1c", (128, DHID // 128))
    s2c = EI("s2c", (128, DHID // 128))
    h1T = EI("h1T", (DHID, DHID))
    h1sc = EI("h1sc", (128, DHID // 128))
    h2T = EI("h2T", (DHID, DHID))
    h2sc = EI("h2sc", (128, DHID // 128))
    w3T = EI("w3T", (DHID, D))
    s3r = EI("s3r", (1, D))
    out = dram_io(nc, "out", (T, D), F32, "ExternalOutput")

    xn_tok = IN("i_xn_tok", (T, D))
    xnT = IN("i_xnT", (D, T), BF16)
    qT0 = IN("i_qT0", (D, T), BF16)
    kvT0 = IN("i_kvT0", (D, T), BF16)
    kvtok = IN("i_kvtok", (H * 16 * 128, 65), BF16)
    xoT0 = IN("i_xoT0", (D, T), BF16)
    s0 = IN("i_s0", (T, D))
    yscr = IN("i_yscr", (T, D))
    x1 = IN("i_x1", (T, D))
    xn1T = IN("i_xn1T", (D, T), BF16)
    q1T = IN("i_q1T", (D, T), BF16)
    k1T = IN("i_k1T", (D, T), BF16)
    ktok1 = IN("i_ktok1", (H * 16 * 128, 64), BF16)
    vtok1 = IN("i_vtok1", (H * 16 * 128, 64), BF16)
    xo1T = IN("i_xo1T", (D, T), BF16)
    s1 = IN("i_s1", (T, D))
    hmat = IN("i_h", (DHID, T), BF16)
    h1m = IN("i_h1", (DHID, T), BF16)
    h2m = IN("i_h2", (DHID, T), BF16)

    with ExitStack() as ctx:
        p = Prog(nc, ctx)
        cs = _const_loader(p, ctx, {"ident": (identf, (128, 128), "bf16"),
                                    "sconsts": (scon, (128, len(CONST_NAMES)), "f32"),
                                    "ones_row": (onesr, (1, 128), "f32")})
        cpool, ones_sb, idb, idf = cs["sconsts"], cs["ones_row"], cs["ident"], cs["ident_f32"]
        lnx = ExitStack()
        ln1xt = [lnx.enter_context(nc.sbuf_tensor(f"sh_lnxt{i}", [128, D], F32))
                 for i in range(2)]
        lctx = ExitStack()
        xnTs0 = lctx.enter_context(nc.sbuf_tensor("sh_xnT0", [128, NC * T], BF16))
        s01 = ExitStack()
        qTs = s01.enter_context(nc.sbuf_tensor("sh_qT0", [128, NC * T], BF16))
        kvTs = s01.enter_context(nc.sbuf_tensor("sh_kvT0", [128, NC * T], BF16))
        ktokS = s01.enter_context(nc.sbuf_tensor("sh_ktok0", [128, H * 16 * 65], BF16))
        bctx = ExitStack()
        bsb = lambda n, shp, dt: bctx.enter_context(nc.sbuf_tensor("pf" + n, list(shp), dt))
        b1b = {"wqb": bsb("_wq", (128, NC * D), BF16),
               "wkvb": bsb("_wkv", (128, NC * D), BF16),
               "cosb": bsb("_cos", (128, T), BF16),
               "sinf": bsb("_sin", (128, T), F32),
               "rotb": bsb("_rot", (128, 128), BF16)}
        b1pre = []

        def preb1(q):
            b1pre.append(p.swdge(q, b1b["wqb"].rearrange("p (c o) -> p c o", c=NC),
                                 wq.rearrange("(c p) o -> p c o", p=128)))
            b1pre.append(p.swdge(q, b1b["wkvb"].rearrange("p (c o) -> p c o", c=NC),
                                 wkv.rearrange("(c p) o -> p c o", p=128)))
            b1pre.append(p.swdge(q, b1b["cosb"][:], cos2[:]))
            b1pre.append(p.swdge(q, b1b["sinf"][:], sin2[:]))
            b1pre.append(p.swdge(q, b1b["rotb"][:], rot128[:]))
        p.block.gpsimd(preb1)
        emit_ln(p, x, xn_tok, xnT, idb, cpool, "ln0", xnT_res=xnTs0)
        p.barrier_sync()
        emit_l0_qkv(p, xnT, wq, wkv, cos2, sin2, rot128, idb, qT0, kvT0, kvtok, tag="b1",
                    xnT_res=xnTs0, res_out=(qTs, kvTs, ktokS), pre=(b1pre, b1b))
        bctx.close()
        p.barrier_sync()
        emit_l0_attn(p, qT0, kvT0, kvtok, cpool, ones_sb, xoT0, tag="b2",
                     res_in=(qTs, kvTs, ktokS))
        s01.close()
        p.barrier_sync()
        emit_out_proj(p, xoT0, wout, bout_row, ones_sb, s0, tag="b3", with_bias=True)
        p.barrier_sync()
        emit_l0_ffn(p, xnT, xn_tok, x, s0, prm, cpool, ones_sb, idf, idb, yscr, x1, tag="c",
                    xnT_res=xnTs0)
        lctx.close()
        xg_sems = [p.sem("c_stxa0"), p.sem("c_stxa1")]
        xg_base = [p.val("c_stxa0") - 16 * (NT // 2), p.val("c_stxa1") - 16 * (NT // 2)]
        l2 = ExitStack()
        xn1Ts = l2.enter_context(nc.sbuf_tensor("sh_xnT1", [128, NC * T], BF16))
        dsh = ExitStack()
        ktokS1 = dsh.enter_context(nc.sbuf_tensor("sh_ktok1", [128, H * 16 * 64], BF16))
        vtokS1 = dsh.enter_context(nc.sbuf_tensor("sh_vtok1", [128, H * 16 * 65], BF16))
        dctx = ExitStack()
        wbs1 = dctx.enter_context(nc.sbuf_tensor("pf_wqkv", [128, NC * 3 * D], BF16))
        d1pre = []
        c_fin = [(p.sem("c_sty0"), p.val("c_sty0")), (p.sem("c_sty1"), p.val("c_sty1")),
                 (p.sem("c_stxa0"), p.val("c_stxa0")), (p.sem("c_stxa1"), p.val("c_stxa1"))]

        def pred1(q):
            for s2, v2 in c_fin:
                q.wait_ge(s2, v2)
            d1pre.append(p.swdge(q, wbs1.rearrange("p (c o) -> p c o", c=NC),
                                 wqkvT.rearrange("(c p) o -> p c o", p=128)))
        p.block.gpsimd(pred1)
        emit_ln(p, x1, None, xn1T, idb, cpool, "ln1", write_xn_tok=False, xnT_res=xn1Ts,
                store_res=True, xt_res=ln1xt, xgate=(xg_sems, xg_base), defer_store=True)
        p.barrier_sync()
        p.add("ln1_stT", 16 * NC)
        emit_l1_qkv(p, xn1T, wqkvT, idb, q1T, k1T, ktok1, vtok1, tag="d1",
                    xnT_res=xn1Ts, wpre=(d1pre, wbs1), res_out=(None, ktokS1, vtokS1))
        dctx.close()
        p.barrier_sync()
        emit_l1_scan(p, q1T, k1T, ktok1, vtok1, maskm, idb, xo1T, tag="d2",
                     res_in=(None, ktokS1, vtokS1))
        dsh.close()
        p.barrier_sync()
        emit_out_proj(p, xo1T, woutT1, bout_row, ones_sb, s1, tag="d3", with_bias=False)
        p.barrier_sync()
        l2.close()
        lnx.close()
        ectx = ExitStack()
        hres = ectx.enter_context(nc.sbuf_tensor("sh_h", [128, (DHID // 128) * T], BF16))
        emit_l1_ffn_in_v2(p, xn1T, w1T, w2T, s1c, s2c, hres, tag="e1")
        p.barrier_sync()
        emit_l1_ffn_mid(p, hmat, h1T, h1sc, h1m, tag="e2", hin_res=hres)
        ectx.close()
        p.barrier_sync()
        emit_l1_ffn_mid(p, h1m, h2T, h2sc, h2m, tag="e3")
        p.barrier_sync()
        emit_l1_ffn_out(p, h2m, w3T, s3r, x1, s1, ones_sb, out, tag="e4")
        nsems = len(p.hsems) + len([k for k in p.hsems])
    return nc


def kernel_fused(**inputs):
    sh, xs = host_prep(inputs)
    nc = _prog("fused", build_fused_program)
    ims = []
    for c in range(B):
        im = {"x": xs[c], "ident": sh["ident"], "sconsts": sh["sconsts"],
              "ones_row": sh["ones_row"], "wq": sh["wq"], "wkv": sh["wkv"],
              "wout": sh["wout"], "bout_row": sh["bout_row"], "cos2": sh["cos2"],
              "sin2": sh["sin2"], "rot128": sh["rot128"], "wqkvT": sh["wqkvT"],
              "woutT1": sh["woutT1"], "maskm": sh["mask_incl"], "w1T": sh["w1T"],
              "w2T": sh["w2T"], "s1c": sh["w1s"], "s2c": sh["w2s"],
              "h1T": sh["h1T"], "h1sc": sh["h1s"], "h2T": sh["h2T"],
              "h2sc": sh["h2s"], "w3T": sh["w3T"], "s3r": sh["w3s_row"]}
        for k2 in ("sigraw", "knots", "temp", "oscale", "cw1", "cb1_cols", "cw2",
                   "cb2_col", "w1g", "b1g", "w2g", "b2_col", "dirs", "gam_row",
                   "bet_row", "comp_rep", "iota64", "iota64m", "expand8", "gmean"):
            im["p_" + k2] = sh[k2]
        ims.append(im)
    res = run_bass_kernel_spmd(nc, ims, core_ids=list(range(B)))
    return np.stack([np.asarray(r["out"]) for r in res.results], axis=0)


_PROG_CACHE = {}


def _prog(name, builder, *args, **kw):
    key = (name,) + tuple(args) + tuple(sorted(kw.items()))
    if key not in _PROG_CACHE:
        _PROG_CACHE[key] = builder(*args, **kw)
    return _PROG_CACHE[key]


def _launch(nc, in_maps, outs):
    res = run_bass_kernel_spmd(nc, in_maps, core_ids=list(range(B)))
    return [{k: np.asarray(r[k]) for k in outs} for r in res.results]


def kernel(**inputs):
    return kernel_fused(**inputs)


def kernel_multilaunch(**inputs):
    sh, xs = host_prep(inputs)
    shared_con = {"ident": sh["ident"], "sconsts": sh["sconsts"], "ones_row": sh["ones_row"]}

    nc = _prog("ln0", build_ln_program, True)
    r = _launch(nc, [{"x": xs[c], "ident": sh["ident"], "sconsts": sh["sconsts"]} for c in range(B)],
                ["xn_tok", "xnT"])
    xn_tok = [r[c]["xn_tok"] for c in range(B)]
    xnT = [r[c]["xnT"] for c in range(B)]

    nc = _prog("b1", build_l0_qkv_program)
    r = _launch(nc, [{"xnT": xnT[c], "wq": sh["wq"], "wkv": sh["wkv"], "cos2": sh["cos2"],
                      "sin2": sh["sin2"], "rot128": sh["rot128"], "ident": sh["ident"]}
                     for c in range(B)], ["qT", "kvT", "kvtok"])
    qT = [r[c]["qT"] for c in range(B)]
    kvT = [r[c]["kvT"] for c in range(B)]
    kvtok = [r[c]["kvtok"] for c in range(B)]

    nc = _prog("b2", build_l0_attn_program)
    r = _launch(nc, [{"qT": qT[c], "kvT": kvT[c], "kvtok": kvtok[c],
                      "sconsts": sh["sconsts"], "ones_row": sh["ones_row"]}
                     for c in range(B)], ["xoT"])
    xoT = [r[c]["xoT"] for c in range(B)]

    nc = _prog("b3", build_out_proj_program, True)
    r = _launch(nc, [{"xoT": xoT[c], "w": sh["wout"], "brow": sh["bout_row"],
                      "ones_row": sh["ones_row"]} for c in range(B)], ["s_out"])
    s0 = [r[c]["s_out"] for c in range(B)]

    nc = _prog("c", build_l0_ffn_program)
    ims = []
    for c in range(B):
        im = {"xnT": xnT[c], "xn_tok": xn_tok[c], "x": xs[c], "s0": s0[c],
              "sconsts": sh["sconsts"], "ones_row": sh["ones_row"], "ident": sh["ident"]}
        for k2 in ("sigraw", "knots", "temp", "oscale", "cw1", "cb1_cols", "cw2",
                   "cb2_col", "w1g", "b1g", "w2g", "b2_col", "dirs", "gam_row",
                   "bet_row", "comp_rep", "iota64", "iota64m", "expand8", "gmean"):
            im["p_" + k2] = sh[k2]
        ims.append(im)
    r = _launch(nc, ims, ["x1"])
    x1 = [r[c]["x1"] for c in range(B)]

    nc = _prog("ln1", build_ln_program, False)
    r = _launch(nc, [{"x": x1[c], "ident": sh["ident"], "sconsts": sh["sconsts"]}
                     for c in range(B)], ["xnT"])
    xn1T = [r[c]["xnT"] for c in range(B)]

    nc = _prog("d1", build_l1_qkv_program)
    r = _launch(nc, [{"xnT": xn1T[c], "wqkvT": sh["wqkvT"], "ident": sh["ident"]}
                     for c in range(B)], ["qT", "kT", "ktok", "vtok"])
    q1T = [r[c]["qT"] for c in range(B)]
    k1T = [r[c]["kT"] for c in range(B)]
    ktok1 = [r[c]["ktok"] for c in range(B)]
    vtok1 = [r[c]["vtok"] for c in range(B)]

    nc = _prog("d2", build_l1_scan_program)
    r = _launch(nc, [{"qT": q1T[c], "kT": k1T[c], "ktok": ktok1[c], "vtok": vtok1[c],
                      "maskm": sh["mask_incl"], "ident": sh["ident"]} for c in range(B)],
                ["xoT"])
    xo1T = [r[c]["xoT"] for c in range(B)]

    nc = _prog("d3", build_out_proj_program, False)
    r = _launch(nc, [{"xoT": xo1T[c], "w": sh["woutT1"], "brow": sh["bout_row"],
                      "ones_row": sh["ones_row"]} for c in range(B)], ["s_out"])
    s1 = [r[c]["s_out"] for c in range(B)]

    nc = _prog("e1", build_l1_ffn_in_program)
    r = _launch(nc, [{"xnT": xn1T[c], "w1T": sh["w1T"], "w2T": sh["w2T"],
                      "s1c": sh["w1s"], "s2c": sh["w2s"]} for c in range(B)], ["h"])
    hmat = [r[c]["h"] for c in range(B)]

    nc = _prog("e2", build_l1_ffn_mid_program)
    r = _launch(nc, [{"hin": hmat[c], "wT": sh["h1T"], "sc": sh["h1s"]} for c in range(B)],
                ["hout"])
    h1m = [r[c]["hout"] for c in range(B)]
    r = _launch(nc, [{"hin": h1m[c], "wT": sh["h2T"], "sc": sh["h2s"]} for c in range(B)],
                ["hout"])
    h2m = [r[c]["hout"] for c in range(B)]

    nc = _prog("e4", build_l1_ffn_out_program)
    r = _launch(nc, [{"h2": h2m[c], "w3T": sh["w3T"], "s3r": sh["w3s_row"], "x1": x1[c],
                      "s1": s1[c], "ones_row": sh["ones_row"]} for c in range(B)], ["out"])
    out = np.stack([r[c]["out"] for c in range(B)], axis=0)
    return out


# version 16 (final)
# speedup vs baseline: 1.0673x; 1.0104x over previous
import numpy as np
import ml_dtypes
from contextlib import ExitStack

import concourse.bass as bass
import concourse.mybir as mybir
from concourse.bass_utils import run_bass_kernel_spmd

import concourse.bass_utils as _BU
_orig_run_command = _BU.run_command


def _run_command_ldwopt(cmd, *a, **kw):
    if isinstance(cmd, list):
        cmd = ["--enable-ldw-opt=true" if c == "--enable-ldw-opt=false" else c for c in cmd]
    return _orig_run_command(cmd, *a, **kw)


_BU.run_command = _run_command_ldwopt

F32 = mybir.dt.float32
BF16 = mybir.dt.bfloat16
AF = mybir.ActivationFunctionType
ALU = mybir.AluOpType
AX = mybir.AxisListType.X
BF = ml_dtypes.bfloat16

CONST_VALS = {
    "eps_ln": 1e-5, "eps_l1a": 1e-6, "one": 1.0, "zero": 0.0,
    "m03": -0.3, "p03": 0.3,
    "rm1": -1.0, "rm2": -2.0, "rm3": -3.0, "rm4": -4.0, "rm5": -5.0, "rm6": -6.0,
    "bigneg": -3e8, "negc": -12.0,
}
CONST_NAMES = list(CONST_VALS)
CONST_COL = {n: i for i, n in enumerate(CONST_NAMES)}

B = 8
T = 2048
D = 1024
H = 16
DH = 64
NT = T // 128
NC = D // 128
DHID = 4096
EPS_LN0 = 1e-5
EPS_FFN = 1e-5
EPS_L1A = 1e-6



def _rope_tables():
    d = DH
    inv_freq = 1.0 / (10000.0 ** (np.arange(0, d, 2, dtype=np.float64) / d))
    tpos = np.arange(T, dtype=np.float64)
    freqs = np.outer(tpos, inv_freq)
    full = np.concatenate([freqs, freqs], axis=1)
    cos = np.cos(full).astype(np.float32).T
    sin = np.sin(full).astype(np.float32).T
    cos2 = np.concatenate([cos, cos], axis=0)
    sin2 = np.concatenate([sin, sin], axis=0)
    return np.ascontiguousarray(cos2), np.ascontiguousarray(sin2)


def _rot_matrix():
    P = np.zeros((64, 64), dtype=np.float32)
    for dp in range(32):
        P[dp + 32, dp] = -1.0
    for dp in range(32, 64):
        P[dp - 32, dp] = 1.0
    R128 = np.zeros((128, 128), dtype=np.float32)
    R128[0:64, 0:64] = P
    R128[64:128, 64:128] = P
    return R128


def host_prep(inputs):
    f32 = lambda a: np.ascontiguousarray(np.asarray(a), dtype=np.float32)
    sh = {}
    sh["wq"] = f32(inputs["L0_attn_Wq"])
    sh["wkv"] = f32(inputs["L0_attn_Wkv"])
    sh["wout"] = f32(inputs["L0_attn_Wout"])
    sh["bout_row"] = f32(np.asarray(inputs["L0_attn_bout"]).reshape(1, D))
    cos2, sin2 = _rope_tables()
    sh["cos2"] = cos2
    sh["sin2"] = sin2
    sh["rot128"] = _rot_matrix()
    sh["ident"] = np.eye(128, dtype=np.float32)
    sh["ones_row"] = np.ones((1, 128), dtype=np.float32)
    sh["sigraw"] = f32(inputs["L0_ffn_signatures_raw"])
    sh["knots"] = f32(np.asarray(inputs["L0_ffn_knot_values"]).reshape(1, 8))
    sh["temp"] = f32(np.asarray(inputs["L0_ffn_temperature"]).reshape(1, 1))
    sh["cw1"] = f32(inputs["L0_ffn_compress_w1"])
    sh["cb1_cols"] = f32(np.asarray(inputs["L0_ffn_compress_b1"]).reshape(2, 128).T)
    sh["cw2"] = f32(inputs["L0_ffn_compress_w2"])
    sh["cb2_col"] = f32(np.asarray(inputs["L0_ffn_compress_b2"]).reshape(2, 1))
    w1 = np.asarray(inputs["L0_ffn_spline_w1"], dtype=np.float32)
    W1g = np.zeros((16, 8, 128), dtype=np.float32)
    for grp in range(8):
        for r in range(8):
            W1g[2 * r:2 * r + 2, grp, 16 * r:16 * r + 16] = w1[grp * 8 + r]
    sh["w1g"] = np.ascontiguousarray(W1g.reshape(16, 8 * 128))
    sh["b1g"] = f32(np.asarray(inputs["L0_ffn_spline_b1"]).reshape(8, 128).T)
    w2 = np.asarray(inputs["L0_ffn_spline_w2"], dtype=np.float32)[:, :, 0]
    W2g = np.zeros((128, 8, 64), dtype=np.float32)
    for grp in range(8):
        for r in range(8):
            W2g[16 * r:16 * r + 16, grp, grp * 8 + r] = w2[grp * 8 + r]
    sh["w2g"] = np.ascontiguousarray(W2g.reshape(128, 8 * 64))
    sh["b2_col"] = f32(np.asarray(inputs["L0_ffn_spline_b2"]).reshape(64, 1))
    sh["dirs"] = f32(inputs["L0_ffn_directions"])
    sh["gam_row"] = f32(np.asarray(inputs["L0_ffn_norm_gamma"]).reshape(1, D))
    sh["bet_row"] = f32(np.asarray(inputs["L0_ffn_norm_beta"]).reshape(1, D))
    sh["oscale"] = f32(np.asarray(inputs["L0_ffn_output_scale"]).reshape(1, 1))
    rep = np.zeros((2, 16), dtype=np.float32)
    for r in range(8):
        rep[0, 2 * r] = 1.0
        rep[1, 2 * r + 1] = 1.0
    sh["comp_rep"] = rep
    sh["iota64"] = np.tile(np.arange(64, dtype=np.float32), (128, 1))
    sh["iota64m"] = sh["iota64"] - 64.0
    E = np.zeros((8, 64), dtype=np.float32)
    for r in range(64):
        E[r // 8, r] = 1.0
    sh["expand8"] = E
    G = np.zeros((64, 8), dtype=np.float32)
    for t in range(64):
        G[t, t // 8] = 0.125
    sh["gmean"] = G
    sh["wqkvT"] = np.ascontiguousarray(np.asarray(inputs["L1_attn_w_qkv"], dtype=np.float32).T)
    sh["woutT1"] = np.ascontiguousarray(np.asarray(inputs["L1_attn_w_out"], dtype=np.float32).T)
    sh["mask_incl"] = np.triu(np.ones((128, 128), dtype=np.float32))
    sh["mask4"] = np.ascontiguousarray(np.tile(sh["mask_incl"], (1, 4)))
    for nm, w, s in (("w1", "L1_ffn_w1_w", "L1_ffn_w1_s"), ("w2", "L1_ffn_w2_w", "L1_ffn_w2_s"),
                     ("h1", "L1_ffn_h1_w", "L1_ffn_h1_s"), ("h2", "L1_ffn_h2_w", "L1_ffn_h2_s"),
                     ("w3", "L1_ffn_w3_w", "L1_ffn_w3_s")):
        wa = np.asarray(inputs[w], dtype=np.float32)
        sh[nm + "T"] = np.ascontiguousarray(wa.T)
        sv = np.asarray(inputs[s], dtype=np.float32)
        sh[nm + "s"] = np.ascontiguousarray(sv.reshape(-1, 128).T)
        sh[nm + "s_row"] = np.ascontiguousarray(sv.reshape(1, -1))
    cvals = [CONST_VALS[n] for n in CONST_NAMES]
    sh["sconsts"] = np.tile(np.asarray(cvals, dtype=np.float32), (128, 1))
    x = np.asarray(inputs["L0_attn_x"], dtype=np.float32)
    xs = [np.ascontiguousarray(x[c]) for c in range(B)]
    return sh, xs



ENGINES = ("sync", "gpsimd", "vector", "scalar", "tensor")


class Prog:
    def __init__(self, nc, ctx):
        self.nc = nc
        self.ctx = ctx
        self.block = ctx.enter_context(nc.Block())
        self.cnt = {}
        self.hsems = {}
        self.nbar = 0

    _ALIAS = {"vone": "mz", "stT": "st", "cast": "ca",
              "stxa0": "stf0", "stxa1": "stf1",
              "sus": "ca", "suv": "dv"}

    @classmethod
    def _canon(cls, name):
        if name.startswith("sw"):
            return name
        if "_" in name:
            name = name.split("_", 1)[1]
        return cls._ALIAS.get(name, name)

    def sem(self, name):
        name = self._canon(name)
        if name not in self.hsems:
            self.hsems[name] = self.ctx.enter_context(self.nc.semaphore(name))
            self.cnt[name] = 0
        return self.hsems[name]

    def add(self, name, d=1):
        name = self._canon(name)
        self.sem(name)
        self.cnt[name] += d
        return self.cnt[name]

    def val(self, name):
        name = self._canon(name)
        self.sem(name)
        return self.cnt[name]

    def swdge(self, q, out, in_):
        n = f"sw{len([k for k in self.hsems if k.startswith('sw')])}"
        s = self.sem(n)
        q.dma_start(out=out, in_=in_).then_inc(s, 16)
        self.add(n, 16)
        return s, 16

    def barrier_sync(self):
        ph = self.sem("ph")
        self.nbar += 1
        target = self.nbar
        waits = [(self.hsems[n], self.cnt[n]) for n in list(self.hsems)
                 if n != "ph" and self.cnt[n] > 0]

        def syncb(q):
            for s, v in waits:
                q.wait_ge(s, v)
            q.nop().then_inc(ph, 1)
        self.block.sync(syncb)

        def body(q):
            q.wait_ge(ph, target)
        for eng in ENGINES:
            getattr(self.block, eng)(body)
        self.cnt["ph"] = target

    def barrier(self):
        ph = self.sem("ph")
        self.nbar += 1
        target = 5 * self.nbar

        def body(q):
            q.nop().then_inc(ph, 1)
            q.wait_ge(ph, target)
        for eng in ENGINES:
            getattr(self.block, eng)(body)
        self.cnt["ph"] = target


def dram_io(nc, name, shape, dtype, kind):
    return nc.dram_tensor(name, list(shape), dtype, kind=kind).ap()



def emit_ln(p, x_hbm, xn_tok_hbm, xnT_hbm, ident_bf, cpool, tag, write_xn_tok=True,
            xnT_res=None, store_res=False, xt_res=None, xgate=None, defer_store=False):
    nc, blk = p.nc, p.block
    ctx = ExitStack()
    sb = lambda n, shp, dt: ctx.enter_context(nc.sbuf_tensor(tag + n, list(shp), dt))
    ps = lambda n, shp, dt: ctx.enter_context(nc.psum_tensor(tag + n, list(shp), dt))
    xt = xt_res if xt_res is not None else [sb(f"_xt{i}", (128, D), F32) for i in range(2)]
    NS = len(xt)
    nsum = sb("_nsum", (128, NT), F32)
    sqc = sb("_sq", (128, NT), F32)
    sqscr = sb("_sqscr", (128, D), F32)
    nmu = sb("_nmu", (128, NT), F32)
    musq = sb("_musq", (128, NT), F32)
    varc = sb("_varc", (128, NT), F32)
    sdev = sb("_sdev", (128, NT), F32)
    rstd = sb("_rstd", (128, NT), F32)
    bcol = sb("_bcol", (128, NT), F32)
    xnf = [sb(f"_xnf{i}", (128, D), F32) for i in range(2)]
    xnb = [sb(f"_xnb{i}", (128, D), BF16) for i in range(2)]
    xnT_full = xnT_res if xnT_res is not None else sb("_xnTf", (128, NC * T), BF16)
    ptp = [ps(f"_ptp{u}", (128, 128), BF16) for u in range(8)]

    ldx = [p.sem(tag + f"_ldx{s}") for s in range(NS)]
    stf = [p.sem(tag + f"_lnf{s}") for s in range(2)]
    lnv = p.sem(tag + "_lnv")
    lns = p.sem(tag + "_lns")
    af = p.sem(tag + "_af")
    tp = p.sem(tag + "_tp")
    ev = p.sem(tag + "_ev")
    stT = p.sem(tag + "_stT")

    b_ldx = [p.val(tag + f"_ldx{s}") for s in range(NS)]
    b_stf = [p.val(tag + f"_lnf{s}") for s in range(2)]
    b_lnv = p.val(tag + "_lnv")
    b_lns = p.val(tag + "_lns")
    b_af = p.val(tag + "_af")
    b_tp = p.val(tag + "_tp")
    b_ev = p.val(tag + "_ev")
    b_stT = p.val(tag + "_stT")
    for i in range(NT):
        p.add(tag + f"_ldx{i % NS}", 16)
        p.add(tag + "_lnv", 6)
        p.add(tag + "_lns", 2)
        p.add(tag + "_af", 2)
        p.add(tag + "_tp", 8)
        p.add(tag + "_ev", 8)
        if write_xn_tok:
            p.add(tag + f"_lnf{i % 2}", 16)
    if (xnT_res is None or store_res) and not defer_store:
        p.add(tag + "_stT", 16 * NC)
    stT_target = p.val(tag + "_stT")

    def sync1(q):
        for i in range(NT):
            if xgate is not None:
                q.wait_ge(xgate[0][i % 2], xgate[1][i % 2] + 16 * (i // 2 + 1))
            if i >= NS:
                q.wait_ge(af, b_af + 2 * (i - NS) + 2)
                q.wait_ge(lns, b_lns + 2 * (i - NS) + 1)
            q.dma_start(out=xt[i % NS][:], in_=x_hbm[128 * i:128 * (i + 1), :]).then_inc(ldx[i % NS], 16)
    blk.sync(sync1)

    def vec1(q):
        for i in range(NT):
            n0 = b_lnv + 6 * i
            q.wait_ge(ldx[i % NS], b_ldx[i % NS] + 16 * (i // NS + 1))
            q.reduce_sum(nsum[:, i:i + 1], xt[i % NS][:], axis=AX, negate=True).then_inc(lnv, 1)
            q.wait_ge(lnv, n0 + 1)
            q.tensor_scalar(nmu[:, i:i + 1], nsum[:, i:i + 1], 1.0 / D, None, ALU.mult).then_inc(lnv, 1)
            q.wait_ge(lnv, n0 + 2)
            q.tensor_tensor(musq[:, i:i + 1], nmu[:, i:i + 1], nmu[:, i:i + 1], ALU.mult).then_inc(lnv, 1)
            q.wait_ge(lnv, n0 + 3)
            q.wait_ge(lns, b_lns + 2 * i + 1)
            q.scalar_tensor_tensor(varc[:, i:i + 1], sqc[:, i:i + 1], 1.0 / D,
                                   musq[:, i:i + 1], ALU.mult, ALU.subtract).then_inc(lnv, 1)
            q.wait_ge(lns, b_lns + 2 * i + 2)
            q.reciprocal(rstd[:, i:i + 1], sdev[:, i:i + 1]).then_inc(lnv, 1)
            q.wait_ge(lnv, n0 + 5)
            q.tensor_tensor(bcol[:, i:i + 1], nmu[:, i:i + 1], rstd[:, i:i + 1], ALU.mult).then_inc(lnv, 1)
            q.wait_ge(lnv, n0 + 6)
            if i >= 2:
                q.wait_ge(tp, b_tp + 8 * (i - 1))
                if write_xn_tok:
                    q.wait_ge(stf[i % 2], b_stf[i % 2] + 16 * ((i - 2) // 2 + 1))
            if write_xn_tok:
                q.tensor_scalar(xnf[i % 2][:], xt[i % NS][:], rstd[:, i:i + 1],
                                bcol[:, i:i + 1], ALU.mult, ALU.add).then_inc(af, 1)
                q.wait_ge(af, b_af + 2 * i + 1)
                q.tensor_copy(xnb[i % 2][:], xnf[i % 2][:]).then_inc(af, 1)
            else:
                q.tensor_scalar(xnb[i % 2][:], xt[i % NS][:], rstd[:, i:i + 1],
                                bcol[:, i:i + 1], ALU.mult, ALU.add).then_inc(af, 2)
    blk.vector(vec1)

    def scal1(q):
        for i in range(NT):
            q.wait_ge(ldx[i % NS], b_ldx[i % NS] + 16 * (i // NS + 1))
            if i > 0:
                q.wait_ge(lns, b_lns + 2 * i - 1)
            q.activation(sqscr[:], xt[i % NS][:], AF.Square,
                         accum_out=sqc[:, i:i + 1]).then_inc(lns, 1)
            q.wait_ge(lnv, b_lnv + 6 * i + 4)
            q.activation(sdev[:, i:i + 1], varc[:, i:i + 1], AF.Sqrt,
                         bias=cpool[:, CONST_COL["eps_ln"]:CONST_COL["eps_ln"] + 1]).then_inc(lns, 1)
            if i >= 1:
                for c in range(NC):
                    q.wait_ge(tp, b_tp + 8 * (i - 1) + c + 1)
                    q.copy(xnT_full[:, c * T + 128 * (i - 1): c * T + 128 * i],
                           ptp[c][:]).then_inc(ev, 1)
        for c in range(NC):
            q.wait_ge(tp, b_tp + 8 * (NT - 1) + c + 1)
            q.copy(xnT_full[:, c * T + 128 * (NT - 1): c * T + 128 * NT],
                   ptp[c][:]).then_inc(ev, 1)
    blk.scalar(scal1)

    def gp1(q):
        if write_xn_tok:
            for i in range(NT):
                q.wait_ge(af, b_af + 2 * i + 2)
                q.dma_start(out=xn_tok_hbm[128 * i:128 * (i + 1), :],
                            in_=xnf[i % 2][:]).then_inc(stf[i % 2], 16)
    blk.gpsimd(gp1)

    def tens1(q):
        for i in range(NT):
            q.wait_ge(af, b_af + 2 * i + 2)
            for c in range(NC):
                g = 8 * i + c
                if g >= 8:
                    q.wait_ge(ev, b_ev + g - 7)
                q.transpose(ptp[c][:], xnb[i % 2][:, 128 * c:128 * (c + 1)],
                            ident_bf[:]).then_inc(tp, 1)
    blk.tensor(tens1)

    def sync3(q):
        if xnT_res is None or store_res:
            q.wait_ge(ev, b_ev + 8 * NT)
            for c in range(NC):
                q.dma_start(out=xnT_hbm[128 * c:128 * (c + 1), :],
                            in_=xnT_full[:, c * T:(c + 1) * T]).then_inc(stT, 16)
            if not defer_store:
                q.wait_ge(stT, stT_target)
        if write_xn_tok:
            for s in range(2):
                q.wait_ge(stf[s], b_stf[s] + 16 * (NT // 2))
    blk.sync(sync3)
    ctx.close()

def _const_loader(p, ctx, consts):
    nc = p.nc
    out = {}
    ldc = p.sem("cst_ld")
    ccc = p.sem("cst_cast")
    loads = []
    casts = []
    for name, (hbm, shape, want) in consts.items():
        if want == "bf16":
            f32t = ctx.enter_context(nc.sbuf_tensor("cst_f_" + name, list(shape), F32))
            bft = ctx.enter_context(nc.sbuf_tensor("cst_b_" + name, list(shape), BF16))
            loads.append((f32t, hbm))
            casts.append((bft, f32t))
            out[name] = bft
            out[name + "_f32"] = f32t
        else:
            f32t = ctx.enter_context(nc.sbuf_tensor("cst_f_" + name, list(shape), F32))
            loads.append((f32t, hbm))
            out[name] = f32t
    nld = len(loads)

    def s0(q):
        for dst, src in loads:
            q.dma_start(out=dst[:], in_=src[:]).then_inc(ldc, 16)
    p.block.sync(s0)
    p.add("cst_ld", 16 * nld)

    def v0(q):
        q.wait_ge(ldc, p.val("cst_ld"))
        for dst, src in casts:
            q.tensor_copy(dst[:], src[:])
        q.engine_nop().then_inc(ccc, 1)
    p.block.vector(v0)
    p.add("cst_cast", 1)

    def w0(q):
        q.wait_ge(ccc, p.val("cst_cast"))
    for e in ENGINES:
        getattr(p.block, e)(w0)
    return out


def build_ln_program(write_xn_tok=True):
    nc = bass.Bass("TRN2", target_bir_lowering=False)
    x = dram_io(nc, "x", (T, D), F32, "ExternalInput")
    identf = dram_io(nc, "ident", (128, 128), F32, "ExternalInput")
    scon = dram_io(nc, "sconsts", (128, len(CONST_NAMES)), F32, "ExternalInput")
    xn_tok = dram_io(nc, "xn_tok", (T, D), F32, "ExternalOutput") if write_xn_tok else None
    xnT = dram_io(nc, "xnT", (D, T), BF16, "ExternalOutput")
    with ExitStack() as ctx:
        p = Prog(nc, ctx)
        cs = _const_loader(p, ctx, {"ident": (identf, (128, 128), "bf16"),
                                    "sconsts": (scon, (128, len(CONST_NAMES)), "f32")})
        emit_ln(p, x, xn_tok, xnT, cs["ident"], cs["sconsts"], "ln0", write_xn_tok=write_xn_tok)
    return nc




def emit_l0_qkv(p, xnT_hbm, wq_hbm, wkv_hbm, cos_hbm, sin_hbm, rot_hbm, ident_bf,
                qT_hbm, kvT_hbm, kvtok_hbm, tag="b1", xnT_res=None, res_out=None,
                pre=None):
    nc, blk = p.nc, p.block
    ctx = ExitStack()
    sb = lambda n, shp, dt: ctx.enter_context(nc.sbuf_tensor(tag + n, list(shp), dt))
    pctx = ExitStack()
    psb = lambda n, shp, dt: pctx.enter_context(nc.psum_tensor(tag + n, list(shp), dt))
    HT = T // 2

    if pre is not None:
        pre_sems, pb = pre
        wqb, wkvb, cosb, sinf, rotb = (pb["wqb"], pb["wkvb"], pb["cosb"],
                                       pb["sinf"], pb["rotb"])
    else:
        wqb = sb("_wq", (128, NC * D), BF16)
        wkvb = sb("_wkv", (128, NC * D), BF16)
        cosb = sb("_cos", (128, T), BF16)
        sinf = sb("_sin", (128, T), F32)
        rotb = sb("_rot", (128, 128), BF16)
    xnT = xnT_res if xnT_res is not None else sb("_xnT", (128, NC * T), BF16)
    tbf = [sb(f"_tbf{i}", (128, HT), BF16) for i in range(2)]
    tcb = [sb(f"_tcb{i}", (128, HT), BF16) for i in range(2)]
    tsb = [sb(f"_tsb{i}", (128, HT), BF16) for i in range(2)]
    if res_out is not None:
        qT, kvT = res_out[0], res_out[1]
    else:
        qT = sb("_qTr", (128, NC * T), BF16)
        kvT = sb("_kvTr", (128, NC * T), BF16)
    pp = [psb(f"_pp{i}", (128, HT), F32) for i in range(2)]
    pr = [psb(f"_pr{i}", (128, HT), F32) for i in range(2)]

    ld = p.sem(tag + "_ld")
    mm = p.sem(tag + "_mm")
    ca = p.sem(tag + "_ca")
    dv = p.sem(tag + "_dv")
    st = p.sem(tag + "_st")
    b_ld = p.val(tag + "_ld")
    b_mm = p.val(tag + "_mm")
    b_ca = p.val(tag + "_ca")
    b_dv = p.val(tag + "_dv")
    b_st = p.val(tag + "_st")
    if pre is None:
        p.add(tag + "_ld", 16)
    NIDX = 2 * NC * 2
    for idx in range(NIDX):
        p.add(tag + "_mm", 2)
        p.add(tag + "_ca", 1)
        p.add(tag + "_dv", 3)
    if res_out is None:
        for i in range(2 * NC):
            p.add(tag + "_st", 16)

    def decode(idx):
        c = idx // 4
        v = (idx // 2) % 2
        hf = idx % 2
        return c, v, hf

    gsems = [] if pre is None else list(pre_sems)

    def gp0(q):
        if xnT_res is None:
            gsems.append(p.swdge(q, xnT.rearrange("p (c t) -> p c t", c=NC), xnT_hbm.rearrange("(c p) t -> p c t", p=128)))
        if pre is None:
            gsems.append(p.swdge(q, wqb.rearrange("p (c o) -> p c o", c=NC), wq_hbm.rearrange("(c p) o -> p c o", p=128)))
            gsems.append(p.swdge(q, wkvb.rearrange("p (c o) -> p c o", c=NC), wkv_hbm.rearrange("(c p) o -> p c o", p=128)))
            gsems.append(p.swdge(q, cosb[:], cos_hbm[:]))
            gsems.append(p.swdge(q, rotb[:], rot_hbm[:]))
    if xnT_res is None or pre is None:
        blk.gpsimd(gp0)

    def sy0(q):
        q.dma_start(out=sinf[:], in_=sin_hbm[:]).then_inc(ld, 16)
    if pre is None:
        blk.sync(sy0)

    def tens(q):
        if pre is None:
            q.wait_ge(ld, b_ld + 16)
        for s, tgt in gsems:
            q.wait_ge(s, tgt)
        for idx in range(NIDX):
            c, v, hf = decode(idx)
            par = idx % 2
            w = wqb if v == 0 else wkvb
            if idx >= 2:
                q.wait_ge(ca, b_ca + (idx - 2) + 1)
            for k in range(NC):
                for ns in range(2):
                    t0 = hf * HT + 512 * ns
                    ins = q.matmul(pp[par][:, 512 * ns:512 * (ns + 1)],
                                   w[:, k * D + 128 * c:k * D + 128 * (c + 1)],
                                   xnT[:, k * T + t0:k * T + t0 + 512],
                                   start=(k == 0), stop=(k == NC - 1))
            ins.then_inc(mm, 1)
            q.wait_ge(ca, b_ca + idx + 1)
            if idx >= 2:
                q.wait_ge(dv, b_dv + 3 * (idx - 2) + 2)
            for ns in range(2):
                ins = q.matmul(pr[par][:, 512 * ns:512 * (ns + 1)], rotb[:],
                               tbf[par][:, 512 * ns:512 * (ns + 1)], start=True, stop=True)
            ins.then_inc(mm, 1)
    blk.tensor(tens)

    def scal(q):
        for idx in range(NIDX):
            par = idx % 2
            q.wait_ge(mm, b_mm + 2 * idx + 1)
            if idx >= 2:
                q.wait_ge(dv, b_dv + 3 * (idx - 2) + 1)
                q.wait_ge(mm, b_mm + 2 * (idx - 2) + 2)
            q.copy(tbf[par][:], pp[par][:]).then_inc(ca, 1)
    blk.scalar(scal)

    def vec(q):
        for idx in range(NIDX):
            c, v, hf = decode(idx)
            par = idx % 2
            dst = qT if v == 0 else kvT
            hslice = lambda t: t[:, hf * HT:(hf + 1) * HT]
            q.wait_ge(ca, b_ca + idx + 1)
            if idx >= 2:
                q.wait_ge(dv, b_dv + 3 * (idx - 2) + 3)
            q.tensor_tensor(tcb[par][:], tbf[par][:], hslice(cosb), ALU.mult).then_inc(dv, 1)
            q.wait_ge(mm, b_mm + 2 * idx + 2)
            q.tensor_tensor(tsb[par][:], pr[par][:], hslice(sinf), ALU.mult).then_inc(dv, 1)
            q.wait_ge(dv, b_dv + 3 * idx + 2)
            q.tensor_tensor(dst[:, c * T + hf * HT:c * T + (hf + 1) * HT],
                            tcb[par][:], tsb[par][:], ALU.add).then_inc(dv, 1)
    blk.vector(vec)

    def sy1(q):
        for i in range(2 * NC):
            c, v = i // 2, i % 2
            q.wait_ge(dv, b_dv + 3 * (4 * c + 2 * v + 1) + 3)
            src = qT if v == 0 else kvT
            dst = qT_hbm if v == 0 else kvT_hbm
            q.dma_start(out=dst[128 * c:128 * (c + 1), :],
                        in_=src[:, c * T:(c + 1) * T]).then_inc(st, 16)
    if res_out is None:
        blk.sync(sy1)

    pctx.close()
    ktok = res_out[2] if res_out is not None else sb("_ktok", (128, H * 16 * 65), BF16)

    tp = p.sem(tag + "_tp")
    ev = p.sem(tag + "_ev")
    vone = p.sem(tag + "_vone")
    stk = p.sem(tag + "_stk") if res_out is None else None
    b_tp = p.val(tag + "_tp")
    b_ev = p.val(tag + "_ev")
    b_stk = p.val(tag + "_stk") if res_out is None else 0
    for c in range(NC):
        p.add(tag + "_tp", 16)
        p.add(tag + "_ev", 16)
    p.add(tag + "_vone", 1)
    if res_out is None:
        p.add(tag + "_stk", 16 * H)
    pctx2 = ExitStack()
    ptt = [pctx2.enter_context(nc.psum_tensor(tag + f"_pt{i}", [128, 128], BF16)) for i in range(4)]

    def tens2(q):
        for c in range(NC):
            q.wait_ge(dv, b_dv + 3 * (4 * c + 3) + 3)
            for j in range(16):
                g = 16 * c + j
                if g >= 4:
                    q.wait_ge(ev, b_ev + g - 3)
                q.transpose(ptt[g % 4][:], kvT[:, c * T + 128 * j:c * T + 128 * (j + 1)],
                            ident_bf[:]).then_inc(tp, 1)
    blk.tensor(tens2)

    def vec2(q):
        for h in range(H):
            q.memset(ktok.rearrange("p (h j e) -> p h j e", h=H, j=16)[:, h, :, 64:65], 1.0)
        q.engine_nop().then_inc(vone, 1)
    blk.vector(vec2)

    def scal2(q):
        for c in range(NC):
            hA, hB = 2 * c, 2 * c + 1
            for j in range(16):
                g = 16 * c + j
                q.wait_ge(tp, b_tp + g + 1)
                q.copy(ktok[:, hA * 16 * 65 + j * 65:hA * 16 * 65 + j * 65 + 64],
                       ptt[g % 4][:, 0:64])
                q.copy(ktok[:, hB * 16 * 65 + j * 65:hB * 16 * 65 + j * 65 + 64],
                       ptt[g % 4][:, 64:128]).then_inc(ev, 1)
    blk.scalar(scal2)

    def sy2(q):
        q.wait_ge(ev, b_ev + 16 * NC)
        q.wait_ge(vone, 1)
        for h in range(H):
            q.dma_start(out=kvtok_hbm.rearrange("(h j p) e -> h p j e", p=128, j=16)[h],
                        in_=ktok.rearrange("p (h j e) -> p h j e", h=H, j=16)[:, h]).then_inc(stk, 16)
        q.wait_ge(stk, b_stk + 16 * H)
        q.wait_ge(st, b_st + 16 * 2 * NC)
    if res_out is None:
        blk.sync(sy2)
    pctx2.close()
    ctx.close()



def emit_l0_attn(p, qT_hbm, kvT_hbm, kvtok_hbm, cpool, ones_row_sb, xoT_hbm, tag="b2",
                 res_in=None, xoT_res=None):
    nc, blk = p.nc, p.block
    ctx = ExitStack()
    sb = lambda n, shp, dt: ctx.enter_context(nc.sbuf_tensor(tag + n, list(shp), dt))
    pctx = ExitStack()
    psb = lambda n, shp, dt: pctx.enter_context(nc.psum_tensor(tag + n, list(shp), dt))

    if res_in is not None:
        qT, kvT, ktok = res_in
    else:
        qT = sb("_qT", (128, NC * T), BF16)
        kvT = sb("_kvT", (128, NC * T), BF16)
        ktok = sb("_ktok", (128, H * 16 * 65), BF16)
    PT = [sb(f"_PT{i}", (128, T), BF16) for i in range(4)]
    zsb = sb("_zsb", (1, T), F32)
    zln = sb("_zln", (1, T), F32)
    zr = sb("_zr", (1, T), F32)
    zbc = sb("_zbc", (128, 512), F32)
    rot2 = res_in is not None and xoT_res is None
    if xoT_res is not None:
        xoT = xoT_res
    elif rot2:
        xoT = sb("_xoT", (128, 2 * T), BF16)
    else:
        xoT = sb("_xoT", (128, NC * T), BF16)
    pOut = psb("_pOut", (128, T), F32)
    pS = [psb(f"_pS{i}", (128, T // 2), F32) for i in range(2)]

    negc = cpool[:, CONST_COL["negc"]:CONST_COL["negc"] + 1]

    ld = p.sem(tag + "_ld")
    sS = p.sem(tag + "_sS")
    sE = p.sem(tag + "_sE")
    sPV = p.sem(tag + "_sPV")
    sZ = p.sem(tag + "_sZ")
    sB = p.sem(tag + "_sB")
    sEv = p.sem(tag + "_sEv")
    sN = p.sem(tag + "_sN")
    st = p.sem(tag + "_st")
    b_ld = p.val(tag + "_ld")
    b = {}
    for n in ("sS", "sE", "sPV", "sZ", "sB", "sEv", "sN"):
        b[n] = p.val(tag + "_" + n)
    b_st = p.val(tag + "_st")
    if res_in is None:
        p.add(tag + "_ld", 16 * 3)
    for h in range(H):
        p.add(tag + "_sS", 32)
        p.add(tag + "_sE", 32)
        p.add(tag + "_sPV", 16)
        p.add(tag + "_sZ", 3)
        p.add(tag + "_sB", 4)
        p.add(tag + "_sEv", 4)
        p.add(tag + "_sN", 4)
    stf = [p.sem(tag + f"_stf{s}") for s in range(2)] if rot2 else None
    b_stf = [p.val(tag + f"_stf{s}") for s in range(2)] if rot2 else None
    if xoT_res is None:
        if rot2:
            for cc in range(NC):
                p.add(tag + f"_stf{cc % 2}", 16)
        else:
            p.add(tag + "_st", 16 * NC)
    per = {"sS": 32, "sE": 32, "sPV": 16, "sZ": 3, "sB": 4, "sEv": 4, "sN": 4}

    def hb(name, h):
        return b[name] + per[name] * h

    def sy0(q):
        q.dma_start(out=qT.rearrange("p (c t) -> p c t", c=NC),
                    in_=qT_hbm.rearrange("(c p) t -> p c t", p=128)).then_inc(ld, 16)
        q.dma_start(out=kvT.rearrange("p (c t) -> p c t", c=NC),
                    in_=kvT_hbm.rearrange("(c p) t -> p c t", p=128)).then_inc(ld, 16)
        q.dma_start(out=ktok.rearrange("p (h j e) -> p h j e", h=H, j=16),
                    in_=kvtok_hbm.rearrange("(h j p) e -> p h j e", p=128, j=16)).then_inc(ld, 16)
    if res_in is None:
        blk.sync(sy0)

    def tens(q):
        if res_in is None:
            q.wait_ge(ld, b_ld + 48)
        for h in range(H):
            c, rb = h // 2, 64 * (h % 2)
            for j in range(16):
                for piece in range(2):
                    if j > 0:
                        q.wait_ge(sE, hb("sE", h) + 2 * (j - 1) + piece + 1)
                    elif h > 0:
                        q.wait_ge(sE, hb("sE", h - 1) + 32)
                    for ns in range(2):
                        qs = 2 * piece + ns
                        ins = q.matmul(pS[piece][:, 512 * ns:512 * (ns + 1)],
                                       kvT[rb:rb + 64, c * T + 128 * j:c * T + 128 * (j + 1)],
                                       qT[rb:rb + 64, c * T + 512 * qs:c * T + 512 * (qs + 1)],
                                       start=True, stop=True)
                    ins.then_inc(sS, 1)
                if j == 2 and h > 0:
                    q.wait_ge(sZ, hb("sZ", h - 1) + 3)
                    for qs in range(4):
                        ins = q.matmul(pOut[64:128, 512 * qs:512 * (qs + 1)], ones_row_sb[0:1, 0:64],
                                       zr[0:1, 512 * qs:512 * (qs + 1)], start=True, stop=True)
                        ins.then_inc(sB, 1)
                jj = j - 3
                if jj >= 0:
                    q.wait_ge(sE, hb("sE", h) + 2 * jj + 2)
                    if jj == 0 and h > 0:
                        q.wait_ge(sN, hb("sN", h - 1) + 4)
                    for qs in range(4):
                        ins = q.matmul(pOut[0:65, 512 * qs:512 * (qs + 1)],
                                       ktok[:, h * 16 * 65 + 65 * jj:h * 16 * 65 + 65 * jj + 65],
                                       PT[jj % 4][:, 512 * qs:512 * (qs + 1)],
                                       start=(jj == 0), stop=False)
                    ins.then_inc(sPV, 1)
            for jj in (13, 14, 15):
                q.wait_ge(sE, hb("sE", h) + 2 * jj + 2)
                for qs in range(4):
                    ins = q.matmul(pOut[0:65, 512 * qs:512 * (qs + 1)],
                                   ktok[:, h * 16 * 65 + 65 * jj:h * 16 * 65 + 65 * jj + 65],
                                   PT[jj % 4][:, 512 * qs:512 * (qs + 1)],
                                   start=False, stop=(jj == 15))
                ins.then_inc(sPV, 1)
        q.wait_ge(sZ, hb("sZ", H - 1) + 3)
        for qs in range(4):
            ins = q.matmul(pOut[64:128, 512 * qs:512 * (qs + 1)], ones_row_sb[0:1, 0:64],
                           zr[0:1, 512 * qs:512 * (qs + 1)], start=True, stop=True)
            ins.then_inc(sB, 1)
    blk.tensor(tens)

    def zpart1(q, hh):
        q.wait_ge(sPV, hb("sPV", hh) + 16)
        q.copy(zsb[:], pOut[64:65, :]).then_inc(sZ, 1)
        q.wait_ge(sZ, hb("sZ", hh) + 1)
        q.activation(zln[:], zsb[:], AF.Ln).then_inc(sZ, 1)

    def zpart2(q, hh):
        q.wait_ge(sZ, hb("sZ", hh) + 2)
        q.activation(zr[:], zln[:], AF.Exp, scale=-1.0).then_inc(sZ, 1)

    def zpart3(q, hh):
        for qs in range(4):
            q.wait_ge(sB, hb("sB", hh) + qs + 1)
            if qs > 0:
                q.wait_ge(sN, hb("sN", hh) + qs)
            elif hh > 0:
                q.wait_ge(sN, hb("sN", hh - 1) + 4)
            q.copy(zbc[0:64, :], pOut[64:128, 512 * qs:512 * (qs + 1)]).then_inc(sEv, 1)

    def scal(q):
        for h in range(H):
            for j in range(16):
                for piece in range(2):
                    q.wait_ge(sS, hb("sS", h) + 2 * j + piece + 1)
                    if j >= 4:
                        q.wait_ge(sPV, hb("sPV", h) + j - 3)
                    elif h > 0:
                        q.wait_ge(sPV, hb("sPV", h - 1) + 16)
                    q.activation(PT[j % 4][:, 1024 * piece:1024 * (piece + 1)],
                                 pS[piece][:], AF.Exp, bias=negc, scale=0.125).then_inc(sE, 1)
                if h > 0:
                    if j == 0:
                        zpart1(q, h - 1)
                    elif j == 1:
                        zpart2(q, h - 1)
                    elif j == 2:
                        zpart3(q, h - 1)
        zpart1(q, H - 1)
        zpart2(q, H - 1)
        zpart3(q, H - 1)
    blk.scalar(scal)

    def vec(q):
        for h in range(H):
            c, rb = h // 2, 64 * (h % 2)
            dc = (c % 2) if rot2 else c
            for qs in range(4):
                q.wait_ge(sEv, hb("sEv", h) + qs + 1)
                if rot2 and h % 2 == 0 and qs == 0 and c >= 2:
                    q.wait_ge(stf[c % 2], b_stf[c % 2] + 16 * ((c - 2) // 2 + 1))
                q.tensor_tensor(xoT[rb:rb + 64, dc * T + 512 * qs:dc * T + 512 * (qs + 1)],
                                pOut[0:64, 512 * qs:512 * (qs + 1)], zbc[0:64, :],
                                ALU.mult).then_inc(sN, 1)
    blk.vector(vec)

    def sy1(q):
        for cc in range(NC):
            q.wait_ge(sN, hb("sN", 2 * cc + 1) + 4)
            sc2 = (cc % 2) if rot2 else cc
            q.dma_start(out=xoT_hbm[128 * cc:128 * (cc + 1), :],
                        in_=xoT[:, sc2 * T:(sc2 + 1) * T]).then_inc(stf[cc % 2] if rot2 else st, 16)
        if rot2:
            for s in range(2):
                q.wait_ge(stf[s], b_stf[s] + 16 * (NC // 2))
        else:
            q.wait_ge(st, b_st + 16 * NC)
    if xoT_res is None:
        blk.sync(sy1)
    pctx.close()
    ctx.close()



def emit_out_proj(p, xoT_hbm, w_hbm, brow_hbm, ones_row_sb, s_hbm, tag="b3",
                  with_bias=True, xoT_res=None):
    nc, blk = p.nc, p.block
    ctx = ExitStack()
    sb = lambda n, shp, dt: ctx.enter_context(nc.sbuf_tensor(tag + n, list(shp), dt))
    pctx = ExitStack()
    psb = lambda n, shp, dt: pctx.enter_context(nc.psum_tensor(tag + n, list(shp), dt))
    xoT = xoT_res if xoT_res is not None else sb("_xoT", (128, NC * T), BF16)
    wb = sb("_w", (128, NC * D), BF16)
    brow = sb("_brow", (1, D), F32) if with_bias else None
    bbc = sb("_bbc", (128, D), F32) if with_bias else None
    ssb = [sb(f"_s{i}", (128, D), F32) for i in range(2)]
    pS = [psb(f"_pS{i}", (128, D), F32) for i in range(2)]
    pB = psb("_pB", (128, D), F32)

    ld = p.sem(tag + "_ld")
    mm = p.sem(tag + "_mm")
    dv = p.sem(tag + "_dv")
    ev = p.sem(tag + "_ev")
    stf = [p.sem(tag + f"_stf{s}") for s in range(2)]
    ldq = [p.sem(tag + f"_ldq{s}") for s in range(2)] if xoT_res is None else None
    b_ld = p.val(tag + "_ld")
    b_mm = p.val(tag + "_mm")
    b_dv = p.val(tag + "_dv")
    b_ev = p.val(tag + "_ev")
    b_stf = [p.val(tag + f"_stf{s}") for s in range(2)]
    nld = 1 if with_bias else 0
    if nld:
        p.add(tag + "_ld", 16 * nld)
    if xoT_res is None:
        b_ldq = [p.val(tag + f"_ldq{s}") for s in range(2)]
        for s in range(2):
            p.add(tag + f"_ldq{s}", 16)
    nbias = 1 if with_bias else 0
    if with_bias:
        p.add(tag + "_mm", 1)
        p.add(tag + "_ev", 1)
    for tt in range(NT):
        p.add(tag + "_mm", 1)
        p.add(tag + "_dv", 1)
        p.add(tag + f"_stf{tt % 2}", 16)

    gsems = []

    def gp0(q):
        gsems.append(p.swdge(q, wb.rearrange("p (c o) -> p c o", c=NC),
                             w_hbm.rearrange("(c p) o -> p c o", p=128)))
    blk.gpsimd(gp0)

    def sy0(q):
        if with_bias:
            q.dma_start(out=brow[:], in_=brow_hbm[:]).then_inc(ld, 16)
        if xoT_res is None:
            for s in range(2):
                q.dma_start(out=xoT.rearrange("p (c t) -> p c t", c=NC)[:, :, 1024 * s:1024 * (s + 1)],
                            in_=xoT_hbm.rearrange("(c p) t -> p c t", p=128)[:, :, 1024 * s:1024 * (s + 1)]).then_inc(ldq[s], 16)
    if nld or xoT_res is None:
        blk.sync(sy0)

    def tens(q):
        if nld:
            q.wait_ge(ld, b_ld + 16 * nld)
        for s, tgt in gsems:
            q.wait_ge(s, tgt)
        if with_bias:
            for ns in range(2):
                ins = q.matmul(pB[:, 512 * ns:512 * (ns + 1)], ones_row_sb[:],
                               brow[0:1, 512 * ns:512 * (ns + 1)], start=True, stop=True)
            ins.then_inc(mm, 1)
        for tt in range(NT):
            par = tt % 2
            if xoT_res is None and tt % 8 == 0:
                q.wait_ge(ldq[tt // 8], b_ldq[tt // 8] + 16)
            if tt >= 2:
                q.wait_ge(dv, b_dv + tt - 1)
            for c in range(NC):
                for ns in range(2):
                    ins = q.matmul(pS[par][:, 512 * ns:512 * (ns + 1)],
                                   xoT[:, c * T + 128 * tt:c * T + 128 * (tt + 1)],
                                   wb[:, c * D + 512 * ns:c * D + 512 * (ns + 1)],
                                   start=(c == 0), stop=(c == NC - 1))
            ins.then_inc(mm, 1)
    blk.tensor(tens)

    def scal(q):
        if with_bias:
            q.wait_ge(mm, b_mm + 1)
            q.copy(bbc[:], pB[:]).then_inc(ev, 1)
    blk.scalar(scal)

    def vec(q):
        if with_bias:
            q.wait_ge(ev, b_ev + 1)
        for tt in range(NT):
            par = tt % 2
            q.wait_ge(mm, b_mm + nbias + tt + 1)
            if tt >= 2:
                q.wait_ge(stf[par], b_stf[par] + 16 * ((tt - 2) // 2 + 1))
            if with_bias:
                q.tensor_tensor(ssb[par][:], pS[par][:], bbc[:], ALU.add).then_inc(dv, 1)
            else:
                q.tensor_copy(ssb[par][:], pS[par][:]).then_inc(dv, 1)
    blk.vector(vec)

    def sy1(q):
        for tt in range(NT):
            par = tt % 2
            q.wait_ge(dv, b_dv + tt + 1)
            q.dma_start(out=s_hbm[128 * tt:128 * (tt + 1), :], in_=ssb[par][:]).then_inc(stf[par], 16)
        for s in range(2):
            q.wait_ge(stf[s], b_stf[s] + 16 * (NT // 2))
    blk.sync(sy1)
    pctx.close()
    ctx.close()


def build_l0_attn_program():
    nc = bass.Bass("TRN2", target_bir_lowering=False)
    qT = dram_io(nc, "qT", (D, T), BF16, "ExternalInput")
    kvT = dram_io(nc, "kvT", (D, T), BF16, "ExternalInput")
    kvtok = dram_io(nc, "kvtok", (H * 16 * 128, 65), BF16, "ExternalInput")
    scon = dram_io(nc, "sconsts", (128, len(CONST_NAMES)), F32, "ExternalInput")
    onesr = dram_io(nc, "ones_row", (1, 128), F32, "ExternalInput")
    xoT = dram_io(nc, "xoT", (D, T), BF16, "ExternalOutput")
    with ExitStack() as ctx:
        p = Prog(nc, ctx)
        cs = _const_loader(p, ctx, {"sconsts": (scon, (128, len(CONST_NAMES)), "f32"),
                                    "ones_row": (onesr, (1, 128), "f32")})
        emit_l0_attn(p, qT, kvT, kvtok, cs["sconsts"], cs["ones_row"], xoT)
    return nc


def build_out_proj_program(with_bias=True):
    nc = bass.Bass("TRN2", target_bir_lowering=False)
    xoT = dram_io(nc, "xoT", (D, T), BF16, "ExternalInput")
    w = dram_io(nc, "w", (D, D), F32, "ExternalInput")
    brow = dram_io(nc, "brow", (1, D), F32, "ExternalInput")
    onesr = dram_io(nc, "ones_row", (1, 128), F32, "ExternalInput")
    s_out = dram_io(nc, "s_out", (T, D), F32, "ExternalOutput")
    with ExitStack() as ctx:
        p = Prog(nc, ctx)
        cs = _const_loader(p, ctx, {"ones_row": (onesr, (1, 128), "f32")})
        emit_out_proj(p, xoT, w, brow, cs["ones_row"], s_out, with_bias=with_bias)
    return nc


def build_l0_qkv_program():
    nc = bass.Bass("TRN2", target_bir_lowering=False)
    xnT = dram_io(nc, "xnT", (D, T), BF16, "ExternalInput")
    wq = dram_io(nc, "wq", (D, D), F32, "ExternalInput")
    wkv = dram_io(nc, "wkv", (D, D), F32, "ExternalInput")
    cos2 = dram_io(nc, "cos2", (128, T), F32, "ExternalInput")
    sin2 = dram_io(nc, "sin2", (128, T), F32, "ExternalInput")
    rot128 = dram_io(nc, "rot128", (128, 128), F32, "ExternalInput")
    identf = dram_io(nc, "ident", (128, 128), F32, "ExternalInput")
    qT = dram_io(nc, "qT", (D, T), BF16, "ExternalOutput")
    kvT = dram_io(nc, "kvT", (D, T), BF16, "ExternalOutput")
    kvtok = dram_io(nc, "kvtok", (H * 16 * 128, 65), BF16, "ExternalOutput")
    with ExitStack() as ctx:
        p = Prog(nc, ctx)
        cs = _const_loader(p, ctx, {"ident": (identf, (128, 128), "bf16")})
        emit_l0_qkv(p, xnT, wq, wkv, cos2, sin2, rot128, cs["ident"], qT, kvT, kvtok)
    return nc



def emit_l0_ffn(p, xnT_hbm, xn_tok_hbm, x_hbm, s0_hbm, prm, cpool, ones_row_sb,
                ident_f32, ident_bf, y_hbm, x1_hbm, tag="c", xnT_res=None):
    nc, blk = p.nc, p.block
    ctx = ExitStack()
    sb = lambda n, shp, dt: ctx.enter_context(nc.sbuf_tensor(tag + n, list(shp), dt))

    xnT = xnT_res if xnT_res is not None else sb("_xnT", (128, NC * T), BF16)
    sigraw = sb("_sigraw", (64, D), F32)
    sigs_bf = sb("_sigs", (64, D), BF16)
    gmean_bf = sb("_gmean", (64, 8), BF16)
    csig_bf = sb("_csig", (8, D), BF16)
    sigsT = sb("_sigsT", (128, NC * 64), BF16)
    csigT = sb("_csigT", (128, NC * 8), BF16)
    knots = sb("_knots", (1, 8), F32)
    temp = sb("_temp", (1, 1), F32)
    oscale = sb("_oscale", (1, 1), F32)
    cols = sb("_cols", (72, 10), F32)
    cw1b = sb("_cw1", (128, NC * 256), BF16)
    cb1c = sb("_cb1", (128, 2), F32)
    cw2b = sb("_cw2", (128, 4), BF16)
    cb2c = sb("_cb2", (2, 1), F32)
    comp_bf = sb("_comp", (2, T), BF16)
    reppat = sb("_reppat", (2, 16), BF16)
    comprep = sb("_comprep", (16, T), BF16)
    gacc = [sb(f"_gacc{i}", (72, T), F32) for i in range(2)]
    iota64c = sb("_iota64", (128, 64), F32)
    iota64m = sb("_iota64m", (128, 64), F32)
    coh8T = sb("_coh8T", (8, T), BF16)
    expand_bf = sb("_expand", (8, 64), BF16)
    tohfm = sb("_tohfm", (64, T), F32)
    w1gb = sb("_w1g", (16, 8 * 128), BF16)
    b1gc = sb("_b1g", (128, 8), F32)
    w2gb = sb("_w2g", (128, 8 * 64), BF16)
    b2c = sb("_b2c", (64, 1), F32)
    dirs_bf = sb("_dirs", (64, D), BF16)
    selmag = sb("_selmag", (64, T), BF16)
    gamr = sb("_gamr", (1, D), F32)
    betr = sb("_betr", (1, D), F32)
    ctxA = ExitStack()
    sbA = lambda n, shp, dt: ctxA.enter_context(nc.sbuf_tensor(tag + n, list(shp), dt))
    sg1 = sbA("_sg1", (64, D), F32)
    sg2 = sbA("_sg2", (64, D), F32)
    sgt = sbA("_sgt", (64, D), F32)
    tmp11 = sbA("_tmp11", (1, 1), F32)
    dsb = sbA("_dsb", (1, 7), F32)
    csb = sbA("_csb", (1, 6), F32)
    prow = sbA("_prow", (1, 10), F32)
    hT = sbA("_hT", (128, 2 * T), BF16)
    nrm = sbA("_nrm", (72, T), F32)
    relu_sb = [sbA(f"_relu{i}", (72, T), F32) for i in range(2)]
    ctxB = ExitStack()
    sbB = lambda n, shp, dt: ctxB.enter_context(nc.sbuf_tensor(tag + n, list(shp), dt))
    coh_tok = sbB("_cohtok", (128, 8), F32)
    mxc = sbB("_mxc", (128, 1), F32)
    eqc = sbB("_eqc", (128, 8), F32)
    t1c = sbB("_t1c", (128, 8), F32)
    t2c = sbB("_t2c", (128, 8), F32)
    cidx = sbB("_cidx", (128, 1), F32)
    gm_t1 = sbB("_gmt1", (64, T), F32)
    gm_v = sbB("_gmv", (64, T), F32)
    gm = sbB("_gm", (64, T), F32)
    mxt = sbB("_mxt", (128, 1), F32)
    eqt = sbB("_eqt", (128, 64), F32)
    t1t = sbB("_t1t", (128, 64), F32)
    t2t = sbB("_t2t", (128, 64), F32)
    tidx = sbB("_tidx", (128, 1), F32)
    toh_tok = sbB("_tohtok", (128, 64), F32)
    S = lambda n: p.sem(tag + n)
    V = lambda n: p.val(tag + n)
    A = lambda n, d=1: p.add(tag + n, d)
    ld = S("_ld")
    c3 = S("_c3")
    c1s = S("_c1s")
    c1v = S("_c1v")
    c2t = S("_c2t")
    c2v = S("_c2v")
    c2e = S("_c2e")
    c4v = S("_c4v")
    c5v = S("_c5v")
    c5w = S("_c5w")
    c5g = S("_c5g")
    c5s = S("_c5s")
    c5e = S("_c5e")
    stxa = [S(f"_stxa{i}") for i in range(2)]

    gsems = []

    def gp0(q):
        if xnT_res is None:
            gsems.append(p.swdge(q, xnT.rearrange("p (c t) -> p c t", c=NC),
                                 xnT_hbm.rearrange("(c p) t -> p c t", p=128)))
        gsems.append(p.swdge(q, cw1b.rearrange("p (c o) -> p c o", c=NC),
                             prm["cw1"].rearrange("(c p) o -> p c o", p=128)))
        gsems.append(p.swdge(q, cw2b.rearrange("p (c o) -> p c o", c=2),
                             prm["cw2"].rearrange("(c p) o -> p c o", p=128)))
        gsems.append(p.swdge(q, w1gb[:], prm["w1g"][:]))
        gsems.append(p.swdge(q, w2gb[:], prm["w2g"][:]))
        gsems.append(p.swdge(q, dirs_bf[:], prm["dirs"][:]))
        gsems.append(p.swdge(q, gmean_bf[:], prm["gmean"][:]))
        gsems.append(p.swdge(q, expand_bf[:], prm["expand8"][:]))
        gsems.append(p.swdge(q, reppat[:], prm["comp_rep"][:]))
    blk.gpsimd(gp0)

    b_ld = V("_ld")
    A("_ld", 16 * 12)
    ld_tgt = V("_ld")

    def sy0(q):
        for dst, key in ((sigraw, "sigraw"), (knots, "knots"), (temp, "temp"),
                         (oscale, "oscale"), (cb1c, "cb1_cols"), (cb2c, "cb2_col"),
                         (b1gc, "b1g"), (b2c, "b2_col"), (iota64c, "iota64"),
                         (iota64m, "iota64m"), (gamr, "gam_row"), (betr, "bet_row")):
            q.dma_start(out=dst[:], in_=prm[key][:]).then_inc(ld, 16)
    blk.sync(sy0)

    c0s = S("_c0s")
    c0v2 = S("_c0v2")
    c0m = S("_c0m")
    c0e = S("_c0e")
    b0s = V("_c0s"); A("_c0s", 2)
    b0v = V("_c0v2"); A("_c0v2", 10)
    b0m = V("_c0m"); A("_c0m", 2)
    b0e = V("_c0e"); A("_c0e", 2)

    def c0_scal(q):
        q.wait_ge(ld, ld_tgt)
        q.sign(sg1[:], sigraw[:], bias=cpool[0:64, CONST_COL["m03"]:CONST_COL["m03"] + 1]).then_inc(c0s, 1)
        q.wait_ge(c0s, b0s + 1)
        q.sign(sg2[:], sigraw[:], bias=cpool[0:64, CONST_COL["p03"]:CONST_COL["p03"] + 1]).then_inc(c0s, 1)
    blk.scalar(c0_scal)

    def c0_vec(q):
        q.wait_ge(ld, ld_tgt)
        q.wait_ge(c0s, b0s + 2)
        q.tensor_tensor(sgt[:], sg1[:], sg2[:], ALU.add).then_inc(c0v2, 1)
        q.wait_ge(c0v2, b0v + 1)
        q.tensor_scalar(sigs_bf[:], sgt[:], 0.5, None, ALU.mult).then_inc(c0v2, 1)
        q.tensor_tensor(dsb[:], knots[0:1, 1:8], knots[0:1, 0:7], ALU.subtract).then_inc(c0v2, 1)
        q.wait_ge(c0v2, b0v + 3)
        q.tensor_tensor(csb[:], dsb[0:1, 1:7], dsb[0:1, 0:6], ALU.subtract).then_inc(c0v2, 1)
        q.wait_ge(c0v2, b0v + 4)
        q.tensor_scalar(tmp11[:], temp[:], 1e-6, None, ALU.add).then_inc(c0v2, 1)
        q.wait_ge(c0v2, b0v + 5)
        q.reciprocal(prow[0:1, 0:1], tmp11[:]).then_inc(c0v2, 1)
        q.wait_ge(c0v2, b0v + 6)
        q.tensor_copy(prow[0:1, 1:2], knots[0:1, 0:1]).then_inc(c0v2, 1)
        q.wait_ge(c0v2, b0v + 7)
        q.tensor_scalar(prow[0:1, 2:3], dsb[0:1, 0:1], 7.0, None, ALU.mult).then_inc(c0v2, 1)
        q.wait_ge(c0v2, b0v + 8)
        q.tensor_copy(prow[0:1, 3:9], csb[:]).then_inc(c0v2, 1)
        q.wait_ge(c0v2, b0v + 9)
        q.tensor_copy(prow[0:1, 9:10], oscale[:]).then_inc(c0v2, 1)
    blk.vector(c0_vec)

    pctx0 = ExitStack()
    pcs = pctx0.enter_context(nc.psum_tensor(tag + "_pcs", [8, D], F32))
    pcols = pctx0.enter_context(nc.psum_tensor(tag + "_pcols", [72, 10], F32))
    ptr = [pctx0.enter_context(nc.psum_tensor(tag + f"_ptr{i}", [128, 128], BF16)) for i in range(2)]

    def c0_tens(q):
        for s, tgt in gsems:
            q.wait_ge(s, tgt)
        q.wait_ge(c0v2, b0v + 2)
        for ns in range(2):
            ins = q.matmul(pcs[:, 512 * ns:512 * (ns + 1)], gmean_bf[:],
                           sigs_bf[:, 512 * ns:512 * (ns + 1)], start=True, stop=True)
        ins.then_inc(c0m, 1)
        q.wait_ge(c0v2, b0v + 10)
        ins = q.matmul(pcols[:], ones_row_sb[0:1, 0:72], prow[:], start=True, stop=True)
        ins.then_inc(c0m, 1)
    blk.tensor(c0_tens)

    def c0_scal2(q):
        q.wait_ge(c0m, b0m + 1)
        q.sign(csig_bf[:], pcs[:]).then_inc(c0e, 1)
        q.wait_ge(c0m, b0m + 2)
        q.copy(cols[:], pcols[:]).then_inc(c0e, 1)
    blk.scalar(c0_scal2)
    c0_done = None

    b_t0 = V("_c2t")
    b_e0 = V("_c2e")
    A("_c2t", NC)
    A("_c2e", NC)

    def c0_tens2(q):
        q.wait_ge(c0e, b0e + 1)
        for k in range(NC):
            if k >= 2:
                q.wait_ge(c2e, b_e0 + k - 1)
            q.transpose(ptr[k % 2][:, 0:64], sigs_bf[:, 128 * k:128 * (k + 1)], ident_bf[0:64, 0:64])
            ins = q.transpose(ptr[k % 2][:, 64:72], csig_bf[:, 128 * k:128 * (k + 1)], ident_bf[0:8, 0:8])
            ins.then_inc(c2t, 1)
    blk.tensor(c0_tens2)

    def c0_scal3(q):
        for k in range(NC):
            q.wait_ge(c2t, b_t0 + k + 1)
            q.copy(sigsT[:, 64 * k:64 * (k + 1)], ptr[k % 2][:, 0:64])
            q.copy(csigT[:, 8 * k:8 * (k + 1)], ptr[k % 2][:, 64:72]).then_inc(c2e, 1)
    blk.scalar(c0_scal3)

    pctx0.close()
    pctx1 = ExitStack()
    ph = pctx1.enter_context(nc.psum_tensor(tag + "_ph", [128, T], F32))
    pc2 = pctx1.enter_context(nc.psum_tensor(tag + "_pc2", [2, T], F32))
    b3 = V("_c3")
    A("_c3", 8)

    def c3_tens(q):
        q.wait_ge(c2e, b_e0 + NC)
        for m in range(2):
            if m == 1:
                q.wait_ge(c3, b3 + 2)
            for k in range(NC):
                for ns in range(4):
                    ins = q.matmul(ph[:, 512 * ns:512 * (ns + 1)],
                                   cw1b[:, k * 256 + 128 * m:k * 256 + 128 * (m + 1)],
                                   xnT[:, k * T + 512 * ns:k * T + 512 * (ns + 1)],
                                   start=(k == 0), stop=(k == NC - 1))
            ins.then_inc(c3, 1)
        q.wait_ge(c3, b3 + 4)
        for m in range(2):
            for ns in range(4):
                ins = q.matmul(pc2[:, 512 * ns:512 * (ns + 1)], cw2b[:, 2 * m:2 * (m + 1)],
                               hT[:, m * T + 512 * ns:m * T + 512 * (ns + 1)],
                               start=(m == 0), stop=(m == 1))
        ins.then_inc(c3, 1)
        q.wait_ge(c3, b3 + 6)
        for ns in range(4):
            ins = q.matmul(ph[0:16, 512 * ns:512 * (ns + 1)], reppat[:],
                           comp_bf[:, 512 * ns:512 * (ns + 1)], start=True, stop=True)
        ins.then_inc(c3, 1)
    blk.tensor(c3_tens)

    def c3_scal(q):
        for m in range(2):
            q.wait_ge(c3, b3 + 2 * m + 1)
            q.activation(hT[:, m * T:(m + 1) * T], ph[:], AF.Gelu,
                         bias=cb1c[:, m:m + 1]).then_inc(c3, 1)
        q.wait_ge(c3, b3 + 5)
        q.activation(comp_bf[:], pc2[:], AF.Tanh, bias=cb2c[:]).then_inc(c3, 1)
        q.wait_ge(c3, b3 + 7)
        q.copy(comprep[:], ph[0:16, :]).then_inc(c3, 1)
    blk.scalar(c3_scal)
    c3_done = b3 + 8

    pctx1.close()
    pctx2 = ExitStack()
    psc = pctx2.enter_context(nc.psum_tensor(tag + "_psc", [64, T], F32))
    pcl = pctx2.enter_context(nc.psum_tensor(tag + "_pcl", [8, T], F32))
    b1s = V("_c1s")
    b1v = V("_c1v")
    A("_c1s", 10)
    A("_c1v", 6)

    def c1_tens(q):
        q.wait_ge(c2e, b_e0 + NC)
        q.wait_ge(c3, c3_done)
        for k in range(NC):
            for ns in range(4):
                q.matmul(psc[:, 512 * ns:512 * (ns + 1)], sigsT[:, 64 * k:64 * (k + 1)],
                         xnT[:, k * T + 512 * ns:k * T + 512 * (ns + 1)],
                         start=(k == 0), stop=(k == NC - 1))
                ins = q.matmul(pcl[:, 512 * ns:512 * (ns + 1)], csigT[:, 8 * k:8 * (k + 1)],
                               xnT[:, k * T + 512 * ns:k * T + 512 * (ns + 1)],
                               start=(k == 0), stop=(k == NC - 1))
        ins.then_inc(c1s, 1)
    blk.tensor(c1_tens)

    def c1_scal(q):
        q.wait_ge(c1s, b1s + 1)
        q.wait_ge(c0e, b0e + 2)
        q.activation(nrm[0:64, :], psc[:], AF.Sigmoid, scale=cols[0:64, 0:1]).then_inc(c1s, 1)
        q.wait_ge(c1s, b1s + 2)
        q.activation(nrm[64:72, :], pcl[:], AF.Sigmoid, scale=cols[64:72, 0:1]).then_inc(c1s, 1)
        q.wait_ge(c1s, b1s + 3)
        q.activation(gacc[0][:], nrm[:], AF.Identity, scale=cols[:, 2:3],
                     bias=cols[:, 1:2]).then_inc(c1s, 1)
        for i in range(1, 7):
            q.wait_ge(c1s, b1s + 3 + i)
            if i >= 3:
                q.wait_ge(c1v, b1v + i - 2)
            q.activation(relu_sb[i % 2][:], nrm[:], AF.Relu, scale=7.0,
                         bias=cpool[0:72, CONST_COL[f"rm{i}"]:CONST_COL[f"rm{i}"] + 1]).then_inc(c1s, 1)
    blk.scalar(c1_scal)

    def c1_vec(q):
        for i in range(1, 7):
            q.wait_ge(c1s, b1s + 4 + i)
            if i >= 2:
                q.wait_ge(c1v, b1v + i - 1)
            q.scalar_tensor_tensor(gacc[i % 2][:], relu_sb[i % 2][:], cols[:, 2 + i:3 + i],
                                   gacc[(i - 1) % 2][:], ALU.mult, ALU.add).then_inc(c1v, 1)
    blk.vector(c1_vec)
    c1v_done = b1v + 6
    gfin = 0

    pctx2.close()
    pctx3 = ExitStack()
    pmsk = pctx3.enter_context(nc.psum_tensor(tag + "_pmsk", [64, T], F32))
    ptg = [pctx3.enter_context(nc.psum_tensor(tag + f"_ptg{i}", [128, 64], F32)) for i in range(2)]
    ptb = [pctx3.enter_context(nc.psum_tensor(tag + f"_ptb{i}", [64, 128], F32)) for i in range(2)]

    bt1 = V("_c2t")
    A("_c2t", NT)
    bv1 = V("_c2v")
    A("_c2v", 6 * NT)
    c2b = S("_c2b")
    bb1 = V("_c2b")
    A("_c2b", NT)
    be1 = V("_c2e")
    A("_c2e", NT)

    def c2_tensAB(q):
        q.wait_ge(c1v, c1v_done)
        for t in range(NT + 1):
            if t < NT:
                if t >= 2:
                    q.wait_ge(c2v, bv1 + 6 * (t - 1))
                ins = q.transpose(ptg[t % 2][:, 0:8], gacc[gfin][64:72, 128 * t:128 * (t + 1)],
                                  ident_f32[64:72, 64:72])
                ins.then_inc(c2t, 1)
            if t >= 1:
                tb = t - 1
                q.wait_ge(c2v, bv1 + 6 * tb + 6)
                if tb >= 2:
                    q.wait_ge(c2e, be1 + tb - 1)
                ins = q.transpose(ptb[tb % 2][0:8, :], coh_tok[:], ident_f32[:])
                ins.then_inc(c2b, 1)
    blk.tensor(c2_tensAB)

    def c2_vecA(q):
        for t in range(NT):
            q.wait_ge(c2t, bt1 + t + 1)
            n0 = bv1 + 6 * t
            q.reduce_max(mxc[:], ptg[t % 2][:, 0:8], axis=AX).then_inc(c2v, 1)
            q.wait_ge(c2v, n0 + 1)
            q.tensor_scalar(eqc[:], ptg[t % 2][:, 0:8], mxc[:], None, ALU.is_equal).then_inc(c2v, 1)
            q.wait_ge(c2v, n0 + 2)
            q.tensor_tensor(t1c[:], eqc[:], iota64m[:, 56:64], ALU.mult).then_inc(c2v, 1)
            q.wait_ge(c2v, n0 + 3)
            q.tensor_scalar(t2c[:], t1c[:], 8.0, None, ALU.add).then_inc(c2v, 1)
            q.wait_ge(c2v, n0 + 4)
            q.tensor_reduce(cidx[:], t2c[:], axis=AX, op=ALU.min).then_inc(c2v, 1)
            q.wait_ge(c2v, n0 + 5)
            if t >= 1:
                q.wait_ge(c2b, bb1 + t)
            q.tensor_scalar(coh_tok[:], iota64c[:, 0:8], cidx[:], None, ALU.is_equal).then_inc(c2v, 1)
    blk.vector(c2_vecA)

    def c2_scalA(q):
        for t in range(NT):
            q.wait_ge(c2b, bb1 + t + 1)
            q.copy(coh8T[:, 128 * t:128 * (t + 1)], ptb[t % 2][0:8, :]).then_inc(c2e, 1)
    blk.scalar(c2_scalA)

    bt3 = V("_c2t")
    A("_c2t", 1)
    bv2 = V("_c2v")
    A("_c2v", 3)
    bt4 = V("_c2t")
    A("_c2t", NT)
    bv3 = V("_c2v")
    A("_c2v", 6 * NT)
    bb2 = V("_c2b")
    A("_c2b", NT)
    be2 = V("_c2e")
    A("_c2e", NT)

    def c2_tensC(q):
        q.wait_ge(c2e, be1 + NT)
        for ns in range(4):
            ins = q.matmul(pmsk[:, 512 * ns:512 * (ns + 1)], expand_bf[:],
                           coh8T[:, 512 * ns:512 * (ns + 1)], start=True, stop=True)
        ins.then_inc(c2t, 1)
    blk.tensor(c2_tensC)

    def c2_vecB(q):
        q.wait_ge(c2t, bt3 + 1)
        q.tensor_tensor(gm_t1[:], gacc[gfin][0:64, :], pmsk[:], ALU.mult).then_inc(c2v, 1)
        q.wait_ge(c2v, bv2 + 1)
        q.tensor_scalar(gm_v[:], pmsk[:], 1.0, None, ALU.subtract).then_inc(c2v, 1)
        q.wait_ge(c2v, bv2 + 2)
        q.scalar_tensor_tensor(gm[:], gm_v[:], 3e8, gm_t1[:], ALU.mult, ALU.add).then_inc(c2v, 1)
    blk.vector(c2_vecB)

    def c2_tensDE(q):
        q.wait_ge(c2v, bv2 + 3)
        for t in range(NT + 1):
            if t < NT:
                if t >= 2:
                    q.wait_ge(c2v, bv3 + 6 * (t - 1))
                ins = q.transpose(ptg[t % 2][:], gm[:, 128 * t:128 * (t + 1)], ident_f32[0:64, 0:64])
                ins.then_inc(c2t, 1)
            if t >= 1:
                tb = t - 1
                q.wait_ge(c2v, bv3 + 6 * tb + 6)
                if tb >= 2:
                    q.wait_ge(c2e, be2 + tb - 1)
                ins = q.transpose(ptb[tb % 2][0:64, :], toh_tok[:], ident_f32[:])
                ins.then_inc(c2b, 1)
    blk.tensor(c2_tensDE)

    def c2_vecC(q):
        for t in range(NT):
            q.wait_ge(c2t, bt4 + t + 1)
            n0 = bv3 + 6 * t
            q.reduce_max(mxt[:], ptg[t % 2][:], axis=AX).then_inc(c2v, 1)
            q.wait_ge(c2v, n0 + 1)
            q.tensor_scalar(eqt[:], ptg[t % 2][:], mxt[:], None, ALU.is_equal).then_inc(c2v, 1)
            q.wait_ge(c2v, n0 + 2)
            q.tensor_tensor(t1t[:], eqt[:], iota64m[:], ALU.mult).then_inc(c2v, 1)
            q.wait_ge(c2v, n0 + 3)
            q.tensor_scalar(t2t[:], t1t[:], 64.0, None, ALU.add).then_inc(c2v, 1)
            q.wait_ge(c2v, n0 + 4)
            q.tensor_reduce(tidx[:], t2t[:], axis=AX, op=ALU.min).then_inc(c2v, 1)
            q.wait_ge(c2v, n0 + 5)
            if t >= 1:
                q.wait_ge(c2b, bb2 + t)
            q.tensor_scalar(toh_tok[:], iota64c[:], tidx[:], None, ALU.is_equal).then_inc(c2v, 1)
    blk.vector(c2_vecC)


    def c2_scalB(q):
        for t in range(NT):
            q.wait_ge(c2b, bb2 + t + 1)
            q.copy(tohfm[:, 128 * t:128 * (t + 1)], ptb[t % 2][0:64, :]).then_inc(c2e, 1)
    blk.scalar(c2_scalB)
    c2_done = be2 + NT

    pctx3.close()
    ctxB.close()
    ctxA.close()
    ctxC = ExitStack()
    sbC = lambda n, shp, dt: ctxC.enter_context(nc.sbuf_tensor(tag + n, list(shp), dt))
    hh_bf = sbC("_hh", (128, T), BF16)
    magp = sbC("_magp", (64, T), F32)
    xnt = [sbC(f"_xnt{i}", (128, D), F32) for i in range(2)]
    ysb = [sbC(f"_y{i}", (128, D), F32) for i in range(2)]
    ytl = [sbC(f"_ytl{i}", (128, D), F32) for i in range(2)]
    sqc = sbC("_sqc", (128, NT), F32)
    sqscr = sbC("_sqscr", (128, D), F32)
    nsum = sbC("_nsum", (128, NT), F32)
    nmu = sbC("_nmu", (128, NT), F32)
    musq = sbC("_musq", (128, NT), F32)
    varc = sbC("_varc", (128, NT), F32)
    sdev = sbC("_sdev", (128, NT), F32)
    rstd = sbC("_rstd", (128, NT), F32)
    bcol = sbC("_bcol", (128, NT), F32)
    gbc = sbC("_gbc", (128, D), F32)
    bbc2 = sbC("_bbc2", (128, D), F32)
    z1 = sbC("_z1", (128, D), F32)
    z2 = sbC("_z2", (128, D), F32)
    z3 = sbC("_z3", (128, D), F32)
    gsl = [sbC(f"_gsl{i}", (128, D), F32) for i in range(2)]
    xti = [sbC(f"_xti{i}", (128, D), F32) for i in range(2)]
    s0t = [sbC(f"_s0t{i}", (128, D), F32) for i in range(2)]
    tt1 = sbC("_tt1", (128, D), F32)
    x1sb = [sbC(f"_x1sb{i}", (128, D), F32) for i in range(2)]
    pctx4 = ExitStack()
    pHH = pctx4.enter_context(nc.psum_tensor(tag + "_pHH", [128, T], F32))
    pMag = pctx4.enter_context(nc.psum_tensor(tag + "_pMag", [64, T], F32))
    c4m = S("_c4m")
    c4s = S("_c4s")
    b4m = V("_c4m"); A("_c4m", 16)
    b4s = V("_c4s"); A("_c4s", 8)
    b4v = V("_c4v")
    A("_c4v", 2)

    def c4_tens(q):
        q.wait_ge(c3, c3_done)
        q.wait_ge(c2e, c2_done)
        for grp in range(8):
            if grp > 0:
                q.wait_ge(c4s, b4s + grp)
            for ns in range(4):
                ins = q.matmul(pHH[:, 512 * ns:512 * (ns + 1)],
                               w1gb[:, 128 * grp:128 * (grp + 1)],
                               comprep[:, 512 * ns:512 * (ns + 1)], start=True, stop=True)
            ins.then_inc(c4m, 1)
            q.wait_ge(c4s, b4s + grp + 1)
            for ns in range(4):
                ins = q.matmul(pMag[:, 512 * ns:512 * (ns + 1)],
                               w2gb[:, 64 * grp:64 * (grp + 1)],
                               hh_bf[:, 512 * ns:512 * (ns + 1)],
                               start=(grp == 0), stop=(grp == 7))
            ins.then_inc(c4m, 1)
    blk.tensor(c4_tens)

    def c4_scal(q):
        for grp in range(8):
            q.wait_ge(c4m, b4m + 2 * grp + 1)
            q.activation(hh_bf[:], pHH[:], AF.Relu, bias=b1gc[:, grp:grp + 1]).then_inc(c4s, 1)
    blk.scalar(c4_scal)

    def c4_vec(q):
        q.wait_ge(c4m, b4m + 16)
        q.wait_ge(c2e, c2_done)
        q.tensor_scalar(magp[:], pMag[:], b2c[:], cols[0:64, 9:10], ALU.add, ALU.mult).then_inc(c4v, 1)
        q.wait_ge(c4v, b4v + 1)
        q.tensor_tensor(selmag[:], magp[:], tohfm[:], ALU.mult).then_inc(c4v, 1)
    blk.vector(c4_vec)
    c4v_done = b4v + 2

    pctx4.close()
    pctx5 = ExitStack()
    pF = [pctx5.enter_context(nc.psum_tensor(tag + f"_pF{i}", [128, D], F32)) for i in range(2)]
    pGB = pctx5.enter_context(nc.psum_tensor(tag + "_pGB", [128, D], F32))
    pBB = pctx5.enter_context(nc.psum_tensor(tag + "_pBB", [128, D], F32))
    b5m = V("_c4m")
    A("_c4m", NT + 2)
    c5la = [S(f"_c5la{i}") for i in range(2)]
    b5la = [V(f"_c5la{i}") for i in range(2)]
    for t in range(NT):
        A(f"_c5la{t % 2}", 16)
    b5v = V("_c5v")
    A("_c5v", 2 * NT)
    b5s = V("_c5s")
    A("_c5s", NT)
    sty = [S(f"_sty{i}") for i in range(2)]
    b_sty = [V(f"_sty{i}") for i in range(2)]
    for t in range(NT):
        A(f"_sty{t % 2}", 16)
    b5e = V("_c5e")
    A("_c5e", 5)
    c5q = S("_c5q")
    b5q = V("_c5q")
    A("_c5q", 3)

    def c5_tens(q):
        q.wait_ge(c4v, c4v_done)
        for t in range(NT):
            par = t % 2
            if t >= 2:
                q.wait_ge(c5v, b5v + 2 * (t - 1))
            for ns in range(2):
                ins = q.matmul(pF[par][:, 512 * ns:512 * (ns + 1)],
                               selmag[:, 128 * t:128 * (t + 1)],
                               dirs_bf[:, 512 * ns:512 * (ns + 1)], start=True, stop=True)
            ins.then_inc(c4m, 1)
        for ns in range(2):
            ins = q.matmul(pGB[:, 512 * ns:512 * (ns + 1)], ones_row_sb[:],
                           gamr[0:1, 512 * ns:512 * (ns + 1)], start=True, stop=True)
        ins.then_inc(c4m, 1)
        for ns in range(2):
            ins = q.matmul(pBB[:, 512 * ns:512 * (ns + 1)], ones_row_sb[:],
                           betr[0:1, 512 * ns:512 * (ns + 1)], start=True, stop=True)
        ins.then_inc(c4m, 1)
    blk.tensor(c5_tens)

    def c5_sy(q):
        q.wait_ge(c4v, c4v_done)
        for t in range(NT):
            if t >= 2:
                q.wait_ge(c5v, b5v + 2 * (t - 1))
            q.dma_start(out=xnt[t % 2][:],
                        in_=xn_tok_hbm[128 * t:128 * (t + 1), :]).then_inc(c5la[t % 2], 16)
    blk.sync(c5_sy)

    def c5_vec(q):
        for t in range(NT):
            par = t % 2
            q.wait_ge(c5la[t % 2], b5la[t % 2] + 16 * (t // 2 + 1))
            q.wait_ge(c4m, b5m + t + 1)
            if t >= 2:
                q.wait_ge(c5s, b5s + t - 1)
                q.wait_ge(sty[par], b_sty[par] + 16 * ((t - 2) // 2 + 1))
            q.tensor_tensor(ysb[par][:], xnt[par][:], pF[par][:], ALU.add).then_inc(c5v, 1)
            q.wait_ge(c5v, b5v + 2 * t + 1)
            q.reduce_sum(nsum[:, t:t + 1], ysb[par][:],
                         axis=AX, negate=True).then_inc(c5v, 1)
    blk.vector(c5_vec)

    def c5_gp(q):
        for t in range(NT):
            par = t % 2
            q.wait_ge(c5v, b5v + 2 * t + 1)
            q.dma_start(out=y_hbm[128 * t:128 * (t + 1), :], in_=ysb[par][:]).then_inc(sty[par], 16)
    blk.gpsimd(c5_gp)

    def c5_scal(q):
        for t in range(NT):
            par = t % 2
            q.wait_ge(c5v, b5v + 2 * t + 1)
            if t > 0:
                q.wait_ge(c5s, b5s + t)
            q.activation(sqscr[:], ysb[par][:], AF.Square,
                         accum_out=sqc[:, t:t + 1]).then_inc(c5s, 1)
    blk.scalar(c5_scal)

    def c5_vec2(q):
        q.wait_ge(c5v, b5v + 2 * NT)
        q.wait_ge(c5s, b5s + NT)
        q.tensor_scalar(nmu[:], nsum[:], 1.0 / D, None, ALU.mult).then_inc(c5e, 1)
        q.wait_ge(c5e, b5e + 1)
        q.tensor_tensor(musq[:], nmu[:], nmu[:], ALU.mult).then_inc(c5e, 1)
        q.wait_ge(c5e, b5e + 2)
        q.scalar_tensor_tensor(varc[:], sqc[:], 1.0 / D, musq[:], ALU.mult, ALU.subtract).then_inc(c5e, 1)
    blk.vector(c5_vec2)

    def c5_scal2(q):
        q.wait_ge(c5e, b5e + 3)
        q.activation(sdev[:], varc[:], AF.Sqrt,
                     bias=cpool[:, CONST_COL["eps_ln"]:CONST_COL["eps_ln"] + 1]).then_inc(c5q, 1)
        q.wait_ge(c4m, b5m + NT + 2)
        q.copy(gbc[:], pGB[:]).then_inc(c5q, 1)
        q.wait_ge(c5q, b5q + 2)
        q.copy(bbc2[:], pBB[:]).then_inc(c5q, 1)
    blk.scalar(c5_scal2)

    def c5_vec3(q):
        q.wait_ge(c5q, b5q + 1)
        q.reciprocal(rstd[:], sdev[:]).then_inc(c5e, 1)
        q.wait_ge(c5e, b5e + 4)
        q.tensor_tensor(bcol[:], nmu[:], rstd[:], ALU.mult).then_inc(c5e, 1)
    blk.vector(c5_vec3)
    c5e_done = b5e + 5

    c5lb = [S(f"_c5lb{i}") for i in range(2)]
    b5lb = [V(f"_c5lb{i}") for i in range(2)]
    for t in range(NT):
        A(f"_c5lb{t % 2}", 48)
    b5w = V("_c5w")
    A("_c5w", 5 * NT)
    b5g = V("_c5g")
    A("_c5g", NT)
    b_stxa = [V(f"_stxa{i}") for i in range(2)]
    for t in range(NT):
        A(f"_stxa{t % 2}", 16)

    def c5_sy2(q):
        for s in range(2):
            q.wait_ge(sty[s], b_sty[s] + 16 * (NT // 2))
        for t in range(NT + 2):
            if t < NT:
                if t >= 2:
                    q.wait_ge(c5w, b5w + 5 * (t - 1))
                q.dma_start(out=xti[t % 2][:], in_=x_hbm[128 * t:128 * (t + 1), :]).then_inc(c5lb[t % 2], 16)
                q.dma_start(out=s0t[t % 2][:], in_=s0_hbm[128 * t:128 * (t + 1), :]).then_inc(c5lb[t % 2], 16)
                q.dma_start(out=ytl[t % 2][:], in_=y_hbm[128 * t:128 * (t + 1), :]).then_inc(c5lb[t % 2], 16)
            if t >= 2:
                ts = t - 2
                q.wait_ge(c5w, b5w + 5 * ts + 5)
                q.dma_start(out=x1_hbm[128 * ts:128 * (ts + 1), :],
                            in_=x1sb[ts % 2][:]).then_inc(stxa[ts % 2], 16)
        for s in range(2):
            q.wait_ge(stxa[s], b_stxa[s] + 16 * (NT // 2))
    blk.sync(c5_sy2)


    def c5_scal3(q):
        for t in range(NT):
            q.wait_ge(c5w, b5w + 5 * t + 3)
            if t >= 2:
                q.wait_ge(c5w, b5w + 5 * (t - 2) + 4)
            q.activation(gsl[t % 2][:], z3[:], AF.Silu).then_inc(c5g, 1)
    blk.scalar(c5_scal3)

    def c5_vec4(q):
        q.wait_ge(c5e, c5e_done)
        q.wait_ge(c5q, b5q + 3)
        for t in range(NT):
            par = t % 2
            q.wait_ge(c5lb[par], b5lb[par] + 48 * (t // 2 + 1))
            q.tensor_scalar(z1[:], ytl[t % 2][:], rstd[:, t:t + 1],
                            bcol[:, t:t + 1], ALU.mult, ALU.add).then_inc(c5w, 1)
            q.wait_ge(c5w, b5w + 5 * t + 1)
            q.tensor_tensor(z2[:], z1[:], gbc[:], ALU.mult).then_inc(c5w, 1)
            q.wait_ge(c5w, b5w + 5 * t + 2)
            if t >= 1:
                q.wait_ge(c5g, b5g + t)
            q.tensor_tensor(z3[:], z2[:], bbc2[:], ALU.add).then_inc(c5w, 1)
            q.wait_ge(c5w, b5w + 5 * t + 3)
            q.wait_ge(c5g, b5g + t + 1)
            q.tensor_tensor(tt1[:], gsl[par][:], s0t[par][:], ALU.mult).then_inc(c5w, 1)
            q.wait_ge(c5w, b5w + 5 * t + 4)
            if t >= 2:
                q.wait_ge(stxa[par], b_stxa[par] + 16 * ((t - 2) // 2 + 1))
            q.tensor_tensor(x1sb[par][:], tt1[:], xti[par][:], ALU.add).then_inc(c5w, 1)
    blk.vector(c5_vec4)

    pctx5.close()
    ctxC.close()
    ctx.close()


def build_l0_ffn_program():
    nc = bass.Bass("TRN2", target_bir_lowering=False)
    xnT = dram_io(nc, "xnT", (D, T), BF16, "ExternalInput")
    xn_tok = dram_io(nc, "xn_tok", (T, D), F32, "ExternalInput")
    x = dram_io(nc, "x", (T, D), F32, "ExternalInput")
    s0 = dram_io(nc, "s0", (T, D), F32, "ExternalInput")
    prm_names = {
        "sigraw": (64, D), "knots": (1, 8), "temp": (1, 1), "oscale": (1, 1),
        "cw1": (D, 256), "cb1_cols": (128, 2), "cw2": (256, 2), "cb2_col": (2, 1),
        "w1g": (16, 8 * 128), "b1g": (128, 8), "w2g": (128, 8 * 64), "b2_col": (64, 1),
        "dirs": (64, D), "gam_row": (1, D), "bet_row": (1, D),
        "comp_rep": (2, 16), "iota64": (128, 64), "iota64m": (128, 64),
        "expand8": (8, 64), "gmean": (64, 8),
    }
    prm = {k: dram_io(nc, "p_" + k, shp, F32, "ExternalInput") for k, shp in prm_names.items()}
    prm["b2_col"] = prm.pop("b2_col")
    scon = dram_io(nc, "sconsts", (128, len(CONST_NAMES)), F32, "ExternalInput")
    onesr = dram_io(nc, "ones_row", (1, 128), F32, "ExternalInput")
    identf = dram_io(nc, "ident", (128, 128), F32, "ExternalInput")
    yscr = dram_io(nc, "yscr", (T, D), F32, "ExternalOutput")
    x1 = dram_io(nc, "x1", (T, D), F32, "ExternalOutput")
    with ExitStack() as ctx:
        p = Prog(nc, ctx)
        cs = _const_loader(p, ctx, {"sconsts": (scon, (128, len(CONST_NAMES)), "f32"),
                                    "ones_row": (onesr, (1, 128), "f32"),
                                    "ident": (identf, (128, 128), "bf16")})
        pm = dict(prm)
        pm["b2_col"] = prm["b2_col"]
        emit_l0_ffn(p, xnT, xn_tok, x, s0, pm, cs["sconsts"], cs["ones_row"],
                    cs["ident_f32"], cs["ident"], yscr, x1)
    return nc



def emit_l1_qkv(p, xnT_hbm, wqkvT_hbm, ident_bf, qT_hbm, kT_hbm, ktok_hbm,
                vtok_hbm, tag="d1", xnT_res=None, wpre=None, res_out=None):
    nc, blk = p.nc, p.block
    ctx = ExitStack()
    sb = lambda n, shp, dt: ctx.enter_context(nc.sbuf_tensor(tag + n, list(shp), dt))
    pctx = ExitStack()
    psb = lambda n, shp, dt: pctx.enter_context(nc.psum_tensor(tag + n, list(shp), dt))
    HT = T // 2

    xnT = xnT_res if xnT_res is not None else sb("_xnT", (128, NC * T), BF16)
    if wpre is not None:
        wpre_sems, wb = wpre
    else:
        wb = sb("_w", (128, NC * 3 * D), BF16)
    rsc = [sb(f"_r{i}", (128, HT), F32) for i in range(2)]
    dsc = [sb(f"_d{i}", (128, HT), F32) for i in range(2)]
    esc = [sb(f"_e{i}", (128, HT), F32) for i in range(2)]
    phib = [sb(f"_phib{i}", (128, T), BF16) for i in range(2)]
    if res_out is not None:
        kTs, ktok, vtok = res_out
        VS = 65
    else:
        kTs = None
        ktok = sb("_ktok", (128, H * 16 * 64), BF16)
        vtok = sb("_vtok", (128, H * 16 * 64), BF16)
        VS = 64
    kres = kTs is not None
    pp = [psb(f"_pp{i}", (128, HT), F32) for i in range(2)]

    mm = p.sem(tag + "_mm")
    ract = p.sem(tag + "_ract")
    dv = p.sem(tag + "_dv")
    tp = p.sem(tag + "_tp")
    ev = p.sem(tag + "_ev")
    sta = [p.sem(tag + f"_sta{i}") for i in range(2)]
    stk = p.sem(tag + "_stk") if res_out is None else None

    b_mm = p.val(tag + "_mm")
    b_ract = p.val(tag + "_ract")
    b_dv = p.val(tag + "_dv")
    b_tp = p.val(tag + "_tp")
    b_ev = p.val(tag + "_ev")
    b_sta = [p.val(tag + f"_sta{i}") for i in range(2)]
    b_stk = p.val(tag + "_stk") if res_out is None else 0
    NM = 3 * NC

    def kind_of(m):
        return m // NC

    def nract(m):
        return sum(4 if kind_of(x) < 2 else 2 for x in range(m))

    def ndv(m):
        return sum(4 if kind_of(x) < 2 else 0 for x in range(m))

    def ntp(m):
        return sum(16 if kind_of(x) >= 1 else 0 for x in range(m))

    def nst(m):
        return sum(16 if kind_of(x) < 2 else 0 for x in range(m))

    for m in range(NM):
        p.add(tag + "_mm", 2)
        p.add(tag + "_ract", 4 if kind_of(m) < 2 else 2)
        if kind_of(m) < 2:
            p.add(tag + "_dv", 4)
            if kind_of(m) == 0 or not kres:
                p.add(tag + f"_sta{m % 2}", 16)
        if kind_of(m) >= 1:
            p.add(tag + "_tp", 16)
            p.add(tag + "_ev", 16)
    if res_out is None:
        p.add(tag + "_stk", 16 * 2 * H)
    ract_tot = p.val(tag + "_ract")
    dv_tot = p.val(tag + "_dv")
    ev_tot = p.val(tag + "_ev")
    sta_tot = [p.val(tag + f"_sta{i}") for i in range(2)]
    stk_tot = p.val(tag + "_stk") if res_out is None else 0

    gsems = [] if wpre is None else list(wpre_sems)

    def gp0(q):
        if xnT_res is None:
            gsems.append(p.swdge(q, xnT.rearrange("p (c t) -> p c t", c=NC),
                                 xnT_hbm.rearrange("(c p) t -> p c t", p=128)))
        if wpre is None:
            gsems.append(p.swdge(q, wb.rearrange("p (c o) -> p c o", c=NC),
                                 wqkvT_hbm.rearrange("(c p) o -> p c o", p=128)))
    if xnT_res is None or wpre is None:
        blk.gpsimd(gp0)

    ptt = [pctx.enter_context(nc.psum_tensor(tag + f"_ptt{i}", [128, 128], BF16)) for i in range(2)]

    def tens(q):
        for s, tgt in gsems:
            q.wait_ge(s, tgt)

        def transposes(mm2):
            kd2 = kind_of(mm2)
            par2 = mm2 % 2
            if kd2 == 1:
                q.wait_ge(dv, b_dv + ndv(mm2) + 4)
            else:
                q.wait_ge(ract, b_ract + nract(mm2) + 2)
            for j in range(16):
                g = ntp(mm2) + j
                if g >= 2:
                    q.wait_ge(ev, b_ev + g - 1)
                if kd2 == 1 and kres:
                    src = kTs[:, (mm2 % NC) * T + 128 * j:(mm2 % NC) * T + 128 * (j + 1)]
                else:
                    src = phib[par2][:, 128 * j:128 * (j + 1)]
                ins2 = q.transpose(ptt[g % 2][:], src, ident_bf[:])
                ins2.then_inc(tp, 1)

        for m in range(NM):
            kd = kind_of(m)
            par = m % 2
            for hf in range(2):
                if m >= 1:
                    if kind_of(m - 1) < 2:
                        q.wait_ge(dv, b_dv + ndv(m - 1) + 2 * hf + 1)
                        q.wait_ge(ract, b_ract + nract(m - 1) + 2 * hf + 1)
                    else:
                        q.wait_ge(ract, b_ract + nract(m - 1) + hf + 1)
                for k in range(NC):
                    for ns in range(2):
                        t0 = hf * HT + 512 * ns
                        ins = q.matmul(pp[hf][:, 512 * ns:512 * (ns + 1)],
                                       wb[:, k * 3 * D + 128 * m:k * 3 * D + 128 * (m + 1)],
                                       xnT[:, k * T + t0:k * T + t0 + 512],
                                       start=(k == 0), stop=(k == NC - 1))
                ins.then_inc(mm, 1)
            if m >= 1 and kind_of(m - 1) >= 1:
                transposes(m - 1)
        transposes(NM - 1)
    blk.tensor(tens)

    def scal2(q):
        for m in range(NM):
            kd = kind_of(m)
            par = m % 2
            if kd < 2:
                for hf in range(2):
                    q.wait_ge(mm, b_mm + 2 * m + hf + 1)
                    if m >= 1 and kind_of(m - 1) < 2:
                        q.wait_ge(dv, b_dv + ndv(m - 1) + 2 * hf + 2)
                    q.activation(rsc[hf][:], pp[hf][:], AF.Relu).then_inc(ract, 1)
                    q.wait_ge(dv, b_dv + ndv(m) + 2 * hf + 1)
                    q.activation(esc[hf][:], dsc[hf][:], AF.Exp).then_inc(ract, 1)
            else:
                for hf in range(2):
                    q.wait_ge(mm, b_mm + 2 * m + hf + 1)
                    if m >= 1 and kind_of(m - 1) >= 2:
                        q.wait_ge(tp, b_tp + ntp(m - 1) + 16)
                    if m >= 2 and kind_of(m - 2) == 1:
                        if not kres:
                            q.wait_ge(sta[par], b_sta[par] + 16 * ((m - 2) // 2 + 1))
                            q.wait_ge(tp, b_tp + ntp(m - 2) + 16)
                        else:
                            q.wait_ge(sta[par], b_sta[par] + 16 * 4)
                    q.activation(phib[par][:, HT * hf:HT * (hf + 1)], pp[hf][:],
                                 AF.Identity).then_inc(ract, 1)
            if kd >= 1:
                c = m % NC
                hA, hB = 2 * c, 2 * c + 1
                dstt, DS = (ktok, 64) if kd == 1 else (vtok, VS)
                for j in range(16):
                    g = ntp(m) + j
                    q.wait_ge(tp, b_tp + g + 1)
                    q.copy(dstt[:, hA * 16 * DS + DS * j:hA * 16 * DS + DS * j + 64],
                           ptt[g % 2][:, 0:64])
                    q.copy(dstt[:, hB * 16 * DS + DS * j:hB * 16 * DS + DS * j + 64],
                           ptt[g % 2][:, 64:128]).then_inc(ev, 1)
    blk.scalar(scal2)

    def vec(q):
        for m in range(NM):
            kd = kind_of(m)
            par = m % 2
            if kd < 2:
                for hf in range(2):
                    q.wait_ge(ract, b_ract + nract(m) + 2 * hf + 1)
                    if m >= 1 and kind_of(m - 1) < 2:
                        q.wait_ge(ract, b_ract + nract(m - 1) + 2 * hf + 2)
                    q.tensor_tensor(dsc[hf][:], pp[hf][:],
                                    rsc[hf][:], ALU.subtract).then_inc(dv, 1)
                    q.wait_ge(ract, b_ract + nract(m) + 2 * hf + 2)
                    if kd == 1 and kres:
                        q.tensor_tensor(kTs[:, (m % NC) * T + HT * hf:(m % NC) * T + HT * (hf + 1)],
                                        rsc[hf][:], esc[hf][:], ALU.add).then_inc(dv, 1)
                        continue
                    if m >= 2 and kind_of(m - 2) < 2:
                        q.wait_ge(sta[m % 2], b_sta[m % 2] + 16 * ((m - 2) // 2 + 1))
                        if kind_of(m - 2) == 1:
                            q.wait_ge(tp, b_tp + ntp(m - 2) + 16)
                    q.tensor_tensor(phib[par][:, HT * hf:HT * (hf + 1)], rsc[hf][:],
                                    esc[hf][:], ALU.add).then_inc(dv, 1)
    blk.vector(vec)

    def sy0(q):
        for m in range(NM):
            kd = kind_of(m)
            par = m % 2
            if kd == 0 or (kd == 1 and not kres):
                q.wait_ge(dv, b_dv + ndv(m) + 4)
                dst = qT_hbm if kd == 0 else kT_hbm
                q.dma_start(out=dst[128 * (m % NC):128 * (m % NC + 1), :],
                            in_=phib[par][:]).then_inc(sta[par], 16)
        if res_out is None:
            q.wait_ge(ev, b_ev + ev_tot - b_ev)
            for h in range(H):
                q.dma_start(out=ktok_hbm.rearrange("(h j p) e -> h p j e", p=128, j=16)[h],
                            in_=ktok.rearrange("p (h j e) -> p h j e", h=H, j=16)[:, h]).then_inc(stk, 16)
                q.dma_start(out=vtok_hbm.rearrange("(h j p) e -> h p j e", p=128, j=16)[h],
                            in_=vtok.rearrange("p (h j e) -> p h j e", h=H, j=16)[:, h]).then_inc(stk, 16)
            q.wait_ge(stk, stk_tot)
        for s2 in range(2):
            q.wait_ge(sta[s2], sta_tot[s2])
    blk.sync(sy0)
    pctx.close()
    ctx.close()



def emit_l1_scan(p, qT_hbm, kT_hbm, ktok_hbm, vtok_hbm, mask_hbm, ident_bf,
                 xoT_hbm, tag="d2", res_in=None, xoT_res=None):
    nc, blk = p.nc, p.block
    ctx = ExitStack()
    sb = lambda n, shp, dt: ctx.enter_context(nc.sbuf_tensor(tag + n, list(shp), dt))
    pctx = ExitStack()
    psb = lambda n, shp, dt: pctx.enter_context(nc.psum_tensor(tag + n, list(shp), dt))

    qT = sb("_qT", (128, NC * T), BF16)
    kload = res_in is not None and res_in[0] is None
    if res_in is not None:
        kT, ktok, vtok = res_in
        if kT is None:
            kT = sb("_kT", (128, NC * (T // 2)), BF16)
    else:
        kT = sb("_kT", (128, NC * T), BF16)
        ktok = sb("_ktok", (128, H * 16 * 64), BF16)
        vtok = sb("_vtok", (128, H * 16 * 65), BF16)
    maskf = sb("_mask", (128, 128), F32)
    Sf = [sb(f"_Sf{i}", (128, H * 65), F32) for i in range(2)]
    Sb = sb("_Sb", (128, H * 65), BF16)
    Am = [sb(f"_Am{i}", (128, 128), BF16) for i in range(2)]
    rcol = sb("_rcol", (128, 1), F32)
    xotok = [sb(f"_xo{i}", (128, 64), BF16) for i in range(2)]
    xoT = xoT_res if xoT_res is not None else sb("_xoT", (128, NC * T), BF16)
    pA = [psb(f"_pA{i}", (128, 128), F32) for i in range(2)]
    pO = [psb(f"_pO{i}", (128, 65), F32) for i in range(2)]
    pS = [psb(f"_pS{i}", (128, 65), F32) for i in range(2)]
    ptt = [psb(f"_ptt{i}", (64, 128), BF16) for i in range(2)]

    ld = p.sem(tag + "_ld")
    mz = p.sem(tag + "_mz")
    sA = p.sem(tag + "_sA")
    sM = p.sem(tag + "_sM")
    sO = p.sem(tag + "_sO")
    sSt = p.sem(tag + "_sSt")
    sSv = p.sem(tag + "_sSv")
    sSb = p.sem(tag + "_sSb")
    sDn = p.sem(tag + "_sDn")
    sTp = p.sem(tag + "_sTp")
    sEv = p.sem(tag + "_sEv")
    st = p.sem(tag + "_st")
    ldq = [p.sem(tag + f"_ldq{s}") for s in range(2)] if res_in is not None else None
    ldk = [p.sem(tag + f"_ldk{s}") for s in range(2)] if kload else None
    Bv = {}
    for n in ("_ld", "_mz", "_sA", "_sM", "_sO", "_sSt", "_sSv", "_sSb", "_sDn",
              "_sTp", "_sEv", "_st"):
        Bv[n] = p.val(tag + n)
    if res_in is not None:
        b_ldq = [p.val(tag + f"_ldq{s}") for s in range(2)]
        p.add(tag + "_ld", 16)
        for s in range(2):
            p.add(tag + f"_ldq{s}", 16)
        if kload:
            b_ldk = [p.val(tag + f"_ldk{s}") for s in range(2)]
            for s in range(2):
                p.add(tag + f"_ldk{s}", 16)
    else:
        p.add(tag + "_ld", 16 * 5)
    p.add(tag + "_mz", 3)
    NJH = 16 * H
    for n, d in (("_sA", NJH), ("_sM", NJH), ("_sO", NJH), ("_sSt", NJH),
                 ("_sSv", NJH), ("_sSb", NJH), ("_sDn", 2 * NJH), ("_sTp", NJH),
                 ("_sEv", NJH)):
        p.add(tag + n, d)
    if xoT_res is None:
        p.add(tag + "_st", 16 * NC)

    idx = lambda j, h: j * H + h

    def sy0(q):
        q.dma_start(out=qT.rearrange("p (c t) -> p c t", c=NC),
                    in_=qT_hbm.rearrange("(c p) t -> p c t", p=128)).then_inc(ld, 16)
        q.dma_start(out=kT.rearrange("p (c t) -> p c t", c=NC),
                    in_=kT_hbm.rearrange("(c p) t -> p c t", p=128)).then_inc(ld, 16)
        q.dma_start(out=ktok.rearrange("p (h j e) -> p h j e", h=H, j=16),
                    in_=ktok_hbm.rearrange("(h j p) e -> p h j e", p=128, j=16)).then_inc(ld, 16)
        q.dma_start(out=vtok.rearrange("p (h j e) -> p h j e", h=H, j=16)[:, :, :, 0:64],
                    in_=vtok_hbm.rearrange("(h j p) e -> p h j e", p=128, j=16)).then_inc(ld, 16)
        q.dma_start(out=maskf[:], in_=mask_hbm[:]).then_inc(ld, 16)

    def sy0r(q):
        q.dma_start(out=maskf[:], in_=mask_hbm[:]).then_inc(ld, 16)
        for s in range(4):
            if s < 2:
                q.dma_start(out=qT.rearrange("p (c t) -> p c t", c=NC)[:, :, 1024 * s:1024 * (s + 1)],
                            in_=qT_hbm.rearrange("(c p) t -> p c t", p=128)[:, :, 1024 * s:1024 * (s + 1)]).then_inc(ldq[s], 16)
            if kload and s < 2:
                if s == 1:
                    q.wait_ge(sA, Bv["_sA"] + idx(7, H - 1) + 1)
                q.dma_start(out=kT.rearrange("p (c t) -> p c t", c=NC),
                            in_=kT_hbm.rearrange("(c p) t -> p c t", p=128)[:, :, 1024 * s:1024 * (s + 1)]).then_inc(ldk[s], 16)
    blk.sync(sy0 if res_in is None else sy0r)

    def vz(q):
        q.wait_ge(ld, Bv["_ld"] + (16 if res_in is not None else 16 * 5))
        q.memset(vtok.rearrange("p (h j e) -> p h j e", h=H, j=16)[:, :, :, 64:65], 1.0).then_inc(mz, 1)
        q.memset(Sf[1][:], 0.0).then_inc(mz, 1)
        q.memset(Sb[:], 0.0).then_inc(mz, 1)
    blk.vector(vz)

    def tens(q):
        q.wait_ge(mz, Bv["_mz"] + 3)
        for j in range(16):
            if res_in is not None and j % 8 == 0:
                q.wait_ge(ldq[j // 8], b_ldq[j // 8] + 16)
                if kload and j % 8 == 0:
                    q.wait_ge(ldk[j // 8], b_ldk[j // 8] + 16)
            jk = (j % 8) if kload else j
            for h in range(H):
                i = idx(j, h)
                par = i % 2
                c, rb = h // 2, 64 * (h % 2)
                if i >= 2:
                    q.wait_ge(sM, Bv["_sM"] + i - 1)
                KW = (T // 2) if kload else T
                ins = q.matmul(pA[par][:], kT[rb:rb + 64, c * KW + 128 * jk:c * KW + 128 * (jk + 1)],
                               qT[rb:rb + 64, c * T + 128 * j:c * T + 128 * (j + 1)],
                               start=True, stop=True)
                ins.then_inc(sA, 1)
                q.wait_ge(sM, Bv["_sM"] + i + 1)
                if i >= 2:
                    q.wait_ge(sDn, Bv["_sDn"] + 2 * (i - 1))
                if j > 0:
                    q.wait_ge(sSb, Bv["_sSb"] + idx(j - 1, h) + 1)
                q.matmul(pO[par][:], Am[par][:],
                         vtok[:, h * 16 * 65 + 65 * j:h * 16 * 65 + 65 * (j + 1)],
                         start=True, stop=False)
                ins = q.matmul(pO[par][:], qT[rb:rb + 64, c * T + 128 * j:c * T + 128 * (j + 1)],
                               Sb[rb:rb + 64, h * 65:(h + 1) * 65], start=False, stop=True)
                ins.then_inc(sO, 1)
                if i >= 2:
                    q.wait_ge(sSv, Bv["_sSv"] + i - 1)
                ins = q.matmul(pS[par][rb:rb + 64, :],
                               ktok[:, h * 16 * 64 + 64 * j:h * 16 * 64 + 64 * (j + 1)],
                               vtok[:, h * 16 * 65 + 65 * j:h * 16 * 65 + 65 * (j + 1)],
                               start=True, stop=True)
                ins.then_inc(sSt, 1)
                q.wait_ge(sDn, Bv["_sDn"] + 2 * i + 2)
                if i >= 2:
                    q.wait_ge(sEv, Bv["_sEv"] + i - 1)
                ins = q.transpose(ptt[par][:], xotok[par][:], ident_bf[:])
                ins.then_inc(sTp, 1)
    blk.tensor(tens)

    def vec(q):
        for j in range(16):
            for h in range(H):
                i = idx(j, h)
                par = i % 2
                jpar = j % 2
                q.wait_ge(sA, Bv["_sA"] + i + 1)
                if i >= 2:
                    q.wait_ge(sO, Bv["_sO"] + i - 1)
                q.tensor_tensor(Am[par][:], pA[par][:], maskf[:], ALU.mult).then_inc(sM, 1)
                q.wait_ge(sSt, Bv["_sSt"] + i + 1)
                if j > 0:
                    q.wait_ge(sSb, Bv["_sSb"] + idx(j - 1, h) + 1)
                c2, rb2 = h // 2, 64 * (h % 2)
                q.tensor_tensor(Sf[jpar][rb2:rb2 + 64, h * 65:(h + 1) * 65],
                                pS[par][rb2:rb2 + 64, :],
                                Sf[1 - jpar][rb2:rb2 + 64, h * 65:(h + 1) * 65],
                                ALU.add).then_inc(sSv, 1)
                q.wait_ge(sO, Bv["_sO"] + i + 1)
                q.reciprocal(rcol[:], pO[par][:, 64:65]).then_inc(sDn, 1)
                q.wait_ge(sDn, Bv["_sDn"] + 2 * i + 1)
                if i >= 2:
                    q.wait_ge(sTp, Bv["_sTp"] + i - 1)
                q.tensor_scalar(xotok[par][:], pO[par][:, 0:64], rcol[:], None,
                                ALU.mult).then_inc(sDn, 1)
    blk.vector(vec)

    def scal(q):
        for j in range(16):
            for h in range(H):
                i = idx(j, h)
                par = i % 2
                jpar = j % 2
                q.wait_ge(sSv, Bv["_sSv"] + i + 1)
                c2, rb2 = h // 2, 64 * (h % 2)
                q.copy(Sb[rb2:rb2 + 64, h * 65:(h + 1) * 65],
                       Sf[jpar][rb2:rb2 + 64, h * 65:(h + 1) * 65]).then_inc(sSb, 1)
                q.wait_ge(sTp, Bv["_sTp"] + i + 1)
                c, rb = h // 2, 64 * (h % 2)
                q.copy(xoT[rb:rb + 64, c * T + 128 * j:c * T + 128 * (j + 1)],
                       ptt[par][:]).then_inc(sEv, 1)
    blk.scalar(scal)

    def sy1(q):
        for cc in range(NC):
            q.wait_ge(sEv, Bv["_sEv"] + idx(15, 2 * cc + 1) + 1)
            q.dma_start(out=xoT_hbm[128 * cc:128 * (cc + 1), :],
                        in_=xoT[:, cc * T:(cc + 1) * T]).then_inc(st, 16)
        q.wait_ge(st, Bv["_st"] + 16 * NC)
    if xoT_res is None:
        blk.sync(sy1)
    pctx.close()
    ctx.close()



def emit_l1_ffn_in(p, xnT_hbm, w1T_hbm, w2T_hbm, s1c_hbm, s2c_hbm, h_hbm, tag="e1",
                   xnT_res=None):
    nc, blk = p.nc, p.block
    ctx = ExitStack()
    sb = lambda n, shp, dt: ctx.enter_context(nc.sbuf_tensor(tag + n, list(shp), dt))
    pctx = ExitStack()
    psb = lambda n, shp, dt: pctx.enter_context(nc.psum_tensor(tag + n, list(shp), dt))
    HT = T // 2
    NH = DHID // 128

    xnT = xnT_res if xnT_res is not None else sb("_xnT", (128, NC * T), BF16)
    w1b = sb("_w1", (128, NC * DHID), BF16)
    w2b = sb("_w2", (128, NC * DHID), BF16)
    s1c = sb("_s1c", (128, NH), F32)
    s2c = sb("_s2c", (128, NH), F32)
    tsl = [sb(f"_tsl{i}", (128, HT), F32) for i in range(2)]
    tg2 = [sb(f"_tg2{i}", (128, HT), F32) for i in range(2)]
    hb = [sb(f"_hb{i}", (128, T), BF16) for i in range(2)]
    pb1 = [psb(f"_pb1{i}", (128, HT), F32) for i in range(2)]
    pb2 = [psb(f"_pb2{i}", (128, HT), F32) for i in range(2)]

    ld = p.sem(tag + "_ld")
    mm = p.sem(tag + "_mm")
    ac = p.sem(tag + "_ac")
    dv = p.sem(tag + "_dv")
    stf = [p.sem(tag + f"_stf{i}") for i in range(2)]
    b_ld = p.val(tag + "_ld")
    b_mm = p.val(tag + "_mm")
    b_ac = p.val(tag + "_ac")
    b_dv = p.val(tag + "_dv")
    b_stf = [p.val(tag + f"_stf{i}") for i in range(2)]
    p.add(tag + "_ld", 32)
    NI = NH * 2
    for i in range(NI):
        p.add(tag + "_mm", 2)
        p.add(tag + "_ac", 1)
        p.add(tag + "_dv", 2)
    for m in range(NH):
        p.add(tag + f"_stf{m % 2}", 16)

    gsems = []

    def gp0(q):
        if xnT_res is None:
            gsems.append(p.swdge(q, xnT.rearrange("p (c t) -> p c t", c=NC),
                                 xnT_hbm.rearrange("(c p) t -> p c t", p=128)))
        gsems.append(p.swdge(q, w1b.rearrange("p (c o) -> p c o", c=NC),
                             w1T_hbm.rearrange("(c p) o -> p c o", p=128)))
        gsems.append(p.swdge(q, w2b.rearrange("p (c o) -> p c o", c=NC),
                             w2T_hbm.rearrange("(c p) o -> p c o", p=128)))
    blk.gpsimd(gp0)

    def sy0(q):
        q.dma_start(out=s1c[:], in_=s1c_hbm[:]).then_inc(ld, 16)
        q.dma_start(out=s2c[:], in_=s2c_hbm[:]).then_inc(ld, 16)
    blk.sync(sy0)

    def tens(q):
        q.wait_ge(ld, b_ld + 32)
        for s, tgt in gsems:
            q.wait_ge(s, tgt)
        for i in range(NI):
            m, hf = i // 2, i % 2
            par = i % 2
            if i >= 2:
                q.wait_ge(ac, b_ac + i - 1)
                q.wait_ge(dv, b_dv + 2 * (i - 2) + 1)
            for k in range(NC):
                for ns in range(2):
                    t0 = hf * HT + 512 * ns
                    ins = q.matmul(pb1[par][:, 512 * ns:512 * (ns + 1)],
                                   w1b[:, k * DHID + 128 * m:k * DHID + 128 * (m + 1)],
                                   xnT[:, k * T + t0:k * T + t0 + 512],
                                   start=(k == 0), stop=(k == NC - 1))
            ins.then_inc(mm, 1)
            for k in range(NC):
                for ns in range(2):
                    t0 = hf * HT + 512 * ns
                    ins = q.matmul(pb2[par][:, 512 * ns:512 * (ns + 1)],
                                   w2b[:, k * DHID + 128 * m:k * DHID + 128 * (m + 1)],
                                   xnT[:, k * T + t0:k * T + t0 + 512],
                                   start=(k == 0), stop=(k == NC - 1))
            ins.then_inc(mm, 1)
    blk.tensor(tens)

    def scal(q):
        for i in range(NI):
            m, hf = i // 2, i % 2
            par = i % 2
            q.wait_ge(mm, b_mm + 2 * i + 1)
            if i >= 2:
                q.wait_ge(dv, b_dv + 2 * (i - 2) + 2)
            q.activation(tsl[par][:], pb1[par][:], AF.Silu,
                         scale=s1c[:, m:m + 1]).then_inc(ac, 1)
    blk.scalar(scal)

    def vec(q):
        for i in range(NI):
            m, hf = i // 2, i % 2
            par = i % 2
            q.wait_ge(mm, b_mm + 2 * i + 2)
            if i >= 2:
                q.wait_ge(dv, b_dv + 2 * (i - 2) + 2)
            q.tensor_scalar(tg2[par][:], pb2[par][:], s2c[:, m:m + 1], None,
                            ALU.mult).then_inc(dv, 1)
            q.wait_ge(dv, b_dv + 2 * i + 1)
            q.wait_ge(ac, b_ac + i + 1)
            if i >= 4:
                mprev = (i - 4) // 2
                q.wait_ge(stf[mprev % 2], b_stf[mprev % 2] + 16 * (mprev // 2 + 1))
            q.tensor_tensor(hb[m % 2][:, HT * hf:HT * (hf + 1)], tsl[par][:],
                            tg2[par][:], ALU.mult).then_inc(dv, 1)
    blk.vector(vec)

    def sy1(q):
        for m in range(NH):
            q.wait_ge(dv, b_dv + 2 * (2 * m + 1) + 2)
            q.dma_start(out=h_hbm[128 * m:128 * (m + 1), :], in_=hb[m % 2][:]).then_inc(stf[m % 2], 16)
        for s2 in range(2):
            q.wait_ge(stf[s2], b_stf[s2] + 16 * (NH // 2))
    blk.sync(sy1)
    pctx.close()
    ctx.close()


def emit_l1_ffn_in_v2(p, xnT_hbm, w1T_hbm, w2T_hbm, s1c_hbm, s2c_hbm, hres, tag="e1"):
    nc, blk = p.nc, p.block
    ctx = ExitStack()
    sb = lambda n, shp, dt: ctx.enter_context(nc.sbuf_tensor(tag + n, list(shp), dt))
    pctx = ExitStack()
    psb = lambda n, shp, dt: pctx.enter_context(nc.psum_tensor(tag + n, list(shp), dt))
    HT = T // 2
    NH = DHID // 128

    xnT = sb("_xnT", (128, NC * T), BF16)
    s1c = sb("_s1c", (128, NH), F32)
    s2c = sb("_s2c", (128, NH), F32)
    w1s = [sb(f"_w1s{i}", (128, NC * 128), F32) for i in range(2)]
    w2s = [sb(f"_w2s{i}", (128, NC * 128), F32) for i in range(2)]
    w1c = [sb(f"_w1c{i}", (128, NC * 128), BF16) for i in range(2)]
    w2c = [sb(f"_w2c{i}", (128, NC * 128), BF16) for i in range(2)]
    tsl = [sb(f"_tsl{i}", (128, HT), F32) for i in range(2)]
    tg2 = [sb(f"_tg2{i}", (128, HT), F32) for i in range(2)]
    pb1 = [psb(f"_pb1{i}", (128, HT), F32) for i in range(2)]
    pb2 = [psb(f"_pb2{i}", (128, HT), F32) for i in range(2)]

    ld = p.sem(tag + "_ld")
    wld = [p.sem(tag + f"_wld{i}") for i in range(2)]
    w2ld = [p.sem(tag + f"_w2ld{i}") for i in range(2)]
    pc = p.sem(tag + "_pc")
    mm = p.sem(tag + "_mm")
    ac = p.sem(tag + "_ac")
    dv = p.sem(tag + "_dv")
    b_ld = p.val(tag + "_ld")
    b_wld = [p.val(tag + f"_wld{i}") for i in range(2)]
    b_w2ld = [p.val(tag + f"_w2ld{i}") for i in range(2)]
    b_pc = p.val(tag + "_pc")
    b_mm = p.val(tag + "_mm")
    b_ac = p.val(tag + "_ac")
    b_dv = p.val(tag + "_dv")
    p.add(tag + "_ld", 32)
    NI = NH * 2
    for m in range(NH):
        p.add(tag + f"_wld{m % 2}", 16)
        p.add(tag + f"_w2ld{m % 2}", 16)
        p.add(tag + "_pc", 2)
    for i in range(NI):
        p.add(tag + "_mm", 2)
        p.add(tag + "_ac", 1)
        p.add(tag + "_dv", 2)

    gsems = []

    def gpx(q):
        gsems.append(p.swdge(q, xnT.rearrange("p (c t) -> p c t", c=NC),
                             xnT_hbm.rearrange("(c p) t -> p c t", p=128)))
    blk.gpsimd(gpx)

    def sy0(q):
        q.dma_start(out=s1c[:], in_=s1c_hbm[:]).then_inc(ld, 16)
        q.dma_start(out=s2c[:], in_=s2c_hbm[:]).then_inc(ld, 16)
        for m in range(NH):
            par = m % 2
            if m >= 2:
                q.wait_ge(pc, b_pc + 2 * (m - 1))
            q.dma_start(out=w1s[par].rearrange("p (k o) -> p k o", k=NC),
                        in_=w1T_hbm.rearrange("(k p) o -> p k o", p=128)[:, :, 128 * m:128 * (m + 1)]).then_inc(wld[par], 16)
            q.dma_start(out=w2s[par].rearrange("p (k o) -> p k o", k=NC),
                        in_=w2T_hbm.rearrange("(k p) o -> p k o", p=128)[:, :, 128 * m:128 * (m + 1)]).then_inc(w2ld[par], 16)
    blk.sync(sy0)

    def gp0(q):
        for m in range(NH):
            par = m % 2
            q.wait_ge(wld[par], b_wld[par] + 16 * (m // 2 + 1))
            q.wait_ge(w2ld[par], b_w2ld[par] + 16 * (m // 2 + 1))
            if m >= 2:
                q.wait_ge(mm, b_mm + 4 * (m - 1))
            q.tensor_copy(w1c[par][:], w1s[par][:]).then_inc(pc, 1)
            q.tensor_copy(w2c[par][:], w2s[par][:]).then_inc(pc, 1)
    blk.gpsimd(gp0)

    def tens(q):
        q.wait_ge(ld, b_ld + 32)
        for s, tgt in gsems:
            q.wait_ge(s, tgt)
        for i in range(NI):
            m, hf = i // 2, i % 2
            par = i % 2
            wpar = m % 2
            if hf == 0:
                q.wait_ge(pc, b_pc + 2 * m + 2)
            if i >= 2:
                q.wait_ge(ac, b_ac + i - 1)
                q.wait_ge(dv, b_dv + 2 * (i - 2) + 1)
            for k in range(NC):
                for ns in range(2):
                    t0 = hf * HT + 512 * ns
                    ins = q.matmul(pb1[par][:, 512 * ns:512 * (ns + 1)],
                                   w1c[wpar][:, 128 * k:128 * (k + 1)],
                                   xnT[:, k * T + t0:k * T + t0 + 512],
                                   start=(k == 0), stop=(k == NC - 1))
            ins.then_inc(mm, 1)
            for k in range(NC):
                for ns in range(2):
                    t0 = hf * HT + 512 * ns
                    ins = q.matmul(pb2[par][:, 512 * ns:512 * (ns + 1)],
                                   w2c[wpar][:, 128 * k:128 * (k + 1)],
                                   xnT[:, k * T + t0:k * T + t0 + 512],
                                   start=(k == 0), stop=(k == NC - 1))
            ins.then_inc(mm, 1)
    blk.tensor(tens)

    def scal(q):
        for i in range(NI):
            m, hf = i // 2, i % 2
            par = i % 2
            q.wait_ge(mm, b_mm + 2 * i + 1)
            if i >= 2:
                q.wait_ge(dv, b_dv + 2 * (i - 2) + 2)
            q.activation(tsl[par][:], pb1[par][:], AF.Silu,
                         scale=s1c[:, m:m + 1]).then_inc(ac, 1)
    blk.scalar(scal)

    def vec(q):
        for i in range(NI):
            m, hf = i // 2, i % 2
            par = i % 2
            q.wait_ge(mm, b_mm + 2 * i + 2)
            if i >= 2:
                q.wait_ge(dv, b_dv + 2 * (i - 2) + 2)
            q.tensor_scalar(tg2[par][:], pb2[par][:], s2c[:, m:m + 1], None,
                            ALU.mult).then_inc(dv, 1)
            q.wait_ge(dv, b_dv + 2 * i + 1)
            q.wait_ge(ac, b_ac + i + 1)
            q.tensor_tensor(hres[:, m * T + HT * hf:m * T + HT * (hf + 1)],
                            tsl[par][:], tg2[par][:], ALU.mult).then_inc(dv, 1)
    blk.vector(vec)
    pctx.close()
    ctx.close()



def emit_l1_ffn_mid(p, in_hbm, wT_hbm, sc_hbm, out_hbm, tag="e2", hin_res=None):
    nc, blk = p.nc, p.block
    ctx = ExitStack()
    sb = lambda n, shp, dt: ctx.enter_context(nc.sbuf_tensor(tag + n, list(shp), dt))
    pctx = ExitStack()
    psb = lambda n, shp, dt: pctx.enter_context(nc.psum_tensor(tag + n, list(shp), dt))
    NH = DHID // 128

    hin = hin_res if hin_res is not None else sb("_hin", (128, NH * T), BF16)
    scb = sb("_sc", (128, NH), F32)
    wstg = [sb(f"_ws{i}", (128, NH * 128), F32) for i in range(2)]
    wcol = [sb(f"_wc{i}", (128, NH * 128), BF16) for i in range(2)]
    ob = [sb(f"_ob{i}", (128, T), BF16) for i in range(2)]
    pp = [psb(f"_pp{i}", (128, T), F32) for i in range(2)]

    ld = p.sem(tag + "_ld")
    wld = [p.sem(tag + f"_wld{i}") for i in range(2)]
    pc = p.sem(tag + "_pc")
    mm = p.sem(tag + "_mm")
    ac = p.sem(tag + "_ac")
    stf = [p.sem(tag + f"_stf{i}") for i in range(2)]
    b_ld = p.val(tag + "_ld")
    b_wld = [p.val(tag + f"_wld{i}") for i in range(2)]
    b_pc = p.val(tag + "_pc")
    b_mm = p.val(tag + "_mm")
    b_ac = p.val(tag + "_ac")
    b_stf = [p.val(tag + f"_stf{i}") for i in range(2)]
    p.add(tag + "_ld", 16)
    for m in range(NH):
        p.add(tag + f"_wld{m % 2}", 16)
        p.add(tag + "_pc", 1)
        p.add(tag + "_mm", 1)
        p.add(tag + "_ac", 1)
        p.add(tag + f"_stf{m % 2}", 16)

    hsem = []

    def gp0(q):
        if hin_res is None:
            hsem.append(p.swdge(q, hin.rearrange("p (c t) -> p c t", c=NH),
                                in_hbm.rearrange("(c p) t -> p c t", p=128)))
        for m in range(NH):
            par = m % 2
            q.wait_ge(wld[par], b_wld[par] + 16 * (m // 2 + 1))
            if m >= 2:
                q.wait_ge(mm, b_mm + m - 1)
            q.tensor_copy(wcol[par][:], wstg[par][:]).then_inc(pc, 1)
    blk.gpsimd(gp0)

    def sy0(q):
        q.dma_start(out=scb[:], in_=sc_hbm[:]).then_inc(ld, 16)
        for m in range(NH):
            par = m % 2
            if m >= 2:
                q.wait_ge(pc, b_pc + m - 1)
            q.dma_start(out=wstg[par].rearrange("p (k o) -> p k o", k=NH),
                        in_=wT_hbm.rearrange("(k p) o -> p k o", p=128)[:, :, 128 * m:128 * (m + 1)]).then_inc(wld[par], 16)
    blk.sync(sy0)

    def tens(q):
        q.wait_ge(ld, b_ld + 16)
        if hin_res is None:
            q.wait_ge(hsem[0][0], hsem[0][1])
        for m in range(NH):
            par = m % 2
            q.wait_ge(pc, b_pc + m + 1)
            if m >= 2:
                q.wait_ge(ac, b_ac + m - 1)
            for k in range(NH):
                for ns in range(4):
                    ins = q.matmul(pp[par][:, 512 * ns:512 * (ns + 1)],
                                   wcol[par][:, 128 * k:128 * (k + 1)],
                                   hin[:, k * T + 512 * ns:k * T + 512 * (ns + 1)],
                                   start=(k == 0), stop=(k == NH - 1))
            ins.then_inc(mm, 1)
    blk.tensor(tens)

    def scal(q):
        for m in range(NH):
            par = m % 2
            q.wait_ge(mm, b_mm + m + 1)
            if m >= 2:
                q.wait_ge(stf[par], b_stf[par] + 16 * ((m - 2) // 2 + 1))
            q.activation(ob[par][:], pp[par][:], AF.Gelu,
                         scale=scb[:, m:m + 1]).then_inc(ac, 1)
            q.wait_ge(ac, b_ac + m + 1)
            q.dma_start(out=out_hbm[128 * m:128 * (m + 1), :], in_=ob[par][:]).then_inc(stf[par], 16)
        for s2 in range(2):
            q.wait_ge(stf[s2], b_stf[s2] + 16 * (NH // 2))
    blk.scalar(scal)
    pctx.close()
    ctx.close()


def emit_l1_ffn_out(p, h2_hbm, w3T_hbm, s3r_hbm, x1_hbm, s1_hbm, ones_row_sb,
                    out_hbm, tag="e4"):
    nc, blk = p.nc, p.block
    ctx = ExitStack()
    sb = lambda n, shp, dt: ctx.enter_context(nc.sbuf_tensor(tag + n, list(shp), dt))
    pctx = ExitStack()
    psb = lambda n, shp, dt: pctx.enter_context(nc.psum_tensor(tag + n, list(shp), dt))
    NH = DHID // 128

    HT2 = T // 2
    hin = sb("_hin", (128, NH * HT2), BF16)
    w3b = sb("_w3", (128, NH * D), BF16)
    s3r = sb("_s3r", (1, D), F32)
    s3bc = sb("_s3bc", (128, D), F32)
    zsc = [sb(f"_zsc{i}", (128, D), F32) for i in range(2)]
    gsl = [sb(f"_gsl{i}", (128, D), F32) for i in range(2)]
    x1t = [sb(f"_x1t{i}", (128, D), F32) for i in range(2)]
    s1t = [sb(f"_s1t{i}", (128, D), F32) for i in range(2)]
    tt1 = sb("_tt1", (128, D), F32)
    x2sb = [sb(f"_x2sb{i}", (128, D), F32) for i in range(2)]
    pp = [psb(f"_pp{i}", (128, D), F32) for i in range(2)]
    pB = psb("_pB", (128, D), F32)

    ld = p.sem(tag + "_ld")
    lda = [p.sem(tag + f"_lda{i}") for i in range(2)]
    mm = p.sem(tag + "_mm")
    ac = p.sem(tag + "_ac")
    dv = p.sem(tag + "_dv")
    ev = p.sem(tag + "_ev")
    stf = [p.sem(tag + f"_stf{i}") for i in range(2)]
    b_ld = p.val(tag + "_ld")
    b_lda = [p.val(tag + f"_lda{i}") for i in range(2)]
    b_mm = p.val(tag + "_mm")
    b_ac = p.val(tag + "_ac")
    b_dv = p.val(tag + "_dv")
    b_ev = p.val(tag + "_ev")
    b_stf = [p.val(tag + f"_stf{i}") for i in range(2)]
    p.add(tag + "_ld", 16)
    p.add(tag + "_mm", 1 + NT)
    p.add(tag + "_ev", 1)
    p.add(tag + "_ac", NT)
    p.add(tag + "_dv", 3 * NT)
    for t in range(NT):
        p.add(tag + f"_lda{t % 2}", 32)
        p.add(tag + f"_stf{t % 2}", 16)

    gsems = []
    hsems = []

    def gp0(q):
        wv = w3b.rearrange("p (c o) -> p c o", c=NH)
        wh = w3T_hbm.rearrange("(c p) o -> p c o", p=128)
        hv = hin.rearrange("p (c t) -> p c t", c=NH)
        hh = h2_hbm.rearrange("(c p) t -> p c t", p=128)
        for s in range(2):
            gsems.append(p.swdge(q, wv[:, 8 * s:8 * (s + 1)], wh[:, 8 * s:8 * (s + 1)]))
            hsems.append(p.swdge(q, hv[:, :, 512 * s:512 * (s + 1)],
                                 hh[:, :, 512 * s:512 * (s + 1)]))
        for s in range(2, 4):
            gsems.append(p.swdge(q, wv[:, 8 * s:8 * (s + 1)], wh[:, 8 * s:8 * (s + 1)]))
        for s in range(2):
            q.wait_ge(mm, b_mm + 1 + 4 * (s + 1))
            hsems.append(p.swdge(q, hv[:, :, 512 * s:512 * (s + 1)],
                                 hh[:, :, HT2 + 512 * s:HT2 + 512 * (s + 1)]))
    blk.gpsimd(gp0)

    def sy0(q):
        q.dma_start(out=s3r[:], in_=s3r_hbm[:]).then_inc(ld, 16)
        for t in range(NT + 2):
            if t < NT:
                if t >= 2:
                    q.wait_ge(dv, b_dv + 3 * (t - 2) + 3)
                q.dma_start(out=x1t[t % 2][:], in_=x1_hbm[128 * t:128 * (t + 1), :]).then_inc(lda[t % 2], 16)
                q.dma_start(out=s1t[t % 2][:], in_=s1_hbm[128 * t:128 * (t + 1), :]).then_inc(lda[t % 2], 16)
            if t >= 2:
                ts2 = t - 2
                q.wait_ge(dv, b_dv + 3 * ts2 + 3)
                q.dma_start(out=out_hbm[128 * ts2:128 * (ts2 + 1), :],
                            in_=x2sb[ts2 % 2][:]).then_inc(stf[ts2 % 2], 16)
        for s2 in range(2):
            q.wait_ge(stf[s2], b_stf[s2] + 16 * (NT // 2))
    blk.sync(sy0)

    def tens(q):
        q.wait_ge(ld, b_ld + 16)
        for ns in range(2):
            ins = q.matmul(pB[:, 512 * ns:512 * (ns + 1)], ones_row_sb[:],
                           s3r[0:1, 512 * ns:512 * (ns + 1)], start=True, stop=True)
        ins.then_inc(mm, 1)
        for t in range(NT):
            par = t % 2
            hf2, tl = t // (NT // 2), t % (NT // 2)
            if tl % 4 == 0:
                s2, tgt = hsems[2 * hf2 + tl // 4]
                q.wait_ge(s2, tgt)
            if t >= 2:
                q.wait_ge(dv, b_dv + 3 * (t - 2) + 1)
            for k in range(NH):
                if t == 0 and k % 8 == 0:
                    s2, tgt = gsems[k // 8]
                    q.wait_ge(s2, tgt)
                for ns in range(2):
                    ins = q.matmul(pp[par][:, 512 * ns:512 * (ns + 1)],
                                   hin[:, k * HT2 + 128 * tl:k * HT2 + 128 * (tl + 1)],
                                   w3b[:, k * D + 512 * ns:k * D + 512 * (ns + 1)],
                                   start=(k == 0), stop=(k == NH - 1))
            ins.then_inc(mm, 1)
    blk.tensor(tens)

    def scal(q):
        q.wait_ge(mm, b_mm + 1)
        q.copy(s3bc[:], pB[:]).then_inc(ev, 1)
        for t in range(NT):
            par = t % 2
            q.wait_ge(dv, b_dv + 3 * t + 1)
            if t >= 2:
                q.wait_ge(dv, b_dv + 3 * (t - 2) + 2)
            q.activation(gsl[par][:], zsc[par][:], AF.Silu).then_inc(ac, 1)
    blk.scalar(scal)

    def vec(q):
        q.wait_ge(ev, b_ev + 1)
        for t in range(NT):
            par = t % 2
            q.wait_ge(mm, b_mm + 1 + t + 1)
            if t >= 2:
                q.wait_ge(ac, b_ac + t - 1)
            q.tensor_tensor(zsc[par][:], pp[par][:], s3bc[:], ALU.mult).then_inc(dv, 1)
            q.wait_ge(ac, b_ac + t + 1)
            q.wait_ge(lda[par], b_lda[par] + 32 * (t // 2 + 1))
            q.tensor_tensor(tt1[:], gsl[par][:], s1t[par][:], ALU.mult).then_inc(dv, 1)
            q.wait_ge(dv, b_dv + 3 * t + 2)
            if t >= 2:
                q.wait_ge(stf[par], b_stf[par] + 16 * ((t - 2) // 2 + 1))
            q.tensor_tensor(x2sb[par][:], tt1[:], x1t[par][:], ALU.add).then_inc(dv, 1)
    blk.vector(vec)

    pctx.close()
    ctx.close()


def build_l1_qkv_program():
    nc = bass.Bass("TRN2", target_bir_lowering=False)
    xnT = dram_io(nc, "xnT", (D, T), BF16, "ExternalInput")
    wqkvT = dram_io(nc, "wqkvT", (D, 3 * D), F32, "ExternalInput")
    identf = dram_io(nc, "ident", (128, 128), F32, "ExternalInput")
    qT = dram_io(nc, "qT", (D, T), BF16, "ExternalOutput")
    kT = dram_io(nc, "kT", (D, T), BF16, "ExternalOutput")
    ktok = dram_io(nc, "ktok", (H * 16 * 128, 64), BF16, "ExternalOutput")
    vtok = dram_io(nc, "vtok", (H * 16 * 128, 64), BF16, "ExternalOutput")
    with ExitStack() as ctx:
        p = Prog(nc, ctx)
        cs = _const_loader(p, ctx, {"ident": (identf, (128, 128), "bf16")})
        emit_l1_qkv(p, xnT, wqkvT, cs["ident"], qT, kT, ktok, vtok)
    return nc


def build_l1_scan_program():
    nc = bass.Bass("TRN2", target_bir_lowering=False)
    qT = dram_io(nc, "qT", (D, T), BF16, "ExternalInput")
    kT = dram_io(nc, "kT", (D, T), BF16, "ExternalInput")
    ktok = dram_io(nc, "ktok", (H * 16 * 128, 64), BF16, "ExternalInput")
    vtok = dram_io(nc, "vtok", (H * 16 * 128, 64), BF16, "ExternalInput")
    maskm = dram_io(nc, "maskm", (128, 128), F32, "ExternalInput")
    identf = dram_io(nc, "ident", (128, 128), F32, "ExternalInput")
    xoT = dram_io(nc, "xoT", (D, T), BF16, "ExternalOutput")
    with ExitStack() as ctx:
        p = Prog(nc, ctx)
        cs = _const_loader(p, ctx, {"ident": (identf, (128, 128), "bf16")})
        emit_l1_scan(p, qT, kT, ktok, vtok, maskm, cs["ident"], xoT)
    return nc


def build_l1_ffn_in_program():
    nc = bass.Bass("TRN2", target_bir_lowering=False)
    xnT = dram_io(nc, "xnT", (D, T), BF16, "ExternalInput")
    w1T = dram_io(nc, "w1T", (D, DHID), F32, "ExternalInput")
    w2T = dram_io(nc, "w2T", (D, DHID), F32, "ExternalInput")
    s1c = dram_io(nc, "s1c", (128, DHID // 128), F32, "ExternalInput")
    s2c = dram_io(nc, "s2c", (128, DHID // 128), F32, "ExternalInput")
    h = dram_io(nc, "h", (DHID, T), BF16, "ExternalOutput")
    with ExitStack() as ctx:
        p = Prog(nc, ctx)
        emit_l1_ffn_in(p, xnT, w1T, w2T, s1c, s2c, h)
    return nc


def build_l1_ffn_mid_program():
    nc = bass.Bass("TRN2", target_bir_lowering=False)
    hin = dram_io(nc, "hin", (DHID, T), BF16, "ExternalInput")
    wT = dram_io(nc, "wT", (DHID, DHID), F32, "ExternalInput")
    sc = dram_io(nc, "sc", (128, DHID // 128), F32, "ExternalInput")
    hout = dram_io(nc, "hout", (DHID, T), BF16, "ExternalOutput")
    with ExitStack() as ctx:
        p = Prog(nc, ctx)
        emit_l1_ffn_mid(p, hin, wT, sc, hout)
    return nc


def build_l1_ffn_out_program():
    nc = bass.Bass("TRN2", target_bir_lowering=False)
    h2 = dram_io(nc, "h2", (DHID, T), BF16, "ExternalInput")
    w3T = dram_io(nc, "w3T", (DHID, D), F32, "ExternalInput")
    s3r = dram_io(nc, "s3r", (1, D), F32, "ExternalInput")
    x1 = dram_io(nc, "x1", (T, D), F32, "ExternalInput")
    s1 = dram_io(nc, "s1", (T, D), F32, "ExternalInput")
    onesr = dram_io(nc, "ones_row", (1, 128), F32, "ExternalInput")
    out = dram_io(nc, "out", (T, D), F32, "ExternalOutput")
    with ExitStack() as ctx:
        p = Prog(nc, ctx)
        cs = _const_loader(p, ctx, {"ones_row": (onesr, (1, 128), "f32")})
        emit_l1_ffn_out(p, h2, w3T, s3r, x1, s1, cs["ones_row"], out)
    return nc




def build_fused_program():
    nc = bass.Bass("TRN2", target_bir_lowering=False)
    EI = lambda n, shp, dt=F32: dram_io(nc, n, shp, dt, "ExternalInput")
    IN = lambda n, shp, dt=F32: dram_io(nc, n, shp, dt, "Internal")

    x = EI("x", (T, D))
    identf = EI("ident", (128, 128))
    scon = EI("sconsts", (128, len(CONST_NAMES)))
    onesr = EI("ones_row", (1, 128))
    wq = EI("wq", (D, D))
    wkv = EI("wkv", (D, D))
    wout = EI("wout", (D, D))
    bout_row = EI("bout_row", (1, D))
    cos2 = EI("cos2", (128, T))
    sin2 = EI("sin2", (128, T))
    rot128 = EI("rot128", (128, 128))
    prm_shapes = {
        "sigraw": (64, D), "knots": (1, 8), "temp": (1, 1), "oscale": (1, 1),
        "cw1": (D, 256), "cb1_cols": (128, 2), "cw2": (256, 2), "cb2_col": (2, 1),
        "w1g": (16, 8 * 128), "b1g": (128, 8), "w2g": (128, 8 * 64), "b2_col": (64, 1),
        "dirs": (64, D), "gam_row": (1, D), "bet_row": (1, D),
        "comp_rep": (2, 16), "iota64": (128, 64), "iota64m": (128, 64),
        "expand8": (8, 64), "gmean": (64, 8),
    }
    prm = {k: EI("p_" + k, shp) for k, shp in prm_shapes.items()}
    wqkvT = EI("wqkvT", (D, 3 * D))
    woutT1 = EI("woutT1", (D, D))
    maskm = EI("maskm", (128, 128))
    w1T = EI("w1T", (D, DHID))
    w2T = EI("w2T", (D, DHID))
    s1c = EI("s1c", (128, DHID // 128))
    s2c = EI("s2c", (128, DHID // 128))
    h1T = EI("h1T", (DHID, DHID))
    h1sc = EI("h1sc", (128, DHID // 128))
    h2T = EI("h2T", (DHID, DHID))
    h2sc = EI("h2sc", (128, DHID // 128))
    w3T = EI("w3T", (DHID, D))
    s3r = EI("s3r", (1, D))
    out = dram_io(nc, "out", (T, D), F32, "ExternalOutput")

    xn_tok = IN("i_xn_tok", (T, D))
    xnT = IN("i_xnT", (D, T), BF16)
    qT0 = IN("i_qT0", (D, T), BF16)
    kvT0 = IN("i_kvT0", (D, T), BF16)
    kvtok = IN("i_kvtok", (H * 16 * 128, 65), BF16)
    xoT0 = IN("i_xoT0", (D, T), BF16)
    s0 = IN("i_s0", (T, D))
    yscr = IN("i_yscr", (T, D))
    x1 = IN("i_x1", (T, D))
    xn1T = IN("i_xn1T", (D, T), BF16)
    q1T = IN("i_q1T", (D, T), BF16)
    k1T = IN("i_k1T", (D, T), BF16)
    ktok1 = IN("i_ktok1", (H * 16 * 128, 64), BF16)
    vtok1 = IN("i_vtok1", (H * 16 * 128, 64), BF16)
    xo1T = IN("i_xo1T", (D, T), BF16)
    s1 = IN("i_s1", (T, D))
    hmat = IN("i_h", (DHID, T), BF16)
    h1m = IN("i_h1", (DHID, T), BF16)
    h2m = IN("i_h2", (DHID, T), BF16)

    with ExitStack() as ctx:
        p = Prog(nc, ctx)
        cs = _const_loader(p, ctx, {"ident": (identf, (128, 128), "bf16"),
                                    "sconsts": (scon, (128, len(CONST_NAMES)), "f32"),
                                    "ones_row": (onesr, (1, 128), "f32")})
        cpool, ones_sb, idb, idf = cs["sconsts"], cs["ones_row"], cs["ident"], cs["ident_f32"]
        lnx = ExitStack()
        ln1xt = [lnx.enter_context(nc.sbuf_tensor(f"sh_lnxt{i}", [128, D], F32))
                 for i in range(2)]
        lctx = ExitStack()
        xnTs0 = lctx.enter_context(nc.sbuf_tensor("sh_xnT0", [128, NC * T], BF16))
        s01 = ExitStack()
        qTs = s01.enter_context(nc.sbuf_tensor("sh_qT0", [128, NC * T], BF16))
        kvTs = s01.enter_context(nc.sbuf_tensor("sh_kvT0", [128, NC * T], BF16))
        ktokS = s01.enter_context(nc.sbuf_tensor("sh_ktok0", [128, H * 16 * 65], BF16))
        bctx = ExitStack()
        bsb = lambda n, shp, dt: bctx.enter_context(nc.sbuf_tensor("pf" + n, list(shp), dt))
        b1b = {"wqb": bsb("_wq", (128, NC * D), BF16),
               "wkvb": bsb("_wkv", (128, NC * D), BF16),
               "cosb": bsb("_cos", (128, T), BF16),
               "sinf": bsb("_sin", (128, T), F32),
               "rotb": bsb("_rot", (128, 128), BF16)}
        b1pre = []

        def preb1(q):
            b1pre.append(p.swdge(q, b1b["wqb"].rearrange("p (c o) -> p c o", c=NC),
                                 wq.rearrange("(c p) o -> p c o", p=128)))
            b1pre.append(p.swdge(q, b1b["wkvb"].rearrange("p (c o) -> p c o", c=NC),
                                 wkv.rearrange("(c p) o -> p c o", p=128)))
            b1pre.append(p.swdge(q, b1b["cosb"][:], cos2[:]))
            b1pre.append(p.swdge(q, b1b["sinf"][:], sin2[:]))
            b1pre.append(p.swdge(q, b1b["rotb"][:], rot128[:]))
        p.block.gpsimd(preb1)
        emit_ln(p, x, xn_tok, xnT, idb, cpool, "ln0", xnT_res=xnTs0)
        p.barrier_sync()
        emit_l0_qkv(p, xnT, wq, wkv, cos2, sin2, rot128, idb, qT0, kvT0, kvtok, tag="b1",
                    xnT_res=xnTs0, res_out=(qTs, kvTs, ktokS), pre=(b1pre, b1b))
        bctx.close()
        p.barrier_sync()
        emit_l0_attn(p, qT0, kvT0, kvtok, cpool, ones_sb, xoT0, tag="b2",
                     res_in=(qTs, kvTs, ktokS))
        s01.close()
        p.barrier_sync()
        emit_out_proj(p, xoT0, wout, bout_row, ones_sb, s0, tag="b3", with_bias=True)
        p.barrier_sync()
        emit_l0_ffn(p, xnT, xn_tok, x, s0, prm, cpool, ones_sb, idf, idb, yscr, x1, tag="c",
                    xnT_res=xnTs0)
        lctx.close()
        xg_sems = [p.sem("c_stxa0"), p.sem("c_stxa1")]
        xg_base = [p.val("c_stxa0") - 16 * (NT // 2), p.val("c_stxa1") - 16 * (NT // 2)]
        l2 = ExitStack()
        xn1Ts = l2.enter_context(nc.sbuf_tensor("sh_xnT1", [128, NC * T], BF16))
        dsh = ExitStack()
        ktokS1 = dsh.enter_context(nc.sbuf_tensor("sh_ktok1", [128, H * 16 * 64], BF16))
        vtokS1 = dsh.enter_context(nc.sbuf_tensor("sh_vtok1", [128, H * 16 * 65], BF16))
        dctx = ExitStack()
        wbs1 = dctx.enter_context(nc.sbuf_tensor("pf_wqkv", [128, NC * 3 * D], BF16))
        d1pre = []
        c_fin = [(p.sem("c_sty0"), p.val("c_sty0")), (p.sem("c_sty1"), p.val("c_sty1")),
                 (p.sem("c_stxa0"), p.val("c_stxa0")), (p.sem("c_stxa1"), p.val("c_stxa1"))]

        def pred1(q):
            for s2, v2 in c_fin:
                q.wait_ge(s2, v2)
            d1pre.append(p.swdge(q, wbs1.rearrange("p (c o) -> p c o", c=NC),
                                 wqkvT.rearrange("(c p) o -> p c o", p=128)))
        p.block.gpsimd(pred1)
        emit_ln(p, x1, None, xn1T, idb, cpool, "ln1", write_xn_tok=False, xnT_res=xn1Ts,
                store_res=True, xt_res=ln1xt, xgate=(xg_sems, xg_base), defer_store=True)
        p.barrier_sync()
        p.add("ln1_stT", 16 * NC)
        emit_l1_qkv(p, xn1T, wqkvT, idb, q1T, k1T, ktok1, vtok1, tag="d1",
                    xnT_res=xn1Ts, wpre=(d1pre, wbs1), res_out=(None, ktokS1, vtokS1))
        dctx.close()
        p.barrier_sync()
        emit_l1_scan(p, q1T, k1T, ktok1, vtok1, maskm, idb, xo1T, tag="d2",
                     res_in=(None, ktokS1, vtokS1))
        dsh.close()
        p.barrier_sync()
        emit_out_proj(p, xo1T, woutT1, bout_row, ones_sb, s1, tag="d3", with_bias=False)
        p.barrier_sync()
        l2.close()
        lnx.close()
        ectx = ExitStack()
        hres = ectx.enter_context(nc.sbuf_tensor("sh_h", [128, (DHID // 128) * T], BF16))
        emit_l1_ffn_in_v2(p, xn1T, w1T, w2T, s1c, s2c, hres, tag="e1")
        p.barrier_sync()
        emit_l1_ffn_mid(p, hmat, h1T, h1sc, h1m, tag="e2", hin_res=hres)
        ectx.close()
        p.barrier_sync()
        emit_l1_ffn_mid(p, h1m, h2T, h2sc, h2m, tag="e3")
        p.barrier_sync()
        emit_l1_ffn_out(p, h2m, w3T, s3r, x1, s1, ones_sb, out, tag="e4")
        nsems = len(p.hsems) + len([k for k in p.hsems])
    return nc


def kernel_fused(**inputs):
    sh, xs = host_prep(inputs)
    nc = _prog("fused", build_fused_program)
    ims = []
    for c in range(B):
        im = {"x": xs[c], "ident": sh["ident"], "sconsts": sh["sconsts"],
              "ones_row": sh["ones_row"], "wq": sh["wq"], "wkv": sh["wkv"],
              "wout": sh["wout"], "bout_row": sh["bout_row"], "cos2": sh["cos2"],
              "sin2": sh["sin2"], "rot128": sh["rot128"], "wqkvT": sh["wqkvT"],
              "woutT1": sh["woutT1"], "maskm": sh["mask_incl"], "w1T": sh["w1T"],
              "w2T": sh["w2T"], "s1c": sh["w1s"], "s2c": sh["w2s"],
              "h1T": sh["h1T"], "h1sc": sh["h1s"], "h2T": sh["h2T"],
              "h2sc": sh["h2s"], "w3T": sh["w3T"], "s3r": sh["w3s_row"]}
        for k2 in ("sigraw", "knots", "temp", "oscale", "cw1", "cb1_cols", "cw2",
                   "cb2_col", "w1g", "b1g", "w2g", "b2_col", "dirs", "gam_row",
                   "bet_row", "comp_rep", "iota64", "iota64m", "expand8", "gmean"):
            im["p_" + k2] = sh[k2]
        ims.append(im)
    res = run_bass_kernel_spmd(nc, ims, core_ids=list(range(B)))
    return np.stack([np.asarray(r["out"]) for r in res.results], axis=0)


_PROG_CACHE = {}


def _prog(name, builder, *args, **kw):
    key = (name,) + tuple(args) + tuple(sorted(kw.items()))
    if key not in _PROG_CACHE:
        _PROG_CACHE[key] = builder(*args, **kw)
    return _PROG_CACHE[key]


def _launch(nc, in_maps, outs):
    res = run_bass_kernel_spmd(nc, in_maps, core_ids=list(range(B)))
    return [{k: np.asarray(r[k]) for k in outs} for r in res.results]


def kernel(**inputs):
    return kernel_fused(**inputs)


def kernel_multilaunch(**inputs):
    sh, xs = host_prep(inputs)
    shared_con = {"ident": sh["ident"], "sconsts": sh["sconsts"], "ones_row": sh["ones_row"]}

    nc = _prog("ln0", build_ln_program, True)
    r = _launch(nc, [{"x": xs[c], "ident": sh["ident"], "sconsts": sh["sconsts"]} for c in range(B)],
                ["xn_tok", "xnT"])
    xn_tok = [r[c]["xn_tok"] for c in range(B)]
    xnT = [r[c]["xnT"] for c in range(B)]

    nc = _prog("b1", build_l0_qkv_program)
    r = _launch(nc, [{"xnT": xnT[c], "wq": sh["wq"], "wkv": sh["wkv"], "cos2": sh["cos2"],
                      "sin2": sh["sin2"], "rot128": sh["rot128"], "ident": sh["ident"]}
                     for c in range(B)], ["qT", "kvT", "kvtok"])
    qT = [r[c]["qT"] for c in range(B)]
    kvT = [r[c]["kvT"] for c in range(B)]
    kvtok = [r[c]["kvtok"] for c in range(B)]

    nc = _prog("b2", build_l0_attn_program)
    r = _launch(nc, [{"qT": qT[c], "kvT": kvT[c], "kvtok": kvtok[c],
                      "sconsts": sh["sconsts"], "ones_row": sh["ones_row"]}
                     for c in range(B)], ["xoT"])
    xoT = [r[c]["xoT"] for c in range(B)]

    nc = _prog("b3", build_out_proj_program, True)
    r = _launch(nc, [{"xoT": xoT[c], "w": sh["wout"], "brow": sh["bout_row"],
                      "ones_row": sh["ones_row"]} for c in range(B)], ["s_out"])
    s0 = [r[c]["s_out"] for c in range(B)]

    nc = _prog("c", build_l0_ffn_program)
    ims = []
    for c in range(B):
        im = {"xnT": xnT[c], "xn_tok": xn_tok[c], "x": xs[c], "s0": s0[c],
              "sconsts": sh["sconsts"], "ones_row": sh["ones_row"], "ident": sh["ident"]}
        for k2 in ("sigraw", "knots", "temp", "oscale", "cw1", "cb1_cols", "cw2",
                   "cb2_col", "w1g", "b1g", "w2g", "b2_col", "dirs", "gam_row",
                   "bet_row", "comp_rep", "iota64", "iota64m", "expand8", "gmean"):
            im["p_" + k2] = sh[k2]
        ims.append(im)
    r = _launch(nc, ims, ["x1"])
    x1 = [r[c]["x1"] for c in range(B)]

    nc = _prog("ln1", build_ln_program, False)
    r = _launch(nc, [{"x": x1[c], "ident": sh["ident"], "sconsts": sh["sconsts"]}
                     for c in range(B)], ["xnT"])
    xn1T = [r[c]["xnT"] for c in range(B)]

    nc = _prog("d1", build_l1_qkv_program)
    r = _launch(nc, [{"xnT": xn1T[c], "wqkvT": sh["wqkvT"], "ident": sh["ident"]}
                     for c in range(B)], ["qT", "kT", "ktok", "vtok"])
    q1T = [r[c]["qT"] for c in range(B)]
    k1T = [r[c]["kT"] for c in range(B)]
    ktok1 = [r[c]["ktok"] for c in range(B)]
    vtok1 = [r[c]["vtok"] for c in range(B)]

    nc = _prog("d2", build_l1_scan_program)
    r = _launch(nc, [{"qT": q1T[c], "kT": k1T[c], "ktok": ktok1[c], "vtok": vtok1[c],
                      "maskm": sh["mask_incl"], "ident": sh["ident"]} for c in range(B)],
                ["xoT"])
    xo1T = [r[c]["xoT"] for c in range(B)]

    nc = _prog("d3", build_out_proj_program, False)
    r = _launch(nc, [{"xoT": xo1T[c], "w": sh["woutT1"], "brow": sh["bout_row"],
                      "ones_row": sh["ones_row"]} for c in range(B)], ["s_out"])
    s1 = [r[c]["s_out"] for c in range(B)]

    nc = _prog("e1", build_l1_ffn_in_program)
    r = _launch(nc, [{"xnT": xn1T[c], "w1T": sh["w1T"], "w2T": sh["w2T"],
                      "s1c": sh["w1s"], "s2c": sh["w2s"]} for c in range(B)], ["h"])
    hmat = [r[c]["h"] for c in range(B)]

    nc = _prog("e2", build_l1_ffn_mid_program)
    r = _launch(nc, [{"hin": hmat[c], "wT": sh["h1T"], "sc": sh["h1s"]} for c in range(B)],
                ["hout"])
    h1m = [r[c]["hout"] for c in range(B)]
    r = _launch(nc, [{"hin": h1m[c], "wT": sh["h2T"], "sc": sh["h2s"]} for c in range(B)],
                ["hout"])
    h2m = [r[c]["hout"] for c in range(B)]

    nc = _prog("e4", build_l1_ffn_out_program)
    r = _launch(nc, [{"h2": h2m[c], "w3T": sh["w3T"], "s3r": sh["w3s_row"], "x1": x1[c],
                      "s1": s1[c], "ones_row": sh["ones_row"]} for c in range(B)], ["out"])
    out = np.stack([r[c]["out"] for c in range(B)], axis=0)
    return out
